# version 19
# baseline (speedup 1.0000x reference)
import math
import numpy as np
import ml_dtypes
import concourse.bass as bass
import concourse.mybir as mybir
import concourse.bass_isa as bass_isa
from concourse.bass_utils import run_bass_kernel_spmd

F32 = mybir.dt.float32
BF16 = mybir.dt.bfloat16
AF = mybir.ActivationFunctionType
ALU = mybir.AluOpType
AX = mybir.AxisListType

DM = 1024
NCH = DM // 128
INNER0 = 512
FFN_INNER = 4096
SCALING1 = float(128 ** -0.5)

CONST_VALS = sorted({0.0, 1.0, -1.0, 0.5, -0.5, 16.0, -16.0,
                     1e-5, 1e-12, 0.0625, 1.0 / DM, 1.0 / FFN_INNER,
                     float(np.float32(0.5 * math.pi))})


class P:
    def __init__(self, nc):
        self.nc = nc
        self.eng = {'t': nc.tensor, 'v': nc.vector, 's': nc.scalar,
                    'g': nc.gpsimd, 'y': nc.sync}
        self.chain = {k: nc.alloc_semaphore(f"ch_{k}") for k in 'tvsg'}
        self.cnt = {k: 0 for k in 'tvsg'}
        self.dsem = {}
        self.dcnt = {}
        self.lastw = {}

    def dma_sem(self, name):
        if name not in self.dsem:
            self.dsem[name] = self.nc.alloc_semaphore(f"dma_{name}")
            self.dcnt[name] = 0
        return self.dsem[name]

    def tok(self, e):
        return (e, self.cnt[e])

    def _sem_of(self, token):
        if token[0] == 'd':
            return self.dsem[token[1]], token[2]
        return self.chain[token[0]], token[1]

    def wait(self, e, token):
        if token is None:
            return
        if token[0] == e:
            return
        sem, val = self._sem_of(token)
        if val <= 0:
            return
        key = (e, id(sem))
        if self.lastw.get(key, -1) >= val:
            return
        self.eng[e].wait_ge(sem, val)
        self.lastw[key] = val

    def _chain(self, e, instr):
        c = self.cnt[e]
        if c > 0:
            instr._wait_ge(self.chain[e], c)
        instr.then_inc(self.chain[e], 1)
        self.cnt[e] += 1
        return (e, self.cnt[e])

    def op(self, e, name, *args, waits=(), **kw):
        for t in waits:
            self.wait(e, t)
        instr = getattr(self.eng[e], name)(*args, **kw)
        return self._chain(e, instr)

    def tt(self, out, a, b, alu, waits=(), e='v'):
        return self.op(e, 'tensor_tensor', out, a, b, alu, waits=waits)

    def ts(self, out, a, s1, s2=None, op0=ALU.mult, op1=None, waits=(), e='v'):
        if op1 is None:
            return self.op(e, 'tensor_scalar', out, a, s1, None, op0, waits=waits)
        return self.op(e, 'tensor_scalar', out, a, s1, s2, op0, op1, waits=waits)

    def stt(self, out, in0, scalar, in1, op0, op1, waits=(), e='v'):
        return self.op(e, 'scalar_tensor_tensor', out, in0, scalar, in1, op0, op1,
                       waits=waits)

    def copy(self, out, a, waits=(), e='v'):
        return self.op(e, 'tensor_copy', out, a, waits=waits)

    def red(self, out, a, alu, absval=None, waits=(), e='v'):
        return self.op(e, 'tensor_reduce', out, a, AX.X, alu,
                       apply_absolute_value=absval, waits=waits)

    def act(self, out, a, func, bias=0.0, scale=1.0, waits=(), accum_out=None):
        return self.op('s', 'activation', out, a, func, bias=bias, scale=scale,
                       accum_out=accum_out, waits=waits)

    def memset(self, ap, val, waits=(), e='v'):
        return self.op(e, 'memset', ap, val, waits=waits)

    def gred(self, out, in_, rop, waits=()):
        return self.op('g', 'partition_all_reduce', out, in_, 128, rop, waits=waits)

    def mm(self, out, lhsT, rhs, start, stop, waits=(), inc=True, skip=False):
        for t in waits:
            self.wait('t', t)
        instr = self.nc.tensor.matmul(out, lhsT, rhs, start=start, stop=stop,
                                      skip_group_check=skip)
        if inc:
            instr.then_inc(self.chain['t'], 1)
            self.cnt['t'] += 1
        return ('t', self.cnt['t'])

    def trans(self, out, in_, ident, waits=(), inc=True):
        for t in waits:
            self.wait('t', t)
        instr = self.nc.tensor.matmul(out, in_, ident, start=True, stop=True,
                                      is_transpose=True)
        if inc:
            instr.then_inc(self.chain['t'], 1)
            self.cnt['t'] += 1
        return ('t', self.cnt['t'])

    def dma(self, out, in_, stream, waits=(), e='y', slow=False):
        for t in waits:
            self.wait(e, t)
        sem = self.dma_sem(stream)
        kw = {}
        if slow:
            kw['allow_slow_non_contiguous'] = True
        self.eng[e].dma_start(out, in_, **kw).then_inc(sem, 16)
        self.dcnt[stream] += 1
        return ('d', stream, 16 * self.dcnt[stream])

    def dtok(self, stream):
        self.dma_sem(stream)
        return ('d', stream, 16 * self.dcnt[stream])

    def barrier(self):
        for name in self.dsem:
            self.wait('y', ('d', name, 16 * self.dcnt[name]))
        self.nc.all_engine_barrier()


def build_program(seq=4096, layers=(0, 1, 2, 3)):
    nc = bass.Bass("TRN2", target_bir_lowering=False)
    S = seq
    NT = S // 512
    NCK = S // 128

    x_in = nc.dram_tensor("x_in", [S, DM], F32, kind="ExternalInput").ap()
    wq_d = nc.dram_tensor("L0_Wq", [DM, INNER0], F32, kind="ExternalInput").ap()
    wkv_d = nc.dram_tensor("L0_Wkv", [DM, 2 * INNER0], F32, kind="ExternalInput").ap()
    wo_d = nc.dram_tensor("L0_Wo", [INNER0, DM], F32, kind="ExternalInput").ap()
    wg_d = nc.dram_tensor("L0_w_gate", [FFN_INNER, DM], F32, kind="ExternalInput").ap()
    wu_d = nc.dram_tensor("L0_w_up", [FFN_INNER, DM], F32, kind="ExternalInput").ap()
    wd_d = nc.dram_tensor("L0_w_down", [DM, FFN_INNER], F32, kind="ExternalInput").ap()
    wqkv_d = nc.dram_tensor("L1_W_qkv", [3 * DM, DM], F32, kind="ExternalInput").ap()
    bqkv_d = nc.dram_tensor("L1_b_qkv", [1, 3 * DM], F32, kind="ExternalInput").ap()
    wout_d = nc.dram_tensor("L1_W_out", [DM, DM], F32, kind="ExternalInput").ap()
    bout_d = nc.dram_tensor("L1_b_out", [1, DM], F32, kind="ExternalInput").ap()
    qw_d = nc.dram_tensor("L1_qw", [4, 64], F32, kind="ExternalInput").ap()
    phot_d = nc.dram_tensor("L1_phot", [16, DM], F32, kind="ExternalInput").ap()
    gates_d = nc.dram_tensor("L1_gates", [3, 32], F32, kind="ExternalInput").ap()
    sup_d = nc.dram_tensor("L1_sup", [1, 4], F32, kind="ExternalInput").ap()
    rhs1_d = nc.dram_tensor("L1_rhs", [DM, DM], F32, kind="ExternalInput").ap()
    identf_d = nc.dram_tensor("identf", [128, 128], F32, kind="ExternalInput").ap()
    identb_d = nc.dram_tensor("identb", [128, 128], BF16, kind="ExternalInput").ap()
    onesf_d = nc.dram_tensor("onesf", [128, 128], F32, kind="ExternalInput").ap()
    mask8_d = nc.dram_tensor("mask8", [128, 128], F32, kind="ExternalInput").ap()
    sel2_d = nc.dram_tensor("sel2", [128, 2], F32, kind="ExternalInput").ap()
    sel2t_d = nc.dram_tensor("sel2t", [2, 128], F32, kind="ExternalInput").ap()

    y_out = nc.dram_tensor("y_out", [S, DM], F32, kind="ExternalOutput").ap()

    xA = nc.dram_tensor("xA", [DM, S], F32, kind="Internal").ap()
    xB = nc.dram_tensor("xB", [DM, S], F32, kind="Internal").ap()
    st_m = nc.dram_tensor("st_m", [1, S], F32, kind="Internal").ap()
    st_nm = nc.dram_tensor("st_nm", [1, S], F32, kind="Internal").ap()
    st_r = nc.dram_tensor("st_r", [1, S], F32, kind="Internal").ap()
    qnh_sc = nc.dram_tensor("qn_hi", [INNER0, S], BF16, kind="Internal").ap()
    qnl_sc = nc.dram_tensor("qn_lo", [INNER0, S], BF16, kind="Internal").ap()
    knh_sc = nc.dram_tensor("kn_hi", [INNER0, S], BF16, kind="Internal").ap()
    knl_sc = nc.dram_tensor("kn_lo", [INNER0, S], BF16, kind="Internal").ap()
    vt_sc = nc.dram_tensor("vt_sc", [S, 8 * 65], F32, kind="Internal").ap()
    wgT_sc = nc.dram_tensor("wgT_sc", [DM, FFN_INNER], BF16, kind="Internal").ap()
    wuT_sc = nc.dram_tensor("wuT_sc", [DM, FFN_INNER], BF16, kind="Internal").ap()
    wdT_sc = nc.dram_tensor("wdT_sc", [FFN_INNER, DM], BF16, kind="Internal").ap()
    qkv_sc = nc.dram_tensor("qkv_sc", [3 * DM, S], F32, kind="Internal").ap()
    wqkvT_f = nc.dram_tensor("wqkvT_f", [3 * DM * 8, 128], F32, kind="Internal").ap()
    woutT_f = nc.dram_tensor("woutT_f", [DM * 8, 128], F32, kind="Internal").ap()
    csqkv_sc = nc.dram_tensor("csqkv_sc", [1, 3 * DM], F32, kind="Internal").ap()

    p = P(nc)

    TA = nc.alloc_sbuf_tensor("TA", [128, 16384], F32)
    TB = nc.alloc_sbuf_tensor("TB", [128, 16640], F32)
    TC = nc.alloc_sbuf_tensor("TC", [128, 6400], F32)
    STG = nc.alloc_sbuf_tensor("STG", [128, 4096], F32)
    WB = nc.alloc_sbuf_tensor("WB", [128, 4096], F32)
    SM = nc.alloc_sbuf_tensor("SM", [128, 4096], F32)
    CP = nc.alloc_sbuf_tensor("CP", [128, 16], F32)
    IDF = nc.alloc_sbuf_tensor("IDF", [128, 128], F32)
    IDB = nc.alloc_sbuf_tensor("IDB", [128, 128], BF16)
    ONE = nc.alloc_sbuf_tensor("ONE", [128, 128], F32)
    MSK8 = nc.alloc_sbuf_tensor("MSK8", [128, 128], F32)
    PS = [nc.alloc_psum_tensor(f"ps{i}", [128, 512], F32) for i in range(8)]

    STT = nc.alloc_sbuf_tensor("STT", [128, 768], F32)

    TAb = TA[:].bitcast(BF16)
    TBb = TB[:].bitcast(BF16)
    TCb = TC[:].bitcast(BF16)
    STGb = STG[:].bitcast(BF16)
    WBb = WB[:].bitcast(BF16)

    assert len(CONST_VALS) <= 16
    for i, val in enumerate(CONST_VALS):
        p.memset(CP[:, i:i + 1], float(val))
        nc.const_aps.register(F32, float(val), CP[:, i:i + 1])
    p.dma(IDF[:], identf_d, "cst")
    p.dma(IDB[:], identb_d, "cst")
    p.dma(ONE[:], onesf_d, "cst")
    p.dma(MSK8[:], mask8_d, "cst")
    PFLAG = CP[:, 14:15]
    p.memset(PFLAG, 1.0)
    p.ts(PFLAG, PFLAG, 2.0 ** -30, -1.0, ALU.add, ALU.add)
    p.ts(PFLAG, PFLAG, 2.0 ** 30, None, ALU.mult)
    M23 = 12582912.0
    M52 = 6755399441055744.0
    CSQ = SM[0:1, 0:512]
    CSK = SM[0:1, 512:1024]
    CSV = SM[0:1, 1024:1536]
    NMR = SM[0:1, 1536:2048]
    RR = SM[0:1, 2048:2560]
    RSA = SM[0:1, 2560:3072]
    RSB = SM[64:65, 2560:3072]
    RSST_A = SM[64:65, 3584:4096]
    RSST_B = SM[64:65, 3072:3584]
    SEL2T_ = SM[0:2, 3072:3200]
    SEL2_ = SM[:, 3200:3202]
    p.dma(SEL2_, sel2_d, "cst")
    p.dma(SEL2T_, sel2t_d, "cst")
    p.barrier()

    def split(hi, lo, src, waits=()):
        p.copy(hi, src, waits=waits)
        return p.tt(lo, src, hi, ALU.subtract)

    def rnd(out, u, s1, s2, waits=()):
        p.ts(s1, u, M23, -M23, ALU.add, ALU.add, waits=waits)
        p.ts(s2, u, M52, -M52, ALU.add, ALU.add)
        p.tt(s2, s2, s1, ALU.subtract)
        return p.stt(out, s2, PFLAG, s1, ALU.mult, ALU.add)

    def ln_stats(x_src, tag):
        TW = 512
        xa = STG[:, 0:TW]
        sq = STG[:, TW:2 * TW]
        mrow = STG[0:1, 2 * TW:3 * TW]
        nrow = STG[0:1, 3 * TW:4 * TW]
        rrow = STG[0:1, 4 * TW:5 * TW]
        for it in range(S // TW):
            t0 = it * TW
            mmtok = None
            for c in range(NCH):
                ld = p.dma(xa, x_src[c * 128:(c + 1) * 128, t0:t0 + TW],
                           f"lnx_{tag}", waits=(p.tok('s'), p.tok('t')))
                p.mm(PS[6][0:1, 0:TW], ONE[:, 0:1], xa, start=(c == 0),
                     stop=(c == NCH - 1), waits=(ld,), inc=True)
                tsq = p.act(sq, xa, AF.Square, waits=(ld,))
                mmtok = p.mm(PS[7][0:1, 0:TW], ONE[:, 0:1], sq, start=(c == 0),
                             stop=(c == NCH - 1), waits=(tsq,), inc=True)
            p.ts(mrow, PS[6][0:1, 0:TW], 1.0 / DM, None, ALU.mult,
                 waits=(mmtok, p.dtok(f"lns_{tag}")))
            p.ts(nrow, mrow, -1.0, None, ALU.mult)
            p.tt(rrow, mrow, mrow, ALU.mult)
            tv = p.stt(rrow, PS[7][0:1, 0:TW], 1.0 / DM, rrow, ALU.mult,
                       ALU.subtract)
            tv = p.act(rrow, rrow, AF.Ln, bias=1e-5, waits=(tv,))
            tv = p.act(rrow, rrow, AF.Exp, scale=-0.5)
            p.dma(st_m[0:1, t0:t0 + TW], mrow, f"lns_{tag}",
                  waits=(p.tok('v'), tv))
            p.dma(st_nm[0:1, t0:t0 + TW], nrow, f"lns_{tag}")
            p.dma(st_r[0:1, t0:t0 + TW], rrow, f"lns_{tag}")
        p.barrier()

    STM = STT[0:1, 0:256]
    STN = STT[0:1, 256:512]
    STR = STT[0:1, 512:768]

    def stats_acc(xn, sq, sumbank, sqbank, first, last, tw, xtok):
        tq = p.act(sq, xn, AF.Square, waits=(xtok,))
        p.mm(sumbank[0:1, 0:tw], ONE[:, 0:1], xn, start=first, stop=last,
             waits=(xtok,), inc=False, skip=True)
        return p.mm(sqbank[0:1, 0:tw], ONE[:, 0:1], sq, start=first,
                    stop=last, waits=(tq,), inc=last, skip=True)

    def stats_fin(t0, tw, sumbank, sqbank, smt):
        for h in range(tw // 256):
            c0 = h * 256
            tv = p.ts(STM, sumbank[0:1, c0:c0 + 256], 1.0 / DM, None, ALU.mult,
                      waits=(smt, p.dtok("st_st")))
            p.ts(STN, STM, -1.0, None, ALU.mult)
            p.tt(STR, STM, STM, ALU.mult)
            tv = p.stt(STR, sqbank[0:1, c0:c0 + 256], 1.0 / DM, STR,
                       ALU.mult, ALU.subtract)
            tv = p.act(STR, STR, AF.Ln, bias=1e-5, waits=(tv,))
            tv = p.act(STR, STR, AF.Exp, scale=-0.5)
            p.dma(st_m[0:1, t0 + c0:t0 + c0 + 256], STM, "st_st",
                  waits=(p.tok('v'), tv))
            p.dma(st_nm[0:1, t0 + c0:t0 + c0 + 256], STN, "st_st")
            p.dma(st_r[0:1, t0 + c0:t0 + c0 + 256], STR, "st_st")

    def epilogue(x_src):
        for it in range(NCK):
            t0 = it * 128
            for c in range(NCH):
                xa = STG[:, (c % 4) * 128:(c % 4) * 128 + 128]
                ld = p.dma(xa, x_src[c * 128:(c + 1) * 128, t0:t0 + 128], "epi_ld",
                           waits=(p.tok('t'),))
                bank = PS[c % 2]
                tk = p.trans(bank[:, 0:128], xa, IDF[:], waits=(ld, p.tok('v')))
                p.copy(STG[:, 1024 + c * 128:1024 + (c + 1) * 128],
                       bank[:, 0:128], waits=(tk,))
            st = p.dma(y_out[t0:t0 + 128, :], STG[:, 1024:1024 + DM], "epi_st",
                       waits=(p.tok('v'),))
            p.wait('t', st)
        p.barrier()

    def phase_input():
        YS = SM[:, 0:32]
        YQ = SM[:, 32:64]
        for it in range(NCK):
            t0 = it * 128
            ld = p.dma(STG[:, 0:DM], x_in[t0:t0 + 128, :], "in_ld",
                       waits=(p.tok('t'), p.tok('v'), p.tok('s')))
            p.red(YS[:, it:it + 1], STG[:, 0:DM], ALU.add, waits=(ld,))
            tq = p.act(STG[:, 2560:3584], STG[:, 0:DM], AF.Square, waits=(ld,))
            p.red(YQ[:, it:it + 1], STG[:, 2560:3584], ALU.add, waits=(tq,))
            for c in range(NCH):
                bank = PS[c % 2]
                tk = p.trans(bank[:, 0:128], STG[:, c * 128:(c + 1) * 128], IDF[:],
                             waits=(ld, p.tok('v')))
                cp = p.copy(STG[:, 1536 + c * 128:1536 + (c + 1) * 128],
                            bank[:, 0:128], waits=(tk,))
                p.dma(xA[c * 128:(c + 1) * 128, t0:t0 + 128],
                      STG[:, 1536 + c * 128:1536 + (c + 1) * 128], "in_st",
                      waits=(cp,))
            p.wait('t', p.dtok("in_st"))
        M_ = SM[:, 64:96]
        V_ = SM[:, 96:128]
        N_ = SM[:, 128:160]
        p.ts(M_, YS, 1.0 / DM, None, ALU.mult)
        p.ts(N_, M_, -1.0, None, ALU.mult)
        p.tt(V_, M_, M_, ALU.mult)
        tv = p.stt(V_, YQ, 1.0 / DM, V_, ALU.mult, ALU.subtract)
        tv = p.act(V_, V_, AF.Ln, bias=1e-5, waits=(tv,))
        tv = p.act(V_, V_, AF.Exp, scale=-0.5)
        for (src, dst, nm) in ((M_, st_m, 0), (N_, st_nm, 1), (V_, st_r, 2)):
            tk = p.trans(PS[2][0:32, 0:128], src, IDF[:], waits=(p.tok('v'), tv))
            cv = p.copy(SM[0:32, 160 + nm * 128:160 + nm * 128 + 128],
                        PS[2][0:32, 0:128], waits=(tk,))
            p.dma(dst.rearrange("o (a b) -> (o a) b", b=128),
                  SM[0:32, 160 + nm * 128:160 + nm * 128 + 128], "st_st",
                  waits=(cv,))
        p.barrier()

    def l0_attn(x_src, x_dst):
        for it in range(NT):
            t0 = it * 512
            for c in range(NCH):
                xa = STG[:, 0:512]
                ld = p.dma(xa, x_src[c * 128:(c + 1) * 128, t0:t0 + 512], "xs_ld",
                           waits=(p.tok('v'),))
                split(TAb[:, c * S + t0: c * S + t0 + 512],
                      TBb[:, c * S + t0: c * S + t0 + 512], xa, waits=(ld,))
        p.barrier()

        def load_w_split(wsrc, cols, hi, lo, csrow):
            cs_tok = None
            nchunks = wsrc.shape[0] // 128
            for c in range(nchunks):
                wt = STG[:, 0:cols]
                ld = p.dma(wt, wsrc[c * 128:(c + 1) * 128, 0:cols], "w_ld",
                           waits=(p.tok('v'), p.tok('t')))
                split(hi[:, c * cols:(c + 1) * cols],
                      lo[:, c * cols:(c + 1) * cols], wt, waits=(ld,))
                if csrow is not None:
                    for j in range(cols // 512):
                        cs_tok = p.mm(PS[7][0:1, j * 512:(j + 1) * 512],
                                      ONE[:, 0:1],
                                      wt[:, j * 512:(j + 1) * 512],
                                      start=(c == 0), stop=(c == nchunks - 1),
                                      waits=(ld,), inc=True)
            if csrow is not None:
                return p.copy(csrow[0:1, 0:cols], PS[7][0:1, 0:cols], waits=(cs_tok,))
            return None

        def proj_qkv(which):
            if which == 'q':
                wsrc, csrow = wq_d, CSQ
            elif which == 'k':
                wsrc, csrow = wkv_d[:, 0:512], CSK
            else:
                wsrc, csrow = wkv_d[:, 512:1024], CSV
            cols = 512
            hi = WBb[:, 0:4096]
            lo = WBb[:, 4096:8192]
            load_w_split(wsrc, cols, hi, lo, csrow)
            p.barrier()
            for it in range(NT):
                t0 = it * 512
                nm_t = p.dma(NMR, st_nm[0:1, t0:t0 + 512], "nm_ld",
                             waits=(p.tok('t'),))
                r_t = None
                if which == 'v':
                    r_t = p.dma(RR, st_r[0:1, t0:t0 + 512], "r_ld",
                                waits=(p.tok('t'),))
                for oc in range(4):
                    bank = PS[oc % 2]
                    for c in range(NCH):
                        st0 = (c == 0)
                        w_h = hi[:, c * cols + oc * 128: c * cols + oc * 128 + 128]
                        w_l = lo[:, c * cols + oc * 128: c * cols + oc * 128 + 128]
                        xh = TAb[:, c * S + t0: c * S + t0 + 512]
                        xl = TBb[:, c * S + t0: c * S + t0 + 512]
                        p.mm(bank[:], w_h, xh, start=st0, stop=False,
                             waits=(p.tok('v'), p.tok('s')) if st0 else (),
                             inc=False)
                        p.mm(bank[:], w_h, xl, start=False, stop=False, inc=False)
                        p.mm(bank[:], w_l, xh, start=False, stop=False, inc=False)
                    mmt = p.mm(bank[:], csrow[0:1, oc * 128:(oc + 1) * 128], NMR,
                               start=False, stop=True, waits=(nm_t,))
                    if which in ('q', 'k'):
                        sq = STG[:, 0:512]
                        tsq = p.act(sq, bank[:], AF.Square, waits=(mmt,))
                        th = p.mm(PS[2][0:2, 0:512], SEL2_, sq, start=True,
                                  stop=True, waits=(tsq, p.tok('v')))
                        ss = STG[:, 512:1024]
                        tv = p.ts(ss[0:2, :], PS[2][0:2, 0:512], 1e-24, None,
                                  ALU.max, waits=(th,))
                        tv = p.act(ss[0:2, :], ss[0:2, :], AF.Ln, waits=(tv,))
                        tv = p.act(ss[0:2, :], ss[0:2, :], AF.Exp, scale=-0.5)
                        tb = p.mm(PS[3][:], SEL2T_, ss[0:2, :], start=True,
                                  stop=True, waits=(tv, p.tok('v')))
                        rnb = STG[:, 1024:1536]
                        tv = p.copy(rnb, PS[3][:], waits=(tb,))
                        qn = STG[:, 1536:2048]
                        tv = p.tt(qn, bank[:], rnb, ALU.mult)
                        hi_o = STGb[:, 4096:4608]
                        lo_o = STGb[:, 4608:5120]
                        tv = split(hi_o, lo_o, qn,
                                   waits=(p.dtok(f"{which}n_st"),))
                        dsth = qnh_sc if which == 'q' else knh_sc
                        dstl = qnl_sc if which == 'q' else knl_sc
                        p.dma(dsth[oc * 128:(oc + 1) * 128, t0:t0 + 512], hi_o,
                              f"{which}n_st", waits=(tv,))
                        p.dma(dstl[oc * 128:(oc + 1) * 128, t0:t0 + 512], lo_o,
                              f"{which}n_st")
                    else:
                        tb = p.mm(PS[2][:], ONE[0:1, :], RR, start=True, stop=True,
                                  waits=(r_t, p.tok('v')))
                        rbs = STG[:, 0:512]
                        tv = p.copy(rbs, PS[2][:], waits=(tb,))
                        vv = STG[:, 512:1024]
                        tv = p.tt(vv, bank[:], rbs, ALU.mult, waits=(mmt,))
                        for sb in range(4):
                            tk = p.trans(PS[3][:, 0:128],
                                         vv[:, sb * 128:sb * 128 + 128],
                                         IDF[:], waits=(tv, p.tok('v')))
                            h0, h1 = 2 * oc, 2 * oc + 1
                            c0 = p.copy(VTS[:, h0 * 65:h0 * 65 + 64],
                                        PS[3][:, 0:64],
                                        waits=(tk, p.dtok("vt_st")))
                            c0 = p.copy(VTS[:, h1 * 65:h1 * 65 + 64],
                                        PS[3][:, 64:128])
                            p.dma(vt_sc[t0 + sb * 128: t0 + sb * 128 + 128,
                                        h0 * 65:h1 * 65 + 65],
                                  VTS[:, h0 * 65:h1 * 65 + 65], "vt_st",
                                  waits=(c0,))
            p.barrier()

        VTS = TC[:, 0:520]
        for h in range(8):
            p.memset(VTS[:, h * 65 + 64:h * 65 + 65], 1.0)
        proj_qkv('q')
        proj_qkv('k')
        proj_qkv('v')

        for c in range(4):
            for it in range(NT):
                t0 = it * 512
                p.dma(TAb[:, c * S + t0:c * S + t0 + 512],
                      knh_sc[c * 128:(c + 1) * 128, t0:t0 + 512], "kn_ld")
                p.dma(TAb[:, 4 * S + c * S + t0:4 * S + c * S + t0 + 512],
                      knl_sc[c * 128:(c + 1) * 128, t0:t0 + 512], "kn_ld")
        for ck in range(NCK):
            p.dma(TB[:, ck * 520:(ck + 1) * 520],
                  vt_sc[ck * 128:(ck + 1) * 128, :], "vt_ld")
        for c in range(4):
            p.dma(WB[:, c * 1024:(c + 1) * 1024], wo_d[c * 128:(c + 1) * 128, :],
                  "w_ld", waits=(p.tok('v'), p.tok('t')))
        p.barrier()

        for sw in range(NT):
            t0 = sw * 512
            qh_sw = STGb[:, 0:2048]
            ql_sw = STGb[:, 2048:4096]
            for c in range(4):
                p.dma(qh_sw[:, c * 512:(c + 1) * 512],
                      qnh_sc[c * 128:(c + 1) * 128, t0:t0 + 512], "qn_ld",
                      waits=(p.tok('t'),))
                p.dma(ql_sw[:, c * 512:(c + 1) * 512],
                      qnl_sc[c * 128:(c + 1) * 128, t0:t0 + 512], "qn_ld")
            qtok = p.dtok("qn_ld")
            kntok = p.dtok("kn_ld")
            vttok = p.dtok("vt_ld")
            pvs_tok = [None] * 4
            for pr in range(4):
                qh_pr = qh_sw[:, pr * 512:(pr + 1) * 512]
                ql_pr = ql_sw[:, pr * 512:(pr + 1) * 512]
                exp_toks = [None, None]
                pend = None
                for ck in range(NCK):
                    knh = TAb[:, pr * S + t0 * 0 + ck * 128: pr * S + ck * 128 + 128]
                    knl = TAb[:, 4 * S + pr * S + ck * 128: 4 * S + pr * S + ck * 128 + 128]
                    lb, lb2 = PS[ck % 2], PS[2 + ck % 2]
                    w_ = [qtok, kntok]
                    if exp_toks[ck % 2] is not None:
                        w_.append(exp_toks[ck % 2])
                    elif ck == 0:
                        w_.append(p.tok('s'))
                        w_.append(p.tok('v'))
                    p.mm(lb[:], knh[0:64, :], qh_pr[0:64, :], start=True,
                         stop=False, waits=w_, inc=False)
                    p.mm(lb[:], knh[0:64, :], ql_pr[0:64, :], start=False,
                         stop=False, inc=False)
                    p.mm(lb[:], knl[0:64, :], qh_pr[0:64, :], start=False,
                         stop=True, inc=False)
                    p.mm(lb2[:], knh[64:128, :], qh_pr[64:128, :], start=True,
                         stop=False, inc=False)
                    p.mm(lb2[:], knh[64:128, :], ql_pr[64:128, :], start=False,
                         stop=False, inc=False)
                    qk_tok = p.mm(lb2[:], knl[64:128, :], qh_pr[64:128, :],
                                  start=False, stop=True)
                    if pend is not None:
                        pea, peb, ppta, pptb, pvth, pvtl, pfirst = pend
                        w0 = [pea, vttok]
                        if pfirst:
                            w0.append(p.tok('v'))
                        p.mm(PS[4][0:65, :], pvth, ppta, start=pfirst,
                             stop=False, waits=w0, inc=False)
                        p.mm(PS[5][0:65, :], pvtl, pptb, start=pfirst,
                             stop=False, waits=(peb,), inc=False)
                    pta = STG[:, 2048 + (ck % 2) * 512:2048 + (ck % 2) * 512 + 512]
                    ptb = TC[:, 520 + (ck % 2) * 512:520 + (ck % 2) * 512 + 512]
                    ea = p.act(pta, lb[:], AF.Exp, bias=-16.0, scale=16.0,
                               waits=(qk_tok,))
                    eb = p.act(ptb, lb2[:], AF.Exp, bias=-16.0, scale=16.0)
                    exp_toks[ck % 2] = eb
                    vth = TB[:, ck * 520 + (2 * pr) * 65: ck * 520 + (2 * pr) * 65 + 65]
                    vtl = TB[:, ck * 520 + (2 * pr + 1) * 65: ck * 520 + (2 * pr + 1) * 65 + 65]
                    pend = (ea, eb, pta, ptb, vth, vtl, ck == 0)
                pea, peb, ppta, pptb, pvth, pvtl, pfirst = pend
                p.mm(PS[4][0:65, :], pvth, ppta, start=pfirst, stop=True,
                     waits=(pea, vttok), inc=True)
                p.mm(PS[5][0:65, :], pvtl, pptb, start=pfirst, stop=True,
                     waits=(peb,), inc=True)
                pv_tok = p.tok('t')
                c1 = p.copy(RSST_A, PS[4][64:65, :],
                            waits=(pv_tok, p.dtok("rs_mv")))
                c2 = p.copy(RSST_B, PS[5][64:65, :])
                d1 = p.dma(RSA, RSST_A, "rs_mv",
                           waits=(c2, p.tok('t')))
                d2 = p.dma(RSB, RSST_B, "rs_mv")
                tv = p.ts(RSA, RSA, 1e-6, None, ALU.max, waits=(d2,))
                tv = p.act(RSA, RSA, AF.Ln, waits=(tv,))
                tv = p.act(RSA, RSA, AF.Exp, scale=-1.0)
                tvb = p.ts(RSB, RSB, 1e-6, None, ALU.max, waits=(d2,))
                tvb = p.act(RSB, RSB, AF.Ln, waits=(tvb,))
                tvb = p.act(RSB, RSB, AF.Exp, scale=-1.0)
                tb = p.mm(PS[6][0:64, :], ONE[0:1, 0:64], RSA,
                          start=True, stop=True, waits=(tv, p.tok('v')))
                rba = TC[:, 1544:2056]
                pvna = TC[:, 4104 + pr * 512:4104 + (pr + 1) * 512]
                tv2 = p.copy(rba[0:64, :], PS[6][0:64, :], waits=(tb,))
                tv2 = p.tt(pvna[0:64, :], PS[4][0:64, :], rba[0:64, :], ALU.mult)
                tb = p.mm(PS[6][0:64, :], ONE[64:65, 0:64], RSB,
                          start=True, stop=True, waits=(tvb, p.tok('v')))
                tv3 = p.copy(rba[0:64, :], PS[6][0:64, :], waits=(tb,))
                hbst = TC[:, 2568:3080]
                tv3 = p.tt(hbst[0:64, :], PS[5][0:64, :], rba[0:64, :],
                           ALU.mult, waits=(p.dtok("pv_mv"),))
                mv = p.dma(pvna[64:128, :], hbst[0:64, :], "pv_mv", waits=(tv3,))
                pvs_tok[pr] = (mv, tv2)
            for oc in range(NCH):
                bank = PS[2 + oc % 2]
                for pr in range(4):
                    pvna = TC[:, 4104 + pr * 512:4104 + (pr + 1) * 512]
                    wh = WB[:, pr * 1024 + oc * 128: pr * 1024 + oc * 128 + 128]
                    w0 = list(pvs_tok[pr])
                    if pr == 0:
                        w0 += [p.tok('s'), p.tok('v')]
                    wo_tok = p.mm(bank[:], wh, pvna, start=(pr == 0),
                                  stop=(pr == 3), waits=w0, inc=(pr == 3))
                xo = STG[:, 3072 + (oc % 2) * 512:3072 + (oc % 2) * 512 + 512]
                ld = p.dma(xo, x_src[oc * 128:(oc + 1) * 128, t0:t0 + 512],
                           "xo_ld", waits=(p.tok('t'),))
                xn = TC[:, 3080 + (oc % 2) * 512:3080 + (oc % 2) * 512 + 512]
                tv = p.tt(xn, bank[:], xo, ALU.add,
                          waits=(wo_tok, ld, p.dtok("xn_st"), p.tok('s')))
                p.dma(x_dst[oc * 128:(oc + 1) * 128, t0:t0 + 512], xn, "xn_st",
                      waits=(tv,))
                smt = stats_acc(xn, STG[:, 2048:2560], PS[7], PS[6], oc == 0,
                                oc == NCH - 1, 512, tv)
            stats_fin(t0, 512, PS[7], PS[6], smt)
        p.barrier()


    SMW = SM[:, 3210:3226]

    def l0_ffn_weights():
        specs = [(wg_d, wgT_sc, 0, 3, FFN_INNER), (wu_d, wuT_sc, 1, 4, FFN_INNER),
                 (wd_d, wdT_sc, 2, 5, DM)]
        for wsrc, _, sci, smi, nrows in specs:
            cols = wsrc.shape[1]
            acc = SM[:, 3226:3227]
            red_o = SM[:, 3227:3228]
            ntiles = nrows // 128
            wbufs = [None] * ntiles
            wtoks = [None] * ntiles
            def ldw(i):
                wbufs[i] = TA[:, (i % 2) * 4096:(i % 2) * 4096 + cols]
                wtoks[i] = p.dma(wbufs[i], wsrc[i * 128:(i + 1) * 128, :],
                                 f"wq_ld{i % 3}", waits=(p.tok('v'),),
                                 e=('y', 'g', 's')[i % 3])
            ldw(0)
            if ntiles > 1:
                ldw(1)
            for i in range(ntiles):
                tr = p.red(red_o, wbufs[i], ALU.add, absval=True,
                           waits=(wtoks[i],))
                if i == 0:
                    p.copy(acc, red_o)
                else:
                    p.tt(acc, acc, red_o, ALU.add)
                if i + 2 < ntiles:
                    ldw(i + 2)
            ts1 = p.mm(PS[7][0:1, 0:1], ONE[:, 0:1], acc, start=True, stop=True,
                       waits=(p.tok('v'),))
            tot = SM[0:1, 3228:3229]
            tv = p.copy(tot, PS[7][0:1, 0:1], waits=(ts1,))
            tb = p.mm(PS[7][:, 4:5], ONE[0:1, :], tot, start=True, stop=True,
                      waits=(tv,))
            tm = p.ts(SMW[:, smi:smi + 1], PS[7][:, 4:5], 1.0 / DM,
                      1.0 / FFN_INNER, ALU.mult, ALU.mult, waits=(tb,))
            tm = p.ts(SMW[:, smi:smi + 1], SMW[:, smi:smi + 1], 1e-5, None,
                      ALU.max)
            p.op('v', 'reciprocal', SMW[:, sci:sci + 1], SMW[:, smi:smi + 1])
        for wsrc, wdst, sci, smi, nrows in specs:
            cols = wsrc.shape[1]
            ntiles = nrows // 128
            wbufs = [None] * ntiles
            wtoks = [None] * ntiles
            def ldw2(i):
                wbufs[i] = TA[:, (i % 2) * 4096:(i % 2) * 4096 + cols]
                wtoks[i] = p.dma(wbufs[i], wsrc[i * 128:(i + 1) * 128, :],
                                 f"wq_ld{i % 3}", waits=(p.tok('v'),),
                                 e=('y', 'g', 's')[i % 3])
            ldw2(0)
            if ntiles > 1:
                ldw2(1)
            for i in range(ntiles):
                u_ = TB[:, (i % 2) * 4096:(i % 2) * 4096 + cols]
                gg = TB[:, 8192 + (i % 2) * 4096:8192 + (i % 2) * 4096 + cols]
                rq = TCb[:, (i % 2) * 2 * cols:(i % 2) * 2 * cols + cols]
                tv = p.ts(u_, wbufs[i], SMW[:, sci:sci + 1], None, ALU.mult,
                          waits=(wtoks[i],))
                tv = p.ts(gg, u_, 0.5, None, ALU.is_gt)
                tv = p.ts(u_, u_, -0.5, None, ALU.is_lt)
                tv = p.tt(rq, gg, u_, ALU.subtract, waits=(p.tok('t'),))
                if i + 2 < ntiles:
                    ldw2(i + 2)
                for b in range(cols // 128):
                    bkb = PS[4 + b % 2][:].bitcast(BF16)
                    tk = p.trans(bkb[:, 0:128], rq[:, b * 128:(b + 1) * 128],
                                 IDB[:], waits=(tv, p.tok('v')))
                    st = TCb[:, 12288 + (b % 4) * 128:12288 + (b % 4) * 128 + 128]
                    ev = p.copy(st, bkb[:, 0:128],
                                waits=(tk, p.dtok("wqT_st")))
                    p.dma(wdst[b * 128:(b + 1) * 128, i * 128:(i + 1) * 128],
                          st, "wqT_st", waits=(ev,))
        p.barrier()

    def l0_ffn(x_src, x_dst):
        l0_ffn_weights()
        for it in range(NT):
            t0 = it * 512
            xt = TB[:, 0:4096]
            ldx = None
            for c in range(NCH):
                ldx = p.dma(xt[:, c * 512:(c + 1) * 512],
                            x_src[c * 128:(c + 1) * 128, t0:t0 + 512], "fx_ld",
                            waits=(p.tok('v'), p.tok('g'), p.tok('t')))
            mrow = SM[0:1, 3584:4096]
            ldm = p.dma(mrow, st_m[0:1, t0:t0 + 512], "fm_ld", waits=(p.tok('t'),))
            rrow = SM[0:1, 2560:3072]
            ldr = p.dma(rrow, st_r[0:1, t0:t0 + 512], "fm_ld")
            tb1 = p.mm(PS[5][:], ONE[0:1, :], mrow, start=True, stop=True,
                       waits=(p.dtok("fm_ld"), p.tok('v')))
            tb2 = p.mm(PS[6][:], ONE[0:1, :], rrow, start=True, stop=True,
                       waits=(p.dtok("fm_ld"),))
            mb = TC[:, 2048:2560]
            rb = TC[:, 2560:3072]
            tv = p.copy(mb, PS[5][:], waits=(tb1,))
            tv = p.copy(rb, PS[6][:], waits=(tb2,))
            y2 = TB[:, 4096:8192]
            YMX = SM[:, 3230:3234]
            RCP = SM[:, 3234:3238]
            tmp1 = SM[:, 3238:3239]
            for c in range(NCH):
                y2c = y2[:, c * 512:(c + 1) * 512]
                tv = p.tt(y2c, xt[:, c * 512:(c + 1) * 512], mb, ALU.subtract,
                          waits=(ldx,))
                tv = p.tt(y2c, y2c, rb, ALU.mult)
            MX_ = TC[:, 4096:4608]
            MN_ = TC[:, 4608:5120]
            p.tt(MX_, y2[:, 0:512], y2[:, 512:1024], ALU.max)
            p.tt(MN_, y2[:, 0:512], y2[:, 512:1024], ALU.min)
            for c in range(2, NCH):
                p.tt(MX_, MX_, y2[:, c * 512:(c + 1) * 512], ALU.max)
                p.tt(MN_, MN_, y2[:, c * 512:(c + 1) * 512], ALU.min)
            tn = p.ts(MN_, MN_, -1.0, None, ALU.mult)
            tn = p.tt(MX_, MX_, MN_, ALU.max)
            tkt = None
            for sb2 in range(4):
                tkt = p.trans(PS[0][:, sb2 * 128:(sb2 + 1) * 128],
                              MX_[:, sb2 * 128:(sb2 + 1) * 128], IDF[:],
                              waits=(tn, p.tok('v'), p.tok('s'))
                              if sb2 == 0 else (tn,), inc=(sb2 == 3))
            for sb2 in range(4):
                p.red(YMX[:, sb2:sb2 + 1], PS[0][:, sb2 * 128:(sb2 + 1) * 128],
                      ALU.max, waits=(tkt,))
            tv = p.ts(YMX, YMX, 1e-5, None, ALU.max)
            tv = p.act(RCP, YMX, AF.Ln, waits=(tv,))
            tv = p.act(RCP, RCP, AF.Exp, scale=-1.0)
            recrow = SM[0:1, 1032:1544]
            for sb2 in range(4):
                tk = p.trans(PS[0][0:1, 0:128], RCP[:, sb2:sb2 + 1], IDF[:],
                             waits=(tv, p.tok('v')))
                p.copy(recrow[0:1, sb2 * 128:(sb2 + 1) * 128],
                       PS[0][0:1, 0:128], waits=(tk,))
            tb3 = p.mm(PS[1][:], ONE[0:1, :], recrow, start=True, stop=True,
                       waits=(p.tok('v'),))
            y2q = TCb[:, 0:4096]
            for c in range(NCH):
                u = TC[:, 3584:4096]
                tv = p.stt(u, y2[:, c * 512:(c + 1) * 512], 127.0, PS[1][:],
                           ALU.mult, ALU.mult,
                           waits=(tb3, p.tok('s'), p.tok('t'), p.dtok("fxn_st")))
                tv = rnd(y2q[:, c * 512:(c + 1) * 512], u, TC[:, 2048:2560],
                         TC[:, 2560:3072])
            wslot = lambda j, w: TBb[:, 16384 + (j % 2) * 8192 + w * 4096:
                                     16384 + (j % 2) * 8192 + (w + 1) * 4096]
            def load_j(j):
                src_g = wgT_sc.rearrange("(c p) f -> p c f", p=128)[:, :, j * 512:(j + 1) * 512]
                src_u = wuT_sc.rearrange("(c p) f -> p c f", p=128)[:, :, j * 512:(j + 1) * 512]
                p.dma(wslot(j, 0), src_g, f"wg_ld{j % 2}", waits=(p.tok('t'),))
                return p.dma(wslot(j, 1), src_u, f"wg_ld{j % 2}")
            wtok = [None] * 8
            wtok[0] = load_j(0)
            wtok[1] = load_j(1)
            for sb in range(4):
                p.ts(SMW[:, 6 + sb:7 + sb], YMX[:, sb:sb + 1], SMW[:, 3:4],
                     None, ALU.mult, waits=(p.tok('s'),) if sb == 0 else ())
                p.ts(SMW[:, 6 + sb:7 + sb], SMW[:, 6 + sb:7 + sb], 1.0 / 127.0,
                     None, ALU.mult)
                p.ts(SMW[:, 10 + sb:11 + sb], YMX[:, sb:sb + 1], SMW[:, 4:5],
                     None, ALU.mult)
                p.ts(SMW[:, 10 + sb:11 + sb], SMW[:, 10 + sb:11 + sb],
                     1.0 / 127.0, None, ALU.mult)
            for j in range(8):
                for sb in range(4):
                    pg, pu = PS[2 * (sb % 2)], PS[2 * (sb % 2) + 1]
                    for c in range(NCH):
                        w0 = [wtok[j]]
                        if c == 0:
                            w0 += [p.tok('s'), p.tok('v')]
                        p.mm(pg[:], y2q[:, c * 512 + sb * 128: c * 512 + sb * 128 + 128],
                             wslot(j, 0)[:, c * 512:(c + 1) * 512],
                             start=(c == 0), stop=(c == NCH - 1), waits=w0,
                             inc=False)
                        gu_tok = p.mm(pu[:], y2q[:, c * 512 + sb * 128: c * 512 + sb * 128 + 128],
                                      wslot(j, 1)[:, c * 512:(c + 1) * 512],
                                      start=(c == 0), stop=(c == NCH - 1),
                                      inc=(c == NCH - 1))
                    ga = TC[:, 3072 + (sb % 2) * 512:3072 + (sb % 2) * 512 + 512]
                    gaw = (gu_tok,)
                    if j == 0 and sb < 2:
                        gaw = (gu_tok, p.dtok("fxn_st"))
                    tsg = p.act(ga, pg[:], AF.Silu, scale=SMW[:, 6 + sb:7 + sb],
                                waits=gaw)
                    th = p.stt(TA[:, sb * 4096 + j * 512:sb * 4096 + (j + 1) * 512],
                               pu[:], SMW[:, 10 + sb:11 + sb], ga,
                               ALU.mult, ALU.mult, waits=(tsg,))
                if j + 2 < 8:
                    wtok[j + 2] = load_j(j + 2)
            dh4 = SM[:, 3226:3230]
            hmc = lambda sb: SM[:, 3238 + sb:3239 + sb]
            r1c = lambda sb: SM[:, 3242 + sb:3243 + sb]
            hmt = [None] * 4
            for sb in range(4):
                H = TA[:, sb * 4096:(sb + 1) * 4096]
                tv = p.red(hmc(sb), H, ALU.max, absval=True)
                tv = p.ts(hmc(sb), hmc(sb), 1e-5, None, ALU.max)
                tv = p.act(hmc(sb), hmc(sb), AF.Ln, waits=(tv,))
                hmt[sb] = p.act(hmc(sb), hmc(sb), AF.Exp, scale=-1.0)
            for sb in range(4):
                H = TA[:, sb * 4096:(sb + 1) * 4096]
                hq = TAb[:, sb * 8192:sb * 8192 + 4096]
                hfm = TAb[:, sb * 8192 + 4096:(sb + 1) * 8192]
                r127 = r1c(sb)
                tv = p.ts(r127, hmc(sb), 127.0, None, ALU.mult,
                          waits=(hmt[sb],))
                p.op('v', 'reciprocal', dh4[:, sb:sb + 1], hmc(sb))
                for k in range(4):
                    u = (TC[:, 4096:5120], TB[:, 4096:5120])[sb % 2]
                    s1 = (TC[:, 5120:6144], TB[:, 5120:6144])[sb % 2]
                    s2 = (TB[:, 6144:7168], TB[:, 7168:8192])[sb % 2]
                    tv = p.ts(u, H[:, k * 1024:(k + 1) * 1024], r127, None,
                              ALU.mult,
                              waits=(p.tok('s'),) if sb == 0 and k == 0 else ())
                    tv = rnd(hq[:, k * 1024:(k + 1) * 1024], u, s1, s2)
                for ic in range(32):
                    bkb = PS[4 + ic % 2][:].bitcast(BF16)
                    tk = p.trans(bkb[:, 0:128],
                                 hq[:, ic * 128:(ic + 1) * 128],
                                 IDB[:], waits=(tv, p.tok('v')))
                    p.copy(hfm[:, ic * 128:(ic + 1) * 128],
                           bkb[:, 0:128], waits=(tk,))
            dhrow = SM[0:1, 3584:4096]
            for sb in range(4):
                tk = p.trans(PS[6][0:1, 0:128], dh4[:, sb:sb + 1], IDF[:],
                             waits=(p.tok('v'),))
                p.copy(dhrow[0:1, sb * 128:(sb + 1) * 128], PS[6][0:1, 0:128],
                       waits=(tk,))
            tbd = p.mm(PS[6][:], ONE[0:1, :], dhrow, start=True, stop=True,
                       waits=(p.tok('v'),))
            dht = SM[:, 1544:2056]
            tv = p.copy(dht, PS[6][:], waits=(tbd,))
            hfm_v = TAb[:].rearrange("p (s r) -> p s r", s=4)
            def load_oc(ocn):
                srcd = wdT_sc.rearrange("(ic p) f -> p ic f", p=128)[:, :, ocn * 128:(ocn + 1) * 128]
                return p.dma(WBb[:, (ocn % 2) * 4096:(ocn % 2) * 4096 + 4096],
                             srcd, f"wd_ld{ocn % 2}", waits=(p.tok('t'),))
            dtoks = {0: load_oc(0), 1: load_oc(1)}
            for ocn in range(NCH):
                bank = PS[2 + ocn % 2]
                for ic in range(32):
                    w0 = [dtoks[ocn]]
                    if ic == 0:
                        w0 += [p.tok('v'), p.tok('s')]
                    dn = p.mm(bank[:], WBb[:, (ocn % 2) * 4096 + ic * 128:
                                           (ocn % 2) * 4096 + (ic + 1) * 128],
                              hfm_v[:, :, 4096 + ic * 128:4096 + (ic + 1) * 128],
                              start=(ic == 0), stop=(ic == 31), waits=w0,
                              inc=(ic == 31))
                if ocn + 2 < NCH:
                    dtoks[ocn + 2] = load_oc(ocn + 2)
                xn = TC[:, 3072 + (ocn % 2) * 512:3072 + (ocn % 2) * 512 + 512]
                tv = p.tt(xn, bank[:], dht, ALU.mult,
                          waits=(dn, p.dtok("fxn_st"), p.tok('s')))
                tv = p.ts(xn, xn, SMW[:, 5:6], 1.0 / 127.0, ALU.mult, ALU.mult)
                tv = p.tt(xn, xn, xt[:, ocn * 512:(ocn + 1) * 512], ALU.add)
                p.dma(x_dst[ocn * 128:(ocn + 1) * 128, t0:t0 + 512], xn, "fxn_st",
                      waits=(tv,))
                smt = stats_acc(xn, STG[:, 0:512], PS[7], PS[6], ocn == 0,
                                ocn == NCH - 1, 512, tv)
            stats_fin(t0, 512, PS[7], PS[6], smt)
        p.barrier()


    PPT = nc.alloc_sbuf_tensor("PPT", [128, 128], F32)
    TWO_PI_HI = float(np.float32(2.0 * math.pi))
    TWO_PI_LO = float(np.float64(2.0 * math.pi) - np.float64(np.float32(2.0 * math.pi)))
    INV_2PI = float(1.0 / (2.0 * math.pi))
    HALF_PI = float(np.float32(0.5 * math.pi))

    def l1_prep():
        ROW = SM[0:1, 0:512]
        ROW2 = SM[0:1, 512:1024]
        grow = SM[0:1, 1024:1120]
        ld = p.dma(grow, gates_d.rearrange("a b -> (a b)").unsqueeze(0), "l1c",
                   waits=(p.tok('t'), p.tok('v'), p.tok('s')))
        for d in range(3):
            th = grow[0:1, 32 * d:32 * d + 28].rearrange("o (i f) -> o i f", f=4)[:, :, 0]
            ph = grow[0:1, 32 * d + 1:32 * d + 29].rearrange("o (i f) -> o i f", f=4)[:, :, 0]
            c_ = ROW[0:1, 0:7]
            s_ = ROW[0:1, 8:15]
            cp = ROW[0:1, 16:23]
            sp = ROW[0:1, 24:31]
            t1 = p.act(c_, th, AF.Sin, bias=HALF_PI, scale=0.5,
                       waits=(ld, p.tok('v')))
            t1 = p.act(s_, th, AF.Sin, scale=0.5)
            t1 = p.act(cp, ph, AF.Sin, bias=HALF_PI)
            t1 = p.act(sp, ph, AF.Sin)
            scp = ROW[0:1, 32:39]
            tv = p.tt(scp, s_, cp, ALU.mult, waits=(t1,))
            tv = p.ts(ROW[0:1, 40:47], c_, -1.0, None, ALU.add)
            tv = p.tt(ROW2[0:1, d * 7:d * 7 + 7], ROW[0:1, 40:47], scp, ALU.add)
            tv = p.tt(ROW2[0:1, 42 + d * 7:42 + d * 7 + 7], ROW[0:1, 40:47], scp,
                      ALU.subtract)
            tv = p.tt(ROW2[0:1, 21 + d * 7:21 + d * 7 + 7], s_, sp, ALU.mult)
        srow = SM[0:1, 1120:1124]
        ld = p.dma(srow, sup_d, "l1c", waits=(p.tok('s'), p.tok('v')))
        t1 = p.act(ROW[0:1, 0:4], srow, AF.Exp, waits=(ld, p.tok('v')))
        tv = p.red(ROW[0:1, 8:9], ROW[0:1, 0:4], ALU.add, waits=(t1,))
        tv = p.op('v', 'reciprocal', ROW[0:1, 9:10], ROW[0:1, 8:9])
        tv = p.tt(ROW2[0:1, 63:67], ROW[0:1, 0:4],
                  ROW[0:1, 9:10].broadcast_to((1, 4)), ALU.mult)
        qrow = SM[0:1, 1152:1408]
        ld = p.dma(qrow, qw_d.rearrange("a b -> (a b)").unsqueeze(0), "l1c",
                   waits=(p.tok('s'),))
        q00 = qrow.rearrange("o (c f) -> o c f", f=64)[:, :, 0]
        t1 = p.act(ROW[0:1, 0:4], q00, AF.Sin, waits=(ld, p.tok('v')))
        tv = p.ts(ROW[0:1, 0:4], ROW[0:1, 0:4], 0.3, 0.5, ALU.mult, ALU.add,
                  waits=(t1,))
        tv = p.ts(ROW[0:1, 8:12], ROW[0:1, 0:4], -1.0, 1.0, ALU.mult, ALU.add)
        tv = p.tt(ROW2[0:1, 67:71], ROW[0:1, 0:4], ROW[0:1, 8:12], ALU.mult)
        tv = p.ts(ROW2[0:1, 67:71], ROW2[0:1, 67:71], SCALING1, None, ALU.mult)
        tbx = p.mm(PS[5][:, 0:71], ONE[0:1, :], ROW2[0:1, 0:71], start=True,
                   stop=True, waits=(p.tok('v'),))
        tb = p.copy(PPT[:, 0:71], PS[5][:, 0:71], waits=(tbx,))
        PHI = SM[:, 1408:1409]
        tmem = p.memset(PHI, 0.0)
        p.wait('y', tmem)
        for r in range(16):
            srcr = phot_d[r:r + 1, :].rearrange("o (q s2) -> o q s2", s2=16)[:, 0:8, r]
            p.dma(PHI[r::16, 0:1], srcr.rearrange("o q -> (o q)").unsqueeze(1),
                  "l1c", slow=True, waits=(p.tok('v'), p.tok('g')))
        phtok = p.dtok("l1c")
        T1 = SM[:, 1409:1410]
        T2 = SM[:, 1410:1411]
        def rr_sin(dst, badd):
            tv = p.ts(T1, PHI, 1.0, badd, ALU.mult, ALU.add,
                      waits=(phtok, p.tok('s')))
            tv = p.ts(T2, T1, INV_2PI, None, ALU.mult)
            tv = rnd(T2, T2, SM[:, 1411:1412], SM[:, 1412:1413])
            tv = p.stt(SM[:, 1411:1412], T2, -TWO_PI_HI, T1, ALU.mult, ALU.add)
            tv = p.stt(SM[:, 1411:1412], T2, -TWO_PI_LO, SM[:, 1411:1412],
                       ALU.mult, ALU.add)
            return p.act(dst, SM[:, 1411:1412], AF.Sin, waits=(tv,))
        rr_sin(PPT[:, 73:74], 0.0)
        rr_sin(PPT[:, 72:73], HALF_PI)
        tv = p.tt(T1, PPT[:, 72:73], PPT[:, 72:73], ALU.mult, waits=(p.tok('s'),))
        tv = p.tt(T2, PPT[:, 73:74], PPT[:, 73:74], ALU.mult)
        tv = p.tt(PPT[:, 74:75], T1, T2, ALU.subtract)
        brow = SM[0:1, 0:1024]
        ld = p.dma(brow, bout_d, "l1c", waits=(p.tok('t'), p.tok('v'), p.tok('s')))
        for ocn in range(8):
            tk = p.trans(PS[4][:, 0:1], brow[0:1, ocn * 128:(ocn + 1) * 128],
                         IDF[0:1, 0:1], waits=(ld, p.tok('v')))
            p.copy(PPT[:, 80 + ocn:81 + ocn], PS[4][:, 0:1], waits=(tk,))
        brow2 = SM[0:1, 1536:2048]
        for half in range(6):
            ld = p.dma(brow2, bqkv_d[0:1, half * 512:(half + 1) * 512], "l1c",
                       waits=(p.tok('t'),))
            for j in range(4):
                ocn = half * 4 + j
                tk = p.trans(PS[4][:, 0:1], brow2[0:1, j * 128:(j + 1) * 128],
                             IDF[0:1, 0:1], waits=(ld, p.tok('v')))
                p.copy(PPT[:, 88 + ocn:89 + ocn], PS[4][:, 0:1], waits=(tk,))
        for (wsrcd, dstf, ntile) in ((wout_d, woutT_f, 8),
                                     (wqkv_d, wqkvT_f, 24)):
            for i in range(ntile):
                wt = STG[:, 0:1024]
                ld = p.dma(wt, wsrcd[i * 128:(i + 1) * 128, :], "w_ld",
                           waits=(p.tok('v'), p.tok('t')))
                for b in range(8):
                    tk = p.trans(PS[4][:, 0:128], wt[:, b * 128:(b + 1) * 128],
                                 IDF[:], waits=(ld, p.tok('v')))
                    stf = STG[:, 1280 + (b % 2) * 128:1280 + (b % 2) * 128 + 128]
                    cv = p.copy(stf, PS[4][:, 0:128],
                                waits=(tk, p.dtok("wT_st")))
                    blk = (i * 8 + b) * 128
                    p.dma(dstf[blk:blk + 128, :], stf, "wT_st", waits=(cv,))
        p.barrier()

    def l1_attn(x_src, x_dst):
        l1_prep()
        PT = 256
        QR = TA[:, 0:2048]
        QI = TA[:, 2048:4096]
        KR = TA[:, 4096:6144]
        KI = TA[:, 6144:8192]
        NV = TA[:, 8192:10240]
        E = [TA[:, 13824 + k * 256:13824 + (k + 1) * 256] for k in range(6)]
        QRI = TB[:, 0:2048]
        QII = TB[:, 2048:4096]
        QIN = TB[:, 4096:6144]
        KRI = TB[:, 6144:8192]
        KII = TB[:, 8192:10240]
        NVT = TB[:, 10240:12288]
        G = TB[:, 12288:14336]
        SD = TB[:, 14336:14592]
        CT = TB[:, 14592:14848]
        QKVT = TB[:, 0:6144]
        sa = TC[:, 0:512]
        sb_ = TC[:, 512:1024]
        SQA = [TC[:, 0:512], TC[:, 1024:1536]]
        SQB = [TC[:, 512:1024], TC[:, 1536:2048]]
        SUS = [TC[:, 2112 + 256 * i:2368 + 256 * i] for i in range(4)]
        M3 = TC[:, 4160:6208]
        xold = STG[:, 2048:2304]
        def emit_qkv(t0):
            ldm = p.dma(SM[0:1, 2048:2048 + PT], st_nm[0:1, t0:t0 + PT], "fm_ld",
                        waits=(p.tok('t'),))
            ldr = p.dma(SM[0:1, 2304:2304 + PT], st_r[0:1, t0:t0 + PT], "fm_ld")
            tb1 = p.mm(PS[6][:, 0:PT], ONE[0:1, :], SM[0:1, 2048:2048 + PT],
                       start=True, stop=True,
                       waits=(p.dtok("fm_ld"), p.tok('v'), p.tok('s')))
            tb2 = p.mm(PS[7][:, 0:PT], ONE[0:1, :], SM[0:1, 2304:2304 + PT],
                       start=True, stop=True)
            nmb = SM[:, 2560:2560 + PT]
            rbb = SM[:, 2816:2816 + PT]
            tv = p.copy(nmb, PS[6][:, 0:PT], waits=(tb1,))
            tv = p.copy(rbb, PS[7][:, 0:PT], waits=(tb2,))
            XN = TC[:, 0:2048]
            for c in range(NCH):
                xc = STG[:, (c % 2) * PT:(c % 2) * PT + PT]
                ldc = p.dma(xc, x_src[c * 128:(c + 1) * 128, t0:t0 + PT],
                            "xq2_ld", waits=(p.tok('v'), p.tok('t')))
                tv = p.tt(XN[:, c * PT:(c + 1) * PT], xc, nmb, ALU.add,
                          waits=(ldc,))
                tv = p.tt(XN[:, c * PT:(c + 1) * PT],
                          XN[:, c * PT:(c + 1) * PT], rbb, ALU.mult)
            def load_qkv_oc(ocn, slot):
                srcf = wqkvT_f[ocn * 1024:(ocn + 1) * 1024, :].rearrange(
                    "(c p) f -> p c f", p=128)
                return p.dma(WB[:, slot * 1024:slot * 1024 + 1024], srcf,
                             f"wv_ld{slot}", waits=(p.tok('t'),))
            qtoks = {0: load_qkv_oc(0, 0), 1: load_qkv_oc(1, 1)}
            for ocn in range(24):
                bank = PS[4 + ocn % 2]
                slot = ocn % 2
                for c in range(NCH):
                    wh = WB[:, slot * 1024 + c * 128:slot * 1024 + (c + 1) * 128]
                    w0 = [qtoks[ocn]]
                    if c == 0:
                        w0 += [p.tok('v'), p.tok('s')]
                    mmt = p.mm(bank[:, 0:PT], wh, XN[:, c * PT:(c + 1) * PT],
                               start=(c == 0), stop=(c == NCH - 1), waits=w0,
                               inc=(c == NCH - 1))
                if ocn + 2 < 24:
                    qtoks[ocn + 2] = load_qkv_oc(ocn + 2, (ocn + 2) % 2)
                p.act(QKVT[:, ocn * PT:(ocn + 1) * PT], bank[:, 0:PT],
                      AF.Identity, bias=PPT[:, 88 + ocn:89 + ocn],
                      waits=(mmt,))

        emit_qkv(0)
        for itq in range(S // PT):
            t0 = itq * PT
            sqq = TA[:, 10240:14336]
            sqv = TA[:, 14336:16384]
            tq1 = p.act(sqq, QKVT[:, 0:4096], AF.Square, waits=(p.tok('v'),))
            tq2 = p.act(sqv, QKVT[:, 4096:6144], AF.Square)
            nrow = lambda k: SM[0:1, (k % 4) * 512:(k % 4) * 512 + 512]
            rbanks = [PS[4], PS[5], PS[6], PS[7]]
            bbanks = [PS[0], PS[1], PS[2], PS[3]]
            rts = [None] * 12
            ats = [None] * 12
            tbs = [None] * 12
            for k in range(12):
                sq_src = (sqq[:, k * 512:(k + 1) * 512] if k < 8 else
                          sqv[:, (k - 8) * 512:(k - 7) * 512])
                rbank = rbanks[k % 4]
                wm = [tq1 if k < 8 else tq2]
                if k >= 4:
                    wm.append(rts[k - 4])
                tm = p.mm(rbank[0:1, 0:512], ONE[:, 0:1], sq_src, start=True,
                          stop=True, waits=tuple(wm))
                rw = nrow(k)
                wr = [tm]
                if k >= 4:
                    wr.append(tbs[k - 4])
                rts[k] = tv = p.ts(rw, rbank[0:1, 0:512], 1e-24, None, ALU.max,
                                   waits=tuple(wr))
                tv = p.act(rw, rw, AF.Ln, waits=(tv,))
                tv = p.act(rw, rw, AF.Exp, scale=-0.5)
                bbank = bbanks[k % 4]
                wb = [tv]
                if k >= 4:
                    wb.append(ats[k - 4])
                tbs[k] = tb = p.mm(bbank[:], ONE[0:1, :], rw, start=True,
                                   stop=True, waits=tuple(wb))
                if k < 4:
                    p.stt(QR[:, k * 512:(k + 1) * 512],
                          QKVT[:, k * 512:(k + 1) * 512],
                          PPT[:, 72:73], bbank[:], ALU.mult, ALU.mult,
                          waits=(tb,))
                    ats[k] = p.stt(QI[:, k * 512:(k + 1) * 512],
                                   QKVT[:, k * 512:(k + 1) * 512],
                                   PPT[:, 73:74], bbank[:], ALU.mult, ALU.mult)
                elif k < 8:
                    kk = k - 4
                    p.stt(KR[:, kk * 512:(kk + 1) * 512],
                          QKVT[:, k * 512:(k + 1) * 512],
                          PPT[:, 72:73], bbank[:], ALU.mult, ALU.mult,
                          waits=(tb,))
                    ats[k] = p.stt(KI[:, kk * 512:(kk + 1) * 512],
                                   QKVT[:, k * 512:(k + 1) * 512],
                                   PPT[:, 73:74], bbank[:], ALU.mult, ALU.mult)
                else:
                    for hh in range(2):
                        h = (k - 8) * 2 + hh
                        nvs = NV[:].rearrange("p (t i) -> p t i", i=8)[:, :, h]
                        ats[k] = p.tt(nvs, QKVT[:, (16 + h) * PT:(17 + h) * PT],
                                      bbank[:, hh * PT:(hh + 1) * PT], ALU.mult,
                                      waits=(tb,) if hh == 0 else ())
            ESQ = TA[:, 10240:13824]
            for (R, I, Rd, Id) in ((QR, QI, QRI, QII), (KR, KI, KRI, KII)):
                ut = [None] * 8
                for d in range(3):
                    stoks = []
                    for i in range(7):
                        p.act(ESQ[:, (2 * i) * PT:(2 * i + 1) * PT],
                              R[:, i * PT:(i + 1) * PT], AF.Square,
                              waits=(p.tok('v'), p.tok('t')) if i == 0 else ())
                        stoks.append(p.act(ESQ[:, (2 * i + 1) * PT:(2 * i + 2) * PT],
                                           I[:, i * PT:(i + 1) * PT], AF.Square))
                    nut = [None] * 8
                    e5u = [None] * 7
                    e5v = [None] * 7
                    ta_prev = ()
                    for i in range(7):
                        if d == 2:
                            u1 = Rd[:].rearrange("p (t j) -> p t j", j=8)[:, :, i]
                            v1 = Id[:].rearrange("p (t j) -> p t j", j=8)[:, :, i]
                        else:
                            u1 = R[:, i * PT:(i + 1) * PT]
                            v1 = I[:, i * PT:(i + 1) * PT]
                        u2 = R[:, (i + 1) * PT:(i + 2) * PT]
                        v2 = I[:, (i + 1) * PT:(i + 2) * PT]
                        Ap = PPT[:, d * 7 + i:d * 7 + i + 1]
                        Bp = PPT[:, 21 + d * 7 + i:21 + d * 7 + i + 1]
                        Cp = PPT[:, 42 + d * 7 + i:42 + d * 7 + i + 1]
                        E3b = E[3] if i % 2 == 0 else E[0]
                        E4b = E[4] if i % 2 == 0 else E[1]
                        w3 = tuple(t for t in (ut[i + 1],
                                               e5u[i - 2] if i >= 2 else None)
                                   if t)
                        w4 = tuple(t for t in (ut[i + 1],
                                               e5v[i - 2] if i >= 2 else None)
                                   if t)
                        ta3 = p.act(E3b, u2, AF.Copy, scale=Ap, waits=w3)
                        ta4 = p.act(E4b, u2, AF.Copy, scale=Bp, waits=w4)
                        tc_ = p.tt(E[2], ESQ[:, (2 * i) * PT:(2 * i + 1) * PT],
                                   ESQ[:, (2 * i + 1) * PT:(2 * i + 2) * PT],
                                   ALU.add,
                                   waits=(stoks[i], p.tok('t')) if i == 0 else
                                         (stoks[i],))
                        tv = p.stt(E3b, v2, Bp, E3b, ALU.mult, ALU.add,
                                   waits=(ta3,))
                        tv = p.stt(E4b, v2, Cp, E4b, ALU.mult, ALU.add,
                                   waits=(ta4,))
                        e5u[i] = p.tt(E[5], E[2], E3b, ALU.mult)
                        nut[i] = p.tt(u1, u2, E[5], ALU.add, waits=ta_prev)
                        e5v[i] = p.tt(E[5], E[2], E4b, ALU.mult)
                        tv = p.tt(v1, v2, E[5], ALU.add)
                        ta_prev = (ta3, ta4)
                    ut = nut
            for (srcT, dstT) in ((QR, QRI), (QI, QII), (KR, KRI), (KI, KII)):
                p.copy(dstT[:].rearrange("p (t j) -> p t j", j=8)[:, :, 7],
                       srcT[:, 7 * PT:8 * PT])
            p.ts(QIN[:], QII[:], -1.0, None, ALU.mult)
            for g in range(PT // 16):
                tk = p.trans(PS[2][:, 0:128], NV[:, g * 128:(g + 1) * 128], IDF[:],
                             waits=(p.tok('v'), p.tok('s')))
                p.act(NVT[:, g * 128:(g + 1) * 128], PS[2][:, 0:128], AF.Copy,
                      waits=(tk,))
            tv = p.red(SD, NV[:].rearrange("p (t i) -> p t i", i=8), ALU.add)
            tsg = p.act(CT, SD, AF.Sign, waits=(tv,))
            tv = p.ts(CT, CT, PPT[:, 74:75], None, ALU.mult, waits=(tsg,))
            S0all = TA[:, 0:2048]
            WT4all = TA[:, 2048:4096]
            ET2 = [TA[:, 4096:6144], TA[:, 6144:8192]]
            ts0 = None
            for gb in range(PT // 64):
                g0 = gb * 4
                pr_t = pi_t = None
                for k in range(4):
                    gs = slice((g0 + k) * 128, (g0 + k + 1) * 128)
                    ks = slice(k * 128, (k + 1) * 128)
                    w_ = [p.tok('s'), p.tok('v')] if k == 0 else []
                    p.mm(PS[0][:, ks], QRI[:, gs], KRI[:, gs], start=True,
                         stop=False, waits=w_, inc=False)
                    pr_t = p.mm(PS[0][:, ks], QII[:, gs], KII[:, gs], start=False,
                                stop=True, inc=(k == 3))
                    p.mm(PS[1][:, ks], QRI[:, gs], KII[:, gs], start=True,
                         stop=False, inc=False)
                    pi_t = p.mm(PS[1][:, ks], QIN[:, gs], KRI[:, gs], start=False,
                                stop=True, inc=(k == 3))
                t1 = p.act(SQA[gb % 2], PS[0][:], AF.Square, waits=(pr_t,))
                t2 = p.act(SQB[gb % 2], PS[1][:], AF.Square, waits=(pi_t,))
                ts0 = p.tt(S0all[:, gb * 512:(gb + 1) * 512], SQA[gb % 2],
                           SQB[gb % 2], ALU.add, waits=(t2,))
            if t0 + PT < S:
                emit_qkv(t0 + PT)
            for c in range(4):
                ET = ET2[c % 2]
                tew = [ts0]
                if c >= 2:
                    tew += [p.tok('v'), p.tok('t')]
                te = p.act(ET, S0all, AF.Exp, scale=PPT[:, 67 + c:68 + c],
                           waits=tuple(tew))
                tv = p.red(SUS[c], ET.rearrange("p (a b) -> p a b", b=8),
                           ALU.add, waits=(te,))
                tv = p.act(SUS[c], SUS[c], AF.Ln, waits=(tv,))
                tv = p.act(SUS[c], SUS[c], AF.Exp, scale=-1.0)
                tv = p.tt(ET.rearrange("p (a b) -> p a b", b=8),
                          ET.rearrange("p (a b) -> p a b", b=8),
                          SUS[c].unsqueeze(2).broadcast_to((128, PT, 8)),
                          ALU.mult, waits=(tv,))
                tv = p.act(ET, ET, AF.Ln, bias=1e-12, waits=(tv,))
                tv = p.act(ET, ET, AF.Exp, scale=0.5)
                tv = p.tt(ET.rearrange("p (a b) -> p a b", b=128),
                          ET.rearrange("p (a b) -> p a b", b=128),
                          MSK8[:].unsqueeze(1).broadcast_to((128, 16, 128)),
                          ALU.mult, waits=(tv,))
                tbanks = [PS[0], PS[1], PS[2], PS[6]]
                tkall = None
                for gb in range(4):
                    for k in range(4):
                        ks = slice(k * 128, (k + 1) * 128)
                        cs = slice(gb * 512 + k * 128, gb * 512 + (k + 1) * 128)
                        tkall = p.trans(tbanks[gb][:, ks], ET[:, cs], IDF[:],
                                        waits=(tv, p.tok('v'), p.tok('s'))
                                        if (gb == 0 and k == 0) else (tv,),
                                        inc=(gb == 3 and k == 3))
                tws = {}
                for gb in range(4):
                    for k in range(4):
                        ks = slice(k * 128, (k + 1) * 128)
                        cs = slice(gb * 512 + k * 128, gb * 512 + (k + 1) * 128)
                        tws[(gb, k)] = p.act(WT4all[:, cs], tbanks[gb][:, ks],
                                             AF.Copy, waits=(tkall,))
                for gb in range(4):
                    g0 = gb * 4
                    wbank = PS[3] if gb % 2 == 0 else PS[7]
                    mt = None
                    for k in range(4):
                        ks = slice(k * 128, (k + 1) * 128)
                        cs = slice(gb * 512 + k * 128, gb * 512 + (k + 1) * 128)
                        gs = slice((g0 + k) * 128, (g0 + k + 1) * 128)
                        mt = p.mm(wbank[:, ks], NVT[:, gs], WT4all[:, cs],
                                  start=True, stop=True,
                                  waits=(tws[(gb, k)], p.tok('s')),
                                  inc=(k == 3))
                    tsg2 = p.act(sa, wbank[:], AF.Sign, waits=(mt, p.tok('v')))
                    tab = p.act(sb_, wbank[:], AF.Abs)
                    tv2 = p.ts(sb_, sb_, 1.0, 1e-6, ALU.min, ALU.max,
                               waits=(tab,))
                    tv2 = p.tt(sb_, sb_, sa, ALU.mult)
                    gsl = slice(gb * 512, (gb + 1) * 512)
                    if c == 0:
                        p.ts(G[:, gsl], sb_, PPT[:, 63:64], None, ALU.mult)
                    else:
                        p.stt(G[:, gsl], sb_, PPT[:, 63 + c:64 + c], G[:, gsl],
                              ALU.mult, ALU.add)
            msp = p.tt(M3.rearrange("p (t i) -> p t i", i=8),
                       G[:].rearrange("p (t i) -> p t i", i=8),
                       CT.unsqueeze(2).broadcast_to((128, PT, 8)), ALU.mult,
                       waits=(p.tok('t'),))
            def load_out_oc(ocn, slot):
                srcf = woutT_f[ocn * 1024:(ocn + 1) * 1024, :].rearrange(
                    "(c p) f -> p c f", p=128)
                return p.dma(WB[:, slot * 1024:slot * 1024 + 1024], srcf,
                             f"wv_ld{slot}", waits=(p.tok('t'),))
            otoks = {0: load_out_oc(0, 0), 1: load_out_oc(1, 1)}
            for ocn in range(NCH):
                ldx = p.dma(xold, x_src[ocn * 128:(ocn + 1) * 128, t0:t0 + PT],
                            "xq_ld", waits=(p.tok('v'),))
                bank = PS[4 + ocn % 2]
                slot = ocn % 2
                for ic in range(NCH):
                    wh = WB[:, slot * 1024 + ic * 128:slot * 1024 + (ic + 1) * 128]
                    m3s = M3.rearrange("p (t i) -> p t i", i=8)[:, :, ic]
                    w0 = [otoks[ocn], msp]
                    if ic == 0:
                        w0 += [p.tok('v'), p.tok('s')]
                    ot = p.mm(bank[:, 0:PT], wh, m3s, start=(ic == 0),
                              stop=(ic == NCH - 1), waits=w0,
                              inc=(ic == NCH - 1))
                if ocn + 2 < NCH:
                    otoks[ocn + 2] = load_out_oc(ocn + 2, (ocn + 2) % 2)
                xn = STG[:, 1024 + (ocn % 2) * 256:1024 + (ocn % 2) * 256 + 256]
                tv = p.stt(xn, xold, PPT[:, 80 + ocn:81 + ocn], bank[:, 0:PT],
                           ALU.add, ALU.add,
                           waits=(ot, ldx, p.dtok("xq_st"), p.tok('s')))
                p.dma(x_dst[ocn * 128:(ocn + 1) * 128, t0:t0 + PT], xn, "xq_st",
                      waits=(tv,))
                smt = stats_acc(xn, STG[:, 512:768], PS[0], PS[1], ocn == 0,
                                ocn == NCH - 1, PT, tv)
            stats_fin(t0, PT, PS[0], PS[1], smt)
        p.barrier()

    def l1_ffn(x_src, x_dst):
        RQ = TAb[:, 0:8192]
        OPP = SM[:, 3230:3238]
        RCPP = SM[:, 3238:3246]
        tmp1 = SM[:, 3246:3247]
        for c in range(NCH):
            wt = STG[:, 0:1024]
            ld = p.dma(wt, rhs1_d[c * 128:(c + 1) * 128, :], "w_ld",
                       waits=(p.tok('v'), p.tok('t')))
            for b in range(8):
                tk = p.trans(PS[0][:, 0:128], wt[:, b * 128:(b + 1) * 128],
                             IDF[:], waits=(ld, p.tok('v')))
                tr = p.red(tmp1, PS[0][:, 0:128], ALU.max, absval=True,
                           waits=(tk,))
                if c == 0:
                    p.copy(OPP[:, b:b + 1], tmp1)
                else:
                    p.tt(OPP[:, b:b + 1], OPP[:, b:b + 1], tmp1, ALU.max)
        tv = p.ts(OPP, OPP, 1e-12, None, ALU.max)
        tv = p.act(RCPP, OPP, AF.Ln, waits=(tv,))
        tv = p.act(RCPP, RCPP, AF.Exp, scale=-1.0)
        rrow = SM[0:1, 1032:2056]
        for b in range(8):
            tk = p.trans(PS[0][0:1, 0:128], RCPP[:, b:b + 1], IDF[:],
                         waits=(tv, p.tok('v')))
            p.copy(rrow[0:1, b * 128:(b + 1) * 128], PS[0][0:1, 0:128],
                   waits=(tk,))
        tb1 = p.mm(PS[1][:], ONE[0:1, :], rrow[0:1, 0:512], start=True,
                   stop=True, waits=(p.tok('v'),))
        tb2 = p.mm(PS[2][:], ONE[0:1, :], rrow[0:1, 512:1024], start=True,
                   stop=True)
        for c in range(NCH):
            wt = STG[:, 0:1024]
            ld = p.dma(wt, rhs1_d[c * 128:(c + 1) * 128, :], "w_ld",
                       waits=(p.tok('v'), p.tok('t')))
            for half in range(2):
                u = TC[:, 2048:2560]
                bcast = PS[1] if half == 0 else PS[2]
                tv = p.stt(u, wt[:, half * 512:(half + 1) * 512], 127.0,
                           bcast[:], ALU.mult, ALU.mult, waits=(ld, tb2))
                tv = rnd(RQ[:, c * 1024 + half * 512:c * 1024 + (half + 1) * 512],
                         u, TC[:, 3072:3584], TC[:, 3584:4096])
        RSC = SM[:, 1024:1032]
        tv = p.ts(RSC, OPP, 1.0 / 127.0, None, ALU.mult)
        p.barrier()
        for it in range(NT):
            t0 = it * 512
            xt = TB[:, 0:4096]
            ldx = None
            for c in range(NCH):
                ldx = p.dma(xt[:, c * 512:(c + 1) * 512],
                            x_src[c * 128:(c + 1) * 128, t0:t0 + 512], "fx_ld",
                            waits=(p.tok('v'), p.tok('t')))
            ldm = p.dma(SM[0:1, 1032:1544], st_m[0:1, t0:t0 + 512], "fm_ld",
                        waits=(p.tok('t'),))
            ldr = p.dma(SM[0:1, 1544:2056], st_r[0:1, t0:t0 + 512], "fm_ld")
            tb1 = p.mm(PS[5][:], ONE[0:1, :], SM[0:1, 1032:1544], start=True,
                       stop=True, waits=(p.dtok("fm_ld"), p.tok('v')))
            tb2 = p.mm(PS[6][:], ONE[0:1, :], SM[0:1, 1544:2056], start=True,
                       stop=True)
            mb = TC[:, 2048:2560]
            rb = TC[:, 2560:3072]
            tv = p.copy(mb, PS[5][:], waits=(tb1,))
            tv = p.copy(rb, PS[6][:], waits=(tb2,))
            y4 = TB[:, 4096:8192]
            YMX = SM[:, 3230:3234]
            RCP4 = SM[:, 3234:3238]
            tmp2 = SM[:, 3246:3247]
            for c in range(NCH):
                y4c = y4[:, c * 512:(c + 1) * 512]
                tv = p.tt(y4c, xt[:, c * 512:(c + 1) * 512], mb, ALU.subtract,
                          waits=(ldx, p.dtok("epi_st")))
                tv = p.tt(y4c, y4c, rb, ALU.mult)
                for sb2 in range(4):
                    tk = p.trans(PS[0][:, 0:128], y4c[:, sb2 * 128:sb2 * 128 + 128],
                                 IDF[:], waits=(tv, p.tok('v')))
                    tr = p.red(tmp2, PS[0][:, 0:128], ALU.max, absval=True,
                               waits=(tk,))
                    if c == 0:
                        p.copy(YMX[:, sb2:sb2 + 1], tmp2)
                    else:
                        p.tt(YMX[:, sb2:sb2 + 1], YMX[:, sb2:sb2 + 1], tmp2,
                             ALU.max)
            tv = p.ts(YMX, YMX, 1e-12, None, ALU.max)
            tv = p.act(RCP4, YMX, AF.Ln, waits=(tv,))
            tv = p.act(RCP4, RCP4, AF.Exp, scale=-1.0)
            recrow = SM[0:1, 2560:3072]
            lsrow = SM[0:1, 0:512]
            for sb2 in range(4):
                tk = p.trans(PS[0][0:1, 0:128], RCP4[:, sb2:sb2 + 1], IDF[:],
                             waits=(tv, p.tok('v')))
                p.copy(recrow[0:1, sb2 * 128:(sb2 + 1) * 128],
                       PS[0][0:1, 0:128], waits=(tk,))
                tk = p.trans(PS[0][0:1, 0:128], YMX[:, sb2:sb2 + 1], IDF[:],
                             waits=(p.tok('v'),))
                p.copy(lsrow[0:1, sb2 * 128:(sb2 + 1) * 128],
                       PS[0][0:1, 0:128], waits=(tk,))
            tv = p.ts(lsrow, lsrow, 1.0 / 127.0, None, ALU.mult)
            tb3 = p.mm(PS[1][:], ONE[0:1, :], recrow, start=True, stop=True,
                       waits=(p.tok('v'),))
            tb4 = p.mm(PS[5][:], ONE[0:1, :], lsrow, start=True, stop=True)
            lsb = TC[:, 2560:3072]
            tv = p.copy(lsb, PS[5][:], waits=(tb4,))
            y4q = TCb[:, 0:4096]
            for c in range(NCH):
                u = TC[:, 3584:4096]
                tv = p.stt(u, y4[:, c * 512:(c + 1) * 512], 127.0, PS[1][:],
                           ALU.mult, ALU.mult,
                           waits=(tb3, p.tok('s'), p.tok('t'), p.dtok("fxn_st")))
                tv = rnd(y4q[:, c * 512:(c + 1) * 512], u, TC[:, 2048:2560],
                         TC[:, 3072:3584])
            for ocn in range(NCH):
                bank = PS[2 + ocn % 2]
                for c in range(NCH):
                    w0 = []
                    if c == 0:
                        w0 += [p.tok('v'), p.tok('s')]
                    dn = p.mm(bank[:], RQ[:, c * 1024 + ocn * 128:
                                          c * 1024 + ocn * 128 + 128],
                              y4q[:, c * 512:(c + 1) * 512],
                              start=(c == 0), stop=(c == NCH - 1), waits=w0,
                              inc=True)
                xn = TC[:, 3072 + (ocn % 2) * 512:3072 + (ocn % 2) * 512 + 512]
                tv = p.tt(xn, bank[:], lsb, ALU.mult, waits=(dn,))
                tv = p.ts(xn, xn, RSC[:, ocn:ocn + 1], None, ALU.mult)
                tv = p.tt(xn, xn, xt[:, ocn * 512:(ocn + 1) * 512], ALU.add)
                for sb2 in range(4):
                    tk = p.trans(PS[sb2 % 2][:, 0:128],
                                 xn[:, sb2 * 128:sb2 * 128 + 128], IDF[:],
                                 waits=(tv, p.tok('v')))
                    p.copy(TB[:, 4096 + sb2 * 1024 + ocn * 128:
                              4096 + sb2 * 1024 + (ocn + 1) * 128],
                           PS[sb2 % 2][:, 0:128],
                           waits=(tk, p.dtok("epi_st")))
            for sb2 in range(4):
                p.dma(y_out[t0 + sb2 * 128:t0 + sb2 * 128 + 128, :],
                      TB[:, 4096 + sb2 * 1024:4096 + (sb2 + 1) * 1024],
                      "epi_st", waits=(p.tok('v'),))
        p.barrier()

    phase_input()
    cur, nxt = xA, xB
    if 0 in layers:
        l0_attn(cur, nxt)
        cur, nxt = nxt, cur
    if 1 in layers:
        l0_ffn(cur, nxt)
        cur, nxt = nxt, cur
    if 2 in layers:
        l1_attn(cur, nxt)
        cur, nxt = nxt, cur
    if 3 in layers:
        l1_ffn(cur, nxt)
        cur, nxt = nxt, cur
    else:
        epilogue(cur)
    return nc


def _consts():
    identf = np.eye(128, dtype=np.float32)
    identb = np.eye(128, dtype=ml_dtypes.bfloat16)
    onesf = np.ones((128, 128), dtype=np.float32)
    mask8 = np.zeros((128, 128), dtype=np.float32)
    for b in range(16):
        mask8[b * 8:(b + 1) * 8, b * 8:(b + 1) * 8] = 1.0
    sel2 = np.zeros((128, 2), dtype=np.float32)
    sel2[0:64, 0] = 1.0
    sel2[64:128, 1] = 1.0
    sel2t = np.ascontiguousarray(sel2.T)
    return identf, identb, onesf, mask8, sel2, sel2t


def make_in_maps(inputs, seq=None):
    x = np.ascontiguousarray(np.asarray(inputs["L0_attn_x"], dtype=np.float32))
    B = x.shape[0]
    identf, identb, onesf, mask8, sel2, sel2t = _consts()
    common = {
        "L0_Wq": np.ascontiguousarray(inputs["L0_attn_Wq"], dtype=np.float32),
        "L0_Wkv": np.ascontiguousarray(inputs["L0_attn_Wkv"], dtype=np.float32),
        "L0_Wo": np.ascontiguousarray(inputs["L0_attn_Wo"], dtype=np.float32),
        "L0_w_gate": np.ascontiguousarray(inputs["L0_ffn_w_gate"], dtype=np.float32),
        "L0_w_up": np.ascontiguousarray(inputs["L0_ffn_w_up"], dtype=np.float32),
        "L0_w_down": np.ascontiguousarray(inputs["L0_ffn_w_down"], dtype=np.float32),
        "L1_W_qkv": np.ascontiguousarray(inputs["L1_attn_W_qkv"], dtype=np.float32),
        "L1_b_qkv": np.ascontiguousarray(inputs["L1_attn_b_qkv"], dtype=np.float32).reshape(1, -1),
        "L1_W_out": np.ascontiguousarray(inputs["L1_attn_W_out"], dtype=np.float32),
        "L1_b_out": np.ascontiguousarray(inputs["L1_attn_b_out"], dtype=np.float32).reshape(1, -1),
        "L1_qw": np.ascontiguousarray(inputs["L1_attn_quantum_weights"], dtype=np.float32).reshape(4, 64),
        "L1_phot": np.ascontiguousarray(inputs["L1_attn_photonic_phases"], dtype=np.float32),
        "L1_gates": np.ascontiguousarray(inputs["L1_attn_entanglement_gates"], dtype=np.float32).reshape(3, 32),
        "L1_sup": np.ascontiguousarray(inputs["L1_attn_superposition_logits"], dtype=np.float32).reshape(1, 4),
        "L1_rhs": np.ascontiguousarray(inputs["L1_ffn_rhs"], dtype=np.float32),
        "identf": identf, "identb": identb, "onesf": onesf, "mask8": mask8,
        "sel2": sel2, "sel2t": sel2t,
    }
    return [dict(common, x_in=np.ascontiguousarray(x[i])) for i in range(B)]


def kernel(**inputs):
    x = np.asarray(inputs["L0_attn_x"])
    B, S, _ = x.shape
    nc = build_program(seq=S, layers=(0, 1, 2, 3))
    in_maps = make_in_maps(inputs)
    res = run_bass_kernel_spmd(nc, in_maps, core_ids=list(range(B)))
    out = np.stack([np.asarray(res.results[i]["y_out"]) for i in range(B)], axis=0)
    return out.astype(np.float32)


# version 20
# speedup vs baseline: 1.0350x; 1.0350x over previous
import math
import numpy as np
import ml_dtypes
import concourse.bass as bass
import concourse.mybir as mybir
import concourse.bass_isa as bass_isa
from concourse.bass_utils import run_bass_kernel_spmd

F32 = mybir.dt.float32
BF16 = mybir.dt.bfloat16
AF = mybir.ActivationFunctionType
ALU = mybir.AluOpType
AX = mybir.AxisListType

DM = 1024
NCH = DM // 128
INNER0 = 512
FFN_INNER = 4096
SCALING1 = float(128 ** -0.5)

CONST_VALS = sorted({0.0, 1.0, -1.0, 0.5, -0.5, 16.0, -16.0,
                     1e-5, 1e-12, 0.0625, 1.0 / DM, 1.0 / FFN_INNER,
                     float(np.float32(0.5 * math.pi))})


class P:
    def __init__(self, nc):
        self.nc = nc
        self.eng = {'t': nc.tensor, 'v': nc.vector, 's': nc.scalar,
                    'g': nc.gpsimd, 'y': nc.sync}
        self.chain = {k: nc.alloc_semaphore(f"ch_{k}") for k in 'tvsg'}
        self.cnt = {k: 0 for k in 'tvsg'}
        self.dsem = {}
        self.dcnt = {}
        self.lastw = {}

    def dma_sem(self, name):
        if name not in self.dsem:
            self.dsem[name] = self.nc.alloc_semaphore(f"dma_{name}")
            self.dcnt[name] = 0
        return self.dsem[name]

    def tok(self, e):
        return (e, self.cnt[e])

    def _sem_of(self, token):
        if token[0] == 'd':
            return self.dsem[token[1]], token[2]
        return self.chain[token[0]], token[1]

    def wait(self, e, token):
        if token is None:
            return
        if token[0] == e:
            return
        sem, val = self._sem_of(token)
        if val <= 0:
            return
        key = (e, id(sem))
        if self.lastw.get(key, -1) >= val:
            return
        self.eng[e].wait_ge(sem, val)
        self.lastw[key] = val

    def _chain(self, e, instr):
        c = self.cnt[e]
        if c > 0:
            instr._wait_ge(self.chain[e], c)
        instr.then_inc(self.chain[e], 1)
        self.cnt[e] += 1
        return (e, self.cnt[e])

    def op(self, e, name, *args, waits=(), **kw):
        for t in waits:
            self.wait(e, t)
        instr = getattr(self.eng[e], name)(*args, **kw)
        return self._chain(e, instr)

    def tt(self, out, a, b, alu, waits=(), e='v'):
        return self.op(e, 'tensor_tensor', out, a, b, alu, waits=waits)

    def ts(self, out, a, s1, s2=None, op0=ALU.mult, op1=None, waits=(), e='v'):
        if op1 is None:
            return self.op(e, 'tensor_scalar', out, a, s1, None, op0, waits=waits)
        return self.op(e, 'tensor_scalar', out, a, s1, s2, op0, op1, waits=waits)

    def stt(self, out, in0, scalar, in1, op0, op1, waits=(), e='v'):
        return self.op(e, 'scalar_tensor_tensor', out, in0, scalar, in1, op0, op1,
                       waits=waits)

    def copy(self, out, a, waits=(), e='v'):
        return self.op(e, 'tensor_copy', out, a, waits=waits)

    def red(self, out, a, alu, absval=None, waits=(), e='v'):
        return self.op(e, 'tensor_reduce', out, a, AX.X, alu,
                       apply_absolute_value=absval, waits=waits)

    def act(self, out, a, func, bias=0.0, scale=1.0, waits=(), accum_out=None):
        return self.op('s', 'activation', out, a, func, bias=bias, scale=scale,
                       accum_out=accum_out, waits=waits)

    def memset(self, ap, val, waits=(), e='v'):
        return self.op(e, 'memset', ap, val, waits=waits)

    def gred(self, out, in_, rop, waits=()):
        return self.op('g', 'partition_all_reduce', out, in_, 128, rop, waits=waits)

    def mm(self, out, lhsT, rhs, start, stop, waits=(), inc=True, skip=False):
        for t in waits:
            self.wait('t', t)
        instr = self.nc.tensor.matmul(out, lhsT, rhs, start=start, stop=stop,
                                      skip_group_check=skip)
        if inc:
            instr.then_inc(self.chain['t'], 1)
            self.cnt['t'] += 1
        return ('t', self.cnt['t'])

    def trans(self, out, in_, ident, waits=(), inc=True):
        for t in waits:
            self.wait('t', t)
        instr = self.nc.tensor.matmul(out, in_, ident, start=True, stop=True,
                                      is_transpose=True)
        if inc:
            instr.then_inc(self.chain['t'], 1)
            self.cnt['t'] += 1
        return ('t', self.cnt['t'])

    def dma(self, out, in_, stream, waits=(), e='y', slow=False):
        for t in waits:
            self.wait(e, t)
        sem = self.dma_sem(stream)
        kw = {}
        if slow:
            kw['allow_slow_non_contiguous'] = True
        self.eng[e].dma_start(out, in_, **kw).then_inc(sem, 16)
        self.dcnt[stream] += 1
        return ('d', stream, 16 * self.dcnt[stream])

    def dtok(self, stream):
        self.dma_sem(stream)
        return ('d', stream, 16 * self.dcnt[stream])

    def barrier(self):
        for name in self.dsem:
            self.wait('y', ('d', name, 16 * self.dcnt[name]))
        self.nc.all_engine_barrier()


def build_program(seq=4096, layers=(0, 1, 2, 3)):
    nc = bass.Bass("TRN2", target_bir_lowering=False)
    S = seq
    NT = S // 512
    NCK = S // 128

    x_in = nc.dram_tensor("x_in", [S, DM], F32, kind="ExternalInput").ap()
    wq_d = nc.dram_tensor("L0_Wq", [DM, INNER0], F32, kind="ExternalInput").ap()
    wkv_d = nc.dram_tensor("L0_Wkv", [DM, 2 * INNER0], F32, kind="ExternalInput").ap()
    wo_d = nc.dram_tensor("L0_Wo", [INNER0, DM], F32, kind="ExternalInput").ap()
    wg_d = nc.dram_tensor("L0_w_gate", [FFN_INNER, DM], F32, kind="ExternalInput").ap()
    wu_d = nc.dram_tensor("L0_w_up", [FFN_INNER, DM], F32, kind="ExternalInput").ap()
    wd_d = nc.dram_tensor("L0_w_down", [DM, FFN_INNER], F32, kind="ExternalInput").ap()
    wqkv_d = nc.dram_tensor("L1_W_qkv", [3 * DM, DM], F32, kind="ExternalInput").ap()
    bqkv_d = nc.dram_tensor("L1_b_qkv", [1, 3 * DM], F32, kind="ExternalInput").ap()
    wout_d = nc.dram_tensor("L1_W_out", [DM, DM], F32, kind="ExternalInput").ap()
    bout_d = nc.dram_tensor("L1_b_out", [1, DM], F32, kind="ExternalInput").ap()
    qw_d = nc.dram_tensor("L1_qw", [4, 64], F32, kind="ExternalInput").ap()
    phot_d = nc.dram_tensor("L1_phot", [16, DM], F32, kind="ExternalInput").ap()
    gates_d = nc.dram_tensor("L1_gates", [3, 32], F32, kind="ExternalInput").ap()
    sup_d = nc.dram_tensor("L1_sup", [1, 4], F32, kind="ExternalInput").ap()
    rhs1_d = nc.dram_tensor("L1_rhs", [DM, DM], F32, kind="ExternalInput").ap()
    identf_d = nc.dram_tensor("identf", [128, 128], F32, kind="ExternalInput").ap()
    identb_d = nc.dram_tensor("identb", [128, 128], BF16, kind="ExternalInput").ap()
    onesf_d = nc.dram_tensor("onesf", [128, 128], F32, kind="ExternalInput").ap()
    mask8_d = nc.dram_tensor("mask8", [128, 128], F32, kind="ExternalInput").ap()
    sel2_d = nc.dram_tensor("sel2", [128, 2], F32, kind="ExternalInput").ap()
    sel2t_d = nc.dram_tensor("sel2t", [2, 128], F32, kind="ExternalInput").ap()

    y_out = nc.dram_tensor("y_out", [S, DM], F32, kind="ExternalOutput").ap()

    xA = nc.dram_tensor("xA", [DM, S], F32, kind="Internal").ap()
    xB = nc.dram_tensor("xB", [DM, S], F32, kind="Internal").ap()
    st_m = nc.dram_tensor("st_m", [1, S], F32, kind="Internal").ap()
    st_nm = nc.dram_tensor("st_nm", [1, S], F32, kind="Internal").ap()
    st_r = nc.dram_tensor("st_r", [1, S], F32, kind="Internal").ap()
    qnh_sc = nc.dram_tensor("qn_hi", [INNER0, S], BF16, kind="Internal").ap()
    qnl_sc = nc.dram_tensor("qn_lo", [INNER0, S], BF16, kind="Internal").ap()
    knh_sc = nc.dram_tensor("kn_hi", [INNER0, S], BF16, kind="Internal").ap()
    knl_sc = nc.dram_tensor("kn_lo", [INNER0, S], BF16, kind="Internal").ap()
    vt_sc = nc.dram_tensor("vt_sc", [S, 8 * 65], F32, kind="Internal").ap()
    wgT_sc = nc.dram_tensor("wgT_sc", [DM, FFN_INNER], BF16, kind="Internal").ap()
    wuT_sc = nc.dram_tensor("wuT_sc", [DM, FFN_INNER], BF16, kind="Internal").ap()
    wdT_sc = nc.dram_tensor("wdT_sc", [FFN_INNER, DM], BF16, kind="Internal").ap()
    qkv_sc = nc.dram_tensor("qkv_sc", [3 * DM, S], F32, kind="Internal").ap()
    wqkvT_f = nc.dram_tensor("wqkvT_f", [3 * DM * 8, 128], F32, kind="Internal").ap()
    woutT_f = nc.dram_tensor("woutT_f", [DM * 8, 128], F32, kind="Internal").ap()
    csqkv_sc = nc.dram_tensor("csqkv_sc", [1, 3 * DM], F32, kind="Internal").ap()

    p = P(nc)

    TA = nc.alloc_sbuf_tensor("TA", [128, 16384], F32)
    TB = nc.alloc_sbuf_tensor("TB", [128, 16640], F32)
    TC = nc.alloc_sbuf_tensor("TC", [128, 6400], F32)
    STG = nc.alloc_sbuf_tensor("STG", [128, 4096], F32)
    WB = nc.alloc_sbuf_tensor("WB", [128, 4096], F32)
    SM = nc.alloc_sbuf_tensor("SM", [128, 4096], F32)
    CP = nc.alloc_sbuf_tensor("CP", [128, 16], F32)
    IDF = nc.alloc_sbuf_tensor("IDF", [128, 128], F32)
    IDB = nc.alloc_sbuf_tensor("IDB", [128, 128], BF16)
    ONE = nc.alloc_sbuf_tensor("ONE", [128, 128], F32)
    MSK8 = nc.alloc_sbuf_tensor("MSK8", [128, 128], F32)
    PS = [nc.alloc_psum_tensor(f"ps{i}", [128, 512], F32) for i in range(8)]

    STT = nc.alloc_sbuf_tensor("STT", [128, 768], F32)

    TAb = TA[:].bitcast(BF16)
    TBb = TB[:].bitcast(BF16)
    TCb = TC[:].bitcast(BF16)
    STGb = STG[:].bitcast(BF16)
    WBb = WB[:].bitcast(BF16)

    assert len(CONST_VALS) <= 16
    for i, val in enumerate(CONST_VALS):
        p.memset(CP[:, i:i + 1], float(val))
        nc.const_aps.register(F32, float(val), CP[:, i:i + 1])
    p.dma(IDF[:], identf_d, "cst")
    p.dma(IDB[:], identb_d, "cst")
    p.dma(ONE[:], onesf_d, "cst")
    p.dma(MSK8[:], mask8_d, "cst")
    PFLAG = CP[:, 14:15]
    p.memset(PFLAG, 1.0)
    p.ts(PFLAG, PFLAG, 2.0 ** -30, -1.0, ALU.add, ALU.add)
    p.ts(PFLAG, PFLAG, 2.0 ** 30, None, ALU.mult)
    M23 = 12582912.0
    M52 = 6755399441055744.0
    CSQ = SM[0:1, 0:512]
    CSK = SM[0:1, 512:1024]
    CSV = SM[0:1, 1024:1536]
    NMR = SM[0:1, 1536:2048]
    RR = SM[0:1, 2048:2560]
    RSA = SM[0:1, 2560:3072]
    RSB = SM[64:65, 2560:3072]
    RSST_A = SM[64:65, 3584:4096]
    RSST_B = SM[64:65, 3072:3584]
    SEL2T_ = SM[0:2, 3072:3200]
    SEL2_ = SM[:, 3200:3202]
    p.dma(SEL2_, sel2_d, "cst")
    p.dma(SEL2T_, sel2t_d, "cst")
    p.barrier()

    def split(hi, lo, src, waits=()):
        p.copy(hi, src, waits=waits)
        return p.tt(lo, src, hi, ALU.subtract)

    def rnd(out, u, s1, s2, waits=()):
        p.ts(s1, u, M23, -M23, ALU.add, ALU.add, waits=waits)
        p.ts(s2, u, M52, -M52, ALU.add, ALU.add)
        p.tt(s2, s2, s1, ALU.subtract)
        return p.stt(out, s2, PFLAG, s1, ALU.mult, ALU.add)

    def ln_stats(x_src, tag):
        TW = 512
        xa = STG[:, 0:TW]
        sq = STG[:, TW:2 * TW]
        mrow = STG[0:1, 2 * TW:3 * TW]
        nrow = STG[0:1, 3 * TW:4 * TW]
        rrow = STG[0:1, 4 * TW:5 * TW]
        for it in range(S // TW):
            t0 = it * TW
            mmtok = None
            for c in range(NCH):
                ld = p.dma(xa, x_src[c * 128:(c + 1) * 128, t0:t0 + TW],
                           f"lnx_{tag}", waits=(p.tok('s'), p.tok('t')))
                p.mm(PS[6][0:1, 0:TW], ONE[:, 0:1], xa, start=(c == 0),
                     stop=(c == NCH - 1), waits=(ld,), inc=True)
                tsq = p.act(sq, xa, AF.Square, waits=(ld,))
                mmtok = p.mm(PS[7][0:1, 0:TW], ONE[:, 0:1], sq, start=(c == 0),
                             stop=(c == NCH - 1), waits=(tsq,), inc=True)
            p.ts(mrow, PS[6][0:1, 0:TW], 1.0 / DM, None, ALU.mult,
                 waits=(mmtok, p.dtok(f"lns_{tag}")))
            p.ts(nrow, mrow, -1.0, None, ALU.mult)
            p.tt(rrow, mrow, mrow, ALU.mult)
            tv = p.stt(rrow, PS[7][0:1, 0:TW], 1.0 / DM, rrow, ALU.mult,
                       ALU.subtract)
            tv = p.act(rrow, rrow, AF.Ln, bias=1e-5, waits=(tv,))
            tv = p.act(rrow, rrow, AF.Exp, scale=-0.5)
            p.dma(st_m[0:1, t0:t0 + TW], mrow, f"lns_{tag}",
                  waits=(p.tok('v'), tv))
            p.dma(st_nm[0:1, t0:t0 + TW], nrow, f"lns_{tag}")
            p.dma(st_r[0:1, t0:t0 + TW], rrow, f"lns_{tag}")
        p.barrier()

    STM = STT[0:1, 0:256]
    STN = STT[0:1, 256:512]
    STR = STT[0:1, 512:768]

    def stats_acc(xn, sq, sumbank, sqbank, first, last, tw, xtok):
        tq = p.act(sq, xn, AF.Square, waits=(xtok,))
        p.mm(sumbank[0:1, 0:tw], ONE[:, 0:1], xn, start=first, stop=last,
             waits=(xtok,), inc=False, skip=True)
        return p.mm(sqbank[0:1, 0:tw], ONE[:, 0:1], sq, start=first,
                    stop=last, waits=(tq,), inc=last, skip=True)

    def stats_fin(t0, tw, sumbank, sqbank, smt):
        for h in range(tw // 256):
            c0 = h * 256
            tv = p.ts(STM, sumbank[0:1, c0:c0 + 256], 1.0 / DM, None, ALU.mult,
                      waits=(smt, p.dtok("st_st")))
            p.ts(STN, STM, -1.0, None, ALU.mult)
            p.tt(STR, STM, STM, ALU.mult)
            tv = p.stt(STR, sqbank[0:1, c0:c0 + 256], 1.0 / DM, STR,
                       ALU.mult, ALU.subtract)
            tv = p.act(STR, STR, AF.Ln, bias=1e-5, waits=(tv,))
            tv = p.act(STR, STR, AF.Exp, scale=-0.5)
            p.dma(st_m[0:1, t0 + c0:t0 + c0 + 256], STM, "st_st",
                  waits=(p.tok('v'), tv))
            p.dma(st_nm[0:1, t0 + c0:t0 + c0 + 256], STN, "st_st")
            p.dma(st_r[0:1, t0 + c0:t0 + c0 + 256], STR, "st_st")

    def epilogue(x_src):
        for it in range(NCK):
            t0 = it * 128
            for c in range(NCH):
                xa = STG[:, (c % 4) * 128:(c % 4) * 128 + 128]
                ld = p.dma(xa, x_src[c * 128:(c + 1) * 128, t0:t0 + 128], "epi_ld",
                           waits=(p.tok('t'),))
                bank = PS[c % 2]
                tk = p.trans(bank[:, 0:128], xa, IDF[:], waits=(ld, p.tok('v')))
                p.copy(STG[:, 1024 + c * 128:1024 + (c + 1) * 128],
                       bank[:, 0:128], waits=(tk,))
            st = p.dma(y_out[t0:t0 + 128, :], STG[:, 1024:1024 + DM], "epi_st",
                       waits=(p.tok('v'),))
            p.wait('t', st)
        p.barrier()

    def phase_input():
        YS = SM[:, 0:32]
        YQ = SM[:, 32:64]
        for it in range(NCK):
            t0 = it * 128
            ld = p.dma(STG[:, 0:DM], x_in[t0:t0 + 128, :], "in_ld",
                       waits=(p.tok('t'), p.tok('v'), p.tok('s')))
            p.red(YS[:, it:it + 1], STG[:, 0:DM], ALU.add, waits=(ld,))
            tq = p.act(STG[:, 2560:3584], STG[:, 0:DM], AF.Square, waits=(ld,))
            p.red(YQ[:, it:it + 1], STG[:, 2560:3584], ALU.add, waits=(tq,))
            for c in range(NCH):
                bank = PS[c % 2]
                tk = p.trans(bank[:, 0:128], STG[:, c * 128:(c + 1) * 128], IDF[:],
                             waits=(ld, p.tok('v')))
                cp = p.copy(STG[:, 1536 + c * 128:1536 + (c + 1) * 128],
                            bank[:, 0:128], waits=(tk,))
                p.dma(xA[c * 128:(c + 1) * 128, t0:t0 + 128],
                      STG[:, 1536 + c * 128:1536 + (c + 1) * 128], "in_st",
                      waits=(cp,))
            p.wait('t', p.dtok("in_st"))
        M_ = SM[:, 64:96]
        V_ = SM[:, 96:128]
        N_ = SM[:, 128:160]
        p.ts(M_, YS, 1.0 / DM, None, ALU.mult)
        p.ts(N_, M_, -1.0, None, ALU.mult)
        p.tt(V_, M_, M_, ALU.mult)
        tv = p.stt(V_, YQ, 1.0 / DM, V_, ALU.mult, ALU.subtract)
        tv = p.act(V_, V_, AF.Ln, bias=1e-5, waits=(tv,))
        tv = p.act(V_, V_, AF.Exp, scale=-0.5)
        for (src, dst, nm) in ((M_, st_m, 0), (N_, st_nm, 1), (V_, st_r, 2)):
            tk = p.trans(PS[2][0:32, 0:128], src, IDF[:], waits=(p.tok('v'), tv))
            cv = p.copy(SM[0:32, 160 + nm * 128:160 + nm * 128 + 128],
                        PS[2][0:32, 0:128], waits=(tk,))
            p.dma(dst.rearrange("o (a b) -> (o a) b", b=128),
                  SM[0:32, 160 + nm * 128:160 + nm * 128 + 128], "st_st",
                  waits=(cv,))
        p.barrier()

    def l0_attn(x_src, x_dst):
        for it in range(NT):
            t0 = it * 512
            for c in range(NCH):
                xa = STG[:, 0:512]
                ld = p.dma(xa, x_src[c * 128:(c + 1) * 128, t0:t0 + 512], "xs_ld",
                           waits=(p.tok('v'),))
                split(TAb[:, c * S + t0: c * S + t0 + 512],
                      TBb[:, c * S + t0: c * S + t0 + 512], xa, waits=(ld,))
        p.barrier()

        def load_w_split(wsrc, cols, hi, lo, csrow):
            cs_tok = None
            nchunks = wsrc.shape[0] // 128
            for c in range(nchunks):
                wt = STG[:, 0:cols]
                ld = p.dma(wt, wsrc[c * 128:(c + 1) * 128, 0:cols], "w_ld",
                           waits=(p.tok('v'), p.tok('t')))
                split(hi[:, c * cols:(c + 1) * cols],
                      lo[:, c * cols:(c + 1) * cols], wt, waits=(ld,))
                if csrow is not None:
                    for j in range(cols // 512):
                        cs_tok = p.mm(PS[7][0:1, j * 512:(j + 1) * 512],
                                      ONE[:, 0:1],
                                      wt[:, j * 512:(j + 1) * 512],
                                      start=(c == 0), stop=(c == nchunks - 1),
                                      waits=(ld,), inc=True)
            if csrow is not None:
                return p.copy(csrow[0:1, 0:cols], PS[7][0:1, 0:cols], waits=(cs_tok,))
            return None

        def proj_qkv(which):
            if which == 'q':
                wsrc, csrow = wq_d, CSQ
            elif which == 'k':
                wsrc, csrow = wkv_d[:, 0:512], CSK
            else:
                wsrc, csrow = wkv_d[:, 512:1024], CSV
            cols = 512
            hi = WBb[:, 0:4096]
            lo = WBb[:, 4096:8192]
            load_w_split(wsrc, cols, hi, lo, csrow)
            p.barrier()
            for it in range(NT):
                t0 = it * 512
                nm_t = p.dma(NMR, st_nm[0:1, t0:t0 + 512], "nm_ld",
                             waits=(p.tok('t'),))
                r_t = None
                if which == 'v':
                    r_t = p.dma(RR, st_r[0:1, t0:t0 + 512], "r_ld",
                                waits=(p.tok('t'),))
                for oc in range(4):
                    bank = PS[oc % 2]
                    for c in range(NCH):
                        st0 = (c == 0)
                        w_h = hi[:, c * cols + oc * 128: c * cols + oc * 128 + 128]
                        w_l = lo[:, c * cols + oc * 128: c * cols + oc * 128 + 128]
                        xh = TAb[:, c * S + t0: c * S + t0 + 512]
                        xl = TBb[:, c * S + t0: c * S + t0 + 512]
                        p.mm(bank[:], w_h, xh, start=st0, stop=False,
                             waits=(p.tok('v'), p.tok('s')) if st0 else (),
                             inc=False)
                        p.mm(bank[:], w_h, xl, start=False, stop=False, inc=False)
                        p.mm(bank[:], w_l, xh, start=False, stop=False, inc=False)
                    mmt = p.mm(bank[:], csrow[0:1, oc * 128:(oc + 1) * 128], NMR,
                               start=False, stop=True, waits=(nm_t,))
                    if which in ('q', 'k'):
                        sq = STG[:, 0:512]
                        tsq = p.act(sq, bank[:], AF.Square, waits=(mmt,))
                        th = p.mm(PS[2][0:2, 0:512], SEL2_, sq, start=True,
                                  stop=True, waits=(tsq, p.tok('v')))
                        ss = STG[:, 512:1024]
                        tv = p.ts(ss[0:2, :], PS[2][0:2, 0:512], 1e-24, None,
                                  ALU.max, waits=(th,))
                        tv = p.act(ss[0:2, :], ss[0:2, :], AF.Ln, waits=(tv,))
                        tv = p.act(ss[0:2, :], ss[0:2, :], AF.Exp, scale=-0.5)
                        tb = p.mm(PS[3][:], SEL2T_, ss[0:2, :], start=True,
                                  stop=True, waits=(tv, p.tok('v')))
                        rnb = STG[:, 1024:1536]
                        tv = p.copy(rnb, PS[3][:], waits=(tb,))
                        qn = STG[:, 1536:2048]
                        tv = p.tt(qn, bank[:], rnb, ALU.mult)
                        hi_o = STGb[:, 4096:4608]
                        lo_o = STGb[:, 4608:5120]
                        tv = split(hi_o, lo_o, qn,
                                   waits=(p.dtok(f"{which}n_st"),))
                        dsth = qnh_sc if which == 'q' else knh_sc
                        dstl = qnl_sc if which == 'q' else knl_sc
                        p.dma(dsth[oc * 128:(oc + 1) * 128, t0:t0 + 512], hi_o,
                              f"{which}n_st", waits=(tv,))
                        p.dma(dstl[oc * 128:(oc + 1) * 128, t0:t0 + 512], lo_o,
                              f"{which}n_st")
                    else:
                        tb = p.mm(PS[2][:], ONE[0:1, :], RR, start=True, stop=True,
                                  waits=(r_t, p.tok('v')))
                        rbs = STG[:, 0:512]
                        tv = p.copy(rbs, PS[2][:], waits=(tb,))
                        vv = STG[:, 512:1024]
                        tv = p.tt(vv, bank[:], rbs, ALU.mult, waits=(mmt,))
                        for sb in range(4):
                            tk = p.trans(PS[3][:, 0:128],
                                         vv[:, sb * 128:sb * 128 + 128],
                                         IDF[:], waits=(tv, p.tok('v')))
                            h0, h1 = 2 * oc, 2 * oc + 1
                            c0 = p.copy(VTS[:, h0 * 65:h0 * 65 + 64],
                                        PS[3][:, 0:64],
                                        waits=(tk, p.dtok("vt_st")))
                            c0 = p.copy(VTS[:, h1 * 65:h1 * 65 + 64],
                                        PS[3][:, 64:128])
                            p.dma(vt_sc[t0 + sb * 128: t0 + sb * 128 + 128,
                                        h0 * 65:h1 * 65 + 65],
                                  VTS[:, h0 * 65:h1 * 65 + 65], "vt_st",
                                  waits=(c0,))
            p.barrier()

        VTS = TC[:, 0:520]
        for h in range(8):
            p.memset(VTS[:, h * 65 + 64:h * 65 + 65], 1.0)
        proj_qkv('q')
        proj_qkv('k')
        proj_qkv('v')

        for c in range(4):
            for it in range(NT):
                t0 = it * 512
                p.dma(TAb[:, c * S + t0:c * S + t0 + 512],
                      knh_sc[c * 128:(c + 1) * 128, t0:t0 + 512], "kn_ld")
                p.dma(TAb[:, 4 * S + c * S + t0:4 * S + c * S + t0 + 512],
                      knl_sc[c * 128:(c + 1) * 128, t0:t0 + 512], "kn_ld")
        for ck in range(NCK):
            p.dma(TB[:, ck * 520:(ck + 1) * 520],
                  vt_sc[ck * 128:(ck + 1) * 128, :], "vt_ld")
        for c in range(4):
            p.dma(WB[:, c * 1024:(c + 1) * 1024], wo_d[c * 128:(c + 1) * 128, :],
                  "w_ld", waits=(p.tok('v'), p.tok('t')))
        p.barrier()

        xst = {}
        for sw in range(NT):
            t0 = sw * 512
            qh_sw = STGb[:, 0:2048]
            ql_sw = STGb[:, 2048:4096]
            for c in range(4):
                p.dma(qh_sw[:, c * 512:(c + 1) * 512],
                      qnh_sc[c * 128:(c + 1) * 128, t0:t0 + 512], "qn_ld",
                      waits=(p.tok('t'),))
                p.dma(ql_sw[:, c * 512:(c + 1) * 512],
                      qnl_sc[c * 128:(c + 1) * 128, t0:t0 + 512], "qn_ld")
            qtok = p.dtok("qn_ld")
            kntok = p.dtok("kn_ld")
            vttok = p.dtok("vt_ld")
            pvs_tok = [None] * 4
            for pr in range(4):
                qh_pr = qh_sw[:, pr * 512:(pr + 1) * 512]
                ql_pr = ql_sw[:, pr * 512:(pr + 1) * 512]
                exp_toks = [None, None]
                pend = None
                for ck in range(NCK):
                    knh = TAb[:, pr * S + t0 * 0 + ck * 128: pr * S + ck * 128 + 128]
                    knl = TAb[:, 4 * S + pr * S + ck * 128: 4 * S + pr * S + ck * 128 + 128]
                    lb, lb2 = PS[ck % 2], PS[2 + ck % 2]
                    w_ = [qtok, kntok]
                    if exp_toks[ck % 2] is not None:
                        w_.append(exp_toks[ck % 2])
                    elif ck == 0:
                        w_.append(p.tok('s'))
                        w_.append(p.tok('v'))
                    p.mm(lb[:], knh[0:64, :], qh_pr[0:64, :], start=True,
                         stop=False, waits=w_, inc=False)
                    p.mm(lb[:], knh[0:64, :], ql_pr[0:64, :], start=False,
                         stop=False, inc=False)
                    p.mm(lb[:], knl[0:64, :], qh_pr[0:64, :], start=False,
                         stop=True, inc=False)
                    p.mm(lb2[:], knh[64:128, :], qh_pr[64:128, :], start=True,
                         stop=False, inc=False)
                    p.mm(lb2[:], knh[64:128, :], ql_pr[64:128, :], start=False,
                         stop=False, inc=False)
                    qk_tok = p.mm(lb2[:], knl[64:128, :], qh_pr[64:128, :],
                                  start=False, stop=True)
                    if pend is not None:
                        pea, peb, ppta, pptb, pvth, pvtl, pfirst = pend
                        w0 = [pea, vttok]
                        if pfirst:
                            w0.append(p.tok('v'))
                        p.mm(PS[4][0:65, :], pvth, ppta, start=pfirst,
                             stop=False, waits=w0, inc=False)
                        p.mm(PS[5][0:65, :], pvtl, pptb, start=pfirst,
                             stop=False, waits=(peb,), inc=False)
                    pta = STG[:, 2048 + (ck % 2) * 512:2048 + (ck % 2) * 512 + 512]
                    ptb = TC[:, 520 + (ck % 2) * 512:520 + (ck % 2) * 512 + 512]
                    ea = p.act(pta, lb[:], AF.Exp, bias=-16.0, scale=16.0,
                               waits=(qk_tok,))
                    eb = p.act(ptb, lb2[:], AF.Exp, bias=-16.0, scale=16.0)
                    exp_toks[ck % 2] = eb
                    vth = TB[:, ck * 520 + (2 * pr) * 65: ck * 520 + (2 * pr) * 65 + 65]
                    vtl = TB[:, ck * 520 + (2 * pr + 1) * 65: ck * 520 + (2 * pr + 1) * 65 + 65]
                    pend = (ea, eb, pta, ptb, vth, vtl, ck == 0)
                pea, peb, ppta, pptb, pvth, pvtl, pfirst = pend
                p.mm(PS[4][0:65, :], pvth, ppta, start=pfirst, stop=True,
                     waits=(pea, vttok), inc=True)
                p.mm(PS[5][0:65, :], pvtl, pptb, start=pfirst, stop=True,
                     waits=(peb,), inc=True)
                pv_tok = p.tok('t')
                c1 = p.copy(RSST_A, PS[4][64:65, :],
                            waits=(pv_tok, p.dtok("rs_mv")))
                c2 = p.copy(RSST_B, PS[5][64:65, :])
                d1 = p.dma(RSA, RSST_A, "rs_mv",
                           waits=(c2, p.tok('t')))
                d2 = p.dma(RSB, RSST_B, "rs_mv")
                tv = p.ts(RSA, RSA, 1e-6, None, ALU.max, waits=(d2,))
                tv = p.act(RSA, RSA, AF.Ln, waits=(tv,))
                tv = p.act(RSA, RSA, AF.Exp, scale=-1.0)
                tvb = p.ts(RSB, RSB, 1e-6, None, ALU.max, waits=(d2,))
                tvb = p.act(RSB, RSB, AF.Ln, waits=(tvb,))
                tvb = p.act(RSB, RSB, AF.Exp, scale=-1.0)
                tb = p.mm(PS[6][0:64, :], ONE[0:1, 0:64], RSA,
                          start=True, stop=True, waits=(tv, p.tok('v')))
                rba = TC[:, 1544:2056]
                pvna = TC[:, 4104 + pr * 512:4104 + (pr + 1) * 512]
                tv2 = p.copy(rba[0:64, :], PS[6][0:64, :], waits=(tb,))
                tv2 = p.tt(pvna[0:64, :], PS[4][0:64, :], rba[0:64, :], ALU.mult)
                tb = p.mm(PS[6][0:64, :], ONE[64:65, 0:64], RSB,
                          start=True, stop=True, waits=(tvb, p.tok('v')))
                tv3 = p.copy(rba[0:64, :], PS[6][0:64, :], waits=(tb,))
                hbst = TC[:, 2568:3080]
                tv3 = p.tt(hbst[0:64, :], PS[5][0:64, :], rba[0:64, :],
                           ALU.mult, waits=(p.dtok("pv_mv"),))
                mv = p.dma(pvna[64:128, :], hbst[0:64, :], "pv_mv", waits=(tv3,))
                pvs_tok[pr] = (mv, tv2)
            for oc in range(NCH):
                bank = PS[2 + oc % 2]
                for pr in range(4):
                    pvna = TC[:, 4104 + pr * 512:4104 + (pr + 1) * 512]
                    wh = WB[:, pr * 1024 + oc * 128: pr * 1024 + oc * 128 + 128]
                    w0 = list(pvs_tok[pr])
                    if pr == 0:
                        w0 += [p.tok('s'), p.tok('v')]
                    wo_tok = p.mm(bank[:], wh, pvna, start=(pr == 0),
                                  stop=(pr == 3), waits=w0, inc=(pr == 3))
                xo = STG[:, 3072 + (oc % 2) * 512:3072 + (oc % 2) * 512 + 512]
                ld = p.dma(xo, x_src[oc * 128:(oc + 1) * 128, t0:t0 + 512],
                           "xo_ld", waits=(p.tok('t'),))
                xn = TC[:, 3080 + (oc % 2) * 512:3080 + (oc % 2) * 512 + 512]
                tv = p.tt(xn, bank[:], xo, ALU.add,
                          waits=tuple(t for t in
                                      (wo_tok, ld, xst.get(oc % 2), p.tok('s'))
                                      if t))
                xst[oc % 2] = p.dma(x_dst[oc * 128:(oc + 1) * 128, t0:t0 + 512],
                                    xn, "xn_st", waits=(tv,))
                smt = stats_acc(xn, STG[:, 2048:2560], PS[7], PS[6], oc == 0,
                                oc == NCH - 1, 512, tv)
            stats_fin(t0, 512, PS[7], PS[6], smt)
        p.barrier()


    SMW = SM[:, 3210:3226]

    def l0_ffn_weights():
        specs = [(wg_d, wgT_sc, 0, 3, FFN_INNER), (wu_d, wuT_sc, 1, 4, FFN_INNER),
                 (wd_d, wdT_sc, 2, 5, DM)]
        for wsrc, _, sci, smi, nrows in specs:
            cols = wsrc.shape[1]
            acc = SM[:, 3226:3227]
            red_o = SM[:, 3227:3228]
            ntiles = nrows // 128
            wbufs = [None] * ntiles
            wtoks = [None] * ntiles
            def ldw(i):
                wbufs[i] = TA[:, (i % 2) * 4096:(i % 2) * 4096 + cols]
                wtoks[i] = p.dma(wbufs[i], wsrc[i * 128:(i + 1) * 128, :],
                                 f"wq_ld{i % 3}", waits=(p.tok('v'),),
                                 e=('y', 'g', 's')[i % 3])
            ldw(0)
            if ntiles > 1:
                ldw(1)
            for i in range(ntiles):
                tr = p.red(red_o, wbufs[i], ALU.add, absval=True,
                           waits=(wtoks[i],))
                if i == 0:
                    p.copy(acc, red_o)
                else:
                    p.tt(acc, acc, red_o, ALU.add)
                if i + 2 < ntiles:
                    ldw(i + 2)
            ts1 = p.mm(PS[7][0:1, 0:1], ONE[:, 0:1], acc, start=True, stop=True,
                       waits=(p.tok('v'),))
            tot = SM[0:1, 3228:3229]
            tv = p.copy(tot, PS[7][0:1, 0:1], waits=(ts1,))
            tb = p.mm(PS[7][:, 4:5], ONE[0:1, :], tot, start=True, stop=True,
                      waits=(tv,))
            tm = p.ts(SMW[:, smi:smi + 1], PS[7][:, 4:5], 1.0 / DM,
                      1.0 / FFN_INNER, ALU.mult, ALU.mult, waits=(tb,))
            tm = p.ts(SMW[:, smi:smi + 1], SMW[:, smi:smi + 1], 1e-5, None,
                      ALU.max)
            p.op('v', 'reciprocal', SMW[:, sci:sci + 1], SMW[:, smi:smi + 1])
        stoks = {}
        for wsrc, wdst, sci, smi, nrows in specs:
            cols = wsrc.shape[1]
            ntiles = nrows // 128
            wbufs = [None] * ntiles
            wtoks = [None] * ntiles
            def ldw2(i):
                wbufs[i] = TA[:, (i % 2) * 4096:(i % 2) * 4096 + cols]
                wtoks[i] = p.dma(wbufs[i], wsrc[i * 128:(i + 1) * 128, :],
                                 f"wq_ld{i % 3}", waits=(p.tok('v'),),
                                 e=('y', 'g', 's')[i % 3])
            ldw2(0)
            if ntiles > 1:
                ldw2(1)
            for i in range(ntiles):
                u_ = TB[:, (i % 2) * 4096:(i % 2) * 4096 + cols]
                gg = TB[:, 8192 + (i % 2) * 4096:8192 + (i % 2) * 4096 + cols]
                rq = TCb[:, (i % 2) * cols:(i % 2) * cols + cols]
                tv = p.ts(u_, wbufs[i], SMW[:, sci:sci + 1], None, ALU.mult,
                          waits=(wtoks[i],))
                tv = p.ts(gg, u_, 0.5, None, ALU.is_gt)
                tv = p.ts(u_, u_, -0.5, None, ALU.is_lt)
                tv = p.tt(rq, gg, u_, ALU.subtract, waits=(p.tok('t'),))
                if i + 2 < ntiles:
                    ldw2(i + 2)
                nb = cols // 128
                if cols <= 2048:
                    stg = TCb[:, 8192 + (i % 2) * cols:8192 + (i % 2) * cols + cols]
                else:
                    stg = TCb[:, 8192:8192 + cols]
                ev = None
                for b in range(nb):
                    bkb = PS[4 + b % 2][:].bitcast(BF16)
                    tk = p.trans(bkb[:, 0:128], rq[:, b * 128:(b + 1) * 128],
                                 IDB[:], waits=(tv, p.tok('v')))
                    w0 = [tk]
                    if b == 0:
                        w0.append(stoks.get(id(wdst), {}).get(i % 2))
                    ev = p.copy(stg[:, b * 128:(b + 1) * 128], bkb[:, 0:128],
                                waits=tuple(t for t in w0 if t))
                dt_ = p.dma(wdst[:, i * 128:(i + 1) * 128].rearrange(
                    "(b p) f -> p b f", p=128),
                    stg.rearrange("p (b f) -> p b f", f=128), "wqT_st",
                    waits=(ev,))
                stoks.setdefault(id(wdst), {})[i % 2] = dt_
        p.barrier()

    def l0_ffn(x_src, x_dst):
        l0_ffn_weights()
        fst = {}
        for it in range(NT):
            t0 = it * 512
            xt = TB[:, 0:4096]
            ldx = None
            for c in range(NCH):
                ldx = p.dma(xt[:, c * 512:(c + 1) * 512],
                            x_src[c * 128:(c + 1) * 128, t0:t0 + 512], "fx_ld",
                            waits=(p.tok('v'), p.tok('g'), p.tok('t')))
            mrow = SM[0:1, 3584:4096]
            ldm = p.dma(mrow, st_m[0:1, t0:t0 + 512], "fm_ld", waits=(p.tok('t'),))
            rrow = SM[0:1, 2560:3072]
            ldr = p.dma(rrow, st_r[0:1, t0:t0 + 512], "fm_ld")
            tb1 = p.mm(PS[5][:], ONE[0:1, :], mrow, start=True, stop=True,
                       waits=(p.dtok("fm_ld"), p.tok('v')))
            tb2 = p.mm(PS[6][:], ONE[0:1, :], rrow, start=True, stop=True,
                       waits=(p.dtok("fm_ld"),))
            mb = TC[:, 2048:2560]
            rb = TC[:, 2560:3072]
            tv = p.copy(mb, PS[5][:], waits=(tb1,))
            tv = p.copy(rb, PS[6][:], waits=(tb2,))
            y2 = TB[:, 4096:8192]
            YMX = SM[:, 3230:3234]
            RCP = SM[:, 3234:3238]
            tmp1 = SM[:, 3238:3239]
            for c in range(NCH):
                y2c = y2[:, c * 512:(c + 1) * 512]
                tv = p.tt(y2c, xt[:, c * 512:(c + 1) * 512], mb, ALU.subtract,
                          waits=(ldx,))
                tv = p.tt(y2c, y2c, rb, ALU.mult)
            MX_ = TC[:, 4096:4608]
            MN_ = TC[:, 4608:5120]
            p.tt(MX_, y2[:, 0:512], y2[:, 512:1024], ALU.max)
            p.tt(MN_, y2[:, 0:512], y2[:, 512:1024], ALU.min)
            for c in range(2, NCH):
                p.tt(MX_, MX_, y2[:, c * 512:(c + 1) * 512], ALU.max)
                p.tt(MN_, MN_, y2[:, c * 512:(c + 1) * 512], ALU.min)
            tn = p.ts(MN_, MN_, -1.0, None, ALU.mult)
            tn = p.tt(MX_, MX_, MN_, ALU.max)
            tkt = None
            for sb2 in range(4):
                tkt = p.trans(PS[0][:, sb2 * 128:(sb2 + 1) * 128],
                              MX_[:, sb2 * 128:(sb2 + 1) * 128], IDF[:],
                              waits=(tn, p.tok('v'), p.tok('s'))
                              if sb2 == 0 else (tn,), inc=(sb2 == 3))
            for sb2 in range(4):
                p.red(YMX[:, sb2:sb2 + 1], PS[0][:, sb2 * 128:(sb2 + 1) * 128],
                      ALU.max, waits=(tkt,))
            tv = p.ts(YMX, YMX, 1e-5, None, ALU.max)
            tv = p.act(RCP, YMX, AF.Ln, waits=(tv,))
            tv = p.act(RCP, RCP, AF.Exp, scale=-1.0)
            recrow = SM[0:1, 1032:1544]
            for sb2 in range(4):
                tk = p.trans(PS[0][0:1, 0:128], RCP[:, sb2:sb2 + 1], IDF[:],
                             waits=(tv, p.tok('v')))
                p.copy(recrow[0:1, sb2 * 128:(sb2 + 1) * 128],
                       PS[0][0:1, 0:128], waits=(tk,))
            tb3 = p.mm(PS[1][:], ONE[0:1, :], recrow, start=True, stop=True,
                       waits=(p.tok('v'),))
            y2q = TCb[:, 0:4096]
            for c in range(NCH):
                u = TC[:, 3584:4096]
                tv = p.stt(u, y2[:, c * 512:(c + 1) * 512], 127.0, PS[1][:],
                           ALU.mult, ALU.mult,
                           waits=tuple(t for t in
                                       (tb3, p.tok('s'), p.tok('t'),
                                        fst.get(1)) if t))
                tv = rnd(y2q[:, c * 512:(c + 1) * 512], u, TC[:, 2048:2560],
                         TC[:, 2560:3072])
            wslot = lambda j, w: TBb[:, 16384 + (j % 2) * 8192 + w * 4096:
                                     16384 + (j % 2) * 8192 + (w + 1) * 4096]
            def load_j(j):
                src_g = wgT_sc.rearrange("(c p) f -> p c f", p=128)[:, :, j * 512:(j + 1) * 512]
                src_u = wuT_sc.rearrange("(c p) f -> p c f", p=128)[:, :, j * 512:(j + 1) * 512]
                p.dma(wslot(j, 0), src_g, f"wg_ld{j % 2}", waits=(p.tok('t'),))
                return p.dma(wslot(j, 1), src_u, f"wg_ld{j % 2}")
            wtok = [None] * 8
            wtok[0] = load_j(0)
            wtok[1] = load_j(1)
            for sb in range(4):
                p.ts(SMW[:, 6 + sb:7 + sb], YMX[:, sb:sb + 1], SMW[:, 3:4],
                     None, ALU.mult, waits=(p.tok('s'),) if sb == 0 else ())
                p.ts(SMW[:, 6 + sb:7 + sb], SMW[:, 6 + sb:7 + sb], 1.0 / 127.0,
                     None, ALU.mult)
                p.ts(SMW[:, 10 + sb:11 + sb], YMX[:, sb:sb + 1], SMW[:, 4:5],
                     None, ALU.mult)
                p.ts(SMW[:, 10 + sb:11 + sb], SMW[:, 10 + sb:11 + sb],
                     1.0 / 127.0, None, ALU.mult)
            for j in range(8):
                for sb in range(4):
                    pg, pu = PS[2 * (sb % 2)], PS[2 * (sb % 2) + 1]
                    for c in range(NCH):
                        w0 = [wtok[j]]
                        if c == 0:
                            w0 += [p.tok('s'), p.tok('v')]
                        p.mm(pg[:], y2q[:, c * 512 + sb * 128: c * 512 + sb * 128 + 128],
                             wslot(j, 0)[:, c * 512:(c + 1) * 512],
                             start=(c == 0), stop=(c == NCH - 1), waits=w0,
                             inc=False)
                        gu_tok = p.mm(pu[:], y2q[:, c * 512 + sb * 128: c * 512 + sb * 128 + 128],
                                      wslot(j, 1)[:, c * 512:(c + 1) * 512],
                                      start=(c == 0), stop=(c == NCH - 1),
                                      inc=(c == NCH - 1))
                    ga = TC[:, 3072 + (sb % 2) * 512:3072 + (sb % 2) * 512 + 512]
                    gaw = (gu_tok,)
                    if j == 0 and sb < 2 and fst.get(sb % 2):
                        gaw = (gu_tok, fst[sb % 2])
                    tsg = p.act(ga, pg[:], AF.Silu, scale=SMW[:, 6 + sb:7 + sb],
                                waits=gaw)
                    th = p.stt(TA[:, sb * 4096 + j * 512:sb * 4096 + (j + 1) * 512],
                               pu[:], SMW[:, 10 + sb:11 + sb], ga,
                               ALU.mult, ALU.mult, waits=(tsg,))
                if j + 2 < 8:
                    wtok[j + 2] = load_j(j + 2)
            dh4 = SM[:, 3226:3230]
            hmc = lambda sb: SM[:, 3238 + sb:3239 + sb]
            r1c = lambda sb: SM[:, 3242 + sb:3243 + sb]
            hmt = [None] * 4
            for sb in range(4):
                H = TA[:, sb * 4096:(sb + 1) * 4096]
                tv = p.red(hmc(sb), H, ALU.max, absval=True)
                tv = p.ts(hmc(sb), hmc(sb), 1e-5, None, ALU.max)
                tv = p.act(hmc(sb), hmc(sb), AF.Ln, waits=(tv,))
                hmt[sb] = p.act(hmc(sb), hmc(sb), AF.Exp, scale=-1.0)
            for sb in range(4):
                H = TA[:, sb * 4096:(sb + 1) * 4096]
                hq = TAb[:, sb * 8192:sb * 8192 + 4096]
                hfm = TAb[:, sb * 8192 + 4096:(sb + 1) * 8192]
                r127 = r1c(sb)
                tv = p.ts(r127, hmc(sb), 127.0, None, ALU.mult,
                          waits=(hmt[sb],))
                p.op('v', 'reciprocal', dh4[:, sb:sb + 1], hmc(sb))
                for k in range(4):
                    u = (TC[:, 4096:5120], TB[:, 4096:5120])[sb % 2]
                    s1 = (TC[:, 5120:6144], TB[:, 5120:6144])[sb % 2]
                    s2 = (TB[:, 6144:7168], TB[:, 7168:8192])[sb % 2]
                    tv = p.ts(u, H[:, k * 1024:(k + 1) * 1024], r127, None,
                              ALU.mult,
                              waits=(p.tok('s'),) if sb == 0 and k == 0 else ())
                    tv = rnd(hq[:, k * 1024:(k + 1) * 1024], u, s1, s2)
                for ic in range(32):
                    bkb = PS[4 + ic % 2][:].bitcast(BF16)
                    tk = p.trans(bkb[:, 0:128],
                                 hq[:, ic * 128:(ic + 1) * 128],
                                 IDB[:], waits=(tv, p.tok('v')))
                    p.copy(hfm[:, ic * 128:(ic + 1) * 128],
                           bkb[:, 0:128], waits=(tk,))
            dhrow = SM[0:1, 3584:4096]
            for sb in range(4):
                tk = p.trans(PS[6][0:1, 0:128], dh4[:, sb:sb + 1], IDF[:],
                             waits=(p.tok('v'),))
                p.copy(dhrow[0:1, sb * 128:(sb + 1) * 128], PS[6][0:1, 0:128],
                       waits=(tk,))
            tbd = p.mm(PS[6][:], ONE[0:1, :], dhrow, start=True, stop=True,
                       waits=(p.tok('v'),))
            dht = SM[:, 1544:2056]
            tv = p.copy(dht, PS[6][:], waits=(tbd,))
            hfm_v = TAb[:].rearrange("p (s r) -> p s r", s=4)
            def load_oc(ocn):
                srcd = wdT_sc.rearrange("(ic p) f -> p ic f", p=128)[:, :, ocn * 128:(ocn + 1) * 128]
                return p.dma(WBb[:, (ocn % 2) * 4096:(ocn % 2) * 4096 + 4096],
                             srcd, f"wd_ld{ocn % 2}", waits=(p.tok('t'),))
            dtoks = {0: load_oc(0), 1: load_oc(1)}
            for ocn in range(NCH):
                bank = PS[2 + ocn % 2]
                for ic in range(32):
                    w0 = [dtoks[ocn]]
                    if ic == 0:
                        w0 += [p.tok('v'), p.tok('s')]
                    dn = p.mm(bank[:], WBb[:, (ocn % 2) * 4096 + ic * 128:
                                           (ocn % 2) * 4096 + (ic + 1) * 128],
                              hfm_v[:, :, 4096 + ic * 128:4096 + (ic + 1) * 128],
                              start=(ic == 0), stop=(ic == 31), waits=w0,
                              inc=(ic == 31))
                if ocn + 2 < NCH:
                    dtoks[ocn + 2] = load_oc(ocn + 2)
                xn = TC[:, 3072 + (ocn % 2) * 512:3072 + (ocn % 2) * 512 + 512]
                tv = p.tt(xn, bank[:], dht, ALU.mult,
                          waits=tuple(t for t in
                                      (dn, fst.get(ocn % 2), p.tok('s')) if t))
                tv = p.ts(xn, xn, SMW[:, 5:6], 1.0 / 127.0, ALU.mult, ALU.mult)
                tv = p.tt(xn, xn, xt[:, ocn * 512:(ocn + 1) * 512], ALU.add)
                fst[ocn % 2] = p.dma(x_dst[ocn * 128:(ocn + 1) * 128,
                                           t0:t0 + 512], xn, "fxn_st",
                                     waits=(tv,))
                smt = stats_acc(xn, STG[:, 0:512], PS[7], PS[6], ocn == 0,
                                ocn == NCH - 1, 512, tv)
            stats_fin(t0, 512, PS[7], PS[6], smt)
        p.barrier()


    PPT = nc.alloc_sbuf_tensor("PPT", [128, 128], F32)
    TWO_PI_HI = float(np.float32(2.0 * math.pi))
    TWO_PI_LO = float(np.float64(2.0 * math.pi) - np.float64(np.float32(2.0 * math.pi)))
    INV_2PI = float(1.0 / (2.0 * math.pi))
    HALF_PI = float(np.float32(0.5 * math.pi))

    def l1_prep():
        ROW = SM[0:1, 0:512]
        ROW2 = SM[0:1, 512:1024]
        grow = SM[0:1, 1024:1120]
        ld = p.dma(grow, gates_d.rearrange("a b -> (a b)").unsqueeze(0), "l1c",
                   waits=(p.tok('t'), p.tok('v'), p.tok('s')))
        for d in range(3):
            th = grow[0:1, 32 * d:32 * d + 28].rearrange("o (i f) -> o i f", f=4)[:, :, 0]
            ph = grow[0:1, 32 * d + 1:32 * d + 29].rearrange("o (i f) -> o i f", f=4)[:, :, 0]
            c_ = ROW[0:1, 0:7]
            s_ = ROW[0:1, 8:15]
            cp = ROW[0:1, 16:23]
            sp = ROW[0:1, 24:31]
            t1 = p.act(c_, th, AF.Sin, bias=HALF_PI, scale=0.5,
                       waits=(ld, p.tok('v')))
            t1 = p.act(s_, th, AF.Sin, scale=0.5)
            t1 = p.act(cp, ph, AF.Sin, bias=HALF_PI)
            t1 = p.act(sp, ph, AF.Sin)
            scp = ROW[0:1, 32:39]
            tv = p.tt(scp, s_, cp, ALU.mult, waits=(t1,))
            tv = p.ts(ROW[0:1, 40:47], c_, -1.0, None, ALU.add)
            tv = p.tt(ROW2[0:1, d * 7:d * 7 + 7], ROW[0:1, 40:47], scp, ALU.add)
            tv = p.tt(ROW2[0:1, 42 + d * 7:42 + d * 7 + 7], ROW[0:1, 40:47], scp,
                      ALU.subtract)
            tv = p.tt(ROW2[0:1, 21 + d * 7:21 + d * 7 + 7], s_, sp, ALU.mult)
        srow = SM[0:1, 1120:1124]
        ld = p.dma(srow, sup_d, "l1c", waits=(p.tok('s'), p.tok('v')))
        t1 = p.act(ROW[0:1, 0:4], srow, AF.Exp, waits=(ld, p.tok('v')))
        tv = p.red(ROW[0:1, 8:9], ROW[0:1, 0:4], ALU.add, waits=(t1,))
        tv = p.op('v', 'reciprocal', ROW[0:1, 9:10], ROW[0:1, 8:9])
        tv = p.tt(ROW2[0:1, 63:67], ROW[0:1, 0:4],
                  ROW[0:1, 9:10].broadcast_to((1, 4)), ALU.mult)
        qrow = SM[0:1, 1152:1408]
        ld = p.dma(qrow, qw_d.rearrange("a b -> (a b)").unsqueeze(0), "l1c",
                   waits=(p.tok('s'),))
        q00 = qrow.rearrange("o (c f) -> o c f", f=64)[:, :, 0]
        t1 = p.act(ROW[0:1, 0:4], q00, AF.Sin, waits=(ld, p.tok('v')))
        tv = p.ts(ROW[0:1, 0:4], ROW[0:1, 0:4], 0.3, 0.5, ALU.mult, ALU.add,
                  waits=(t1,))
        tv = p.ts(ROW[0:1, 8:12], ROW[0:1, 0:4], -1.0, 1.0, ALU.mult, ALU.add)
        tv = p.tt(ROW2[0:1, 67:71], ROW[0:1, 0:4], ROW[0:1, 8:12], ALU.mult)
        tv = p.ts(ROW2[0:1, 67:71], ROW2[0:1, 67:71], SCALING1, None, ALU.mult)
        tbx = p.mm(PS[5][:, 0:71], ONE[0:1, :], ROW2[0:1, 0:71], start=True,
                   stop=True, waits=(p.tok('v'),))
        tb = p.copy(PPT[:, 0:71], PS[5][:, 0:71], waits=(tbx,))
        PHI = SM[:, 1408:1409]
        tmem = p.memset(PHI, 0.0)
        p.wait('y', tmem)
        for r in range(16):
            srcr = phot_d[r:r + 1, :].rearrange("o (q s2) -> o q s2", s2=16)[:, 0:8, r]
            p.dma(PHI[r::16, 0:1], srcr.rearrange("o q -> (o q)").unsqueeze(1),
                  "l1c", slow=True, waits=(p.tok('v'), p.tok('g')))
        phtok = p.dtok("l1c")
        T1 = SM[:, 1409:1410]
        T2 = SM[:, 1410:1411]
        def rr_sin(dst, badd):
            tv = p.ts(T1, PHI, 1.0, badd, ALU.mult, ALU.add,
                      waits=(phtok, p.tok('s')))
            tv = p.ts(T2, T1, INV_2PI, None, ALU.mult)
            tv = rnd(T2, T2, SM[:, 1411:1412], SM[:, 1412:1413])
            tv = p.stt(SM[:, 1411:1412], T2, -TWO_PI_HI, T1, ALU.mult, ALU.add)
            tv = p.stt(SM[:, 1411:1412], T2, -TWO_PI_LO, SM[:, 1411:1412],
                       ALU.mult, ALU.add)
            return p.act(dst, SM[:, 1411:1412], AF.Sin, waits=(tv,))
        rr_sin(PPT[:, 73:74], 0.0)
        rr_sin(PPT[:, 72:73], HALF_PI)
        tv = p.tt(T1, PPT[:, 72:73], PPT[:, 72:73], ALU.mult, waits=(p.tok('s'),))
        tv = p.tt(T2, PPT[:, 73:74], PPT[:, 73:74], ALU.mult)
        tv = p.tt(PPT[:, 74:75], T1, T2, ALU.subtract)
        brow = SM[0:1, 0:1024]
        ld = p.dma(brow, bout_d, "l1c", waits=(p.tok('t'), p.tok('v'), p.tok('s')))
        for ocn in range(8):
            tk = p.trans(PS[4][:, 0:1], brow[0:1, ocn * 128:(ocn + 1) * 128],
                         IDF[0:1, 0:1], waits=(ld, p.tok('v')))
            p.copy(PPT[:, 80 + ocn:81 + ocn], PS[4][:, 0:1], waits=(tk,))
        brow2 = SM[0:1, 1536:2048]
        for half in range(6):
            ld = p.dma(brow2, bqkv_d[0:1, half * 512:(half + 1) * 512], "l1c",
                       waits=(p.tok('t'),))
            for j in range(4):
                ocn = half * 4 + j
                tk = p.trans(PS[4][:, 0:1], brow2[0:1, j * 128:(j + 1) * 128],
                             IDF[0:1, 0:1], waits=(ld, p.tok('v')))
                p.copy(PPT[:, 88 + ocn:89 + ocn], PS[4][:, 0:1], waits=(tk,))
        for (wsrcd, dstf, ntile) in ((wout_d, woutT_f, 8),
                                     (wqkv_d, wqkvT_f, 24)):
            for i in range(ntile):
                wt = STG[:, 0:1024]
                ld = p.dma(wt, wsrcd[i * 128:(i + 1) * 128, :], "w_ld",
                           waits=(p.tok('v'), p.tok('t')))
                for b in range(8):
                    tk = p.trans(PS[4][:, 0:128], wt[:, b * 128:(b + 1) * 128],
                                 IDF[:], waits=(ld, p.tok('v')))
                    stf = STG[:, 1280 + (b % 2) * 128:1280 + (b % 2) * 128 + 128]
                    cv = p.copy(stf, PS[4][:, 0:128],
                                waits=(tk, p.dtok("wT_st")))
                    blk = (i * 8 + b) * 128
                    p.dma(dstf[blk:blk + 128, :], stf, "wT_st", waits=(cv,))
        p.barrier()

    def l1_attn(x_src, x_dst):
        l1_prep()
        PT = 256
        QR = TA[:, 0:2048]
        QI = TA[:, 2048:4096]
        KR = TA[:, 4096:6144]
        KI = TA[:, 6144:8192]
        NV = TA[:, 8192:10240]
        E = [TA[:, 13824 + k * 256:13824 + (k + 1) * 256] for k in range(6)]
        QRI = TB[:, 0:2048]
        QII = TB[:, 2048:4096]
        QIN = TB[:, 4096:6144]
        KRI = TB[:, 6144:8192]
        KII = TB[:, 8192:10240]
        NVT = TB[:, 10240:12288]
        G = TB[:, 12288:14336]
        SD = TB[:, 14336:14592]
        CT = TB[:, 14592:14848]
        QKVT = TB[:, 0:6144]
        sa = TC[:, 0:512]
        sb_ = TC[:, 512:1024]
        SQA = [TC[:, 0:512], TC[:, 1024:1536]]
        SQB = [TC[:, 512:1024], TC[:, 1536:2048]]
        SUS = [TC[:, 2112 + 256 * i:2368 + 256 * i] for i in range(4)]
        M3 = TC[:, 4160:6208]
        xold = STG[:, 2048:2304]
        def emit_qkv(t0):
            ldm = p.dma(SM[0:1, 2048:2048 + PT], st_nm[0:1, t0:t0 + PT], "fm_ld",
                        waits=(p.tok('t'),))
            ldr = p.dma(SM[0:1, 2304:2304 + PT], st_r[0:1, t0:t0 + PT], "fm_ld")
            tb1 = p.mm(PS[6][:, 0:PT], ONE[0:1, :], SM[0:1, 2048:2048 + PT],
                       start=True, stop=True,
                       waits=(p.dtok("fm_ld"), p.tok('v'), p.tok('s')))
            tb2 = p.mm(PS[7][:, 0:PT], ONE[0:1, :], SM[0:1, 2304:2304 + PT],
                       start=True, stop=True)
            nmb = SM[:, 2560:2560 + PT]
            rbb = SM[:, 2816:2816 + PT]
            tv = p.copy(nmb, PS[6][:, 0:PT], waits=(tb1,))
            tv = p.copy(rbb, PS[7][:, 0:PT], waits=(tb2,))
            XN = TC[:, 0:2048]
            for c in range(NCH):
                xc = STG[:, (c % 2) * PT:(c % 2) * PT + PT]
                ldc = p.dma(xc, x_src[c * 128:(c + 1) * 128, t0:t0 + PT],
                            "xq2_ld", waits=(p.tok('v'), p.tok('t')))
                tv = p.tt(XN[:, c * PT:(c + 1) * PT], xc, nmb, ALU.add,
                          waits=(ldc,))
                tv = p.tt(XN[:, c * PT:(c + 1) * PT],
                          XN[:, c * PT:(c + 1) * PT], rbb, ALU.mult)
            def load_qkv_oc(ocn, slot):
                srcf = wqkvT_f[ocn * 1024:(ocn + 1) * 1024, :].rearrange(
                    "(c p) f -> p c f", p=128)
                return p.dma(WB[:, slot * 1024:slot * 1024 + 1024], srcf,
                             f"wv_ld{slot}", waits=(p.tok('t'),))
            qtoks = {0: load_qkv_oc(0, 0), 1: load_qkv_oc(1, 1)}
            for ocn in range(24):
                bank = PS[4 + ocn % 2]
                slot = ocn % 2
                for c in range(NCH):
                    wh = WB[:, slot * 1024 + c * 128:slot * 1024 + (c + 1) * 128]
                    w0 = [qtoks[ocn]]
                    if c == 0:
                        w0 += [p.tok('v'), p.tok('s')]
                    mmt = p.mm(bank[:, 0:PT], wh, XN[:, c * PT:(c + 1) * PT],
                               start=(c == 0), stop=(c == NCH - 1), waits=w0,
                               inc=(c == NCH - 1))
                if ocn + 2 < 24:
                    qtoks[ocn + 2] = load_qkv_oc(ocn + 2, (ocn + 2) % 2)
                p.act(QKVT[:, ocn * PT:(ocn + 1) * PT], bank[:, 0:PT],
                      AF.Identity, bias=PPT[:, 88 + ocn:89 + ocn],
                      waits=(mmt,))

        emit_qkv(0)
        qst = {}
        for itq in range(S // PT):
            t0 = itq * PT
            sqq = TA[:, 10240:14336]
            sqv = TA[:, 14336:16384]
            tq1 = p.act(sqq, QKVT[:, 0:4096], AF.Square, waits=(p.tok('v'),))
            tq2 = p.act(sqv, QKVT[:, 4096:6144], AF.Square)
            nrow = lambda k: SM[0:1, (k % 4) * 512:(k % 4) * 512 + 512]
            rbanks = [PS[4], PS[5], PS[6], PS[7]]
            bbanks = [PS[0], PS[1], PS[2], PS[3]]
            rts = [None] * 12
            ats = [None] * 12
            tbs = [None] * 12
            for k in range(12):
                sq_src = (sqq[:, k * 512:(k + 1) * 512] if k < 8 else
                          sqv[:, (k - 8) * 512:(k - 7) * 512])
                rbank = rbanks[k % 4]
                wm = [tq1 if k < 8 else tq2]
                if k >= 4:
                    wm.append(rts[k - 4])
                tm = p.mm(rbank[0:1, 0:512], ONE[:, 0:1], sq_src, start=True,
                          stop=True, waits=tuple(wm))
                rw = nrow(k)
                wr = [tm]
                if k >= 4:
                    wr.append(tbs[k - 4])
                rts[k] = tv = p.ts(rw, rbank[0:1, 0:512], 1e-24, None, ALU.max,
                                   waits=tuple(wr))
                tv = p.act(rw, rw, AF.Ln, waits=(tv,))
                tv = p.act(rw, rw, AF.Exp, scale=-0.5)
                bbank = bbanks[k % 4]
                wb = [tv]
                if k >= 4:
                    wb.append(ats[k - 4])
                tbs[k] = tb = p.mm(bbank[:], ONE[0:1, :], rw, start=True,
                                   stop=True, waits=tuple(wb))
                if k < 4:
                    p.stt(QR[:, k * 512:(k + 1) * 512],
                          QKVT[:, k * 512:(k + 1) * 512],
                          PPT[:, 72:73], bbank[:], ALU.mult, ALU.mult,
                          waits=(tb,))
                    ats[k] = p.stt(QI[:, k * 512:(k + 1) * 512],
                                   QKVT[:, k * 512:(k + 1) * 512],
                                   PPT[:, 73:74], bbank[:], ALU.mult, ALU.mult)
                elif k < 8:
                    kk = k - 4
                    p.stt(KR[:, kk * 512:(kk + 1) * 512],
                          QKVT[:, k * 512:(k + 1) * 512],
                          PPT[:, 72:73], bbank[:], ALU.mult, ALU.mult,
                          waits=(tb,))
                    ats[k] = p.stt(KI[:, kk * 512:(kk + 1) * 512],
                                   QKVT[:, k * 512:(k + 1) * 512],
                                   PPT[:, 73:74], bbank[:], ALU.mult, ALU.mult)
                else:
                    for hh in range(2):
                        h = (k - 8) * 2 + hh
                        nvs = NV[:].rearrange("p (t i) -> p t i", i=8)[:, :, h]
                        ats[k] = p.tt(nvs, QKVT[:, (16 + h) * PT:(17 + h) * PT],
                                      bbank[:, hh * PT:(hh + 1) * PT], ALU.mult,
                                      waits=(tb,) if hh == 0 else ())
            ESQ = TA[:, 10240:13824]
            for (R, I, Rd, Id) in ((QR, QI, QRI, QII), (KR, KI, KRI, KII)):
                ut = [None] * 8
                for d in range(3):
                    stoks = []
                    for i in range(7):
                        p.act(ESQ[:, (2 * i) * PT:(2 * i + 1) * PT],
                              R[:, i * PT:(i + 1) * PT], AF.Square,
                              waits=(p.tok('v'), p.tok('t')) if i == 0 else ())
                        stoks.append(p.act(ESQ[:, (2 * i + 1) * PT:(2 * i + 2) * PT],
                                           I[:, i * PT:(i + 1) * PT], AF.Square))
                    nut = [None] * 8
                    e5u = [None] * 7
                    e5v = [None] * 7
                    ta_prev = ()
                    for i in range(7):
                        if d == 2:
                            u1 = Rd[:].rearrange("p (t j) -> p t j", j=8)[:, :, i]
                            v1 = Id[:].rearrange("p (t j) -> p t j", j=8)[:, :, i]
                        else:
                            u1 = R[:, i * PT:(i + 1) * PT]
                            v1 = I[:, i * PT:(i + 1) * PT]
                        u2 = R[:, (i + 1) * PT:(i + 2) * PT]
                        v2 = I[:, (i + 1) * PT:(i + 2) * PT]
                        Ap = PPT[:, d * 7 + i:d * 7 + i + 1]
                        Bp = PPT[:, 21 + d * 7 + i:21 + d * 7 + i + 1]
                        Cp = PPT[:, 42 + d * 7 + i:42 + d * 7 + i + 1]
                        E3b = E[3] if i % 2 == 0 else E[0]
                        E4b = E[4] if i % 2 == 0 else E[1]
                        w3 = tuple(t for t in (ut[i + 1],
                                               e5u[i - 2] if i >= 2 else None)
                                   if t)
                        w4 = tuple(t for t in (ut[i + 1],
                                               e5v[i - 2] if i >= 2 else None)
                                   if t)
                        ta3 = p.act(E3b, u2, AF.Copy, scale=Ap, waits=w3)
                        ta4 = p.act(E4b, u2, AF.Copy, scale=Bp, waits=w4)
                        tc_ = p.tt(E[2], ESQ[:, (2 * i) * PT:(2 * i + 1) * PT],
                                   ESQ[:, (2 * i + 1) * PT:(2 * i + 2) * PT],
                                   ALU.add,
                                   waits=(stoks[i], p.tok('t')) if i == 0 else
                                         (stoks[i],))
                        tv = p.stt(E3b, v2, Bp, E3b, ALU.mult, ALU.add,
                                   waits=(ta3,))
                        tv = p.stt(E4b, v2, Cp, E4b, ALU.mult, ALU.add,
                                   waits=(ta4,))
                        e5u[i] = p.tt(E[5], E[2], E3b, ALU.mult)
                        nut[i] = p.tt(u1, u2, E[5], ALU.add, waits=ta_prev)
                        e5v[i] = p.tt(E[5], E[2], E4b, ALU.mult)
                        tv = p.tt(v1, v2, E[5], ALU.add)
                        ta_prev = (ta3, ta4)
                    ut = nut
            for (srcT, dstT) in ((QR, QRI), (QI, QII), (KR, KRI), (KI, KII)):
                p.copy(dstT[:].rearrange("p (t j) -> p t j", j=8)[:, :, 7],
                       srcT[:, 7 * PT:8 * PT])
            p.ts(QIN[:], QII[:], -1.0, None, ALU.mult)
            for g in range(PT // 16):
                tk = p.trans(PS[2][:, 0:128], NV[:, g * 128:(g + 1) * 128], IDF[:],
                             waits=(p.tok('v'), p.tok('s')))
                p.act(NVT[:, g * 128:(g + 1) * 128], PS[2][:, 0:128], AF.Copy,
                      waits=(tk,))
            tv = p.red(SD, NV[:].rearrange("p (t i) -> p t i", i=8), ALU.add)
            tsg = p.act(CT, SD, AF.Sign, waits=(tv,))
            tv = p.ts(CT, CT, PPT[:, 74:75], None, ALU.mult, waits=(tsg,))
            S0all = TA[:, 0:2048]
            WT4all = TA[:, 2048:4096]
            ET2 = [TA[:, 4096:6144], TA[:, 6144:8192]]
            ts0 = None
            for gb in range(PT // 64):
                g0 = gb * 4
                pr_t = pi_t = None
                for k in range(4):
                    gs = slice((g0 + k) * 128, (g0 + k + 1) * 128)
                    ks = slice(k * 128, (k + 1) * 128)
                    w_ = [p.tok('s'), p.tok('v')] if k == 0 else []
                    p.mm(PS[0][:, ks], QRI[:, gs], KRI[:, gs], start=True,
                         stop=False, waits=w_, inc=False)
                    pr_t = p.mm(PS[0][:, ks], QII[:, gs], KII[:, gs], start=False,
                                stop=True, inc=(k == 3))
                    p.mm(PS[1][:, ks], QRI[:, gs], KII[:, gs], start=True,
                         stop=False, inc=False)
                    pi_t = p.mm(PS[1][:, ks], QIN[:, gs], KRI[:, gs], start=False,
                                stop=True, inc=(k == 3))
                t1 = p.act(SQA[gb % 2], PS[0][:], AF.Square, waits=(pr_t,))
                t2 = p.act(SQB[gb % 2], PS[1][:], AF.Square, waits=(pi_t,))
                ts0 = p.tt(S0all[:, gb * 512:(gb + 1) * 512], SQA[gb % 2],
                           SQB[gb % 2], ALU.add, waits=(t2,))
            if t0 + PT < S:
                emit_qkv(t0 + PT)
            for c in range(4):
                ET = ET2[c % 2]
                tew = [ts0]
                if c >= 2:
                    tew += [p.tok('v'), p.tok('t')]
                te = p.act(ET, S0all, AF.Exp, scale=PPT[:, 67 + c:68 + c],
                           waits=tuple(tew))
                tv = p.red(SUS[c], ET.rearrange("p (a b) -> p a b", b=8),
                           ALU.add, waits=(te,))
                tv = p.act(SUS[c], SUS[c], AF.Ln, waits=(tv,))
                tv = p.act(SUS[c], SUS[c], AF.Exp, scale=-1.0)
                tv = p.tt(ET.rearrange("p (a b) -> p a b", b=8),
                          ET.rearrange("p (a b) -> p a b", b=8),
                          SUS[c].unsqueeze(2).broadcast_to((128, PT, 8)),
                          ALU.mult, waits=(tv,))
                tv = p.act(ET, ET, AF.Ln, bias=1e-12, waits=(tv,))
                tv = p.act(ET, ET, AF.Exp, scale=0.5)
                tv = p.tt(ET.rearrange("p (a b) -> p a b", b=128),
                          ET.rearrange("p (a b) -> p a b", b=128),
                          MSK8[:].unsqueeze(1).broadcast_to((128, 16, 128)),
                          ALU.mult, waits=(tv,))
                tbanks = [PS[0], PS[1], PS[2], PS[6]]
                tkall = None
                for gb in range(4):
                    for k in range(4):
                        ks = slice(k * 128, (k + 1) * 128)
                        cs = slice(gb * 512 + k * 128, gb * 512 + (k + 1) * 128)
                        tkall = p.trans(tbanks[gb][:, ks], ET[:, cs], IDF[:],
                                        waits=(tv, p.tok('v'), p.tok('s'))
                                        if (gb == 0 and k == 0) else (tv,),
                                        inc=(gb == 3 and k == 3))
                tws = {}
                for gb in range(4):
                    for k in range(4):
                        ks = slice(k * 128, (k + 1) * 128)
                        cs = slice(gb * 512 + k * 128, gb * 512 + (k + 1) * 128)
                        tws[(gb, k)] = p.act(WT4all[:, cs], tbanks[gb][:, ks],
                                             AF.Copy, waits=(tkall,))
                for gb in range(4):
                    g0 = gb * 4
                    wbank = PS[3] if gb % 2 == 0 else PS[7]
                    mt = None
                    for k in range(4):
                        ks = slice(k * 128, (k + 1) * 128)
                        cs = slice(gb * 512 + k * 128, gb * 512 + (k + 1) * 128)
                        gs = slice((g0 + k) * 128, (g0 + k + 1) * 128)
                        mt = p.mm(wbank[:, ks], NVT[:, gs], WT4all[:, cs],
                                  start=True, stop=True,
                                  waits=(tws[(gb, k)], p.tok('s')),
                                  inc=(k == 3))
                    tsg2 = p.act(sa, wbank[:], AF.Sign, waits=(mt, p.tok('v')))
                    tab = p.act(sb_, wbank[:], AF.Abs)
                    tv2 = p.ts(sb_, sb_, 1.0, 1e-6, ALU.min, ALU.max,
                               waits=(tab,))
                    tv2 = p.tt(sb_, sb_, sa, ALU.mult)
                    gsl = slice(gb * 512, (gb + 1) * 512)
                    if c == 0:
                        p.ts(G[:, gsl], sb_, PPT[:, 63:64], None, ALU.mult)
                    else:
                        p.stt(G[:, gsl], sb_, PPT[:, 63 + c:64 + c], G[:, gsl],
                              ALU.mult, ALU.add)
            msp = p.tt(M3.rearrange("p (t i) -> p t i", i=8),
                       G[:].rearrange("p (t i) -> p t i", i=8),
                       CT.unsqueeze(2).broadcast_to((128, PT, 8)), ALU.mult,
                       waits=(p.tok('t'),))
            def load_out_oc(ocn, slot):
                srcf = woutT_f[ocn * 1024:(ocn + 1) * 1024, :].rearrange(
                    "(c p) f -> p c f", p=128)
                return p.dma(WB[:, slot * 1024:slot * 1024 + 1024], srcf,
                             f"wv_ld{slot}", waits=(p.tok('t'),))
            otoks = {0: load_out_oc(0, 0), 1: load_out_oc(1, 1)}
            for ocn in range(NCH):
                ldx = p.dma(xold, x_src[ocn * 128:(ocn + 1) * 128, t0:t0 + PT],
                            "xq_ld", waits=(p.tok('v'),))
                bank = PS[4 + ocn % 2]
                slot = ocn % 2
                for ic in range(NCH):
                    wh = WB[:, slot * 1024 + ic * 128:slot * 1024 + (ic + 1) * 128]
                    m3s = M3.rearrange("p (t i) -> p t i", i=8)[:, :, ic]
                    w0 = [otoks[ocn], msp]
                    if ic == 0:
                        w0 += [p.tok('v'), p.tok('s')]
                    ot = p.mm(bank[:, 0:PT], wh, m3s, start=(ic == 0),
                              stop=(ic == NCH - 1), waits=w0,
                              inc=(ic == NCH - 1))
                if ocn + 2 < NCH:
                    otoks[ocn + 2] = load_out_oc(ocn + 2, (ocn + 2) % 2)
                xn = STG[:, 1024 + (ocn % 2) * 256:1024 + (ocn % 2) * 256 + 256]
                tv = p.stt(xn, xold, PPT[:, 80 + ocn:81 + ocn], bank[:, 0:PT],
                           ALU.add, ALU.add,
                           waits=tuple(t for t in
                                       (ot, ldx, qst.get(ocn % 2), p.tok('s'))
                                       if t))
                qst[ocn % 2] = p.dma(x_dst[ocn * 128:(ocn + 1) * 128,
                                           t0:t0 + PT], xn, "xq_st",
                                     waits=(tv,))
                smt = stats_acc(xn, STG[:, 512:768], PS[0], PS[1], ocn == 0,
                                ocn == NCH - 1, PT, tv)
            stats_fin(t0, PT, PS[0], PS[1], smt)
        p.barrier()

    def l1_ffn(x_src, x_dst):
        RQ = TAb[:, 0:8192]
        OPP = SM[:, 3230:3238]
        RCPP = SM[:, 3238:3246]
        tmp1 = SM[:, 3246:3247]
        for c in range(NCH):
            wt = STG[:, 0:1024]
            ld = p.dma(wt, rhs1_d[c * 128:(c + 1) * 128, :], "w_ld",
                       waits=(p.tok('v'), p.tok('t')))
            for b in range(8):
                tk = p.trans(PS[0][:, 0:128], wt[:, b * 128:(b + 1) * 128],
                             IDF[:], waits=(ld, p.tok('v')))
                tr = p.red(tmp1, PS[0][:, 0:128], ALU.max, absval=True,
                           waits=(tk,))
                if c == 0:
                    p.copy(OPP[:, b:b + 1], tmp1)
                else:
                    p.tt(OPP[:, b:b + 1], OPP[:, b:b + 1], tmp1, ALU.max)
        tv = p.ts(OPP, OPP, 1e-12, None, ALU.max)
        tv = p.act(RCPP, OPP, AF.Ln, waits=(tv,))
        tv = p.act(RCPP, RCPP, AF.Exp, scale=-1.0)
        rrow = SM[0:1, 1032:2056]
        for b in range(8):
            tk = p.trans(PS[0][0:1, 0:128], RCPP[:, b:b + 1], IDF[:],
                         waits=(tv, p.tok('v')))
            p.copy(rrow[0:1, b * 128:(b + 1) * 128], PS[0][0:1, 0:128],
                   waits=(tk,))
        tb1 = p.mm(PS[1][:], ONE[0:1, :], rrow[0:1, 0:512], start=True,
                   stop=True, waits=(p.tok('v'),))
        tb2 = p.mm(PS[2][:], ONE[0:1, :], rrow[0:1, 512:1024], start=True,
                   stop=True)
        for c in range(NCH):
            wt = STG[:, 0:1024]
            ld = p.dma(wt, rhs1_d[c * 128:(c + 1) * 128, :], "w_ld",
                       waits=(p.tok('v'), p.tok('t')))
            for half in range(2):
                u = TC[:, 2048:2560]
                bcast = PS[1] if half == 0 else PS[2]
                tv = p.stt(u, wt[:, half * 512:(half + 1) * 512], 127.0,
                           bcast[:], ALU.mult, ALU.mult, waits=(ld, tb2))
                tv = rnd(RQ[:, c * 1024 + half * 512:c * 1024 + (half + 1) * 512],
                         u, TC[:, 3072:3584], TC[:, 3584:4096])
        RSC = SM[:, 1024:1032]
        tv = p.ts(RSC, OPP, 1.0 / 127.0, None, ALU.mult)
        p.barrier()
        for it in range(NT):
            t0 = it * 512
            xt = TB[:, 0:4096]
            ldx = None
            for c in range(NCH):
                ldx = p.dma(xt[:, c * 512:(c + 1) * 512],
                            x_src[c * 128:(c + 1) * 128, t0:t0 + 512], "fx_ld",
                            waits=(p.tok('v'), p.tok('t')))
            ldm = p.dma(SM[0:1, 1032:1544], st_m[0:1, t0:t0 + 512], "fm_ld",
                        waits=(p.tok('t'),))
            ldr = p.dma(SM[0:1, 1544:2056], st_r[0:1, t0:t0 + 512], "fm_ld")
            tb1 = p.mm(PS[5][:], ONE[0:1, :], SM[0:1, 1032:1544], start=True,
                       stop=True, waits=(p.dtok("fm_ld"), p.tok('v')))
            tb2 = p.mm(PS[6][:], ONE[0:1, :], SM[0:1, 1544:2056], start=True,
                       stop=True)
            mb = TC[:, 2048:2560]
            rb = TC[:, 2560:3072]
            tv = p.copy(mb, PS[5][:], waits=(tb1,))
            tv = p.copy(rb, PS[6][:], waits=(tb2,))
            y4 = TB[:, 4096:8192]
            YMX = SM[:, 3230:3234]
            RCP4 = SM[:, 3234:3238]
            tmp2 = SM[:, 3246:3247]
            for c in range(NCH):
                y4c = y4[:, c * 512:(c + 1) * 512]
                tv = p.tt(y4c, xt[:, c * 512:(c + 1) * 512], mb, ALU.subtract,
                          waits=(ldx, p.dtok("epi_st")))
                tv = p.tt(y4c, y4c, rb, ALU.mult)
                for sb2 in range(4):
                    tk = p.trans(PS[0][:, 0:128], y4c[:, sb2 * 128:sb2 * 128 + 128],
                                 IDF[:], waits=(tv, p.tok('v')))
                    tr = p.red(tmp2, PS[0][:, 0:128], ALU.max, absval=True,
                               waits=(tk,))
                    if c == 0:
                        p.copy(YMX[:, sb2:sb2 + 1], tmp2)
                    else:
                        p.tt(YMX[:, sb2:sb2 + 1], YMX[:, sb2:sb2 + 1], tmp2,
                             ALU.max)
            tv = p.ts(YMX, YMX, 1e-12, None, ALU.max)
            tv = p.act(RCP4, YMX, AF.Ln, waits=(tv,))
            tv = p.act(RCP4, RCP4, AF.Exp, scale=-1.0)
            recrow = SM[0:1, 2560:3072]
            lsrow = SM[0:1, 0:512]
            for sb2 in range(4):
                tk = p.trans(PS[0][0:1, 0:128], RCP4[:, sb2:sb2 + 1], IDF[:],
                             waits=(tv, p.tok('v')))
                p.copy(recrow[0:1, sb2 * 128:(sb2 + 1) * 128],
                       PS[0][0:1, 0:128], waits=(tk,))
                tk = p.trans(PS[0][0:1, 0:128], YMX[:, sb2:sb2 + 1], IDF[:],
                             waits=(p.tok('v'),))
                p.copy(lsrow[0:1, sb2 * 128:(sb2 + 1) * 128],
                       PS[0][0:1, 0:128], waits=(tk,))
            tv = p.ts(lsrow, lsrow, 1.0 / 127.0, None, ALU.mult)
            tb3 = p.mm(PS[1][:], ONE[0:1, :], recrow, start=True, stop=True,
                       waits=(p.tok('v'),))
            tb4 = p.mm(PS[5][:], ONE[0:1, :], lsrow, start=True, stop=True)
            lsb = TC[:, 2560:3072]
            tv = p.copy(lsb, PS[5][:], waits=(tb4,))
            y4q = TCb[:, 0:4096]
            for c in range(NCH):
                u = TC[:, 3584:4096]
                tv = p.stt(u, y4[:, c * 512:(c + 1) * 512], 127.0, PS[1][:],
                           ALU.mult, ALU.mult,
                           waits=(tb3, p.tok('s'), p.tok('t'), p.dtok("fxn_st")))
                tv = rnd(y4q[:, c * 512:(c + 1) * 512], u, TC[:, 2048:2560],
                         TC[:, 3072:3584])
            for ocn in range(NCH):
                bank = PS[2 + ocn % 2]
                for c in range(NCH):
                    w0 = []
                    if c == 0:
                        w0 += [p.tok('v'), p.tok('s')]
                    dn = p.mm(bank[:], RQ[:, c * 1024 + ocn * 128:
                                          c * 1024 + ocn * 128 + 128],
                              y4q[:, c * 512:(c + 1) * 512],
                              start=(c == 0), stop=(c == NCH - 1), waits=w0,
                              inc=True)
                xn = TC[:, 3072 + (ocn % 2) * 512:3072 + (ocn % 2) * 512 + 512]
                tv = p.tt(xn, bank[:], lsb, ALU.mult, waits=(dn,))
                tv = p.ts(xn, xn, RSC[:, ocn:ocn + 1], None, ALU.mult)
                tv = p.tt(xn, xn, xt[:, ocn * 512:(ocn + 1) * 512], ALU.add)
                for sb2 in range(4):
                    tk = p.trans(PS[sb2 % 2][:, 0:128],
                                 xn[:, sb2 * 128:sb2 * 128 + 128], IDF[:],
                                 waits=(tv, p.tok('v')))
                    p.copy(TB[:, 4096 + sb2 * 1024 + ocn * 128:
                              4096 + sb2 * 1024 + (ocn + 1) * 128],
                           PS[sb2 % 2][:, 0:128],
                           waits=(tk, p.dtok("epi_st")))
            for sb2 in range(4):
                p.dma(y_out[t0 + sb2 * 128:t0 + sb2 * 128 + 128, :],
                      TB[:, 4096 + sb2 * 1024:4096 + (sb2 + 1) * 1024],
                      "epi_st", waits=(p.tok('v'),))
        p.barrier()

    phase_input()
    cur, nxt = xA, xB
    if 0 in layers:
        l0_attn(cur, nxt)
        cur, nxt = nxt, cur
    if 1 in layers:
        l0_ffn(cur, nxt)
        cur, nxt = nxt, cur
    if 2 in layers:
        l1_attn(cur, nxt)
        cur, nxt = nxt, cur
    if 3 in layers:
        l1_ffn(cur, nxt)
        cur, nxt = nxt, cur
    else:
        epilogue(cur)
    return nc


def _consts():
    identf = np.eye(128, dtype=np.float32)
    identb = np.eye(128, dtype=ml_dtypes.bfloat16)
    onesf = np.ones((128, 128), dtype=np.float32)
    mask8 = np.zeros((128, 128), dtype=np.float32)
    for b in range(16):
        mask8[b * 8:(b + 1) * 8, b * 8:(b + 1) * 8] = 1.0
    sel2 = np.zeros((128, 2), dtype=np.float32)
    sel2[0:64, 0] = 1.0
    sel2[64:128, 1] = 1.0
    sel2t = np.ascontiguousarray(sel2.T)
    return identf, identb, onesf, mask8, sel2, sel2t


def make_in_maps(inputs, seq=None):
    x = np.ascontiguousarray(np.asarray(inputs["L0_attn_x"], dtype=np.float32))
    B = x.shape[0]
    identf, identb, onesf, mask8, sel2, sel2t = _consts()
    common = {
        "L0_Wq": np.ascontiguousarray(inputs["L0_attn_Wq"], dtype=np.float32),
        "L0_Wkv": np.ascontiguousarray(inputs["L0_attn_Wkv"], dtype=np.float32),
        "L0_Wo": np.ascontiguousarray(inputs["L0_attn_Wo"], dtype=np.float32),
        "L0_w_gate": np.ascontiguousarray(inputs["L0_ffn_w_gate"], dtype=np.float32),
        "L0_w_up": np.ascontiguousarray(inputs["L0_ffn_w_up"], dtype=np.float32),
        "L0_w_down": np.ascontiguousarray(inputs["L0_ffn_w_down"], dtype=np.float32),
        "L1_W_qkv": np.ascontiguousarray(inputs["L1_attn_W_qkv"], dtype=np.float32),
        "L1_b_qkv": np.ascontiguousarray(inputs["L1_attn_b_qkv"], dtype=np.float32).reshape(1, -1),
        "L1_W_out": np.ascontiguousarray(inputs["L1_attn_W_out"], dtype=np.float32),
        "L1_b_out": np.ascontiguousarray(inputs["L1_attn_b_out"], dtype=np.float32).reshape(1, -1),
        "L1_qw": np.ascontiguousarray(inputs["L1_attn_quantum_weights"], dtype=np.float32).reshape(4, 64),
        "L1_phot": np.ascontiguousarray(inputs["L1_attn_photonic_phases"], dtype=np.float32),
        "L1_gates": np.ascontiguousarray(inputs["L1_attn_entanglement_gates"], dtype=np.float32).reshape(3, 32),
        "L1_sup": np.ascontiguousarray(inputs["L1_attn_superposition_logits"], dtype=np.float32).reshape(1, 4),
        "L1_rhs": np.ascontiguousarray(inputs["L1_ffn_rhs"], dtype=np.float32),
        "identf": identf, "identb": identb, "onesf": onesf, "mask8": mask8,
        "sel2": sel2, "sel2t": sel2t,
    }
    return [dict(common, x_in=np.ascontiguousarray(x[i])) for i in range(B)]


def kernel(**inputs):
    x = np.asarray(inputs["L0_attn_x"])
    B, S, _ = x.shape
    nc = build_program(seq=S, layers=(0, 1, 2, 3))
    in_maps = make_in_maps(inputs)
    res = run_bass_kernel_spmd(nc, in_maps, core_ids=list(range(B)))
    out = np.stack([np.asarray(res.results[i]["y_out"]) for i in range(B)], axis=0)
    return out.astype(np.float32)


# version 22
# speedup vs baseline: 1.0415x; 1.0063x over previous
import math
import numpy as np
import ml_dtypes
import concourse.bass as bass
import concourse.mybir as mybir
import concourse.bass_isa as bass_isa
from concourse.bass_utils import run_bass_kernel_spmd

F32 = mybir.dt.float32
BF16 = mybir.dt.bfloat16
AF = mybir.ActivationFunctionType
ALU = mybir.AluOpType
AX = mybir.AxisListType

DM = 1024
NCH = DM // 128
INNER0 = 512
FFN_INNER = 4096
SCALING1 = float(128 ** -0.5)

CONST_VALS = sorted({0.0, 1.0, -1.0, 0.5, -0.5, 16.0, -16.0,
                     1e-5, 1e-12, 0.0625, 1.0 / DM, 1.0 / FFN_INNER,
                     float(np.float32(0.5 * math.pi))})


class P:
    def __init__(self, nc):
        self.nc = nc
        self.eng = {'t': nc.tensor, 'v': nc.vector, 's': nc.scalar,
                    'g': nc.gpsimd, 'y': nc.sync}
        self.chain = {k: nc.alloc_semaphore(f"ch_{k}") for k in 'tvsg'}
        self.cnt = {k: 0 for k in 'tvsg'}
        self.dsem = {}
        self.dcnt = {}
        self.lastw = {}

    def dma_sem(self, name):
        if name not in self.dsem:
            self.dsem[name] = self.nc.alloc_semaphore(f"dma_{name}")
            self.dcnt[name] = 0
        return self.dsem[name]

    def tok(self, e):
        return (e, self.cnt[e])

    def _sem_of(self, token):
        if token[0] == 'd':
            return self.dsem[token[1]], token[2]
        return self.chain[token[0]], token[1]

    def wait(self, e, token):
        if token is None:
            return
        if token[0] == e:
            return
        sem, val = self._sem_of(token)
        if val <= 0:
            return
        key = (e, id(sem))
        if self.lastw.get(key, -1) >= val:
            return
        self.eng[e].wait_ge(sem, val)
        self.lastw[key] = val

    def _chain(self, e, instr):
        c = self.cnt[e]
        if c > 0:
            instr._wait_ge(self.chain[e], c)
        instr.then_inc(self.chain[e], 1)
        self.cnt[e] += 1
        return (e, self.cnt[e])

    def op(self, e, name, *args, waits=(), **kw):
        for t in waits:
            self.wait(e, t)
        instr = getattr(self.eng[e], name)(*args, **kw)
        return self._chain(e, instr)

    def tt(self, out, a, b, alu, waits=(), e='v'):
        return self.op(e, 'tensor_tensor', out, a, b, alu, waits=waits)

    def ts(self, out, a, s1, s2=None, op0=ALU.mult, op1=None, waits=(), e='v'):
        if op1 is None:
            return self.op(e, 'tensor_scalar', out, a, s1, None, op0, waits=waits)
        return self.op(e, 'tensor_scalar', out, a, s1, s2, op0, op1, waits=waits)

    def stt(self, out, in0, scalar, in1, op0, op1, waits=(), e='v'):
        return self.op(e, 'scalar_tensor_tensor', out, in0, scalar, in1, op0, op1,
                       waits=waits)

    def copy(self, out, a, waits=(), e='v'):
        return self.op(e, 'tensor_copy', out, a, waits=waits)

    def red(self, out, a, alu, absval=None, waits=(), e='v'):
        return self.op(e, 'tensor_reduce', out, a, AX.X, alu,
                       apply_absolute_value=absval, waits=waits)

    def act(self, out, a, func, bias=0.0, scale=1.0, waits=(), accum_out=None):
        return self.op('s', 'activation', out, a, func, bias=bias, scale=scale,
                       accum_out=accum_out, waits=waits)

    def memset(self, ap, val, waits=(), e='v'):
        return self.op(e, 'memset', ap, val, waits=waits)

    def gred(self, out, in_, rop, waits=()):
        return self.op('g', 'partition_all_reduce', out, in_, 128, rop, waits=waits)

    def mm(self, out, lhsT, rhs, start, stop, waits=(), inc=True, skip=False):
        for t in waits:
            self.wait('t', t)
        instr = self.nc.tensor.matmul(out, lhsT, rhs, start=start, stop=stop,
                                      skip_group_check=skip)
        if inc:
            instr.then_inc(self.chain['t'], 1)
            self.cnt['t'] += 1
        return ('t', self.cnt['t'])

    def trans(self, out, in_, ident, waits=(), inc=True):
        for t in waits:
            self.wait('t', t)
        instr = self.nc.tensor.matmul(out, in_, ident, start=True, stop=True,
                                      is_transpose=True)
        if inc:
            instr.then_inc(self.chain['t'], 1)
            self.cnt['t'] += 1
        return ('t', self.cnt['t'])

    def dma(self, out, in_, stream, waits=(), e='y', slow=False):
        for t in waits:
            self.wait(e, t)
        sem = self.dma_sem(stream)
        kw = {}
        if slow:
            kw['allow_slow_non_contiguous'] = True
        self.eng[e].dma_start(out, in_, **kw).then_inc(sem, 16)
        self.dcnt[stream] += 1
        return ('d', stream, 16 * self.dcnt[stream])

    def dtok(self, stream):
        self.dma_sem(stream)
        return ('d', stream, 16 * self.dcnt[stream])

    def barrier(self):
        for name in self.dsem:
            self.wait('y', ('d', name, 16 * self.dcnt[name]))
        self.nc.all_engine_barrier()


def build_program(seq=4096, layers=(0, 1, 2, 3)):
    nc = bass.Bass("TRN2", target_bir_lowering=False)
    S = seq
    NT = S // 512
    NCK = S // 128

    x_in = nc.dram_tensor("x_in", [S, DM], F32, kind="ExternalInput").ap()
    wq_d = nc.dram_tensor("L0_Wq", [DM, INNER0], F32, kind="ExternalInput").ap()
    wkv_d = nc.dram_tensor("L0_Wkv", [DM, 2 * INNER0], F32, kind="ExternalInput").ap()
    wo_d = nc.dram_tensor("L0_Wo", [INNER0, DM], F32, kind="ExternalInput").ap()
    wg_d = nc.dram_tensor("L0_w_gate", [FFN_INNER, DM], F32, kind="ExternalInput").ap()
    wu_d = nc.dram_tensor("L0_w_up", [FFN_INNER, DM], F32, kind="ExternalInput").ap()
    wd_d = nc.dram_tensor("L0_w_down", [DM, FFN_INNER], F32, kind="ExternalInput").ap()
    wqkv_d = nc.dram_tensor("L1_W_qkv", [3 * DM, DM], F32, kind="ExternalInput").ap()
    bqkv_d = nc.dram_tensor("L1_b_qkv", [1, 3 * DM], F32, kind="ExternalInput").ap()
    wout_d = nc.dram_tensor("L1_W_out", [DM, DM], F32, kind="ExternalInput").ap()
    bout_d = nc.dram_tensor("L1_b_out", [1, DM], F32, kind="ExternalInput").ap()
    qw_d = nc.dram_tensor("L1_qw", [4, 64], F32, kind="ExternalInput").ap()
    phot_d = nc.dram_tensor("L1_phot", [16, DM], F32, kind="ExternalInput").ap()
    gates_d = nc.dram_tensor("L1_gates", [3, 32], F32, kind="ExternalInput").ap()
    sup_d = nc.dram_tensor("L1_sup", [1, 4], F32, kind="ExternalInput").ap()
    rhs1_d = nc.dram_tensor("L1_rhs", [DM, DM], F32, kind="ExternalInput").ap()
    identf_d = nc.dram_tensor("identf", [128, 128], F32, kind="ExternalInput").ap()
    identb_d = nc.dram_tensor("identb", [128, 128], BF16, kind="ExternalInput").ap()
    onesf_d = nc.dram_tensor("onesf", [128, 128], F32, kind="ExternalInput").ap()
    mask8_d = nc.dram_tensor("mask8", [128, 128], F32, kind="ExternalInput").ap()
    sel2_d = nc.dram_tensor("sel2", [128, 2], F32, kind="ExternalInput").ap()
    sel2t_d = nc.dram_tensor("sel2t", [2, 128], F32, kind="ExternalInput").ap()

    y_out = nc.dram_tensor("y_out", [S, DM], F32, kind="ExternalOutput").ap()

    xA = nc.dram_tensor("xA", [DM, S], F32, kind="Internal").ap()
    xB = nc.dram_tensor("xB", [DM, S], F32, kind="Internal").ap()
    st_m = nc.dram_tensor("st_m", [1, S], F32, kind="Internal").ap()
    st_nm = nc.dram_tensor("st_nm", [1, S], F32, kind="Internal").ap()
    st_r = nc.dram_tensor("st_r", [1, S], F32, kind="Internal").ap()
    qnh_sc = nc.dram_tensor("qn_hi", [INNER0, S], BF16, kind="Internal").ap()
    qnl_sc = nc.dram_tensor("qn_lo", [INNER0, S], BF16, kind="Internal").ap()
    knh_sc = nc.dram_tensor("kn_hi", [INNER0, S], BF16, kind="Internal").ap()
    knl_sc = nc.dram_tensor("kn_lo", [INNER0, S], BF16, kind="Internal").ap()
    vt_sc = nc.dram_tensor("vt_sc", [S, 8 * 65], F32, kind="Internal").ap()
    wgT_sc = nc.dram_tensor("wgT_sc", [DM, FFN_INNER], BF16, kind="Internal").ap()
    wuT_sc = nc.dram_tensor("wuT_sc", [DM, FFN_INNER], BF16, kind="Internal").ap()
    wdT_sc = nc.dram_tensor("wdT_sc", [FFN_INNER, DM], BF16, kind="Internal").ap()
    qkv_sc = nc.dram_tensor("qkv_sc", [3 * DM, S], F32, kind="Internal").ap()
    wqkvT_f = nc.dram_tensor("wqkvT_f", [3 * DM * 8, 128], F32, kind="Internal").ap()
    woutT_f = nc.dram_tensor("woutT_f", [DM * 8, 128], F32, kind="Internal").ap()
    csqkv_sc = nc.dram_tensor("csqkv_sc", [1, 3 * DM], F32, kind="Internal").ap()

    p = P(nc)

    TA = nc.alloc_sbuf_tensor("TA", [128, 16384], F32)
    TB = nc.alloc_sbuf_tensor("TB", [128, 16640], F32)
    TC = nc.alloc_sbuf_tensor("TC", [128, 6400], F32)
    STG = nc.alloc_sbuf_tensor("STG", [128, 4096], F32)
    WB = nc.alloc_sbuf_tensor("WB", [128, 4096], F32)
    SM = nc.alloc_sbuf_tensor("SM", [128, 4096], F32)
    CP = nc.alloc_sbuf_tensor("CP", [128, 16], F32)
    IDF = nc.alloc_sbuf_tensor("IDF", [128, 128], F32)
    IDB = nc.alloc_sbuf_tensor("IDB", [128, 128], BF16)
    ONE = nc.alloc_sbuf_tensor("ONE", [128, 128], F32)
    MSK8 = nc.alloc_sbuf_tensor("MSK8", [128, 128], F32)
    PS = [nc.alloc_psum_tensor(f"ps{i}", [128, 512], F32) for i in range(8)]

    STT = nc.alloc_sbuf_tensor("STT", [128, 768], F32)

    TAb = TA[:].bitcast(BF16)
    TBb = TB[:].bitcast(BF16)
    TCb = TC[:].bitcast(BF16)
    STGb = STG[:].bitcast(BF16)
    WBb = WB[:].bitcast(BF16)

    assert len(CONST_VALS) <= 16
    for i, val in enumerate(CONST_VALS):
        p.memset(CP[:, i:i + 1], float(val))
        nc.const_aps.register(F32, float(val), CP[:, i:i + 1])
    p.dma(IDF[:], identf_d, "cst")
    p.dma(IDB[:], identb_d, "cst")
    p.dma(ONE[:], onesf_d, "cst")
    p.dma(MSK8[:], mask8_d, "cst")
    PFLAG = CP[:, 14:15]
    p.memset(PFLAG, 1.0)
    p.ts(PFLAG, PFLAG, 2.0 ** -30, -1.0, ALU.add, ALU.add)
    p.ts(PFLAG, PFLAG, 2.0 ** 30, None, ALU.mult)
    M23 = 12582912.0
    M52 = 6755399441055744.0
    CSQ = SM[0:1, 0:512]
    CSK = SM[0:1, 512:1024]
    CSV = SM[0:1, 1024:1536]
    NMR = SM[0:1, 1536:2048]
    RR = SM[0:1, 2048:2560]
    RSA = SM[0:1, 2560:3072]
    RSB = SM[64:65, 2560:3072]
    RSST_A = SM[64:65, 3584:4096]
    RSST_B = SM[64:65, 3072:3584]
    SEL2T_ = SM[0:2, 3072:3200]
    SEL2_ = SM[:, 3200:3202]
    p.dma(SEL2_, sel2_d, "cst")
    p.dma(SEL2T_, sel2t_d, "cst")
    p.barrier()

    def split(hi, lo, src, waits=()):
        p.copy(hi, src, waits=waits)
        return p.tt(lo, src, hi, ALU.subtract)

    def rnd(out, u, s1, s2, waits=()):
        p.ts(s1, u, M23, -M23, ALU.add, ALU.add, waits=waits)
        p.ts(s2, u, M52, -M52, ALU.add, ALU.add)
        p.tt(s2, s2, s1, ALU.subtract)
        return p.stt(out, s2, PFLAG, s1, ALU.mult, ALU.add)

    def ln_stats(x_src, tag):
        TW = 512
        xa = STG[:, 0:TW]
        sq = STG[:, TW:2 * TW]
        mrow = STG[0:1, 2 * TW:3 * TW]
        nrow = STG[0:1, 3 * TW:4 * TW]
        rrow = STG[0:1, 4 * TW:5 * TW]
        for it in range(S // TW):
            t0 = it * TW
            mmtok = None
            for c in range(NCH):
                ld = p.dma(xa, x_src[c * 128:(c + 1) * 128, t0:t0 + TW],
                           f"lnx_{tag}", waits=(p.tok('s'), p.tok('t')))
                p.mm(PS[6][0:1, 0:TW], ONE[:, 0:1], xa, start=(c == 0),
                     stop=(c == NCH - 1), waits=(ld,), inc=True)
                tsq = p.act(sq, xa, AF.Square, waits=(ld,))
                mmtok = p.mm(PS[7][0:1, 0:TW], ONE[:, 0:1], sq, start=(c == 0),
                             stop=(c == NCH - 1), waits=(tsq,), inc=True)
            p.ts(mrow, PS[6][0:1, 0:TW], 1.0 / DM, None, ALU.mult,
                 waits=(mmtok, p.dtok(f"lns_{tag}")))
            p.ts(nrow, mrow, -1.0, None, ALU.mult)
            p.tt(rrow, mrow, mrow, ALU.mult)
            tv = p.stt(rrow, PS[7][0:1, 0:TW], 1.0 / DM, rrow, ALU.mult,
                       ALU.subtract)
            tv = p.act(rrow, rrow, AF.Ln, bias=1e-5, waits=(tv,))
            tv = p.act(rrow, rrow, AF.Exp, scale=-0.5)
            p.dma(st_m[0:1, t0:t0 + TW], mrow, f"lns_{tag}",
                  waits=(p.tok('v'), tv))
            p.dma(st_nm[0:1, t0:t0 + TW], nrow, f"lns_{tag}")
            p.dma(st_r[0:1, t0:t0 + TW], rrow, f"lns_{tag}")
        p.barrier()

    STM = STT[0:1, 0:256]
    STN = STT[0:1, 256:512]
    STR = STT[0:1, 512:768]

    def stats_acc(xn, sq, sumbank, sqbank, first, last, tw, xtok):
        tq = p.act(sq, xn, AF.Square, waits=(xtok,))
        p.mm(sumbank[0:1, 0:tw], ONE[:, 0:1], xn, start=first, stop=last,
             waits=(xtok,), inc=False, skip=True)
        return p.mm(sqbank[0:1, 0:tw], ONE[:, 0:1], sq, start=first,
                    stop=last, waits=(tq,), inc=last, skip=True)

    def stats_fin(t0, tw, sumbank, sqbank, smt):
        for h in range(tw // 256):
            c0 = h * 256
            tv = p.ts(STM, sumbank[0:1, c0:c0 + 256], 1.0 / DM, None, ALU.mult,
                      waits=(smt, p.dtok("st_st")))
            p.ts(STN, STM, -1.0, None, ALU.mult)
            p.tt(STR, STM, STM, ALU.mult)
            tv = p.stt(STR, sqbank[0:1, c0:c0 + 256], 1.0 / DM, STR,
                       ALU.mult, ALU.subtract)
            tv = p.act(STR, STR, AF.Ln, bias=1e-5, waits=(tv,))
            tv = p.act(STR, STR, AF.Exp, scale=-0.5)
            p.dma(st_m[0:1, t0 + c0:t0 + c0 + 256], STM, "st_st",
                  waits=(p.tok('v'), tv))
            p.dma(st_nm[0:1, t0 + c0:t0 + c0 + 256], STN, "st_st")
            p.dma(st_r[0:1, t0 + c0:t0 + c0 + 256], STR, "st_st")

    def epilogue(x_src):
        for it in range(NCK):
            t0 = it * 128
            for c in range(NCH):
                xa = STG[:, (c % 4) * 128:(c % 4) * 128 + 128]
                ld = p.dma(xa, x_src[c * 128:(c + 1) * 128, t0:t0 + 128], "epi_ld",
                           waits=(p.tok('t'),))
                bank = PS[c % 2]
                tk = p.trans(bank[:, 0:128], xa, IDF[:], waits=(ld, p.tok('v')))
                p.copy(STG[:, 1024 + c * 128:1024 + (c + 1) * 128],
                       bank[:, 0:128], waits=(tk,))
            st = p.dma(y_out[t0:t0 + 128, :], STG[:, 1024:1024 + DM], "epi_st",
                       waits=(p.tok('v'),))
            p.wait('t', st)
        p.barrier()

    def phase_input():
        YS = SM[:, 0:32]
        YQ = SM[:, 32:64]
        for it in range(NCK):
            t0 = it * 128
            ld = p.dma(STG[:, 0:DM], x_in[t0:t0 + 128, :], "in_ld",
                       waits=(p.tok('t'), p.tok('v'), p.tok('s')))
            p.red(YS[:, it:it + 1], STG[:, 0:DM], ALU.add, waits=(ld,))
            tq = p.act(STG[:, 2560:3584], STG[:, 0:DM], AF.Square, waits=(ld,))
            p.red(YQ[:, it:it + 1], STG[:, 2560:3584], ALU.add, waits=(tq,))
            for c in range(NCH):
                bank = PS[c % 2]
                tk = p.trans(bank[:, 0:128], STG[:, c * 128:(c + 1) * 128], IDF[:],
                             waits=(ld, p.tok('v')))
                cp = p.copy(STG[:, 1536 + c * 128:1536 + (c + 1) * 128],
                            bank[:, 0:128], waits=(tk,))
                p.dma(xA[c * 128:(c + 1) * 128, t0:t0 + 128],
                      STG[:, 1536 + c * 128:1536 + (c + 1) * 128], "in_st",
                      waits=(cp,))
            p.wait('t', p.dtok("in_st"))
        M_ = SM[:, 64:96]
        V_ = SM[:, 96:128]
        N_ = SM[:, 128:160]
        p.ts(M_, YS, 1.0 / DM, None, ALU.mult)
        p.ts(N_, M_, -1.0, None, ALU.mult)
        p.tt(V_, M_, M_, ALU.mult)
        tv = p.stt(V_, YQ, 1.0 / DM, V_, ALU.mult, ALU.subtract)
        tv = p.act(V_, V_, AF.Ln, bias=1e-5, waits=(tv,))
        tv = p.act(V_, V_, AF.Exp, scale=-0.5)
        for (src, dst, nm) in ((M_, st_m, 0), (N_, st_nm, 1), (V_, st_r, 2)):
            tk = p.trans(PS[2][0:32, 0:128], src, IDF[:], waits=(p.tok('v'), tv))
            cv = p.copy(SM[0:32, 160 + nm * 128:160 + nm * 128 + 128],
                        PS[2][0:32, 0:128], waits=(tk,))
            p.dma(dst.rearrange("o (a b) -> (o a) b", b=128),
                  SM[0:32, 160 + nm * 128:160 + nm * 128 + 128], "st_st",
                  waits=(cv,))
        p.barrier()

    def l0_attn(x_src, x_dst):
        for c in range(NCH):
            dst = TA if c < 4 else TB
            p.dma(dst[:, (c % 4) * S:(c % 4) * S + S],
                  x_src[c * 128:(c + 1) * 128, :], f"xs_ld{c % 3}",
                  e=('y', 'g', 's')[c % 3])
        p.barrier()

        def load_w_f(wsrc, cols, csrow):
            cs_tok = None
            nchunks = wsrc.shape[0] // 128
            for c in range(nchunks):
                ld = p.dma(WB[:, c * cols:(c + 1) * cols],
                           wsrc[c * 128:(c + 1) * 128, 0:cols], "w_ld",
                           waits=(p.tok('t'),))
                for j in range(cols // 512):
                    cs_tok = p.mm(PS[7][0:1, j * 512:(j + 1) * 512],
                                  ONE[:, 0:1],
                                  WB[:, c * cols + j * 512:c * cols + (j + 1) * 512],
                                  start=(c == 0), stop=(c == nchunks - 1),
                                  waits=(ld,), inc=True)
            return p.copy(csrow[0:1, 0:cols], PS[7][0:1, 0:cols],
                          waits=(cs_tok,))

        def proj_qkv(which):
            if which == 'q':
                wsrc, csrow = wq_d, CSQ
            elif which == 'k':
                wsrc, csrow = wkv_d[:, 0:512], CSK
            else:
                wsrc, csrow = wkv_d[:, 512:1024], CSV
            cols = 512
            load_w_f(wsrc, cols, csrow)
            p.barrier()
            for it in range(NT):
                t0 = it * 512
                nm_t = p.dma(NMR, st_nm[0:1, t0:t0 + 512], "nm_ld",
                             waits=(p.tok('t'),))
                r_t = None
                if which == 'v':
                    r_t = p.dma(RR, st_r[0:1, t0:t0 + 512], "r_ld",
                                waits=(p.tok('t'),))
                for oc in range(4):
                    bank = PS[oc % 2]
                    for c in range(NCH):
                        st0 = (c == 0)
                        w_f = WB[:, c * cols + oc * 128: c * cols + oc * 128 + 128]
                        xf = (TA if c < 4 else TB)[:, (c % 4) * S + t0:
                                                   (c % 4) * S + t0 + 512]
                        p.mm(bank[:], w_f, xf, start=st0, stop=False,
                             waits=(p.tok('v'), p.tok('s')) if st0 else (),
                             inc=False)
                    mmt = p.mm(bank[:], csrow[0:1, oc * 128:(oc + 1) * 128], NMR,
                               start=False, stop=True, waits=(nm_t,))
                    if which in ('q', 'k'):
                        sq = STG[:, 0:512]
                        tsq = p.act(sq, bank[:], AF.Square, waits=(mmt,))
                        th = p.mm(PS[2][0:2, 0:512], SEL2_, sq, start=True,
                                  stop=True, waits=(tsq, p.tok('v')))
                        ss = STG[:, 512:1024]
                        tv = p.ts(ss[0:2, :], PS[2][0:2, 0:512], 1e-24, None,
                                  ALU.max, waits=(th,))
                        tv = p.act(ss[0:2, :], ss[0:2, :], AF.Ln, waits=(tv,))
                        tv = p.act(ss[0:2, :], ss[0:2, :], AF.Exp, scale=-0.5)
                        tb = p.mm(PS[3][:], SEL2T_, ss[0:2, :], start=True,
                                  stop=True, waits=(tv, p.tok('v')))
                        rnb = STG[:, 1024:1536]
                        tv = p.copy(rnb, PS[3][:], waits=(tb,))
                        qn = STG[:, 1536:2048]
                        tv = p.tt(qn, bank[:], rnb, ALU.mult)
                        hi_o = STGb[:, 4096:4608]
                        lo_o = STGb[:, 4608:5120]
                        tv = split(hi_o, lo_o, qn,
                                   waits=(p.dtok(f"{which}n_st"),))
                        dsth = qnh_sc if which == 'q' else knh_sc
                        dstl = qnl_sc if which == 'q' else knl_sc
                        p.dma(dsth[oc * 128:(oc + 1) * 128, t0:t0 + 512], hi_o,
                              f"{which}n_st", waits=(tv,))
                        p.dma(dstl[oc * 128:(oc + 1) * 128, t0:t0 + 512], lo_o,
                              f"{which}n_st")
                    else:
                        tb = p.mm(PS[2][:], ONE[0:1, :], RR, start=True, stop=True,
                                  waits=(r_t, p.tok('v')))
                        rbs = STG[:, 0:512]
                        tv = p.copy(rbs, PS[2][:], waits=(tb,))
                        vv = STG[:, 512:1024]
                        tv = p.tt(vv, bank[:], rbs, ALU.mult, waits=(mmt,))
                        for sb in range(4):
                            tk = p.trans(PS[3][:, 0:128],
                                         vv[:, sb * 128:sb * 128 + 128],
                                         IDF[:], waits=(tv, p.tok('v')))
                            h0, h1 = 2 * oc, 2 * oc + 1
                            c0 = p.copy(VTS[:, h0 * 65:h0 * 65 + 64],
                                        PS[3][:, 0:64],
                                        waits=(tk, p.dtok("vt_st")))
                            c0 = p.copy(VTS[:, h1 * 65:h1 * 65 + 64],
                                        PS[3][:, 64:128])
                            p.dma(vt_sc[t0 + sb * 128: t0 + sb * 128 + 128,
                                        h0 * 65:h1 * 65 + 65],
                                  VTS[:, h0 * 65:h1 * 65 + 65], "vt_st",
                                  waits=(c0,))
            p.barrier()

        VTS = TC[:, 0:520]
        for h in range(8):
            p.memset(VTS[:, h * 65 + 64:h * 65 + 65], 1.0)
        proj_qkv('q')
        proj_qkv('k')
        proj_qkv('v')

        for c in range(4):
            for it in range(NT):
                t0 = it * 512
                p.dma(TAb[:, c * S + t0:c * S + t0 + 512],
                      knh_sc[c * 128:(c + 1) * 128, t0:t0 + 512], "kn_ld")
                p.dma(TAb[:, 4 * S + c * S + t0:4 * S + c * S + t0 + 512],
                      knl_sc[c * 128:(c + 1) * 128, t0:t0 + 512], "kn_ld")
        for ck in range(NCK):
            p.dma(TB[:, ck * 520:(ck + 1) * 520],
                  vt_sc[ck * 128:(ck + 1) * 128, :], "vt_ld")
        for c in range(4):
            p.dma(WB[:, c * 1024:(c + 1) * 1024], wo_d[c * 128:(c + 1) * 128, :],
                  "w_ld", waits=(p.tok('v'), p.tok('t')))
        p.barrier()

        xst = {}
        for sw in range(NT):
            t0 = sw * 512
            qh_sw = STGb[:, 0:2048]
            ql_sw = STGb[:, 2048:4096]
            for c in range(4):
                p.dma(qh_sw[:, c * 512:(c + 1) * 512],
                      qnh_sc[c * 128:(c + 1) * 128, t0:t0 + 512], "qn_ld",
                      waits=(p.tok('t'),))
                p.dma(ql_sw[:, c * 512:(c + 1) * 512],
                      qnl_sc[c * 128:(c + 1) * 128, t0:t0 + 512], "qn_ld")
            qtok = p.dtok("qn_ld")
            kntok = p.dtok("kn_ld")
            vttok = p.dtok("vt_ld")
            pvs_tok = [None] * 4
            for pr in range(4):
                qh_pr = qh_sw[:, pr * 512:(pr + 1) * 512]
                ql_pr = ql_sw[:, pr * 512:(pr + 1) * 512]
                exp_toks = [None, None]
                pend = None
                for ck in range(NCK):
                    knh = TAb[:, pr * S + t0 * 0 + ck * 128: pr * S + ck * 128 + 128]
                    knl = TAb[:, 4 * S + pr * S + ck * 128: 4 * S + pr * S + ck * 128 + 128]
                    lb, lb2 = PS[ck % 2], PS[2 + ck % 2]
                    w_ = [qtok, kntok]
                    if exp_toks[ck % 2] is not None:
                        w_.append(exp_toks[ck % 2])
                    elif ck == 0:
                        w_.append(p.tok('s'))
                        w_.append(p.tok('v'))
                    p.mm(lb[:], knh[0:64, :], qh_pr[0:64, :], start=True,
                         stop=False, waits=w_, inc=False)
                    p.mm(lb[:], knh[0:64, :], ql_pr[0:64, :], start=False,
                         stop=False, inc=False)
                    p.mm(lb[:], knl[0:64, :], qh_pr[0:64, :], start=False,
                         stop=True, inc=False)
                    p.mm(lb2[:], knh[64:128, :], qh_pr[64:128, :], start=True,
                         stop=False, inc=False)
                    p.mm(lb2[:], knh[64:128, :], ql_pr[64:128, :], start=False,
                         stop=False, inc=False)
                    qk_tok = p.mm(lb2[:], knl[64:128, :], qh_pr[64:128, :],
                                  start=False, stop=True)
                    if pend is not None:
                        pea, peb, ppta, pptb, pvth, pvtl, pfirst = pend
                        w0 = [pea, vttok]
                        if pfirst:
                            w0.append(p.tok('v'))
                        p.mm(PS[4][0:65, :], pvth, ppta, start=pfirst,
                             stop=False, waits=w0, inc=False)
                        p.mm(PS[5][0:65, :], pvtl, pptb, start=pfirst,
                             stop=False, waits=(peb,), inc=False)
                    pta = STG[:, 2048 + (ck % 2) * 512:2048 + (ck % 2) * 512 + 512]
                    ptb = TC[:, 520 + (ck % 2) * 512:520 + (ck % 2) * 512 + 512]
                    ea = p.act(pta, lb[:], AF.Exp, bias=-16.0, scale=16.0,
                               waits=(qk_tok,))
                    eb = p.act(ptb, lb2[:], AF.Exp, bias=-16.0, scale=16.0)
                    exp_toks[ck % 2] = eb
                    vth = TB[:, ck * 520 + (2 * pr) * 65: ck * 520 + (2 * pr) * 65 + 65]
                    vtl = TB[:, ck * 520 + (2 * pr + 1) * 65: ck * 520 + (2 * pr + 1) * 65 + 65]
                    pend = (ea, eb, pta, ptb, vth, vtl, ck == 0)
                pea, peb, ppta, pptb, pvth, pvtl, pfirst = pend
                p.mm(PS[4][0:65, :], pvth, ppta, start=pfirst, stop=True,
                     waits=(pea, vttok), inc=True)
                p.mm(PS[5][0:65, :], pvtl, pptb, start=pfirst, stop=True,
                     waits=(peb,), inc=True)
                pv_tok = p.tok('t')
                c1 = p.copy(RSST_A, PS[4][64:65, :],
                            waits=(pv_tok, p.dtok("rs_mv")))
                c2 = p.copy(RSST_B, PS[5][64:65, :])
                d1 = p.dma(RSA, RSST_A, "rs_mv",
                           waits=(c2, p.tok('t')))
                d2 = p.dma(RSB, RSST_B, "rs_mv")
                tv = p.ts(RSA, RSA, 1e-6, None, ALU.max, waits=(d2,))
                tv = p.act(RSA, RSA, AF.Ln, waits=(tv,))
                tv = p.act(RSA, RSA, AF.Exp, scale=-1.0)
                tvb = p.ts(RSB, RSB, 1e-6, None, ALU.max, waits=(d2,))
                tvb = p.act(RSB, RSB, AF.Ln, waits=(tvb,))
                tvb = p.act(RSB, RSB, AF.Exp, scale=-1.0)
                tb = p.mm(PS[6][0:64, :], ONE[0:1, 0:64], RSA,
                          start=True, stop=True, waits=(tv, p.tok('v')))
                rba = TC[:, 1544:2056]
                pvna = TC[:, 4104 + pr * 512:4104 + (pr + 1) * 512]
                tv2 = p.copy(rba[0:64, :], PS[6][0:64, :], waits=(tb,))
                tv2 = p.tt(pvna[0:64, :], PS[4][0:64, :], rba[0:64, :], ALU.mult)
                tb = p.mm(PS[6][0:64, :], ONE[64:65, 0:64], RSB,
                          start=True, stop=True, waits=(tvb, p.tok('v')))
                tv3 = p.copy(rba[0:64, :], PS[6][0:64, :], waits=(tb,))
                hbst = TC[:, 2568:3080]
                tv3 = p.tt(hbst[0:64, :], PS[5][0:64, :], rba[0:64, :],
                           ALU.mult, waits=(p.dtok("pv_mv"),))
                mv = p.dma(pvna[64:128, :], hbst[0:64, :], "pv_mv", waits=(tv3,))
                pvs_tok[pr] = (mv, tv2)
            for oc in range(NCH):
                bank = PS[2 + oc % 2]
                for pr in range(4):
                    pvna = TC[:, 4104 + pr * 512:4104 + (pr + 1) * 512]
                    wh = WB[:, pr * 1024 + oc * 128: pr * 1024 + oc * 128 + 128]
                    w0 = list(pvs_tok[pr])
                    if pr == 0:
                        w0 += [p.tok('s'), p.tok('v')]
                    wo_tok = p.mm(bank[:], wh, pvna, start=(pr == 0),
                                  stop=(pr == 3), waits=w0, inc=(pr == 3))
                xo = STG[:, 3072 + (oc % 2) * 512:3072 + (oc % 2) * 512 + 512]
                ld = p.dma(xo, x_src[oc * 128:(oc + 1) * 128, t0:t0 + 512],
                           "xo_ld", waits=(p.tok('t'),))
                xn = TC[:, 3080 + (oc % 2) * 512:3080 + (oc % 2) * 512 + 512]
                tv = p.tt(xn, bank[:], xo, ALU.add,
                          waits=tuple(t for t in
                                      (wo_tok, ld, xst.get(oc % 2), p.tok('s'))
                                      if t))
                xst[oc % 2] = p.dma(x_dst[oc * 128:(oc + 1) * 128, t0:t0 + 512],
                                    xn, "xn_st", waits=(tv,))
                smt = stats_acc(xn, STG[:, 2048:2560], PS[7], PS[6], oc == 0,
                                oc == NCH - 1, 512, tv)
            stats_fin(t0, 512, PS[7], PS[6], smt)
        p.barrier()


    SMW = SM[:, 3210:3226]

    def l0_ffn_weights():
        specs = [(wg_d, wgT_sc, 0, 3, FFN_INNER), (wu_d, wuT_sc, 1, 4, FFN_INNER),
                 (wd_d, wdT_sc, 2, 5, DM)]
        for wsrc, _, sci, smi, nrows in specs:
            cols = wsrc.shape[1]
            acc = SM[:, 3226:3227]
            red_o = SM[:, 3227:3228]
            ntiles = nrows // 128
            wbufs = [None] * ntiles
            wtoks = [None] * ntiles
            def ldw(i):
                wbufs[i] = TA[:, (i % 2) * 4096:(i % 2) * 4096 + cols]
                wtoks[i] = p.dma(wbufs[i], wsrc[i * 128:(i + 1) * 128, :],
                                 f"wq_ld{i % 3}", waits=(p.tok('v'),),
                                 e=('y', 'g', 's')[i % 3])
            ldw(0)
            if ntiles > 1:
                ldw(1)
            for i in range(ntiles):
                tr = p.red(red_o, wbufs[i], ALU.add, absval=True,
                           waits=(wtoks[i],))
                if i == 0:
                    p.copy(acc, red_o)
                else:
                    p.tt(acc, acc, red_o, ALU.add)
                if i + 2 < ntiles:
                    ldw(i + 2)
            ts1 = p.mm(PS[7][0:1, 0:1], ONE[:, 0:1], acc, start=True, stop=True,
                       waits=(p.tok('v'),))
            tot = SM[0:1, 3228:3229]
            tv = p.copy(tot, PS[7][0:1, 0:1], waits=(ts1,))
            tb = p.mm(PS[7][:, 4:5], ONE[0:1, :], tot, start=True, stop=True,
                      waits=(tv,))
            tm = p.ts(SMW[:, smi:smi + 1], PS[7][:, 4:5], 1.0 / DM,
                      1.0 / FFN_INNER, ALU.mult, ALU.mult, waits=(tb,))
            tm = p.ts(SMW[:, smi:smi + 1], SMW[:, smi:smi + 1], 1e-5, None,
                      ALU.max)
            p.op('v', 'reciprocal', SMW[:, sci:sci + 1], SMW[:, smi:smi + 1])
        stoks = {}
        for wsrc, wdst, sci, smi, nrows in specs:
            cols = wsrc.shape[1]
            ntiles = nrows // 128
            wbufs = [None] * ntiles
            wtoks = [None] * ntiles
            def ldw2(i):
                wbufs[i] = TA[:, (i % 2) * 4096:(i % 2) * 4096 + cols]
                wtoks[i] = p.dma(wbufs[i], wsrc[i * 128:(i + 1) * 128, :],
                                 f"wq_ld{i % 3}", waits=(p.tok('v'),),
                                 e=('y', 'g', 's')[i % 3])
            ldw2(0)
            if ntiles > 1:
                ldw2(1)
            for i in range(ntiles):
                u_ = TB[:, (i % 2) * 4096:(i % 2) * 4096 + cols]
                gg = TB[:, 8192 + (i % 2) * 4096:8192 + (i % 2) * 4096 + cols]
                rq = TCb[:, (i % 2) * cols:(i % 2) * cols + cols]
                tv = p.ts(u_, wbufs[i], SMW[:, sci:sci + 1], None, ALU.mult,
                          waits=(wtoks[i],))
                tv = p.ts(gg, u_, 0.5, None, ALU.is_gt)
                tv = p.ts(u_, u_, -0.5, None, ALU.is_lt)
                tv = p.tt(rq, gg, u_, ALU.subtract, waits=(p.tok('t'),))
                if i + 2 < ntiles:
                    ldw2(i + 2)
                nb = cols // 128
                if cols <= 2048:
                    stg = TCb[:, 8192 + (i % 2) * cols:8192 + (i % 2) * cols + cols]
                else:
                    stg = TCb[:, 8192:8192 + cols]
                ev = None
                for b in range(nb):
                    bkb = PS[4 + b % 2][:].bitcast(BF16)
                    tk = p.trans(bkb[:, 0:128], rq[:, b * 128:(b + 1) * 128],
                                 IDB[:], waits=(tv, p.tok('v')))
                    w0 = [tk]
                    if b == 0:
                        w0.append(stoks.get(id(wdst), {}).get(i % 2))
                    ev = p.copy(stg[:, b * 128:(b + 1) * 128], bkb[:, 0:128],
                                waits=tuple(t for t in w0 if t))
                dt_ = p.dma(wdst[:, i * 128:(i + 1) * 128].rearrange(
                    "(b p) f -> p b f", p=128),
                    stg.rearrange("p (b f) -> p b f", f=128), "wqT_st",
                    waits=(ev,))
                stoks.setdefault(id(wdst), {})[i % 2] = dt_
        p.barrier()

    def l0_ffn(x_src, x_dst):
        l0_ffn_weights()
        fst = {}
        for it in range(NT):
            t0 = it * 512
            xt = TB[:, 0:4096]
            ldx = None
            for c in range(NCH):
                ldx = p.dma(xt[:, c * 512:(c + 1) * 512],
                            x_src[c * 128:(c + 1) * 128, t0:t0 + 512], "fx_ld",
                            waits=(p.tok('v'), p.tok('g'), p.tok('t')))
            mrow = SM[0:1, 3584:4096]
            ldm = p.dma(mrow, st_m[0:1, t0:t0 + 512], "fm_ld", waits=(p.tok('t'),))
            rrow = SM[0:1, 2560:3072]
            ldr = p.dma(rrow, st_r[0:1, t0:t0 + 512], "fm_ld")
            tb1 = p.mm(PS[5][:], ONE[0:1, :], mrow, start=True, stop=True,
                       waits=(p.dtok("fm_ld"), p.tok('v')))
            tb2 = p.mm(PS[6][:], ONE[0:1, :], rrow, start=True, stop=True,
                       waits=(p.dtok("fm_ld"),))
            mb = TC[:, 2048:2560]
            rb = TC[:, 2560:3072]
            tv = p.copy(mb, PS[5][:], waits=(tb1,))
            tv = p.copy(rb, PS[6][:], waits=(tb2,))
            y2 = TB[:, 4096:8192]
            YMX = SM[:, 3230:3234]
            RCP = SM[:, 3234:3238]
            tmp1 = SM[:, 3238:3239]
            for c in range(NCH):
                y2c = y2[:, c * 512:(c + 1) * 512]
                tv = p.tt(y2c, xt[:, c * 512:(c + 1) * 512], mb, ALU.subtract,
                          waits=(ldx,))
                tv = p.tt(y2c, y2c, rb, ALU.mult)
            MX_ = TC[:, 4096:4608]
            MN_ = TC[:, 4608:5120]
            p.tt(MX_, y2[:, 0:512], y2[:, 512:1024], ALU.max)
            p.tt(MN_, y2[:, 0:512], y2[:, 512:1024], ALU.min)
            for c in range(2, NCH):
                p.tt(MX_, MX_, y2[:, c * 512:(c + 1) * 512], ALU.max)
                p.tt(MN_, MN_, y2[:, c * 512:(c + 1) * 512], ALU.min)
            tn = p.ts(MN_, MN_, -1.0, None, ALU.mult)
            tn = p.tt(MX_, MX_, MN_, ALU.max)
            tkt = None
            for sb2 in range(4):
                tkt = p.trans(PS[0][:, sb2 * 128:(sb2 + 1) * 128],
                              MX_[:, sb2 * 128:(sb2 + 1) * 128], IDF[:],
                              waits=(tn, p.tok('v'), p.tok('s'))
                              if sb2 == 0 else (tn,), inc=(sb2 == 3))
            for sb2 in range(4):
                p.red(YMX[:, sb2:sb2 + 1], PS[0][:, sb2 * 128:(sb2 + 1) * 128],
                      ALU.max, waits=(tkt,))
            tv = p.ts(YMX, YMX, 1e-5, None, ALU.max)
            tv = p.act(RCP, YMX, AF.Ln, waits=(tv,))
            tv = p.act(RCP, RCP, AF.Exp, scale=-1.0)
            recrow = SM[0:1, 1032:1544]
            for sb2 in range(4):
                tk = p.trans(PS[0][0:1, 0:128], RCP[:, sb2:sb2 + 1], IDF[:],
                             waits=(tv, p.tok('v')))
                p.copy(recrow[0:1, sb2 * 128:(sb2 + 1) * 128],
                       PS[0][0:1, 0:128], waits=(tk,))
            tb3 = p.mm(PS[1][:], ONE[0:1, :], recrow, start=True, stop=True,
                       waits=(p.tok('v'),))
            y2q = TCb[:, 0:4096]
            for c in range(NCH):
                u = TC[:, 3584:4096]
                tv = p.stt(u, y2[:, c * 512:(c + 1) * 512], 127.0, PS[1][:],
                           ALU.mult, ALU.mult,
                           waits=tuple(t for t in
                                       (tb3, p.tok('s'), p.tok('t'),
                                        fst.get(1)) if t))
                tv = rnd(y2q[:, c * 512:(c + 1) * 512], u, TC[:, 2048:2560],
                         TC[:, 2560:3072])
            wslot = lambda j, w: TBb[:, 16384 + (j % 2) * 8192 + w * 4096:
                                     16384 + (j % 2) * 8192 + (w + 1) * 4096]
            def load_j(j):
                src_g = wgT_sc.rearrange("(c p) f -> p c f", p=128)[:, :, j * 512:(j + 1) * 512]
                src_u = wuT_sc.rearrange("(c p) f -> p c f", p=128)[:, :, j * 512:(j + 1) * 512]
                p.dma(wslot(j, 0), src_g, f"wg_ld{j % 2}", waits=(p.tok('t'),))
                return p.dma(wslot(j, 1), src_u, f"wg_ld{j % 2}")
            wtok = [None] * 8
            wtok[0] = load_j(0)
            wtok[1] = load_j(1)
            for sb in range(4):
                p.ts(SMW[:, 6 + sb:7 + sb], YMX[:, sb:sb + 1], SMW[:, 3:4],
                     None, ALU.mult, waits=(p.tok('s'),) if sb == 0 else ())
                p.ts(SMW[:, 6 + sb:7 + sb], SMW[:, 6 + sb:7 + sb], 1.0 / 127.0,
                     None, ALU.mult)
                p.ts(SMW[:, 10 + sb:11 + sb], YMX[:, sb:sb + 1], SMW[:, 4:5],
                     None, ALU.mult)
                p.ts(SMW[:, 10 + sb:11 + sb], SMW[:, 10 + sb:11 + sb],
                     1.0 / 127.0, None, ALU.mult)
            for j in range(8):
                for sb in range(4):
                    pg, pu = PS[2 * (sb % 2)], PS[2 * (sb % 2) + 1]
                    for c in range(NCH):
                        w0 = [wtok[j]]
                        if c == 0:
                            w0 += [p.tok('s'), p.tok('v')]
                        p.mm(pg[:], y2q[:, c * 512 + sb * 128: c * 512 + sb * 128 + 128],
                             wslot(j, 0)[:, c * 512:(c + 1) * 512],
                             start=(c == 0), stop=(c == NCH - 1), waits=w0,
                             inc=False)
                        gu_tok = p.mm(pu[:], y2q[:, c * 512 + sb * 128: c * 512 + sb * 128 + 128],
                                      wslot(j, 1)[:, c * 512:(c + 1) * 512],
                                      start=(c == 0), stop=(c == NCH - 1),
                                      inc=(c == NCH - 1))
                    ga = TC[:, 3072 + (sb % 2) * 512:3072 + (sb % 2) * 512 + 512]
                    gaw = (gu_tok,)
                    if j == 0 and sb < 2 and fst.get(sb % 2):
                        gaw = (gu_tok, fst[sb % 2])
                    tsg = p.act(ga, pg[:], AF.Silu, scale=SMW[:, 6 + sb:7 + sb],
                                waits=gaw)
                    th = p.stt(TA[:, sb * 4096 + j * 512:sb * 4096 + (j + 1) * 512],
                               pu[:], SMW[:, 10 + sb:11 + sb], ga,
                               ALU.mult, ALU.mult, waits=(tsg,))
                if j + 2 < 8:
                    wtok[j + 2] = load_j(j + 2)
            dh4 = SM[:, 3226:3230]
            hmc = lambda sb: SM[:, 3238 + sb:3239 + sb]
            r1c = lambda sb: SM[:, 3242 + sb:3243 + sb]
            hmt = [None] * 4
            for sb in range(4):
                H = TA[:, sb * 4096:(sb + 1) * 4096]
                tv = p.red(hmc(sb), H, ALU.max, absval=True)
                tv = p.ts(hmc(sb), hmc(sb), 1e-5, None, ALU.max)
                tv = p.act(hmc(sb), hmc(sb), AF.Ln, waits=(tv,))
                hmt[sb] = p.act(hmc(sb), hmc(sb), AF.Exp, scale=-1.0)
            for sb in range(4):
                H = TA[:, sb * 4096:(sb + 1) * 4096]
                hq = TAb[:, sb * 8192:sb * 8192 + 4096]
                hfm = TAb[:, sb * 8192 + 4096:(sb + 1) * 8192]
                r127 = r1c(sb)
                tv = p.ts(r127, hmc(sb), 127.0, None, ALU.mult,
                          waits=(hmt[sb],))
                p.op('v', 'reciprocal', dh4[:, sb:sb + 1], hmc(sb))
                for k in range(4):
                    u = (TC[:, 4096:5120], TB[:, 4096:5120])[sb % 2]
                    s1 = (TC[:, 5120:6144], TB[:, 5120:6144])[sb % 2]
                    s2 = (TB[:, 6144:7168], TB[:, 7168:8192])[sb % 2]
                    tv = p.ts(u, H[:, k * 1024:(k + 1) * 1024], r127, None,
                              ALU.mult,
                              waits=(p.tok('s'),) if sb == 0 and k == 0 else ())
                    tv = rnd(hq[:, k * 1024:(k + 1) * 1024], u, s1, s2)
                for ic in range(32):
                    bkb = PS[4 + ic % 2][:].bitcast(BF16)
                    tk = p.trans(bkb[:, 0:128],
                                 hq[:, ic * 128:(ic + 1) * 128],
                                 IDB[:], waits=(tv, p.tok('v')))
                    p.copy(hfm[:, ic * 128:(ic + 1) * 128],
                           bkb[:, 0:128], waits=(tk,))
            dhrow = SM[0:1, 3584:4096]
            for sb in range(4):
                tk = p.trans(PS[6][0:1, 0:128], dh4[:, sb:sb + 1], IDF[:],
                             waits=(p.tok('v'),))
                p.copy(dhrow[0:1, sb * 128:(sb + 1) * 128], PS[6][0:1, 0:128],
                       waits=(tk,))
            tbd = p.mm(PS[6][:], ONE[0:1, :], dhrow, start=True, stop=True,
                       waits=(p.tok('v'),))
            dht = SM[:, 1544:2056]
            tv = p.copy(dht, PS[6][:], waits=(tbd,))
            hfm_v = TAb[:].rearrange("p (s r) -> p s r", s=4)
            def load_oc(ocn):
                srcd = wdT_sc.rearrange("(ic p) f -> p ic f", p=128)[:, :, ocn * 128:(ocn + 1) * 128]
                return p.dma(WBb[:, (ocn % 2) * 4096:(ocn % 2) * 4096 + 4096],
                             srcd, f"wd_ld{ocn % 2}", waits=(p.tok('t'),))
            dtoks = {0: load_oc(0), 1: load_oc(1)}
            for ocn in range(NCH):
                bank = PS[2 + ocn % 2]
                for ic in range(32):
                    w0 = [dtoks[ocn]]
                    if ic == 0:
                        w0 += [p.tok('v'), p.tok('s')]
                    dn = p.mm(bank[:], WBb[:, (ocn % 2) * 4096 + ic * 128:
                                           (ocn % 2) * 4096 + (ic + 1) * 128],
                              hfm_v[:, :, 4096 + ic * 128:4096 + (ic + 1) * 128],
                              start=(ic == 0), stop=(ic == 31), waits=w0,
                              inc=(ic == 31))
                if ocn + 2 < NCH:
                    dtoks[ocn + 2] = load_oc(ocn + 2)
                xn = TC[:, 3072 + (ocn % 2) * 512:3072 + (ocn % 2) * 512 + 512]
                tv = p.tt(xn, bank[:], dht, ALU.mult,
                          waits=tuple(t for t in
                                      (dn, fst.get(ocn % 2), p.tok('s')) if t))
                tv = p.ts(xn, xn, SMW[:, 5:6], 1.0 / 127.0, ALU.mult, ALU.mult)
                tv = p.tt(xn, xn, xt[:, ocn * 512:(ocn + 1) * 512], ALU.add)
                fst[ocn % 2] = p.dma(x_dst[ocn * 128:(ocn + 1) * 128,
                                           t0:t0 + 512], xn, "fxn_st",
                                     waits=(tv,))
                smt = stats_acc(xn, STG[:, 0:512], PS[7], PS[6], ocn == 0,
                                ocn == NCH - 1, 512, tv)
            stats_fin(t0, 512, PS[7], PS[6], smt)
        p.barrier()


    PPT = nc.alloc_sbuf_tensor("PPT", [128, 128], F32)
    TWO_PI_HI = float(np.float32(2.0 * math.pi))
    TWO_PI_LO = float(np.float64(2.0 * math.pi) - np.float64(np.float32(2.0 * math.pi)))
    INV_2PI = float(1.0 / (2.0 * math.pi))
    HALF_PI = float(np.float32(0.5 * math.pi))

    def l1_prep():
        ROW = SM[0:1, 0:512]
        ROW2 = SM[0:1, 512:1024]
        grow = SM[0:1, 1024:1120]
        ld = p.dma(grow, gates_d.rearrange("a b -> (a b)").unsqueeze(0), "l1c",
                   waits=(p.tok('t'), p.tok('v'), p.tok('s')))
        for d in range(3):
            th = grow[0:1, 32 * d:32 * d + 28].rearrange("o (i f) -> o i f", f=4)[:, :, 0]
            ph = grow[0:1, 32 * d + 1:32 * d + 29].rearrange("o (i f) -> o i f", f=4)[:, :, 0]
            c_ = ROW[0:1, 0:7]
            s_ = ROW[0:1, 8:15]
            cp = ROW[0:1, 16:23]
            sp = ROW[0:1, 24:31]
            t1 = p.act(c_, th, AF.Sin, bias=HALF_PI, scale=0.5,
                       waits=(ld, p.tok('v')))
            t1 = p.act(s_, th, AF.Sin, scale=0.5)
            t1 = p.act(cp, ph, AF.Sin, bias=HALF_PI)
            t1 = p.act(sp, ph, AF.Sin)
            scp = ROW[0:1, 32:39]
            tv = p.tt(scp, s_, cp, ALU.mult, waits=(t1,))
            tv = p.ts(ROW[0:1, 40:47], c_, -1.0, None, ALU.add)
            tv = p.tt(ROW2[0:1, d * 7:d * 7 + 7], ROW[0:1, 40:47], scp, ALU.add)
            tv = p.tt(ROW2[0:1, 42 + d * 7:42 + d * 7 + 7], ROW[0:1, 40:47], scp,
                      ALU.subtract)
            tv = p.tt(ROW2[0:1, 21 + d * 7:21 + d * 7 + 7], s_, sp, ALU.mult)
        srow = SM[0:1, 1120:1124]
        ld = p.dma(srow, sup_d, "l1c", waits=(p.tok('s'), p.tok('v')))
        t1 = p.act(ROW[0:1, 0:4], srow, AF.Exp, waits=(ld, p.tok('v')))
        tv = p.red(ROW[0:1, 8:9], ROW[0:1, 0:4], ALU.add, waits=(t1,))
        tv = p.op('v', 'reciprocal', ROW[0:1, 9:10], ROW[0:1, 8:9])
        tv = p.tt(ROW2[0:1, 63:67], ROW[0:1, 0:4],
                  ROW[0:1, 9:10].broadcast_to((1, 4)), ALU.mult)
        qrow = SM[0:1, 1152:1408]
        ld = p.dma(qrow, qw_d.rearrange("a b -> (a b)").unsqueeze(0), "l1c",
                   waits=(p.tok('s'),))
        q00 = qrow.rearrange("o (c f) -> o c f", f=64)[:, :, 0]
        t1 = p.act(ROW[0:1, 0:4], q00, AF.Sin, waits=(ld, p.tok('v')))
        tv = p.ts(ROW[0:1, 0:4], ROW[0:1, 0:4], 0.3, 0.5, ALU.mult, ALU.add,
                  waits=(t1,))
        tv = p.ts(ROW[0:1, 8:12], ROW[0:1, 0:4], -1.0, 1.0, ALU.mult, ALU.add)
        tv = p.tt(ROW2[0:1, 67:71], ROW[0:1, 0:4], ROW[0:1, 8:12], ALU.mult)
        tv = p.ts(ROW2[0:1, 67:71], ROW2[0:1, 67:71], SCALING1, None, ALU.mult)
        tbx = p.mm(PS[5][:, 0:71], ONE[0:1, :], ROW2[0:1, 0:71], start=True,
                   stop=True, waits=(p.tok('v'),))
        tb = p.copy(PPT[:, 0:71], PS[5][:, 0:71], waits=(tbx,))
        PHI = SM[:, 1408:1409]
        tmem = p.memset(PHI, 0.0)
        p.wait('y', tmem)
        for r in range(16):
            srcr = phot_d[r:r + 1, :].rearrange("o (q s2) -> o q s2", s2=16)[:, 0:8, r]
            p.dma(PHI[r::16, 0:1], srcr.rearrange("o q -> (o q)").unsqueeze(1),
                  "l1c", slow=True, waits=(p.tok('v'), p.tok('g')))
        phtok = p.dtok("l1c")
        T1 = SM[:, 1409:1410]
        T2 = SM[:, 1410:1411]
        def rr_sin(dst, badd):
            tv = p.ts(T1, PHI, 1.0, badd, ALU.mult, ALU.add,
                      waits=(phtok, p.tok('s')))
            tv = p.ts(T2, T1, INV_2PI, None, ALU.mult)
            tv = rnd(T2, T2, SM[:, 1411:1412], SM[:, 1412:1413])
            tv = p.stt(SM[:, 1411:1412], T2, -TWO_PI_HI, T1, ALU.mult, ALU.add)
            tv = p.stt(SM[:, 1411:1412], T2, -TWO_PI_LO, SM[:, 1411:1412],
                       ALU.mult, ALU.add)
            return p.act(dst, SM[:, 1411:1412], AF.Sin, waits=(tv,))
        rr_sin(PPT[:, 73:74], 0.0)
        rr_sin(PPT[:, 72:73], HALF_PI)
        tv = p.tt(T1, PPT[:, 72:73], PPT[:, 72:73], ALU.mult, waits=(p.tok('s'),))
        tv = p.tt(T2, PPT[:, 73:74], PPT[:, 73:74], ALU.mult)
        tv = p.tt(PPT[:, 74:75], T1, T2, ALU.subtract)
        brow = SM[0:1, 0:1024]
        ld = p.dma(brow, bout_d, "l1c", waits=(p.tok('t'), p.tok('v'), p.tok('s')))
        for ocn in range(8):
            tk = p.trans(PS[4][:, 0:1], brow[0:1, ocn * 128:(ocn + 1) * 128],
                         IDF[0:1, 0:1], waits=(ld, p.tok('v')))
            p.copy(PPT[:, 80 + ocn:81 + ocn], PS[4][:, 0:1], waits=(tk,))
        brow2 = SM[0:1, 1536:2048]
        for half in range(6):
            ld = p.dma(brow2, bqkv_d[0:1, half * 512:(half + 1) * 512], "l1c",
                       waits=(p.tok('t'),))
            for j in range(4):
                ocn = half * 4 + j
                tk = p.trans(PS[4][:, 0:1], brow2[0:1, j * 128:(j + 1) * 128],
                             IDF[0:1, 0:1], waits=(ld, p.tok('v')))
                p.copy(PPT[:, 88 + ocn:89 + ocn], PS[4][:, 0:1], waits=(tk,))
        for (wsrcd, dstf, ntile) in ((wout_d, woutT_f, 8),
                                     (wqkv_d, wqkvT_f, 24)):
            for i in range(ntile):
                wt = STG[:, 0:1024]
                ld = p.dma(wt, wsrcd[i * 128:(i + 1) * 128, :], "w_ld",
                           waits=(p.tok('v'), p.tok('t')))
                for b in range(8):
                    tk = p.trans(PS[4][:, 0:128], wt[:, b * 128:(b + 1) * 128],
                                 IDF[:], waits=(ld, p.tok('v')))
                    stf = STG[:, 1280 + (b % 2) * 128:1280 + (b % 2) * 128 + 128]
                    cv = p.copy(stf, PS[4][:, 0:128],
                                waits=(tk, p.dtok("wT_st")))
                    blk = (i * 8 + b) * 128
                    p.dma(dstf[blk:blk + 128, :], stf, "wT_st", waits=(cv,))
        p.barrier()

    def l1_attn(x_src, x_dst):
        l1_prep()
        PT = 256
        QR = TA[:, 0:2048]
        QI = TA[:, 2048:4096]
        KR = TA[:, 4096:6144]
        KI = TA[:, 6144:8192]
        NV = TA[:, 8192:10240]
        E = [TA[:, 13824 + k * 256:13824 + (k + 1) * 256] for k in range(6)]
        QRI = TB[:, 0:2048]
        QII = TB[:, 2048:4096]
        QIN = TB[:, 4096:6144]
        KRI = TB[:, 6144:8192]
        KII = TB[:, 8192:10240]
        NVT = TB[:, 10240:12288]
        G = TB[:, 12288:14336]
        SD = TB[:, 14336:14592]
        CT = TB[:, 14592:14848]
        QKVT = TB[:, 0:6144]
        sa = TC[:, 0:512]
        sb_ = TC[:, 512:1024]
        SQA = [TC[:, 0:512], TC[:, 1024:1536]]
        SQB = [TC[:, 512:1024], TC[:, 1536:2048]]
        SUS = [TC[:, 2112 + 256 * i:2368 + 256 * i] for i in range(4)]
        M3 = TC[:, 4160:6208]
        xold = STG[:, 2048:2304]
        def emit_qkv(t0):
            ldm = p.dma(SM[0:1, 2048:2048 + PT], st_nm[0:1, t0:t0 + PT], "fm_ld",
                        waits=(p.tok('t'),))
            ldr = p.dma(SM[0:1, 2304:2304 + PT], st_r[0:1, t0:t0 + PT], "fm_ld")
            tb1 = p.mm(PS[6][:, 0:PT], ONE[0:1, :], SM[0:1, 2048:2048 + PT],
                       start=True, stop=True,
                       waits=(p.dtok("fm_ld"), p.tok('v'), p.tok('s')))
            tb2 = p.mm(PS[7][:, 0:PT], ONE[0:1, :], SM[0:1, 2304:2304 + PT],
                       start=True, stop=True)
            nmb = SM[:, 2560:2560 + PT]
            rbb = SM[:, 2816:2816 + PT]
            tv = p.copy(nmb, PS[6][:, 0:PT], waits=(tb1,))
            tv = p.copy(rbb, PS[7][:, 0:PT], waits=(tb2,))
            XN = TC[:, 0:2048]
            for c in range(NCH):
                xc = STG[:, (c % 2) * PT:(c % 2) * PT + PT]
                ldc = p.dma(xc, x_src[c * 128:(c + 1) * 128, t0:t0 + PT],
                            "xq2_ld", waits=(p.tok('v'), p.tok('t')))
                tv = p.tt(XN[:, c * PT:(c + 1) * PT], xc, nmb, ALU.add,
                          waits=(ldc,))
                tv = p.tt(XN[:, c * PT:(c + 1) * PT],
                          XN[:, c * PT:(c + 1) * PT], rbb, ALU.mult)
            def load_qkv_oc(ocn, slot):
                srcf = wqkvT_f[ocn * 1024:(ocn + 1) * 1024, :].rearrange(
                    "(c p) f -> p c f", p=128)
                return p.dma(WB[:, slot * 1024:slot * 1024 + 1024], srcf,
                             f"wv_ld{slot}", waits=(p.tok('t'),))
            qtoks = {0: load_qkv_oc(0, 0), 1: load_qkv_oc(1, 1)}
            for ocn in range(24):
                bank = PS[4 + ocn % 2]
                slot = ocn % 2
                for c in range(NCH):
                    wh = WB[:, slot * 1024 + c * 128:slot * 1024 + (c + 1) * 128]
                    w0 = [qtoks[ocn]]
                    if c == 0:
                        w0 += [p.tok('v'), p.tok('s')]
                    mmt = p.mm(bank[:, 0:PT], wh, XN[:, c * PT:(c + 1) * PT],
                               start=(c == 0), stop=(c == NCH - 1), waits=w0,
                               inc=(c == NCH - 1))
                if ocn + 2 < 24:
                    qtoks[ocn + 2] = load_qkv_oc(ocn + 2, (ocn + 2) % 2)
                p.act(QKVT[:, ocn * PT:(ocn + 1) * PT], bank[:, 0:PT],
                      AF.Identity, bias=PPT[:, 88 + ocn:89 + ocn],
                      waits=(mmt,))

        emit_qkv(0)
        qst = {}
        for itq in range(S // PT):
            t0 = itq * PT
            sqq = TA[:, 10240:14336]
            sqv = TA[:, 14336:16384]
            tq1 = p.act(sqq, QKVT[:, 0:4096], AF.Square, waits=(p.tok('v'),))
            tq2 = p.act(sqv, QKVT[:, 4096:6144], AF.Square)
            nrow = lambda k: SM[0:1, (k % 4) * 512:(k % 4) * 512 + 512]
            rbanks = [PS[4], PS[5], PS[6], PS[7]]
            bbanks = [PS[0], PS[1], PS[2], PS[3]]
            rts = [None] * 12
            ats = [None] * 12
            tbs = [None] * 12
            for k in range(12):
                sq_src = (sqq[:, k * 512:(k + 1) * 512] if k < 8 else
                          sqv[:, (k - 8) * 512:(k - 7) * 512])
                rbank = rbanks[k % 4]
                wm = [tq1 if k < 8 else tq2]
                if k >= 4:
                    wm.append(rts[k - 4])
                tm = p.mm(rbank[0:1, 0:512], ONE[:, 0:1], sq_src, start=True,
                          stop=True, waits=tuple(wm))
                rw = nrow(k)
                wr = [tm]
                if k >= 4:
                    wr.append(tbs[k - 4])
                rts[k] = tv = p.ts(rw, rbank[0:1, 0:512], 1e-24, None, ALU.max,
                                   waits=tuple(wr))
                tv = p.act(rw, rw, AF.Ln, waits=(tv,))
                tv = p.act(rw, rw, AF.Exp, scale=-0.5)
                bbank = bbanks[k % 4]
                wb = [tv]
                if k >= 4:
                    wb.append(ats[k - 4])
                tbs[k] = tb = p.mm(bbank[:], ONE[0:1, :], rw, start=True,
                                   stop=True, waits=tuple(wb))
                if k < 4:
                    p.stt(QR[:, k * 512:(k + 1) * 512],
                          QKVT[:, k * 512:(k + 1) * 512],
                          PPT[:, 72:73], bbank[:], ALU.mult, ALU.mult,
                          waits=(tb,))
                    ats[k] = p.stt(QI[:, k * 512:(k + 1) * 512],
                                   QKVT[:, k * 512:(k + 1) * 512],
                                   PPT[:, 73:74], bbank[:], ALU.mult, ALU.mult)
                elif k < 8:
                    kk = k - 4
                    p.stt(KR[:, kk * 512:(kk + 1) * 512],
                          QKVT[:, k * 512:(k + 1) * 512],
                          PPT[:, 72:73], bbank[:], ALU.mult, ALU.mult,
                          waits=(tb,))
                    ats[k] = p.stt(KI[:, kk * 512:(kk + 1) * 512],
                                   QKVT[:, k * 512:(k + 1) * 512],
                                   PPT[:, 73:74], bbank[:], ALU.mult, ALU.mult)
                else:
                    for hh in range(2):
                        h = (k - 8) * 2 + hh
                        nvs = NV[:].rearrange("p (t i) -> p t i", i=8)[:, :, h]
                        ats[k] = p.tt(nvs, QKVT[:, (16 + h) * PT:(17 + h) * PT],
                                      bbank[:, hh * PT:(hh + 1) * PT], ALU.mult,
                                      waits=(tb,) if hh == 0 else ())
            ESQ = TA[:, 10240:13824]
            for (R, I, Rd, Id) in ((QR, QI, QRI, QII), (KR, KI, KRI, KII)):
                ut = [None] * 8
                for d in range(3):
                    stoks = []
                    for i in range(7):
                        p.act(ESQ[:, (2 * i) * PT:(2 * i + 1) * PT],
                              R[:, i * PT:(i + 1) * PT], AF.Square,
                              waits=(p.tok('v'), p.tok('t')) if i == 0 else ())
                        stoks.append(p.act(ESQ[:, (2 * i + 1) * PT:(2 * i + 2) * PT],
                                           I[:, i * PT:(i + 1) * PT], AF.Square))
                    nut = [None] * 8
                    e5u = [None] * 7
                    e5v = [None] * 7
                    ta_prev = ()
                    for i in range(7):
                        if d == 2:
                            u1 = Rd[:].rearrange("p (t j) -> p t j", j=8)[:, :, i]
                            v1 = Id[:].rearrange("p (t j) -> p t j", j=8)[:, :, i]
                        else:
                            u1 = R[:, i * PT:(i + 1) * PT]
                            v1 = I[:, i * PT:(i + 1) * PT]
                        u2 = R[:, (i + 1) * PT:(i + 2) * PT]
                        v2 = I[:, (i + 1) * PT:(i + 2) * PT]
                        Ap = PPT[:, d * 7 + i:d * 7 + i + 1]
                        Bp = PPT[:, 21 + d * 7 + i:21 + d * 7 + i + 1]
                        Cp = PPT[:, 42 + d * 7 + i:42 + d * 7 + i + 1]
                        E3b = E[3] if i % 2 == 0 else E[0]
                        E4b = E[4] if i % 2 == 0 else E[1]
                        w3 = tuple(t for t in (ut[i + 1],
                                               e5u[i - 2] if i >= 2 else None)
                                   if t)
                        w4 = tuple(t for t in (ut[i + 1],
                                               e5v[i - 2] if i >= 2 else None)
                                   if t)
                        ta3 = p.act(E3b, u2, AF.Copy, scale=Ap, waits=w3)
                        ta4 = p.act(E4b, u2, AF.Copy, scale=Bp, waits=w4)
                        tc_ = p.tt(E[2], ESQ[:, (2 * i) * PT:(2 * i + 1) * PT],
                                   ESQ[:, (2 * i + 1) * PT:(2 * i + 2) * PT],
                                   ALU.add,
                                   waits=(stoks[i], p.tok('t')) if i == 0 else
                                         (stoks[i],))
                        tv = p.stt(E3b, v2, Bp, E3b, ALU.mult, ALU.add,
                                   waits=(ta3,))
                        tv = p.stt(E4b, v2, Cp, E4b, ALU.mult, ALU.add,
                                   waits=(ta4,))
                        e5u[i] = p.tt(E[5], E[2], E3b, ALU.mult)
                        nut[i] = p.tt(u1, u2, E[5], ALU.add, waits=ta_prev)
                        e5v[i] = p.tt(E[5], E[2], E4b, ALU.mult)
                        tv = p.tt(v1, v2, E[5], ALU.add)
                        ta_prev = (ta3, ta4)
                    ut = nut
            for (srcT, dstT) in ((QR, QRI), (QI, QII), (KR, KRI), (KI, KII)):
                p.copy(dstT[:].rearrange("p (t j) -> p t j", j=8)[:, :, 7],
                       srcT[:, 7 * PT:8 * PT])
            p.ts(QIN[:], QII[:], -1.0, None, ALU.mult)
            for g in range(PT // 16):
                tk = p.trans(PS[2][:, 0:128], NV[:, g * 128:(g + 1) * 128], IDF[:],
                             waits=(p.tok('v'), p.tok('s')))
                p.act(NVT[:, g * 128:(g + 1) * 128], PS[2][:, 0:128], AF.Copy,
                      waits=(tk,))
            tv = p.red(SD, NV[:].rearrange("p (t i) -> p t i", i=8), ALU.add)
            tsg = p.act(CT, SD, AF.Sign, waits=(tv,))
            tv = p.ts(CT, CT, PPT[:, 74:75], None, ALU.mult, waits=(tsg,))
            S0all = TA[:, 0:2048]
            WT4all = TA[:, 2048:4096]
            ET2 = [TA[:, 4096:6144], TA[:, 6144:8192]]
            ts0 = None
            for gb in range(PT // 64):
                g0 = gb * 4
                pr_t = pi_t = None
                for k in range(4):
                    gs = slice((g0 + k) * 128, (g0 + k + 1) * 128)
                    ks = slice(k * 128, (k + 1) * 128)
                    w_ = [p.tok('s'), p.tok('v')] if k == 0 else []
                    p.mm(PS[0][:, ks], QRI[:, gs], KRI[:, gs], start=True,
                         stop=False, waits=w_, inc=False)
                    pr_t = p.mm(PS[0][:, ks], QII[:, gs], KII[:, gs], start=False,
                                stop=True, inc=(k == 3))
                    p.mm(PS[1][:, ks], QRI[:, gs], KII[:, gs], start=True,
                         stop=False, inc=False)
                    pi_t = p.mm(PS[1][:, ks], QIN[:, gs], KRI[:, gs], start=False,
                                stop=True, inc=(k == 3))
                t1 = p.act(SQA[gb % 2], PS[0][:], AF.Square, waits=(pr_t,))
                t2 = p.act(SQB[gb % 2], PS[1][:], AF.Square, waits=(pi_t,))
                ts0 = p.tt(S0all[:, gb * 512:(gb + 1) * 512], SQA[gb % 2],
                           SQB[gb % 2], ALU.add, waits=(t2,))
            if t0 + PT < S:
                emit_qkv(t0 + PT)
            for c in range(4):
                ET = ET2[c % 2]
                tew = [ts0]
                if c >= 2:
                    tew += [p.tok('v'), p.tok('t')]
                te = p.act(ET, S0all, AF.Exp, scale=PPT[:, 67 + c:68 + c],
                           waits=tuple(tew))
                tv = p.red(SUS[c], ET.rearrange("p (a b) -> p a b", b=8),
                           ALU.add, waits=(te,))
                tv = p.act(SUS[c], SUS[c], AF.Ln, waits=(tv,))
                tv = p.act(SUS[c], SUS[c], AF.Exp, scale=-1.0)
                tv = p.tt(ET.rearrange("p (a b) -> p a b", b=8),
                          ET.rearrange("p (a b) -> p a b", b=8),
                          SUS[c].unsqueeze(2).broadcast_to((128, PT, 8)),
                          ALU.mult, waits=(tv,))
                tv = p.act(ET, ET, AF.Ln, bias=1e-12, waits=(tv,))
                tv = p.act(ET, ET, AF.Exp, scale=0.5)
                tv = p.tt(ET.rearrange("p (a b) -> p a b", b=128),
                          ET.rearrange("p (a b) -> p a b", b=128),
                          MSK8[:].unsqueeze(1).broadcast_to((128, 16, 128)),
                          ALU.mult, waits=(tv,))
                tbanks = [PS[0], PS[1], PS[2], PS[6]]
                tkall = None
                for gb in range(4):
                    for k in range(4):
                        ks = slice(k * 128, (k + 1) * 128)
                        cs = slice(gb * 512 + k * 128, gb * 512 + (k + 1) * 128)
                        tkall = p.trans(tbanks[gb][:, ks], ET[:, cs], IDF[:],
                                        waits=(tv, p.tok('v'), p.tok('s'))
                                        if (gb == 0 and k == 0) else (tv,),
                                        inc=(gb == 3 and k == 3))
                tws = {}
                for gb in range(4):
                    for k in range(4):
                        ks = slice(k * 128, (k + 1) * 128)
                        cs = slice(gb * 512 + k * 128, gb * 512 + (k + 1) * 128)
                        tws[(gb, k)] = p.act(WT4all[:, cs], tbanks[gb][:, ks],
                                             AF.Copy, waits=(tkall,))
                for gb in range(4):
                    g0 = gb * 4
                    wbank = PS[3] if gb % 2 == 0 else PS[7]
                    mt = None
                    for k in range(4):
                        ks = slice(k * 128, (k + 1) * 128)
                        cs = slice(gb * 512 + k * 128, gb * 512 + (k + 1) * 128)
                        gs = slice((g0 + k) * 128, (g0 + k + 1) * 128)
                        mt = p.mm(wbank[:, ks], NVT[:, gs], WT4all[:, cs],
                                  start=True, stop=True,
                                  waits=(tws[(gb, k)], p.tok('s')),
                                  inc=(k == 3))
                    tsg2 = p.act(sa, wbank[:], AF.Sign, waits=(mt, p.tok('v')))
                    tab = p.act(sb_, wbank[:], AF.Abs)
                    tv2 = p.ts(sb_, sb_, 1.0, 1e-6, ALU.min, ALU.max,
                               waits=(tab,))
                    tv2 = p.tt(sb_, sb_, sa, ALU.mult)
                    gsl = slice(gb * 512, (gb + 1) * 512)
                    if c == 0:
                        p.ts(G[:, gsl], sb_, PPT[:, 63:64], None, ALU.mult)
                    else:
                        p.stt(G[:, gsl], sb_, PPT[:, 63 + c:64 + c], G[:, gsl],
                              ALU.mult, ALU.add)
            msp = p.tt(M3.rearrange("p (t i) -> p t i", i=8),
                       G[:].rearrange("p (t i) -> p t i", i=8),
                       CT.unsqueeze(2).broadcast_to((128, PT, 8)), ALU.mult,
                       waits=(p.tok('t'),))
            def load_out_oc(ocn, slot):
                srcf = woutT_f[ocn * 1024:(ocn + 1) * 1024, :].rearrange(
                    "(c p) f -> p c f", p=128)
                return p.dma(WB[:, slot * 1024:slot * 1024 + 1024], srcf,
                             f"wv_ld{slot}", waits=(p.tok('t'),))
            otoks = {0: load_out_oc(0, 0), 1: load_out_oc(1, 1)}
            for ocn in range(NCH):
                ldx = p.dma(xold, x_src[ocn * 128:(ocn + 1) * 128, t0:t0 + PT],
                            "xq_ld", waits=(p.tok('v'),))
                bank = PS[4 + ocn % 2]
                slot = ocn % 2
                for ic in range(NCH):
                    wh = WB[:, slot * 1024 + ic * 128:slot * 1024 + (ic + 1) * 128]
                    m3s = M3.rearrange("p (t i) -> p t i", i=8)[:, :, ic]
                    w0 = [otoks[ocn], msp]
                    if ic == 0:
                        w0 += [p.tok('v'), p.tok('s')]
                    ot = p.mm(bank[:, 0:PT], wh, m3s, start=(ic == 0),
                              stop=(ic == NCH - 1), waits=w0,
                              inc=(ic == NCH - 1))
                if ocn + 2 < NCH:
                    otoks[ocn + 2] = load_out_oc(ocn + 2, (ocn + 2) % 2)
                xn = STG[:, 1024 + (ocn % 2) * 256:1024 + (ocn % 2) * 256 + 256]
                tv = p.stt(xn, xold, PPT[:, 80 + ocn:81 + ocn], bank[:, 0:PT],
                           ALU.add, ALU.add,
                           waits=tuple(t for t in
                                       (ot, ldx, qst.get(ocn % 2), p.tok('s'))
                                       if t))
                qst[ocn % 2] = p.dma(x_dst[ocn * 128:(ocn + 1) * 128,
                                           t0:t0 + PT], xn, "xq_st",
                                     waits=(tv,))
                smt = stats_acc(xn, STG[:, 512:768], PS[0], PS[1], ocn == 0,
                                ocn == NCH - 1, PT, tv)
            stats_fin(t0, PT, PS[0], PS[1], smt)
        p.barrier()

    def l1_ffn(x_src, x_dst):
        RQ = TAb[:, 0:8192]
        OPP = SM[:, 3230:3238]
        RCPP = SM[:, 3238:3246]
        tmp1 = SM[:, 3246:3247]
        for c in range(NCH):
            wt = STG[:, 0:1024]
            ld = p.dma(wt, rhs1_d[c * 128:(c + 1) * 128, :], "w_ld",
                       waits=(p.tok('v'), p.tok('t')))
            for b in range(8):
                tk = p.trans(PS[0][:, 0:128], wt[:, b * 128:(b + 1) * 128],
                             IDF[:], waits=(ld, p.tok('v')))
                tr = p.red(tmp1, PS[0][:, 0:128], ALU.max, absval=True,
                           waits=(tk,))
                if c == 0:
                    p.copy(OPP[:, b:b + 1], tmp1)
                else:
                    p.tt(OPP[:, b:b + 1], OPP[:, b:b + 1], tmp1, ALU.max)
        tv = p.ts(OPP, OPP, 1e-12, None, ALU.max)
        tv = p.act(RCPP, OPP, AF.Ln, waits=(tv,))
        tv = p.act(RCPP, RCPP, AF.Exp, scale=-1.0)
        rrow = SM[0:1, 1032:2056]
        for b in range(8):
            tk = p.trans(PS[0][0:1, 0:128], RCPP[:, b:b + 1], IDF[:],
                         waits=(tv, p.tok('v')))
            p.copy(rrow[0:1, b * 128:(b + 1) * 128], PS[0][0:1, 0:128],
                   waits=(tk,))
        tb1 = p.mm(PS[1][:], ONE[0:1, :], rrow[0:1, 0:512], start=True,
                   stop=True, waits=(p.tok('v'),))
        tb2 = p.mm(PS[2][:], ONE[0:1, :], rrow[0:1, 512:1024], start=True,
                   stop=True)
        for c in range(NCH):
            wt = STG[:, 0:1024]
            ld = p.dma(wt, rhs1_d[c * 128:(c + 1) * 128, :], "w_ld",
                       waits=(p.tok('v'), p.tok('t')))
            for half in range(2):
                u = TC[:, 2048:2560]
                bcast = PS[1] if half == 0 else PS[2]
                tv = p.stt(u, wt[:, half * 512:(half + 1) * 512], 127.0,
                           bcast[:], ALU.mult, ALU.mult, waits=(ld, tb2))
                tv = rnd(RQ[:, c * 1024 + half * 512:c * 1024 + (half + 1) * 512],
                         u, TC[:, 3072:3584], TC[:, 3584:4096])
        RSC = SM[:, 1024:1032]
        tv = p.ts(RSC, OPP, 1.0 / 127.0, None, ALU.mult)
        p.barrier()
        for it in range(NT):
            t0 = it * 512
            xt = TB[:, 0:4096]
            ldx = None
            for c in range(NCH):
                ldx = p.dma(xt[:, c * 512:(c + 1) * 512],
                            x_src[c * 128:(c + 1) * 128, t0:t0 + 512], "fx_ld",
                            waits=(p.tok('v'), p.tok('t')))
            ldm = p.dma(SM[0:1, 1032:1544], st_m[0:1, t0:t0 + 512], "fm_ld",
                        waits=(p.tok('t'),))
            ldr = p.dma(SM[0:1, 1544:2056], st_r[0:1, t0:t0 + 512], "fm_ld")
            tb1 = p.mm(PS[5][:], ONE[0:1, :], SM[0:1, 1032:1544], start=True,
                       stop=True, waits=(p.dtok("fm_ld"), p.tok('v')))
            tb2 = p.mm(PS[6][:], ONE[0:1, :], SM[0:1, 1544:2056], start=True,
                       stop=True)
            mb = TC[:, 2048:2560]
            rb = TC[:, 2560:3072]
            tv = p.copy(mb, PS[5][:], waits=(tb1,))
            tv = p.copy(rb, PS[6][:], waits=(tb2,))
            y4 = TB[:, 4096:8192]
            YMX = SM[:, 3230:3234]
            RCP4 = SM[:, 3234:3238]
            tmp2 = SM[:, 3246:3247]
            for c in range(NCH):
                y4c = y4[:, c * 512:(c + 1) * 512]
                tv = p.tt(y4c, xt[:, c * 512:(c + 1) * 512], mb, ALU.subtract,
                          waits=(ldx, p.dtok("epi_st")))
                tv = p.tt(y4c, y4c, rb, ALU.mult)
                for sb2 in range(4):
                    tk = p.trans(PS[0][:, 0:128], y4c[:, sb2 * 128:sb2 * 128 + 128],
                                 IDF[:], waits=(tv, p.tok('v')))
                    tr = p.red(tmp2, PS[0][:, 0:128], ALU.max, absval=True,
                               waits=(tk,))
                    if c == 0:
                        p.copy(YMX[:, sb2:sb2 + 1], tmp2)
                    else:
                        p.tt(YMX[:, sb2:sb2 + 1], YMX[:, sb2:sb2 + 1], tmp2,
                             ALU.max)
            tv = p.ts(YMX, YMX, 1e-12, None, ALU.max)
            tv = p.act(RCP4, YMX, AF.Ln, waits=(tv,))
            tv = p.act(RCP4, RCP4, AF.Exp, scale=-1.0)
            recrow = SM[0:1, 2560:3072]
            lsrow = SM[0:1, 0:512]
            for sb2 in range(4):
                tk = p.trans(PS[0][0:1, 0:128], RCP4[:, sb2:sb2 + 1], IDF[:],
                             waits=(tv, p.tok('v')))
                p.copy(recrow[0:1, sb2 * 128:(sb2 + 1) * 128],
                       PS[0][0:1, 0:128], waits=(tk,))
                tk = p.trans(PS[0][0:1, 0:128], YMX[:, sb2:sb2 + 1], IDF[:],
                             waits=(p.tok('v'),))
                p.copy(lsrow[0:1, sb2 * 128:(sb2 + 1) * 128],
                       PS[0][0:1, 0:128], waits=(tk,))
            tv = p.ts(lsrow, lsrow, 1.0 / 127.0, None, ALU.mult)
            tb3 = p.mm(PS[1][:], ONE[0:1, :], recrow, start=True, stop=True,
                       waits=(p.tok('v'),))
            tb4 = p.mm(PS[5][:], ONE[0:1, :], lsrow, start=True, stop=True)
            lsb = TC[:, 2560:3072]
            tv = p.copy(lsb, PS[5][:], waits=(tb4,))
            y4q = TCb[:, 0:4096]
            for c in range(NCH):
                u = TC[:, 3584:4096]
                tv = p.stt(u, y4[:, c * 512:(c + 1) * 512], 127.0, PS[1][:],
                           ALU.mult, ALU.mult,
                           waits=(tb3, p.tok('s'), p.tok('t'), p.dtok("fxn_st")))
                tv = rnd(y4q[:, c * 512:(c + 1) * 512], u, TC[:, 2048:2560],
                         TC[:, 3072:3584])
            for ocn in range(NCH):
                bank = PS[2 + ocn % 2]
                for c in range(NCH):
                    w0 = []
                    if c == 0:
                        w0 += [p.tok('v'), p.tok('s')]
                    dn = p.mm(bank[:], RQ[:, c * 1024 + ocn * 128:
                                          c * 1024 + ocn * 128 + 128],
                              y4q[:, c * 512:(c + 1) * 512],
                              start=(c == 0), stop=(c == NCH - 1), waits=w0,
                              inc=True)
                xn = TC[:, 3072 + (ocn % 2) * 512:3072 + (ocn % 2) * 512 + 512]
                tv = p.tt(xn, bank[:], lsb, ALU.mult, waits=(dn,))
                tv = p.ts(xn, xn, RSC[:, ocn:ocn + 1], None, ALU.mult)
                tv = p.tt(xn, xn, xt[:, ocn * 512:(ocn + 1) * 512], ALU.add)
                for sb2 in range(4):
                    tk = p.trans(PS[sb2 % 2][:, 0:128],
                                 xn[:, sb2 * 128:sb2 * 128 + 128], IDF[:],
                                 waits=(tv, p.tok('v')))
                    p.copy(TB[:, 4096 + sb2 * 1024 + ocn * 128:
                              4096 + sb2 * 1024 + (ocn + 1) * 128],
                           PS[sb2 % 2][:, 0:128],
                           waits=(tk, p.dtok("epi_st")))
            for sb2 in range(4):
                p.dma(y_out[t0 + sb2 * 128:t0 + sb2 * 128 + 128, :],
                      TB[:, 4096 + sb2 * 1024:4096 + (sb2 + 1) * 1024],
                      "epi_st", waits=(p.tok('v'),))
        p.barrier()

    phase_input()
    cur, nxt = xA, xB
    if 0 in layers:
        l0_attn(cur, nxt)
        cur, nxt = nxt, cur
    if 1 in layers:
        l0_ffn(cur, nxt)
        cur, nxt = nxt, cur
    if 2 in layers:
        l1_attn(cur, nxt)
        cur, nxt = nxt, cur
    if 3 in layers:
        l1_ffn(cur, nxt)
        cur, nxt = nxt, cur
    else:
        epilogue(cur)
    return nc


def _consts():
    identf = np.eye(128, dtype=np.float32)
    identb = np.eye(128, dtype=ml_dtypes.bfloat16)
    onesf = np.ones((128, 128), dtype=np.float32)
    mask8 = np.zeros((128, 128), dtype=np.float32)
    for b in range(16):
        mask8[b * 8:(b + 1) * 8, b * 8:(b + 1) * 8] = 1.0
    sel2 = np.zeros((128, 2), dtype=np.float32)
    sel2[0:64, 0] = 1.0
    sel2[64:128, 1] = 1.0
    sel2t = np.ascontiguousarray(sel2.T)
    return identf, identb, onesf, mask8, sel2, sel2t


def make_in_maps(inputs, seq=None):
    x = np.ascontiguousarray(np.asarray(inputs["L0_attn_x"], dtype=np.float32))
    B = x.shape[0]
    identf, identb, onesf, mask8, sel2, sel2t = _consts()
    common = {
        "L0_Wq": np.ascontiguousarray(inputs["L0_attn_Wq"], dtype=np.float32),
        "L0_Wkv": np.ascontiguousarray(inputs["L0_attn_Wkv"], dtype=np.float32),
        "L0_Wo": np.ascontiguousarray(inputs["L0_attn_Wo"], dtype=np.float32),
        "L0_w_gate": np.ascontiguousarray(inputs["L0_ffn_w_gate"], dtype=np.float32),
        "L0_w_up": np.ascontiguousarray(inputs["L0_ffn_w_up"], dtype=np.float32),
        "L0_w_down": np.ascontiguousarray(inputs["L0_ffn_w_down"], dtype=np.float32),
        "L1_W_qkv": np.ascontiguousarray(inputs["L1_attn_W_qkv"], dtype=np.float32),
        "L1_b_qkv": np.ascontiguousarray(inputs["L1_attn_b_qkv"], dtype=np.float32).reshape(1, -1),
        "L1_W_out": np.ascontiguousarray(inputs["L1_attn_W_out"], dtype=np.float32),
        "L1_b_out": np.ascontiguousarray(inputs["L1_attn_b_out"], dtype=np.float32).reshape(1, -1),
        "L1_qw": np.ascontiguousarray(inputs["L1_attn_quantum_weights"], dtype=np.float32).reshape(4, 64),
        "L1_phot": np.ascontiguousarray(inputs["L1_attn_photonic_phases"], dtype=np.float32),
        "L1_gates": np.ascontiguousarray(inputs["L1_attn_entanglement_gates"], dtype=np.float32).reshape(3, 32),
        "L1_sup": np.ascontiguousarray(inputs["L1_attn_superposition_logits"], dtype=np.float32).reshape(1, 4),
        "L1_rhs": np.ascontiguousarray(inputs["L1_ffn_rhs"], dtype=np.float32),
        "identf": identf, "identb": identb, "onesf": onesf, "mask8": mask8,
        "sel2": sel2, "sel2t": sel2t,
    }
    return [dict(common, x_in=np.ascontiguousarray(x[i])) for i in range(B)]


def kernel(**inputs):
    x = np.asarray(inputs["L0_attn_x"])
    B, S, _ = x.shape
    nc = build_program(seq=S, layers=(0, 1, 2, 3))
    in_maps = make_in_maps(inputs)
    res = run_bass_kernel_spmd(nc, in_maps, core_ids=list(range(B)))
    out = np.stack([np.asarray(res.results[i]["y_out"]) for i in range(B)], axis=0)
    return out.astype(np.float32)


# version 23
# speedup vs baseline: 1.0656x; 1.0231x over previous
import math
import numpy as np
import ml_dtypes
import concourse.bass as bass
import concourse.mybir as mybir
import concourse.bass_isa as bass_isa
from concourse.bass_utils import run_bass_kernel_spmd

F32 = mybir.dt.float32
BF16 = mybir.dt.bfloat16
AF = mybir.ActivationFunctionType
ALU = mybir.AluOpType
AX = mybir.AxisListType

DM = 1024
NCH = DM // 128
INNER0 = 512
FFN_INNER = 4096
SCALING1 = float(128 ** -0.5)

CONST_VALS = sorted({0.0, 1.0, -1.0, 0.5, -0.5, 16.0, -16.0,
                     1e-5, 1e-12, 0.0625, 1.0 / DM, 1.0 / FFN_INNER,
                     float(np.float32(0.5 * math.pi))})


class P:
    def __init__(self, nc):
        self.nc = nc
        self.eng = {'t': nc.tensor, 'v': nc.vector, 's': nc.scalar,
                    'g': nc.gpsimd, 'y': nc.sync}
        self.chain = {k: nc.alloc_semaphore(f"ch_{k}") for k in 'tvsg'}
        self.cnt = {k: 0 for k in 'tvsg'}
        self.dsem = {}
        self.dcnt = {}
        self.lastw = {}

    def dma_sem(self, name):
        if name not in self.dsem:
            self.dsem[name] = self.nc.alloc_semaphore(f"dma_{name}")
            self.dcnt[name] = 0
        return self.dsem[name]

    def tok(self, e):
        return (e, self.cnt[e])

    def _sem_of(self, token):
        if token[0] == 'd':
            return self.dsem[token[1]], token[2]
        return self.chain[token[0]], token[1]

    def wait(self, e, token):
        if token is None:
            return
        if token[0] == e:
            return
        sem, val = self._sem_of(token)
        if val <= 0:
            return
        key = (e, id(sem))
        if self.lastw.get(key, -1) >= val:
            return
        self.eng[e].wait_ge(sem, val)
        self.lastw[key] = val

    def _chain(self, e, instr):
        c = self.cnt[e]
        if c > 0:
            instr._wait_ge(self.chain[e], c)
        instr.then_inc(self.chain[e], 1)
        self.cnt[e] += 1
        return (e, self.cnt[e])

    def op(self, e, name, *args, waits=(), **kw):
        for t in waits:
            self.wait(e, t)
        instr = getattr(self.eng[e], name)(*args, **kw)
        return self._chain(e, instr)

    def tt(self, out, a, b, alu, waits=(), e='v'):
        return self.op(e, 'tensor_tensor', out, a, b, alu, waits=waits)

    def ts(self, out, a, s1, s2=None, op0=ALU.mult, op1=None, waits=(), e='v'):
        if op1 is None:
            return self.op(e, 'tensor_scalar', out, a, s1, None, op0, waits=waits)
        return self.op(e, 'tensor_scalar', out, a, s1, s2, op0, op1, waits=waits)

    def stt(self, out, in0, scalar, in1, op0, op1, waits=(), e='v'):
        return self.op(e, 'scalar_tensor_tensor', out, in0, scalar, in1, op0, op1,
                       waits=waits)

    def copy(self, out, a, waits=(), e='v'):
        return self.op(e, 'tensor_copy', out, a, waits=waits)

    def red(self, out, a, alu, absval=None, waits=(), e='v'):
        return self.op(e, 'tensor_reduce', out, a, AX.X, alu,
                       apply_absolute_value=absval, waits=waits)

    def act(self, out, a, func, bias=0.0, scale=1.0, waits=(), accum_out=None):
        return self.op('s', 'activation', out, a, func, bias=bias, scale=scale,
                       accum_out=accum_out, waits=waits)

    def memset(self, ap, val, waits=(), e='v'):
        return self.op(e, 'memset', ap, val, waits=waits)

    def gred(self, out, in_, rop, waits=()):
        return self.op('g', 'partition_all_reduce', out, in_, 128, rop, waits=waits)

    def mm(self, out, lhsT, rhs, start, stop, waits=(), inc=True, skip=False):
        for t in waits:
            self.wait('t', t)
        instr = self.nc.tensor.matmul(out, lhsT, rhs, start=start, stop=stop,
                                      skip_group_check=skip)
        if inc:
            instr.then_inc(self.chain['t'], 1)
            self.cnt['t'] += 1
        return ('t', self.cnt['t'])

    def trans(self, out, in_, ident, waits=(), inc=True):
        for t in waits:
            self.wait('t', t)
        instr = self.nc.tensor.matmul(out, in_, ident, start=True, stop=True,
                                      is_transpose=True)
        if inc:
            instr.then_inc(self.chain['t'], 1)
            self.cnt['t'] += 1
        return ('t', self.cnt['t'])

    def dma(self, out, in_, stream, waits=(), e='y', slow=False):
        for t in waits:
            self.wait(e, t)
        sem = self.dma_sem(stream)
        kw = {}
        if slow:
            kw['allow_slow_non_contiguous'] = True
        self.eng[e].dma_start(out, in_, **kw).then_inc(sem, 16)
        self.dcnt[stream] += 1
        return ('d', stream, 16 * self.dcnt[stream])

    def dtok(self, stream):
        self.dma_sem(stream)
        return ('d', stream, 16 * self.dcnt[stream])

    def barrier(self):
        for name in self.dsem:
            self.wait('y', ('d', name, 16 * self.dcnt[name]))
        self.nc.all_engine_barrier()


def build_program(seq=4096, layers=(0, 1, 2, 3)):
    nc = bass.Bass("TRN2", target_bir_lowering=False)
    S = seq
    NT = S // 512
    NCK = S // 128

    x_in = nc.dram_tensor("x_in", [S, DM], F32, kind="ExternalInput").ap()
    wq_d = nc.dram_tensor("L0_Wq", [DM, INNER0], F32, kind="ExternalInput").ap()
    wkv_d = nc.dram_tensor("L0_Wkv", [DM, 2 * INNER0], F32, kind="ExternalInput").ap()
    wo_d = nc.dram_tensor("L0_Wo", [INNER0, DM], F32, kind="ExternalInput").ap()
    wg_d = nc.dram_tensor("L0_w_gate", [FFN_INNER, DM], F32, kind="ExternalInput").ap()
    wu_d = nc.dram_tensor("L0_w_up", [FFN_INNER, DM], F32, kind="ExternalInput").ap()
    wd_d = nc.dram_tensor("L0_w_down", [DM, FFN_INNER], F32, kind="ExternalInput").ap()
    wqkv_d = nc.dram_tensor("L1_W_qkv", [3 * DM, DM], F32, kind="ExternalInput").ap()
    bqkv_d = nc.dram_tensor("L1_b_qkv", [1, 3 * DM], F32, kind="ExternalInput").ap()
    wout_d = nc.dram_tensor("L1_W_out", [DM, DM], F32, kind="ExternalInput").ap()
    bout_d = nc.dram_tensor("L1_b_out", [1, DM], F32, kind="ExternalInput").ap()
    qw_d = nc.dram_tensor("L1_qw", [4, 64], F32, kind="ExternalInput").ap()
    phot_d = nc.dram_tensor("L1_phot", [16, DM], F32, kind="ExternalInput").ap()
    gates_d = nc.dram_tensor("L1_gates", [3, 32], F32, kind="ExternalInput").ap()
    sup_d = nc.dram_tensor("L1_sup", [1, 4], F32, kind="ExternalInput").ap()
    rhs1_d = nc.dram_tensor("L1_rhs", [DM, DM], F32, kind="ExternalInput").ap()
    identf_d = nc.dram_tensor("identf", [128, 128], F32, kind="ExternalInput").ap()
    identb_d = nc.dram_tensor("identb", [128, 128], BF16, kind="ExternalInput").ap()
    onesf_d = nc.dram_tensor("onesf", [128, 128], F32, kind="ExternalInput").ap()
    mask8_d = nc.dram_tensor("mask8", [128, 128], F32, kind="ExternalInput").ap()
    sel2_d = nc.dram_tensor("sel2", [128, 2], F32, kind="ExternalInput").ap()
    sel2t_d = nc.dram_tensor("sel2t", [2, 128], F32, kind="ExternalInput").ap()

    y_out = nc.dram_tensor("y_out", [S, DM], F32, kind="ExternalOutput").ap()

    xA = nc.dram_tensor("xA", [DM, S], F32, kind="Internal").ap()
    xB = nc.dram_tensor("xB", [DM, S], F32, kind="Internal").ap()
    st_m = nc.dram_tensor("st_m", [1, S], F32, kind="Internal").ap()
    st_nm = nc.dram_tensor("st_nm", [1, S], F32, kind="Internal").ap()
    st_r = nc.dram_tensor("st_r", [1, S], F32, kind="Internal").ap()
    qnh_sc = nc.dram_tensor("qn_hi", [INNER0, S], BF16, kind="Internal").ap()
    qnl_sc = nc.dram_tensor("qn_lo", [INNER0, S], BF16, kind="Internal").ap()
    knh_sc = nc.dram_tensor("kn_hi", [INNER0, S], BF16, kind="Internal").ap()
    knl_sc = nc.dram_tensor("kn_lo", [INNER0, S], BF16, kind="Internal").ap()
    vt_sc = nc.dram_tensor("vt_sc", [S, 8 * 65], F32, kind="Internal").ap()
    wgT_sc = nc.dram_tensor("wgT_sc", [DM, FFN_INNER], BF16, kind="Internal").ap()
    wuT_sc = nc.dram_tensor("wuT_sc", [DM, FFN_INNER], BF16, kind="Internal").ap()
    wdT_sc = nc.dram_tensor("wdT_sc", [FFN_INNER, DM], BF16, kind="Internal").ap()
    qkv_sc = nc.dram_tensor("qkv_sc", [3 * DM, S], F32, kind="Internal").ap()
    wqkvT_f = nc.dram_tensor("wqkvT_f", [3 * DM * 8, 128], F32, kind="Internal").ap()
    woutT_f = nc.dram_tensor("woutT_f", [DM * 8, 128], F32, kind="Internal").ap()
    csqkv_sc = nc.dram_tensor("csqkv_sc", [1, 3 * DM], F32, kind="Internal").ap()

    p = P(nc)

    TA = nc.alloc_sbuf_tensor("TA", [128, 16384], F32)
    TB = nc.alloc_sbuf_tensor("TB", [128, 16640], F32)
    TC = nc.alloc_sbuf_tensor("TC", [128, 6400], F32)
    STG = nc.alloc_sbuf_tensor("STG", [128, 4096], F32)
    WB = nc.alloc_sbuf_tensor("WB", [128, 4096], F32)
    SM = nc.alloc_sbuf_tensor("SM", [128, 4096], F32)
    CP = nc.alloc_sbuf_tensor("CP", [128, 16], F32)
    IDF = nc.alloc_sbuf_tensor("IDF", [128, 128], F32)
    IDB = nc.alloc_sbuf_tensor("IDB", [128, 128], BF16)
    ONE = nc.alloc_sbuf_tensor("ONE", [128, 128], F32)
    MSK8 = nc.alloc_sbuf_tensor("MSK8", [128, 128], F32)
    PS = [nc.alloc_psum_tensor(f"ps{i}", [128, 512], F32) for i in range(8)]

    STT = nc.alloc_sbuf_tensor("STT", [128, 768], F32)

    TAb = TA[:].bitcast(BF16)
    TBb = TB[:].bitcast(BF16)
    TCb = TC[:].bitcast(BF16)
    STGb = STG[:].bitcast(BF16)
    WBb = WB[:].bitcast(BF16)

    assert len(CONST_VALS) <= 16
    for i, val in enumerate(CONST_VALS):
        p.memset(CP[:, i:i + 1], float(val))
        nc.const_aps.register(F32, float(val), CP[:, i:i + 1])
    p.dma(IDF[:], identf_d, "cst")
    p.dma(IDB[:], identb_d, "cst")
    p.dma(ONE[:], onesf_d, "cst")
    p.dma(MSK8[:], mask8_d, "cst")
    PFLAG = CP[:, 14:15]
    p.memset(PFLAG, 1.0)
    p.ts(PFLAG, PFLAG, 2.0 ** -30, -1.0, ALU.add, ALU.add)
    p.ts(PFLAG, PFLAG, 2.0 ** 30, None, ALU.mult)
    M23 = 12582912.0
    M52 = 6755399441055744.0
    CSQ = SM[0:1, 0:512]
    CSK = SM[0:1, 512:1024]
    CSV = SM[0:1, 1024:1536]
    NMR = SM[0:1, 1536:2048]
    RR = SM[0:1, 2048:2560]
    RSA = SM[0:1, 2560:3072]
    RSB = SM[64:65, 2560:3072]
    RSST_A = SM[64:65, 3584:4096]
    RSST_B = SM[64:65, 3072:3584]
    SEL2T_ = SM[0:2, 3072:3200]
    SEL2_ = SM[:, 3200:3202]
    p.dma(SEL2_, sel2_d, "cst")
    p.dma(SEL2T_, sel2t_d, "cst")
    p.barrier()

    def split(hi, lo, src, waits=()):
        p.copy(hi, src, waits=waits)
        return p.tt(lo, src, hi, ALU.subtract)

    def rnd(out, u, s1, s2, waits=()):
        p.ts(s1, u, M23, -M23, ALU.add, ALU.add, waits=waits)
        p.ts(s2, u, M52, -M52, ALU.add, ALU.add)
        p.tt(s2, s2, s1, ALU.subtract)
        return p.stt(out, s2, PFLAG, s1, ALU.mult, ALU.add)

    def ln_stats(x_src, tag):
        TW = 512
        xa = STG[:, 0:TW]
        sq = STG[:, TW:2 * TW]
        mrow = STG[0:1, 2 * TW:3 * TW]
        nrow = STG[0:1, 3 * TW:4 * TW]
        rrow = STG[0:1, 4 * TW:5 * TW]
        for it in range(S // TW):
            t0 = it * TW
            mmtok = None
            for c in range(NCH):
                ld = p.dma(xa, x_src[c * 128:(c + 1) * 128, t0:t0 + TW],
                           f"lnx_{tag}", waits=(p.tok('s'), p.tok('t')))
                p.mm(PS[6][0:1, 0:TW], ONE[:, 0:1], xa, start=(c == 0),
                     stop=(c == NCH - 1), waits=(ld,), inc=True)
                tsq = p.act(sq, xa, AF.Square, waits=(ld,))
                mmtok = p.mm(PS[7][0:1, 0:TW], ONE[:, 0:1], sq, start=(c == 0),
                             stop=(c == NCH - 1), waits=(tsq,), inc=True)
            p.ts(mrow, PS[6][0:1, 0:TW], 1.0 / DM, None, ALU.mult,
                 waits=(mmtok, p.dtok(f"lns_{tag}")))
            p.ts(nrow, mrow, -1.0, None, ALU.mult)
            p.tt(rrow, mrow, mrow, ALU.mult)
            tv = p.stt(rrow, PS[7][0:1, 0:TW], 1.0 / DM, rrow, ALU.mult,
                       ALU.subtract)
            tv = p.act(rrow, rrow, AF.Ln, bias=1e-5, waits=(tv,))
            tv = p.act(rrow, rrow, AF.Exp, scale=-0.5)
            p.dma(st_m[0:1, t0:t0 + TW], mrow, f"lns_{tag}",
                  waits=(p.tok('v'), tv))
            p.dma(st_nm[0:1, t0:t0 + TW], nrow, f"lns_{tag}")
            p.dma(st_r[0:1, t0:t0 + TW], rrow, f"lns_{tag}")
        p.barrier()

    STM = STT[0:1, 0:256]
    STN = STT[0:1, 256:512]
    STR = STT[0:1, 512:768]

    def stats_acc(xn, sq, sumbank, sqbank, first, last, tw, xtok):
        tq = p.act(sq, xn, AF.Square, waits=(xtok,))
        p.mm(sumbank[0:1, 0:tw], ONE[:, 0:1], xn, start=first, stop=last,
             waits=(xtok,), inc=False, skip=True)
        return p.mm(sqbank[0:1, 0:tw], ONE[:, 0:1], sq, start=first,
                    stop=last, waits=(tq,), inc=last, skip=True)

    def stats_fin(t0, tw, sumbank, sqbank, smt):
        for h in range(tw // 256):
            c0 = h * 256
            tv = p.ts(STM, sumbank[0:1, c0:c0 + 256], 1.0 / DM, None, ALU.mult,
                      waits=(smt, p.dtok("st_st")))
            p.ts(STN, STM, -1.0, None, ALU.mult)
            p.tt(STR, STM, STM, ALU.mult)
            tv = p.stt(STR, sqbank[0:1, c0:c0 + 256], 1.0 / DM, STR,
                       ALU.mult, ALU.subtract)
            tv = p.act(STR, STR, AF.Ln, bias=1e-5, waits=(tv,))
            tv = p.act(STR, STR, AF.Exp, scale=-0.5)
            p.dma(st_m[0:1, t0 + c0:t0 + c0 + 256], STM, "st_st",
                  waits=(p.tok('v'), tv))
            p.dma(st_nm[0:1, t0 + c0:t0 + c0 + 256], STN, "st_st")
            p.dma(st_r[0:1, t0 + c0:t0 + c0 + 256], STR, "st_st")

    def epilogue(x_src):
        for it in range(NCK):
            t0 = it * 128
            for c in range(NCH):
                xa = STG[:, (c % 4) * 128:(c % 4) * 128 + 128]
                ld = p.dma(xa, x_src[c * 128:(c + 1) * 128, t0:t0 + 128], "epi_ld",
                           waits=(p.tok('t'),))
                bank = PS[c % 2]
                tk = p.trans(bank[:, 0:128], xa, IDF[:], waits=(ld, p.tok('v')))
                p.copy(STG[:, 1024 + c * 128:1024 + (c + 1) * 128],
                       bank[:, 0:128], waits=(tk,))
            st = p.dma(y_out[t0:t0 + 128, :], STG[:, 1024:1024 + DM], "epi_st",
                       waits=(p.tok('v'),))
            p.wait('t', st)
        p.barrier()

    def phase_input():
        YS = SM[:, 0:32]
        YQ = SM[:, 32:64]
        for it in range(NCK):
            t0 = it * 128
            ld = p.dma(STG[:, 0:DM], x_in[t0:t0 + 128, :], "in_ld",
                       waits=(p.tok('t'), p.tok('v'), p.tok('s')))
            p.red(YS[:, it:it + 1], STG[:, 0:DM], ALU.add, waits=(ld,))
            tq = p.act(STG[:, 2560:3584], STG[:, 0:DM], AF.Square, waits=(ld,))
            p.red(YQ[:, it:it + 1], STG[:, 2560:3584], ALU.add, waits=(tq,))
            for c in range(NCH):
                bank = PS[c % 2]
                tk = p.trans(bank[:, 0:128], STG[:, c * 128:(c + 1) * 128], IDF[:],
                             waits=(ld, p.tok('v')))
                cp = p.copy(STG[:, 1536 + c * 128:1536 + (c + 1) * 128],
                            bank[:, 0:128], waits=(tk,))
                p.dma(xA[c * 128:(c + 1) * 128, t0:t0 + 128],
                      STG[:, 1536 + c * 128:1536 + (c + 1) * 128], "in_st",
                      waits=(cp,))
            p.wait('t', p.dtok("in_st"))
        M_ = SM[:, 64:96]
        V_ = SM[:, 96:128]
        N_ = SM[:, 128:160]
        p.ts(M_, YS, 1.0 / DM, None, ALU.mult)
        p.ts(N_, M_, -1.0, None, ALU.mult)
        p.tt(V_, M_, M_, ALU.mult)
        tv = p.stt(V_, YQ, 1.0 / DM, V_, ALU.mult, ALU.subtract)
        tv = p.act(V_, V_, AF.Ln, bias=1e-5, waits=(tv,))
        tv = p.act(V_, V_, AF.Exp, scale=-0.5)
        for (src, dst, nm) in ((M_, st_m, 0), (N_, st_nm, 1), (V_, st_r, 2)):
            tk = p.trans(PS[2][0:32, 0:128], src, IDF[:], waits=(p.tok('v'), tv))
            cv = p.copy(SM[0:32, 160 + nm * 128:160 + nm * 128 + 128],
                        PS[2][0:32, 0:128], waits=(tk,))
            p.dma(dst.rearrange("o (a b) -> (o a) b", b=128),
                  SM[0:32, 160 + nm * 128:160 + nm * 128 + 128], "st_st",
                  waits=(cv,))
        p.barrier()

    def l0_attn(x_src, x_dst):
        for it in range(NT):
            t0 = it * 512
            for c in range(NCH):
                xa = STG[:, 0:512]
                ld = p.dma(xa, x_src[c * 128:(c + 1) * 128, t0:t0 + 512], "xs_ld",
                           waits=(p.tok('v'),))
                split(TAb[:, c * S + t0: c * S + t0 + 512],
                      TBb[:, c * S + t0: c * S + t0 + 512], xa, waits=(ld,))
        p.barrier()

        def load_w_split(wsrc, cols, hi, lo, csrow):
            cs_tok = None
            nchunks = wsrc.shape[0] // 128
            for c in range(nchunks):
                wt = STG[:, 0:cols]
                ld = p.dma(wt, wsrc[c * 128:(c + 1) * 128, 0:cols], "w_ld",
                           waits=(p.tok('v'), p.tok('t')))
                split(hi[:, c * cols:(c + 1) * cols],
                      lo[:, c * cols:(c + 1) * cols], wt, waits=(ld,))
                if csrow is not None:
                    for j in range(cols // 512):
                        cs_tok = p.mm(PS[7][0:1, j * 512:(j + 1) * 512],
                                      ONE[:, 0:1],
                                      wt[:, j * 512:(j + 1) * 512],
                                      start=(c == 0), stop=(c == nchunks - 1),
                                      waits=(ld,), inc=True)
            if csrow is not None:
                return p.copy(csrow[0:1, 0:cols], PS[7][0:1, 0:cols], waits=(cs_tok,))
            return None

        def proj_qkv(which):
            if which == 'q':
                wsrc, csrow = wq_d, CSQ
            elif which == 'k':
                wsrc, csrow = wkv_d[:, 0:512], CSK
            else:
                wsrc, csrow = wkv_d[:, 512:1024], CSV
            cols = 512
            hi = WBb[:, 0:4096]
            lo = WBb[:, 4096:8192]
            load_w_split(wsrc, cols, hi, lo, csrow)
            p.barrier()
            for it in range(NT):
                t0 = it * 512
                nm_t = p.dma(NMR, st_nm[0:1, t0:t0 + 512], "nm_ld",
                             waits=(p.tok('t'),))
                r_t = None
                if which == 'v':
                    r_t = p.dma(RR, st_r[0:1, t0:t0 + 512], "r_ld",
                                waits=(p.tok('t'),))
                for oc in range(4):
                    bank = PS[oc % 2]
                    for c in range(NCH):
                        st0 = (c == 0)
                        w_h = hi[:, c * cols + oc * 128: c * cols + oc * 128 + 128]
                        w_l = lo[:, c * cols + oc * 128: c * cols + oc * 128 + 128]
                        xh = TAb[:, c * S + t0: c * S + t0 + 512]
                        xl = TBb[:, c * S + t0: c * S + t0 + 512]
                        p.mm(bank[:], w_h, xh, start=st0, stop=False,
                             waits=(p.tok('v'), p.tok('s')) if st0 else (),
                             inc=False)
                        p.mm(bank[:], w_h, xl, start=False, stop=False, inc=False)
                        p.mm(bank[:], w_l, xh, start=False, stop=False, inc=False)
                    mmt = p.mm(bank[:], csrow[0:1, oc * 128:(oc + 1) * 128], NMR,
                               start=False, stop=True, waits=(nm_t,))
                    if which in ('q', 'k'):
                        sq = STG[:, 0:512]
                        tsq = p.act(sq, bank[:], AF.Square, waits=(mmt,))
                        th = p.mm(PS[2][0:2, 0:512], SEL2_, sq, start=True,
                                  stop=True, waits=(tsq, p.tok('v')))
                        ss = STG[:, 512:1024]
                        tv = p.ts(ss[0:2, :], PS[2][0:2, 0:512], 1e-24, None,
                                  ALU.max, waits=(th,))
                        tv = p.act(ss[0:2, :], ss[0:2, :], AF.Ln, waits=(tv,))
                        tv = p.act(ss[0:2, :], ss[0:2, :], AF.Exp, scale=-0.5)
                        tb = p.mm(PS[3][:], SEL2T_, ss[0:2, :], start=True,
                                  stop=True, waits=(tv, p.tok('v')))
                        rnb = STG[:, 1024:1536]
                        tv = p.copy(rnb, PS[3][:], waits=(tb,))
                        qn = STG[:, 1536:2048]
                        tv = p.tt(qn, bank[:], rnb, ALU.mult)
                        hi_o = STGb[:, 4096:4608]
                        lo_o = STGb[:, 4608:5120]
                        tv = split(hi_o, lo_o, qn,
                                   waits=(p.dtok(f"{which}n_st"),))
                        dsth = qnh_sc if which == 'q' else knh_sc
                        dstl = qnl_sc if which == 'q' else knl_sc
                        p.dma(dsth[oc * 128:(oc + 1) * 128, t0:t0 + 512], hi_o,
                              f"{which}n_st", waits=(tv,))
                        p.dma(dstl[oc * 128:(oc + 1) * 128, t0:t0 + 512], lo_o,
                              f"{which}n_st")
                    else:
                        tb = p.mm(PS[2][:], ONE[0:1, :], RR, start=True, stop=True,
                                  waits=(r_t, p.tok('v')))
                        rbs = STG[:, 0:512]
                        tv = p.copy(rbs, PS[2][:], waits=(tb,))
                        vv = STG[:, 512:1024]
                        tv = p.tt(vv, bank[:], rbs, ALU.mult, waits=(mmt,))
                        for sb in range(4):
                            tk = p.trans(PS[3][:, 0:128],
                                         vv[:, sb * 128:sb * 128 + 128],
                                         IDF[:], waits=(tv, p.tok('v')))
                            h0, h1 = 2 * oc, 2 * oc + 1
                            c0 = p.copy(VTS[:, h0 * 65:h0 * 65 + 64],
                                        PS[3][:, 0:64],
                                        waits=(tk, p.dtok("vt_st")))
                            c0 = p.copy(VTS[:, h1 * 65:h1 * 65 + 64],
                                        PS[3][:, 64:128])
                            p.dma(vt_sc[t0 + sb * 128: t0 + sb * 128 + 128,
                                        h0 * 65:h1 * 65 + 65],
                                  VTS[:, h0 * 65:h1 * 65 + 65], "vt_st",
                                  waits=(c0,))
            p.barrier()

        VTS = TC[:, 0:520]
        for h in range(8):
            p.memset(VTS[:, h * 65 + 64:h * 65 + 65], 1.0)
        proj_qkv('q')
        proj_qkv('k')
        proj_qkv('v')

        for c in range(4):
            for it in range(NT):
                t0 = it * 512
                p.dma(TAb[:, c * S + t0:c * S + t0 + 512],
                      knh_sc[c * 128:(c + 1) * 128, t0:t0 + 512], "kn_ld")
                p.dma(TAb[:, 4 * S + c * S + t0:4 * S + c * S + t0 + 512],
                      knl_sc[c * 128:(c + 1) * 128, t0:t0 + 512], "kn_ld")
        for ck in range(NCK):
            p.dma(TB[:, ck * 520:(ck + 1) * 520],
                  vt_sc[ck * 128:(ck + 1) * 128, :], "vt_ld")
        for c in range(4):
            p.dma(WB[:, c * 1024:(c + 1) * 1024], wo_d[c * 128:(c + 1) * 128, :],
                  "w_ld", waits=(p.tok('v'), p.tok('t')))
        p.barrier()

        xst = {}
        for sw in range(NT):
            t0 = sw * 512
            qh_sw = STGb[:, 0:2048]
            ql_sw = STGb[:, 2048:4096]
            for c in range(4):
                p.dma(qh_sw[:, c * 512:(c + 1) * 512],
                      qnh_sc[c * 128:(c + 1) * 128, t0:t0 + 512], "qn_ld",
                      waits=(p.tok('t'),))
                p.dma(ql_sw[:, c * 512:(c + 1) * 512],
                      qnl_sc[c * 128:(c + 1) * 128, t0:t0 + 512], "qn_ld")
            qtok = p.dtok("qn_ld")
            kntok = p.dtok("kn_ld")
            vttok = p.dtok("vt_ld")
            pvs_tok = [None] * 4
            for pr in range(4):
                qh_pr = qh_sw[:, pr * 512:(pr + 1) * 512]
                ql_pr = ql_sw[:, pr * 512:(pr + 1) * 512]
                exp_toks = [None, None]
                pend = None
                for ck in range(NCK):
                    knh = TAb[:, pr * S + t0 * 0 + ck * 128: pr * S + ck * 128 + 128]
                    knl = TAb[:, 4 * S + pr * S + ck * 128: 4 * S + pr * S + ck * 128 + 128]
                    lb, lb2 = PS[ck % 2], PS[2 + ck % 2]
                    w_ = [qtok, kntok]
                    if exp_toks[ck % 2] is not None:
                        w_.append(exp_toks[ck % 2])
                    elif ck == 0:
                        w_.append(p.tok('s'))
                        w_.append(p.tok('v'))
                    p.mm(lb[:], knh[0:64, :], qh_pr[0:64, :], start=True,
                         stop=False, waits=w_, inc=False)
                    p.mm(lb[:], knh[0:64, :], ql_pr[0:64, :], start=False,
                         stop=False, inc=False)
                    p.mm(lb[:], knl[0:64, :], qh_pr[0:64, :], start=False,
                         stop=True, inc=False)
                    p.mm(lb2[:], knh[64:128, :], qh_pr[64:128, :], start=True,
                         stop=False, inc=False)
                    p.mm(lb2[:], knh[64:128, :], ql_pr[64:128, :], start=False,
                         stop=False, inc=False)
                    qk_tok = p.mm(lb2[:], knl[64:128, :], qh_pr[64:128, :],
                                  start=False, stop=True)
                    if pend is not None:
                        pea, peb, ppta, pptb, pvth, pvtl, pfirst = pend
                        w0 = [pea, vttok]
                        if pfirst:
                            w0.append(p.tok('v'))
                        p.mm(PS[4][0:65, :], pvth, ppta, start=pfirst,
                             stop=False, waits=w0, inc=False)
                        p.mm(PS[5][0:65, :], pvtl, pptb, start=pfirst,
                             stop=False, waits=(peb,), inc=False)
                    pta = STG[:, 2048 + (ck % 2) * 512:2048 + (ck % 2) * 512 + 512]
                    ptb = TC[:, 520 + (ck % 2) * 512:520 + (ck % 2) * 512 + 512]
                    ea = p.act(pta, lb[:], AF.Exp, bias=-16.0, scale=16.0,
                               waits=(qk_tok,))
                    eb = p.act(ptb, lb2[:], AF.Exp, bias=-16.0, scale=16.0)
                    exp_toks[ck % 2] = eb
                    vth = TB[:, ck * 520 + (2 * pr) * 65: ck * 520 + (2 * pr) * 65 + 65]
                    vtl = TB[:, ck * 520 + (2 * pr + 1) * 65: ck * 520 + (2 * pr + 1) * 65 + 65]
                    pend = (ea, eb, pta, ptb, vth, vtl, ck == 0)
                pea, peb, ppta, pptb, pvth, pvtl, pfirst = pend
                p.mm(PS[4][0:65, :], pvth, ppta, start=pfirst, stop=True,
                     waits=(pea, vttok), inc=True)
                p.mm(PS[5][0:65, :], pvtl, pptb, start=pfirst, stop=True,
                     waits=(peb,), inc=True)
                pv_tok = p.tok('t')
                c1 = p.copy(RSST_A, PS[4][64:65, :],
                            waits=(pv_tok, p.dtok("rs_mv")))
                c2 = p.copy(RSST_B, PS[5][64:65, :])
                d1 = p.dma(RSA, RSST_A, "rs_mv",
                           waits=(c2, p.tok('t')))
                d2 = p.dma(RSB, RSST_B, "rs_mv")
                tv = p.ts(RSA, RSA, 1e-6, None, ALU.max, waits=(d2,))
                tv = p.act(RSA, RSA, AF.Ln, waits=(tv,))
                tv = p.act(RSA, RSA, AF.Exp, scale=-1.0)
                tvb = p.ts(RSB, RSB, 1e-6, None, ALU.max, waits=(d2,))
                tvb = p.act(RSB, RSB, AF.Ln, waits=(tvb,))
                tvb = p.act(RSB, RSB, AF.Exp, scale=-1.0)
                tb = p.mm(PS[6][0:64, :], ONE[0:1, 0:64], RSA,
                          start=True, stop=True, waits=(tv, p.tok('v')))
                rba = TC[:, 1544:2056]
                pvna = TC[:, 4104 + pr * 512:4104 + (pr + 1) * 512]
                tv2 = p.copy(rba[0:64, :], PS[6][0:64, :], waits=(tb,))
                tv2 = p.tt(pvna[0:64, :], PS[4][0:64, :], rba[0:64, :], ALU.mult)
                tb = p.mm(PS[6][0:64, :], ONE[64:65, 0:64], RSB,
                          start=True, stop=True, waits=(tvb, p.tok('v')))
                tv3 = p.copy(rba[0:64, :], PS[6][0:64, :], waits=(tb,))
                hbst = TC[:, 2568:3080]
                tv3 = p.tt(hbst[0:64, :], PS[5][0:64, :], rba[0:64, :],
                           ALU.mult, waits=(p.dtok("pv_mv"),))
                mv = p.dma(pvna[64:128, :], hbst[0:64, :], "pv_mv", waits=(tv3,))
                pvs_tok[pr] = (mv, tv2)
            for oc in range(NCH):
                bank = PS[2 + oc % 2]
                for pr in range(4):
                    pvna = TC[:, 4104 + pr * 512:4104 + (pr + 1) * 512]
                    wh = WB[:, pr * 1024 + oc * 128: pr * 1024 + oc * 128 + 128]
                    w0 = list(pvs_tok[pr])
                    if pr == 0:
                        w0 += [p.tok('s'), p.tok('v')]
                    wo_tok = p.mm(bank[:], wh, pvna, start=(pr == 0),
                                  stop=(pr == 3), waits=w0, inc=(pr == 3))
                xo = STG[:, 3072 + (oc % 2) * 512:3072 + (oc % 2) * 512 + 512]
                ld = p.dma(xo, x_src[oc * 128:(oc + 1) * 128, t0:t0 + 512],
                           "xo_ld", waits=(p.tok('t'),))
                xn = TC[:, 3080 + (oc % 2) * 512:3080 + (oc % 2) * 512 + 512]
                tv = p.tt(xn, bank[:], xo, ALU.add,
                          waits=tuple(t for t in
                                      (wo_tok, ld, xst.get(oc % 2), p.tok('s'))
                                      if t))
                xst[oc % 2] = p.dma(x_dst[oc * 128:(oc + 1) * 128, t0:t0 + 512],
                                    xn, "xn_st", waits=(tv,))
                smt = stats_acc(xn, STG[:, 2048:2560], PS[7], PS[6], oc == 0,
                                oc == NCH - 1, 512, tv)
            stats_fin(t0, 512, PS[7], PS[6], smt)
        p.barrier()


    SMW = SM[:, 3210:3226]

    def l0_ffn_weights():
        specs = [(wg_d, wgT_sc, 0, 3, FFN_INNER), (wu_d, wuT_sc, 1, 4, FFN_INNER),
                 (wd_d, wdT_sc, 2, 5, DM)]
        for wsrc, _, sci, smi, nrows in specs:
            cols = wsrc.shape[1]
            acc = SM[:, 3226:3227]
            red_o = SM[:, 3227:3228]
            ntiles = nrows // 128
            wbufs = [None] * ntiles
            wtoks = [None] * ntiles
            def ldw(i):
                wbufs[i] = TA[:, (i % 2) * 4096:(i % 2) * 4096 + cols]
                wtoks[i] = p.dma(wbufs[i], wsrc[i * 128:(i + 1) * 128, :],
                                 f"wq_ld{i % 3}", waits=(p.tok('v'),),
                                 e=('y', 'g', 's')[i % 3])
            ldw(0)
            if ntiles > 1:
                ldw(1)
            for i in range(ntiles):
                tr = p.red(red_o, wbufs[i], ALU.add, absval=True,
                           waits=(wtoks[i],))
                if i == 0:
                    p.copy(acc, red_o)
                else:
                    p.tt(acc, acc, red_o, ALU.add)
                if i + 2 < ntiles:
                    ldw(i + 2)
            ts1 = p.mm(PS[7][0:1, 0:1], ONE[:, 0:1], acc, start=True, stop=True,
                       waits=(p.tok('v'),))
            tot = SM[0:1, 3228:3229]
            tv = p.copy(tot, PS[7][0:1, 0:1], waits=(ts1,))
            tb = p.mm(PS[7][:, 4:5], ONE[0:1, :], tot, start=True, stop=True,
                      waits=(tv,))
            tm = p.ts(SMW[:, smi:smi + 1], PS[7][:, 4:5], 1.0 / DM,
                      1.0 / FFN_INNER, ALU.mult, ALU.mult, waits=(tb,))
            tm = p.ts(SMW[:, smi:smi + 1], SMW[:, smi:smi + 1], 1e-5, None,
                      ALU.max)
            p.op('v', 'reciprocal', SMW[:, sci:sci + 1], SMW[:, smi:smi + 1])
        stoks = {}
        for wsrc, wdst, sci, smi, nrows in specs:
            cols = wsrc.shape[1]
            ntiles = nrows // 128
            wbufs = [None] * ntiles
            wtoks = [None] * ntiles
            def ldw2(i):
                wbufs[i] = TA[:, (i % 2) * 4096:(i % 2) * 4096 + cols]
                wtoks[i] = p.dma(wbufs[i], wsrc[i * 128:(i + 1) * 128, :],
                                 f"wq_ld{i % 3}", waits=(p.tok('v'),),
                                 e=('y', 'g', 's')[i % 3])
            ldw2(0)
            if ntiles > 1:
                ldw2(1)
            for i in range(ntiles):
                u_ = TB[:, (i % 2) * 4096:(i % 2) * 4096 + cols]
                gg = TB[:, 8192 + (i % 2) * 4096:8192 + (i % 2) * 4096 + cols]
                rq = TCb[:, (i % 2) * cols:(i % 2) * cols + cols]
                tv = p.ts(u_, wbufs[i], SMW[:, sci:sci + 1], None, ALU.mult,
                          waits=(wtoks[i],))
                tv = p.ts(gg, u_, 0.5, None, ALU.is_gt)
                tv = p.ts(u_, u_, -0.5, None, ALU.is_lt)
                tv = p.tt(rq, gg, u_, ALU.subtract, waits=(p.tok('t'),))
                if i + 2 < ntiles:
                    ldw2(i + 2)
                nb = cols // 128
                if cols <= 2048:
                    stg = TCb[:, 8192 + (i % 2) * cols:8192 + (i % 2) * cols + cols]
                else:
                    stg = TCb[:, 8192:8192 + cols]
                ev = None
                for b in range(nb):
                    bkb = PS[4 + b % 2][:].bitcast(BF16)
                    tk = p.trans(bkb[:, 0:128], rq[:, b * 128:(b + 1) * 128],
                                 IDB[:], waits=(tv, p.tok('v')))
                    w0 = [tk]
                    if b == 0:
                        w0.append(stoks.get(id(wdst), {}).get(i % 2))
                    ev = p.copy(stg[:, b * 128:(b + 1) * 128], bkb[:, 0:128],
                                waits=tuple(t for t in w0 if t))
                dt_ = p.dma(wdst[:, i * 128:(i + 1) * 128].rearrange(
                    "(b p) f -> p b f", p=128),
                    stg.rearrange("p (b f) -> p b f", f=128), "wqT_st",
                    waits=(ev,))
                stoks.setdefault(id(wdst), {})[i % 2] = dt_
        p.barrier()

    def l0_ffn(x_src, x_dst):
        l0_ffn_weights()
        fst = {}
        for it in range(NT):
            t0 = it * 512
            xt = TB[:, 0:4096]
            ldx = None
            for c in range(NCH):
                ldx = p.dma(xt[:, c * 512:(c + 1) * 512],
                            x_src[c * 128:(c + 1) * 128, t0:t0 + 512], "fx_ld",
                            waits=(p.tok('v'), p.tok('g'), p.tok('t')))
            mrow = SM[0:1, 3584:4096]
            ldm = p.dma(mrow, st_m[0:1, t0:t0 + 512], "fm_ld", waits=(p.tok('t'),))
            rrow = SM[0:1, 2560:3072]
            ldr = p.dma(rrow, st_r[0:1, t0:t0 + 512], "fm_ld")
            tb1 = p.mm(PS[5][:], ONE[0:1, :], mrow, start=True, stop=True,
                       waits=(p.dtok("fm_ld"), p.tok('v')))
            tb2 = p.mm(PS[6][:], ONE[0:1, :], rrow, start=True, stop=True,
                       waits=(p.dtok("fm_ld"),))
            mb = TC[:, 2048:2560]
            rb = TC[:, 2560:3072]
            tv = p.copy(mb, PS[5][:], waits=(tb1,))
            tv = p.copy(rb, PS[6][:], waits=(tb2,))
            y2 = TB[:, 4096:8192]
            YMX = SM[:, 3230:3234]
            RCP = SM[:, 3234:3238]
            tmp1 = SM[:, 3238:3239]
            for c in range(NCH):
                y2c = y2[:, c * 512:(c + 1) * 512]
                tv = p.tt(y2c, xt[:, c * 512:(c + 1) * 512], mb, ALU.subtract,
                          waits=(ldx,))
                tv = p.tt(y2c, y2c, rb, ALU.mult)
            MX_ = TC[:, 4096:4608]
            MN_ = TC[:, 4608:5120]
            p.tt(MX_, y2[:, 0:512], y2[:, 512:1024], ALU.max)
            p.tt(MN_, y2[:, 0:512], y2[:, 512:1024], ALU.min)
            for c in range(2, NCH):
                p.tt(MX_, MX_, y2[:, c * 512:(c + 1) * 512], ALU.max)
                p.tt(MN_, MN_, y2[:, c * 512:(c + 1) * 512], ALU.min)
            tn = p.ts(MN_, MN_, -1.0, None, ALU.mult)
            tn = p.tt(MX_, MX_, MN_, ALU.max)
            tkt = None
            for sb2 in range(4):
                tkt = p.trans(PS[0][:, sb2 * 128:(sb2 + 1) * 128],
                              MX_[:, sb2 * 128:(sb2 + 1) * 128], IDF[:],
                              waits=(tn, p.tok('v'), p.tok('s'))
                              if sb2 == 0 else (tn,), inc=(sb2 == 3))
            for sb2 in range(4):
                p.red(YMX[:, sb2:sb2 + 1], PS[0][:, sb2 * 128:(sb2 + 1) * 128],
                      ALU.max, waits=(tkt,))
            tv = p.ts(YMX, YMX, 1e-5, None, ALU.max)
            tv = p.act(RCP, YMX, AF.Ln, waits=(tv,))
            tv = p.act(RCP, RCP, AF.Exp, scale=-1.0)
            recrow = SM[0:1, 1032:1544]
            for sb2 in range(4):
                tk = p.trans(PS[0][0:1, 0:128], RCP[:, sb2:sb2 + 1], IDF[:],
                             waits=(tv, p.tok('v')))
                p.copy(recrow[0:1, sb2 * 128:(sb2 + 1) * 128],
                       PS[0][0:1, 0:128], waits=(tk,))
            tb3 = p.mm(PS[1][:], ONE[0:1, :], recrow, start=True, stop=True,
                       waits=(p.tok('v'),))
            y2q = TCb[:, 0:4096]
            for c in range(NCH):
                u = TC[:, 3584:4096]
                tv = p.stt(u, y2[:, c * 512:(c + 1) * 512], 127.0, PS[1][:],
                           ALU.mult, ALU.mult,
                           waits=tuple(t for t in
                                       (tb3, p.tok('s'), p.tok('t'),
                                        fst.get(1)) if t))
                tv = rnd(y2q[:, c * 512:(c + 1) * 512], u, TC[:, 2048:2560],
                         TC[:, 2560:3072])
            wslot = lambda j, w: TBb[:, 16384 + (j % 2) * 8192 + w * 4096:
                                     16384 + (j % 2) * 8192 + (w + 1) * 4096]
            def load_j(j):
                src_g = wgT_sc.rearrange("(c p) f -> p c f", p=128)[:, :, j * 512:(j + 1) * 512]
                src_u = wuT_sc.rearrange("(c p) f -> p c f", p=128)[:, :, j * 512:(j + 1) * 512]
                p.dma(wslot(j, 0), src_g, f"wg_ld{j % 2}", waits=(p.tok('t'),))
                return p.dma(wslot(j, 1), src_u, f"wg_ld{j % 2}")
            wtok = [None] * 8
            wtok[0] = load_j(0)
            wtok[1] = load_j(1)
            for sb in range(4):
                p.ts(SMW[:, 6 + sb:7 + sb], YMX[:, sb:sb + 1], SMW[:, 3:4],
                     None, ALU.mult, waits=(p.tok('s'),) if sb == 0 else ())
                p.ts(SMW[:, 6 + sb:7 + sb], SMW[:, 6 + sb:7 + sb], 1.0 / 127.0,
                     None, ALU.mult)
                p.ts(SMW[:, 10 + sb:11 + sb], YMX[:, sb:sb + 1], SMW[:, 4:5],
                     None, ALU.mult)
                p.ts(SMW[:, 10 + sb:11 + sb], SMW[:, 10 + sb:11 + sb],
                     1.0 / 127.0, None, ALU.mult)
            gdone = {}
            for j in range(8):
                for sb in range(4):
                    pg, pu = PS[2 * (sb % 2)], PS[2 * (sb % 2) + 1]
                    for c in range(NCH):
                        w0 = [wtok[j]]
                        if c == 0:
                            if j == 0:
                                w0 += [p.tok('s'), p.tok('v')]
                            else:
                                w0 += list(gdone.get(sb % 2, ()))
                        p.mm(pg[:], y2q[:, c * 512 + sb * 128: c * 512 + sb * 128 + 128],
                             wslot(j, 0)[:, c * 512:(c + 1) * 512],
                             start=(c == 0), stop=(c == NCH - 1), waits=w0,
                             inc=False)
                        gu_tok = p.mm(pu[:], y2q[:, c * 512 + sb * 128: c * 512 + sb * 128 + 128],
                                      wslot(j, 1)[:, c * 512:(c + 1) * 512],
                                      start=(c == 0), stop=(c == NCH - 1),
                                      inc=(c == NCH - 1))
                    ga = TC[:, 3072 + (sb % 2) * 512:3072 + (sb % 2) * 512 + 512]
                    gaw = (gu_tok,)
                    if j == 0 and sb < 2 and fst.get(sb % 2):
                        gaw = (gu_tok, fst[sb % 2])
                    tsg = p.act(ga, pg[:], AF.Silu, scale=SMW[:, 6 + sb:7 + sb],
                                waits=gaw)
                    th = p.stt(TA[:, sb * 4096 + j * 512:sb * 4096 + (j + 1) * 512],
                               pu[:], SMW[:, 10 + sb:11 + sb], ga,
                               ALU.mult, ALU.mult, waits=(tsg,))
                    gdone[sb % 2] = (tsg, th)
                if j + 2 < 8:
                    wtok[j + 2] = load_j(j + 2)
            dh4 = SM[:, 3226:3230]
            hmc = lambda sb: SM[:, 3238 + sb:3239 + sb]
            r1c = lambda sb: SM[:, 3242 + sb:3243 + sb]
            hmt = [None] * 4
            for sb in range(4):
                H = TA[:, sb * 4096:(sb + 1) * 4096]
                tv = p.red(hmc(sb), H, ALU.max, absval=True)
                tv = p.ts(hmc(sb), hmc(sb), 1e-5, None, ALU.max)
                tv = p.act(hmc(sb), hmc(sb), AF.Ln, waits=(tv,))
                hmt[sb] = p.act(hmc(sb), hmc(sb), AF.Exp, scale=-1.0)
            for sb in range(4):
                H = TA[:, sb * 4096:(sb + 1) * 4096]
                hq = TAb[:, sb * 8192:sb * 8192 + 4096]
                hfm = TAb[:, sb * 8192 + 4096:(sb + 1) * 8192]
                r127 = r1c(sb)
                tv = p.ts(r127, hmc(sb), 127.0, None, ALU.mult,
                          waits=(hmt[sb],))
                p.op('v', 'reciprocal', dh4[:, sb:sb + 1], hmc(sb))
                for k in range(4):
                    u = (TC[:, 4096:5120], TB[:, 4096:5120])[sb % 2]
                    s1 = (TC[:, 5120:6144], TB[:, 5120:6144])[sb % 2]
                    s2 = (TB[:, 6144:7168], TB[:, 7168:8192])[sb % 2]
                    tv = p.ts(u, H[:, k * 1024:(k + 1) * 1024], r127, None,
                              ALU.mult,
                              waits=(p.tok('s'),) if sb == 0 and k == 0 else ())
                    tv = rnd(hq[:, k * 1024:(k + 1) * 1024], u, s1, s2)
                for ic in range(32):
                    bkb = PS[4 + ic % 2][:].bitcast(BF16)
                    tk = p.trans(bkb[:, 0:128],
                                 hq[:, ic * 128:(ic + 1) * 128],
                                 IDB[:], waits=(tv, p.tok('v')))
                    p.copy(hfm[:, ic * 128:(ic + 1) * 128],
                           bkb[:, 0:128], waits=(tk,))
            dhrow = SM[0:1, 3584:4096]
            for sb in range(4):
                tk = p.trans(PS[6][0:1, 0:128], dh4[:, sb:sb + 1], IDF[:],
                             waits=(p.tok('v'),))
                p.copy(dhrow[0:1, sb * 128:(sb + 1) * 128], PS[6][0:1, 0:128],
                       waits=(tk,))
            tbd = p.mm(PS[6][:], ONE[0:1, :], dhrow, start=True, stop=True,
                       waits=(p.tok('v'),))
            dht = SM[:, 1544:2056]
            tv = p.copy(dht, PS[6][:], waits=(tbd,))
            hfm_v = TAb[:].rearrange("p (s r) -> p s r", s=4)
            def load_oc(ocn):
                srcd = wdT_sc.rearrange("(ic p) f -> p ic f", p=128)[:, :, ocn * 128:(ocn + 1) * 128]
                return p.dma(WBb[:, (ocn % 2) * 4096:(ocn % 2) * 4096 + 4096],
                             srcd, f"wd_ld{ocn % 2}", waits=(p.tok('t'),))
            dtoks = {0: load_oc(0), 1: load_oc(1)}
            ddone = {}
            for ocn in range(NCH):
                bank = PS[2 + ocn % 2]
                for ic in range(32):
                    w0 = [dtoks[ocn]]
                    if ic == 0:
                        if ocn < 2:
                            w0 += [p.tok('v'), p.tok('s')]
                        else:
                            w0.append(ddone[ocn % 2])
                    dn = p.mm(bank[:], WBb[:, (ocn % 2) * 4096 + ic * 128:
                                           (ocn % 2) * 4096 + (ic + 1) * 128],
                              hfm_v[:, :, 4096 + ic * 128:4096 + (ic + 1) * 128],
                              start=(ic == 0), stop=(ic == 31), waits=w0,
                              inc=(ic == 31))
                if ocn + 2 < NCH:
                    dtoks[ocn + 2] = load_oc(ocn + 2)
                xn = TC[:, 3072 + (ocn % 2) * 512:3072 + (ocn % 2) * 512 + 512]
                tv = p.tt(xn, bank[:], dht, ALU.mult,
                          waits=tuple(t for t in
                                      (dn, fst.get(ocn % 2), p.tok('s')) if t))
                ddone[ocn % 2] = tv
                tv = p.ts(xn, xn, SMW[:, 5:6], 1.0 / 127.0, ALU.mult, ALU.mult)
                tv = p.tt(xn, xn, xt[:, ocn * 512:(ocn + 1) * 512], ALU.add)
                fst[ocn % 2] = p.dma(x_dst[ocn * 128:(ocn + 1) * 128,
                                           t0:t0 + 512], xn, "fxn_st",
                                     waits=(tv,))
                smt = stats_acc(xn, STG[:, 0:512], PS[7], PS[6], ocn == 0,
                                ocn == NCH - 1, 512, tv)
            stats_fin(t0, 512, PS[7], PS[6], smt)
        p.barrier()


    PPT = nc.alloc_sbuf_tensor("PPT", [128, 128], F32)
    TWO_PI_HI = float(np.float32(2.0 * math.pi))
    TWO_PI_LO = float(np.float64(2.0 * math.pi) - np.float64(np.float32(2.0 * math.pi)))
    INV_2PI = float(1.0 / (2.0 * math.pi))
    HALF_PI = float(np.float32(0.5 * math.pi))

    def l1_prep():
        ROW = SM[0:1, 0:512]
        ROW2 = SM[0:1, 512:1024]
        grow = SM[0:1, 1024:1120]
        ld = p.dma(grow, gates_d.rearrange("a b -> (a b)").unsqueeze(0), "l1c",
                   waits=(p.tok('t'), p.tok('v'), p.tok('s')))
        for d in range(3):
            th = grow[0:1, 32 * d:32 * d + 28].rearrange("o (i f) -> o i f", f=4)[:, :, 0]
            ph = grow[0:1, 32 * d + 1:32 * d + 29].rearrange("o (i f) -> o i f", f=4)[:, :, 0]
            c_ = ROW[0:1, 0:7]
            s_ = ROW[0:1, 8:15]
            cp = ROW[0:1, 16:23]
            sp = ROW[0:1, 24:31]
            t1 = p.act(c_, th, AF.Sin, bias=HALF_PI, scale=0.5,
                       waits=(ld, p.tok('v')))
            t1 = p.act(s_, th, AF.Sin, scale=0.5)
            t1 = p.act(cp, ph, AF.Sin, bias=HALF_PI)
            t1 = p.act(sp, ph, AF.Sin)
            scp = ROW[0:1, 32:39]
            tv = p.tt(scp, s_, cp, ALU.mult, waits=(t1,))
            tv = p.ts(ROW[0:1, 40:47], c_, -1.0, None, ALU.add)
            tv = p.tt(ROW2[0:1, d * 7:d * 7 + 7], ROW[0:1, 40:47], scp, ALU.add)
            tv = p.tt(ROW2[0:1, 42 + d * 7:42 + d * 7 + 7], ROW[0:1, 40:47], scp,
                      ALU.subtract)
            tv = p.tt(ROW2[0:1, 21 + d * 7:21 + d * 7 + 7], s_, sp, ALU.mult)
        srow = SM[0:1, 1120:1124]
        ld = p.dma(srow, sup_d, "l1c", waits=(p.tok('s'), p.tok('v')))
        t1 = p.act(ROW[0:1, 0:4], srow, AF.Exp, waits=(ld, p.tok('v')))
        tv = p.red(ROW[0:1, 8:9], ROW[0:1, 0:4], ALU.add, waits=(t1,))
        tv = p.op('v', 'reciprocal', ROW[0:1, 9:10], ROW[0:1, 8:9])
        tv = p.tt(ROW2[0:1, 63:67], ROW[0:1, 0:4],
                  ROW[0:1, 9:10].broadcast_to((1, 4)), ALU.mult)
        qrow = SM[0:1, 1152:1408]
        ld = p.dma(qrow, qw_d.rearrange("a b -> (a b)").unsqueeze(0), "l1c",
                   waits=(p.tok('s'),))
        q00 = qrow.rearrange("o (c f) -> o c f", f=64)[:, :, 0]
        t1 = p.act(ROW[0:1, 0:4], q00, AF.Sin, waits=(ld, p.tok('v')))
        tv = p.ts(ROW[0:1, 0:4], ROW[0:1, 0:4], 0.3, 0.5, ALU.mult, ALU.add,
                  waits=(t1,))
        tv = p.ts(ROW[0:1, 8:12], ROW[0:1, 0:4], -1.0, 1.0, ALU.mult, ALU.add)
        tv = p.tt(ROW2[0:1, 67:71], ROW[0:1, 0:4], ROW[0:1, 8:12], ALU.mult)
        tv = p.ts(ROW2[0:1, 67:71], ROW2[0:1, 67:71], SCALING1, None, ALU.mult)
        tbx = p.mm(PS[5][:, 0:71], ONE[0:1, :], ROW2[0:1, 0:71], start=True,
                   stop=True, waits=(p.tok('v'),))
        tb = p.copy(PPT[:, 0:71], PS[5][:, 0:71], waits=(tbx,))
        PHI = SM[:, 1408:1409]
        tmem = p.memset(PHI, 0.0)
        p.wait('y', tmem)
        for r in range(16):
            srcr = phot_d[r:r + 1, :].rearrange("o (q s2) -> o q s2", s2=16)[:, 0:8, r]
            p.dma(PHI[r::16, 0:1], srcr.rearrange("o q -> (o q)").unsqueeze(1),
                  "l1c", slow=True, waits=(p.tok('v'), p.tok('g')))
        phtok = p.dtok("l1c")
        T1 = SM[:, 1409:1410]
        T2 = SM[:, 1410:1411]
        def rr_sin(dst, badd):
            tv = p.ts(T1, PHI, 1.0, badd, ALU.mult, ALU.add,
                      waits=(phtok, p.tok('s')))
            tv = p.ts(T2, T1, INV_2PI, None, ALU.mult)
            tv = rnd(T2, T2, SM[:, 1411:1412], SM[:, 1412:1413])
            tv = p.stt(SM[:, 1411:1412], T2, -TWO_PI_HI, T1, ALU.mult, ALU.add)
            tv = p.stt(SM[:, 1411:1412], T2, -TWO_PI_LO, SM[:, 1411:1412],
                       ALU.mult, ALU.add)
            return p.act(dst, SM[:, 1411:1412], AF.Sin, waits=(tv,))
        rr_sin(PPT[:, 73:74], 0.0)
        rr_sin(PPT[:, 72:73], HALF_PI)
        tv = p.tt(T1, PPT[:, 72:73], PPT[:, 72:73], ALU.mult, waits=(p.tok('s'),))
        tv = p.tt(T2, PPT[:, 73:74], PPT[:, 73:74], ALU.mult)
        tv = p.tt(PPT[:, 74:75], T1, T2, ALU.subtract)
        brow = SM[0:1, 0:1024]
        ld = p.dma(brow, bout_d, "l1c", waits=(p.tok('t'), p.tok('v'), p.tok('s')))
        for ocn in range(8):
            tk = p.trans(PS[4][:, 0:1], brow[0:1, ocn * 128:(ocn + 1) * 128],
                         IDF[0:1, 0:1], waits=(ld, p.tok('v')))
            p.copy(PPT[:, 80 + ocn:81 + ocn], PS[4][:, 0:1], waits=(tk,))
        brow2 = SM[0:1, 1536:2048]
        for half in range(6):
            ld = p.dma(brow2, bqkv_d[0:1, half * 512:(half + 1) * 512], "l1c",
                       waits=(p.tok('t'),))
            for j in range(4):
                ocn = half * 4 + j
                tk = p.trans(PS[4][:, 0:1], brow2[0:1, j * 128:(j + 1) * 128],
                             IDF[0:1, 0:1], waits=(ld, p.tok('v')))
                p.copy(PPT[:, 88 + ocn:89 + ocn], PS[4][:, 0:1], waits=(tk,))
        for (wsrcd, dstf, ntile) in ((wout_d, woutT_f, 8),
                                     (wqkv_d, wqkvT_f, 24)):
            for i in range(ntile):
                wt = STG[:, 0:1024]
                ld = p.dma(wt, wsrcd[i * 128:(i + 1) * 128, :], "w_ld",
                           waits=(p.tok('v'), p.tok('t')))
                for b in range(8):
                    tk = p.trans(PS[4][:, 0:128], wt[:, b * 128:(b + 1) * 128],
                                 IDF[:], waits=(ld, p.tok('v')))
                    stf = STG[:, 1280 + (b % 2) * 128:1280 + (b % 2) * 128 + 128]
                    cv = p.copy(stf, PS[4][:, 0:128],
                                waits=(tk, p.dtok("wT_st")))
                    blk = (i * 8 + b) * 128
                    p.dma(dstf[blk:blk + 128, :], stf, "wT_st", waits=(cv,))
        p.barrier()

    def l1_attn(x_src, x_dst):
        l1_prep()
        PT = 256
        QR = TA[:, 0:2048]
        QI = TA[:, 2048:4096]
        KR = TA[:, 4096:6144]
        KI = TA[:, 6144:8192]
        NV = TA[:, 8192:10240]
        E = [TA[:, 13824 + k * 256:13824 + (k + 1) * 256] for k in range(6)]
        QRI = TB[:, 0:2048]
        QII = TB[:, 2048:4096]
        QIN = TB[:, 4096:6144]
        KRI = TB[:, 6144:8192]
        KII = TB[:, 8192:10240]
        NVT = TB[:, 10240:12288]
        G = TB[:, 12288:14336]
        SD = TB[:, 14336:14592]
        CT = TB[:, 14592:14848]
        QKVT = TB[:, 0:6144]
        sa = TC[:, 0:512]
        sb_ = TC[:, 512:1024]
        SQA = [TC[:, 0:512], TC[:, 1024:1536]]
        SQB = [TC[:, 512:1024], TC[:, 1536:2048]]
        SUS = [TC[:, 2112 + 256 * i:2368 + 256 * i] for i in range(4)]
        M3 = TC[:, 4160:6208]
        xold = STG[:, 2048:2304]
        def emit_qkv(t0):
            ldm = p.dma(SM[0:1, 2048:2048 + PT], st_nm[0:1, t0:t0 + PT], "fm_ld",
                        waits=(p.tok('t'),))
            ldr = p.dma(SM[0:1, 2304:2304 + PT], st_r[0:1, t0:t0 + PT], "fm_ld")
            tb1 = p.mm(PS[6][:, 0:PT], ONE[0:1, :], SM[0:1, 2048:2048 + PT],
                       start=True, stop=True,
                       waits=(p.dtok("fm_ld"), p.tok('v'), p.tok('s')))
            tb2 = p.mm(PS[7][:, 0:PT], ONE[0:1, :], SM[0:1, 2304:2304 + PT],
                       start=True, stop=True)
            nmb = SM[:, 2560:2560 + PT]
            rbb = SM[:, 2816:2816 + PT]
            tv = p.copy(nmb, PS[6][:, 0:PT], waits=(tb1,))
            tv = p.copy(rbb, PS[7][:, 0:PT], waits=(tb2,))
            XN = TC[:, 0:2048]
            for c in range(NCH):
                xc = STG[:, (c % 2) * PT:(c % 2) * PT + PT]
                ldc = p.dma(xc, x_src[c * 128:(c + 1) * 128, t0:t0 + PT],
                            "xq2_ld", waits=(p.tok('v'), p.tok('t')))
                tv = p.tt(XN[:, c * PT:(c + 1) * PT], xc, nmb, ALU.add,
                          waits=(ldc,))
                tv = p.tt(XN[:, c * PT:(c + 1) * PT],
                          XN[:, c * PT:(c + 1) * PT], rbb, ALU.mult)
            def load_qkv_oc(ocn, slot):
                srcf = wqkvT_f[ocn * 1024:(ocn + 1) * 1024, :].rearrange(
                    "(c p) f -> p c f", p=128)
                return p.dma(WB[:, slot * 1024:slot * 1024 + 1024], srcf,
                             f"wv_ld{slot}", waits=(p.tok('t'),))
            qtoks = {0: load_qkv_oc(0, 0), 1: load_qkv_oc(1, 1)}
            qdone = {}
            for ocn in range(24):
                bank = PS[4 + ocn % 2]
                slot = ocn % 2
                for c in range(NCH):
                    wh = WB[:, slot * 1024 + c * 128:slot * 1024 + (c + 1) * 128]
                    w0 = [qtoks[ocn]]
                    if c == 0:
                        if ocn < 2:
                            w0 += [p.tok('v'), p.tok('s')]
                        else:
                            w0.append(qdone[slot])
                    mmt = p.mm(bank[:, 0:PT], wh, XN[:, c * PT:(c + 1) * PT],
                               start=(c == 0), stop=(c == NCH - 1), waits=w0,
                               inc=(c == NCH - 1))
                if ocn + 2 < 24:
                    qtoks[ocn + 2] = load_qkv_oc(ocn + 2, (ocn + 2) % 2)
                qdone[slot] = p.act(QKVT[:, ocn * PT:(ocn + 1) * PT],
                                    bank[:, 0:PT], AF.Identity,
                                    bias=PPT[:, 88 + ocn:89 + ocn],
                                    waits=(mmt,))

        emit_qkv(0)
        qst = {}
        for itq in range(S // PT):
            t0 = itq * PT
            sqq = TA[:, 10240:14336]
            sqv = TA[:, 14336:16384]
            tq1 = p.act(sqq, QKVT[:, 0:4096], AF.Square, waits=(p.tok('v'),))
            tq2 = p.act(sqv, QKVT[:, 4096:6144], AF.Square)
            nrow = lambda k: SM[0:1, (k % 4) * 512:(k % 4) * 512 + 512]
            rbanks = [PS[4], PS[5], PS[6], PS[7]]
            bbanks = [PS[0], PS[1], PS[2], PS[3]]
            rts = [None] * 12
            ats = [None] * 12
            tbs = [None] * 12
            for k in range(12):
                sq_src = (sqq[:, k * 512:(k + 1) * 512] if k < 8 else
                          sqv[:, (k - 8) * 512:(k - 7) * 512])
                rbank = rbanks[k % 4]
                wm = [tq1 if k < 8 else tq2]
                if k >= 4:
                    wm.append(rts[k - 4])
                tm = p.mm(rbank[0:1, 0:512], ONE[:, 0:1], sq_src, start=True,
                          stop=True, waits=tuple(wm))
                rw = nrow(k)
                wr = [tm]
                if k >= 4:
                    wr.append(tbs[k - 4])
                rts[k] = tv = p.ts(rw, rbank[0:1, 0:512], 1e-24, None, ALU.max,
                                   waits=tuple(wr))
                tv = p.act(rw, rw, AF.Ln, waits=(tv,))
                tv = p.act(rw, rw, AF.Exp, scale=-0.5)
                bbank = bbanks[k % 4]
                wb = [tv]
                if k >= 4:
                    wb.append(ats[k - 4])
                tbs[k] = tb = p.mm(bbank[:], ONE[0:1, :], rw, start=True,
                                   stop=True, waits=tuple(wb))
                if k < 4:
                    p.stt(QR[:, k * 512:(k + 1) * 512],
                          QKVT[:, k * 512:(k + 1) * 512],
                          PPT[:, 72:73], bbank[:], ALU.mult, ALU.mult,
                          waits=(tb,))
                    ats[k] = p.stt(QI[:, k * 512:(k + 1) * 512],
                                   QKVT[:, k * 512:(k + 1) * 512],
                                   PPT[:, 73:74], bbank[:], ALU.mult, ALU.mult)
                elif k < 8:
                    kk = k - 4
                    p.stt(KR[:, kk * 512:(kk + 1) * 512],
                          QKVT[:, k * 512:(k + 1) * 512],
                          PPT[:, 72:73], bbank[:], ALU.mult, ALU.mult,
                          waits=(tb,))
                    ats[k] = p.stt(KI[:, kk * 512:(kk + 1) * 512],
                                   QKVT[:, k * 512:(k + 1) * 512],
                                   PPT[:, 73:74], bbank[:], ALU.mult, ALU.mult)
                else:
                    for hh in range(2):
                        h = (k - 8) * 2 + hh
                        nvs = NV[:].rearrange("p (t i) -> p t i", i=8)[:, :, h]
                        ats[k] = p.tt(nvs, QKVT[:, (16 + h) * PT:(17 + h) * PT],
                                      bbank[:, hh * PT:(hh + 1) * PT], ALU.mult,
                                      waits=(tb,) if hh == 0 else ())
            ESQ = TA[:, 10240:13824]
            for (R, I, Rd, Id) in ((QR, QI, QRI, QII), (KR, KI, KRI, KII)):
                ut = [None] * 8
                for d in range(3):
                    stoks = []
                    for i in range(7):
                        p.act(ESQ[:, (2 * i) * PT:(2 * i + 1) * PT],
                              R[:, i * PT:(i + 1) * PT], AF.Square,
                              waits=(p.tok('v'), p.tok('t')) if i == 0 else ())
                        stoks.append(p.act(ESQ[:, (2 * i + 1) * PT:(2 * i + 2) * PT],
                                           I[:, i * PT:(i + 1) * PT], AF.Square))
                    nut = [None] * 8
                    e5u = [None] * 7
                    e5v = [None] * 7
                    ta_prev = ()
                    for i in range(7):
                        if d == 2:
                            u1 = Rd[:].rearrange("p (t j) -> p t j", j=8)[:, :, i]
                            v1 = Id[:].rearrange("p (t j) -> p t j", j=8)[:, :, i]
                        else:
                            u1 = R[:, i * PT:(i + 1) * PT]
                            v1 = I[:, i * PT:(i + 1) * PT]
                        u2 = R[:, (i + 1) * PT:(i + 2) * PT]
                        v2 = I[:, (i + 1) * PT:(i + 2) * PT]
                        Ap = PPT[:, d * 7 + i:d * 7 + i + 1]
                        Bp = PPT[:, 21 + d * 7 + i:21 + d * 7 + i + 1]
                        Cp = PPT[:, 42 + d * 7 + i:42 + d * 7 + i + 1]
                        E3b = E[3] if i % 2 == 0 else E[0]
                        E4b = E[4] if i % 2 == 0 else E[1]
                        w3 = tuple(t for t in (ut[i + 1],
                                               e5u[i - 2] if i >= 2 else None)
                                   if t)
                        w4 = tuple(t for t in (ut[i + 1],
                                               e5v[i - 2] if i >= 2 else None)
                                   if t)
                        ta3 = p.act(E3b, u2, AF.Copy, scale=Ap, waits=w3)
                        ta4 = p.act(E4b, u2, AF.Copy, scale=Bp, waits=w4)
                        tc_ = p.tt(E[2], ESQ[:, (2 * i) * PT:(2 * i + 1) * PT],
                                   ESQ[:, (2 * i + 1) * PT:(2 * i + 2) * PT],
                                   ALU.add,
                                   waits=(stoks[i], p.tok('t')) if i == 0 else
                                         (stoks[i],))
                        tv = p.stt(E3b, v2, Bp, E3b, ALU.mult, ALU.add,
                                   waits=(ta3,))
                        tv = p.stt(E4b, v2, Cp, E4b, ALU.mult, ALU.add,
                                   waits=(ta4,))
                        e5u[i] = p.tt(E[5], E[2], E3b, ALU.mult)
                        nut[i] = p.tt(u1, u2, E[5], ALU.add, waits=ta_prev)
                        e5v[i] = p.tt(E[5], E[2], E4b, ALU.mult)
                        tv = p.tt(v1, v2, E[5], ALU.add)
                        ta_prev = (ta3, ta4)
                    ut = nut
            for (srcT, dstT) in ((QR, QRI), (QI, QII), (KR, KRI), (KI, KII)):
                p.copy(dstT[:].rearrange("p (t j) -> p t j", j=8)[:, :, 7],
                       srcT[:, 7 * PT:8 * PT])
            p.ts(QIN[:], QII[:], -1.0, None, ALU.mult)
            for g in range(PT // 16):
                tk = p.trans(PS[2][:, 0:128], NV[:, g * 128:(g + 1) * 128], IDF[:],
                             waits=(p.tok('v'), p.tok('s')))
                p.act(NVT[:, g * 128:(g + 1) * 128], PS[2][:, 0:128], AF.Copy,
                      waits=(tk,))
            tv = p.red(SD, NV[:].rearrange("p (t i) -> p t i", i=8), ALU.add)
            tsg = p.act(CT, SD, AF.Sign, waits=(tv,))
            tv = p.ts(CT, CT, PPT[:, 74:75], None, ALU.mult, waits=(tsg,))
            S0all = TA[:, 0:2048]
            WT4all = TA[:, 2048:4096]
            ET2 = [TA[:, 4096:6144], TA[:, 6144:8192]]
            ts0 = None
            for gb in range(PT // 64):
                g0 = gb * 4
                pr_t = pi_t = None
                for k in range(4):
                    gs = slice((g0 + k) * 128, (g0 + k + 1) * 128)
                    ks = slice(k * 128, (k + 1) * 128)
                    w_ = [p.tok('s'), p.tok('v')] if k == 0 else []
                    p.mm(PS[0][:, ks], QRI[:, gs], KRI[:, gs], start=True,
                         stop=False, waits=w_, inc=False)
                    pr_t = p.mm(PS[0][:, ks], QII[:, gs], KII[:, gs], start=False,
                                stop=True, inc=(k == 3))
                    p.mm(PS[1][:, ks], QRI[:, gs], KII[:, gs], start=True,
                         stop=False, inc=False)
                    pi_t = p.mm(PS[1][:, ks], QIN[:, gs], KRI[:, gs], start=False,
                                stop=True, inc=(k == 3))
                t1 = p.act(SQA[gb % 2], PS[0][:], AF.Square, waits=(pr_t,))
                t2 = p.act(SQB[gb % 2], PS[1][:], AF.Square, waits=(pi_t,))
                ts0 = p.tt(S0all[:, gb * 512:(gb + 1) * 512], SQA[gb % 2],
                           SQB[gb % 2], ALU.add, waits=(t2,))
            if t0 + PT < S:
                emit_qkv(t0 + PT)
            for c in range(4):
                ET = ET2[c % 2]
                tew = [ts0]
                if c >= 2:
                    tew += [p.tok('v'), p.tok('t')]
                te = p.act(ET, S0all, AF.Exp, scale=PPT[:, 67 + c:68 + c],
                           waits=tuple(tew))
                tv = p.red(SUS[c], ET.rearrange("p (a b) -> p a b", b=8),
                           ALU.add, waits=(te,))
                tv = p.act(SUS[c], SUS[c], AF.Ln, waits=(tv,))
                tv = p.act(SUS[c], SUS[c], AF.Exp, scale=-1.0)
                tv = p.tt(ET.rearrange("p (a b) -> p a b", b=8),
                          ET.rearrange("p (a b) -> p a b", b=8),
                          SUS[c].unsqueeze(2).broadcast_to((128, PT, 8)),
                          ALU.mult, waits=(tv,))
                tv = p.act(ET, ET, AF.Ln, bias=1e-12, waits=(tv,))
                tv = p.act(ET, ET, AF.Exp, scale=0.5)
                tv = p.tt(ET.rearrange("p (a b) -> p a b", b=128),
                          ET.rearrange("p (a b) -> p a b", b=128),
                          MSK8[:].unsqueeze(1).broadcast_to((128, 16, 128)),
                          ALU.mult, waits=(tv,))
                tbanks = [PS[0], PS[1], PS[2], PS[6]]
                tkall = None
                for gb in range(4):
                    for k in range(4):
                        ks = slice(k * 128, (k + 1) * 128)
                        cs = slice(gb * 512 + k * 128, gb * 512 + (k + 1) * 128)
                        tkall = p.trans(tbanks[gb][:, ks], ET[:, cs], IDF[:],
                                        waits=(tv, p.tok('v'), p.tok('s'))
                                        if (gb == 0 and k == 0) else (tv,),
                                        inc=(gb == 3 and k == 3))
                tws = {}
                for gb in range(4):
                    for k in range(4):
                        ks = slice(k * 128, (k + 1) * 128)
                        cs = slice(gb * 512 + k * 128, gb * 512 + (k + 1) * 128)
                        tws[(gb, k)] = p.act(WT4all[:, cs], tbanks[gb][:, ks],
                                             AF.Copy, waits=(tkall,))
                for gb in range(4):
                    g0 = gb * 4
                    wbank = PS[3] if gb % 2 == 0 else PS[7]
                    mt = None
                    for k in range(4):
                        ks = slice(k * 128, (k + 1) * 128)
                        cs = slice(gb * 512 + k * 128, gb * 512 + (k + 1) * 128)
                        gs = slice((g0 + k) * 128, (g0 + k + 1) * 128)
                        mt = p.mm(wbank[:, ks], NVT[:, gs], WT4all[:, cs],
                                  start=True, stop=True,
                                  waits=(tws[(gb, k)], p.tok('s')),
                                  inc=(k == 3))
                    tsg2 = p.act(sa, wbank[:], AF.Sign, waits=(mt, p.tok('v')))
                    tab = p.act(sb_, wbank[:], AF.Abs)
                    tv2 = p.ts(sb_, sb_, 1.0, 1e-6, ALU.min, ALU.max,
                               waits=(tab,))
                    tv2 = p.tt(sb_, sb_, sa, ALU.mult)
                    gsl = slice(gb * 512, (gb + 1) * 512)
                    if c == 0:
                        p.ts(G[:, gsl], sb_, PPT[:, 63:64], None, ALU.mult)
                    else:
                        p.stt(G[:, gsl], sb_, PPT[:, 63 + c:64 + c], G[:, gsl],
                              ALU.mult, ALU.add)
            msp = p.tt(M3.rearrange("p (t i) -> p t i", i=8),
                       G[:].rearrange("p (t i) -> p t i", i=8),
                       CT.unsqueeze(2).broadcast_to((128, PT, 8)), ALU.mult,
                       waits=(p.tok('t'),))
            def load_out_oc(ocn, slot):
                srcf = woutT_f[ocn * 1024:(ocn + 1) * 1024, :].rearrange(
                    "(c p) f -> p c f", p=128)
                return p.dma(WB[:, slot * 1024:slot * 1024 + 1024], srcf,
                             f"wv_ld{slot}", waits=(p.tok('t'),))
            otoks = {0: load_out_oc(0, 0), 1: load_out_oc(1, 1)}
            odone = {}
            for ocn in range(NCH):
                ldx = p.dma(xold, x_src[ocn * 128:(ocn + 1) * 128, t0:t0 + PT],
                            "xq_ld", waits=(p.tok('v'),))
                bank = PS[4 + ocn % 2]
                slot = ocn % 2
                for ic in range(NCH):
                    wh = WB[:, slot * 1024 + ic * 128:slot * 1024 + (ic + 1) * 128]
                    m3s = M3.rearrange("p (t i) -> p t i", i=8)[:, :, ic]
                    w0 = [otoks[ocn], msp]
                    if ic == 0:
                        if ocn < 2:
                            w0 += [p.tok('v'), p.tok('s')]
                        else:
                            w0.append(odone[slot])
                    ot = p.mm(bank[:, 0:PT], wh, m3s, start=(ic == 0),
                              stop=(ic == NCH - 1), waits=w0,
                              inc=(ic == NCH - 1))
                if ocn + 2 < NCH:
                    otoks[ocn + 2] = load_out_oc(ocn + 2, (ocn + 2) % 2)
                xn = STG[:, 1024 + (ocn % 2) * 256:1024 + (ocn % 2) * 256 + 256]
                tv = p.stt(xn, xold, PPT[:, 80 + ocn:81 + ocn], bank[:, 0:PT],
                           ALU.add, ALU.add,
                           waits=tuple(t for t in
                                       (ot, ldx, qst.get(ocn % 2), p.tok('s'))
                                       if t))
                odone[ocn % 2] = tv
                qst[ocn % 2] = p.dma(x_dst[ocn * 128:(ocn + 1) * 128,
                                           t0:t0 + PT], xn, "xq_st",
                                     waits=(tv,))
                smt = stats_acc(xn, STG[:, 512:768], PS[0], PS[1], ocn == 0,
                                ocn == NCH - 1, PT, tv)
            stats_fin(t0, PT, PS[0], PS[1], smt)
        p.barrier()

    def l1_ffn(x_src, x_dst):
        RQ = TAb[:, 0:8192]
        OPP = SM[:, 3230:3238]
        RCPP = SM[:, 3238:3246]
        tmp1 = SM[:, 3246:3247]
        for c in range(NCH):
            wt = STG[:, 0:1024]
            ld = p.dma(wt, rhs1_d[c * 128:(c + 1) * 128, :], "w_ld",
                       waits=(p.tok('v'), p.tok('t')))
            for b in range(8):
                tk = p.trans(PS[0][:, 0:128], wt[:, b * 128:(b + 1) * 128],
                             IDF[:], waits=(ld, p.tok('v')))
                tr = p.red(tmp1, PS[0][:, 0:128], ALU.max, absval=True,
                           waits=(tk,))
                if c == 0:
                    p.copy(OPP[:, b:b + 1], tmp1)
                else:
                    p.tt(OPP[:, b:b + 1], OPP[:, b:b + 1], tmp1, ALU.max)
        tv = p.ts(OPP, OPP, 1e-12, None, ALU.max)
        tv = p.act(RCPP, OPP, AF.Ln, waits=(tv,))
        tv = p.act(RCPP, RCPP, AF.Exp, scale=-1.0)
        rrow = SM[0:1, 1032:2056]
        for b in range(8):
            tk = p.trans(PS[0][0:1, 0:128], RCPP[:, b:b + 1], IDF[:],
                         waits=(tv, p.tok('v')))
            p.copy(rrow[0:1, b * 128:(b + 1) * 128], PS[0][0:1, 0:128],
                   waits=(tk,))
        tb1 = p.mm(PS[1][:], ONE[0:1, :], rrow[0:1, 0:512], start=True,
                   stop=True, waits=(p.tok('v'),))
        tb2 = p.mm(PS[2][:], ONE[0:1, :], rrow[0:1, 512:1024], start=True,
                   stop=True)
        for c in range(NCH):
            wt = STG[:, 0:1024]
            ld = p.dma(wt, rhs1_d[c * 128:(c + 1) * 128, :], "w_ld",
                       waits=(p.tok('v'), p.tok('t')))
            for half in range(2):
                u = TC[:, 2048:2560]
                bcast = PS[1] if half == 0 else PS[2]
                tv = p.stt(u, wt[:, half * 512:(half + 1) * 512], 127.0,
                           bcast[:], ALU.mult, ALU.mult, waits=(ld, tb2))
                tv = rnd(RQ[:, c * 1024 + half * 512:c * 1024 + (half + 1) * 512],
                         u, TC[:, 3072:3584], TC[:, 3584:4096])
        RSC = SM[:, 1024:1032]
        tv = p.ts(RSC, OPP, 1.0 / 127.0, None, ALU.mult)
        p.barrier()
        for it in range(NT):
            t0 = it * 512
            xt = TB[:, 0:4096]
            ldx = None
            for c in range(NCH):
                ldx = p.dma(xt[:, c * 512:(c + 1) * 512],
                            x_src[c * 128:(c + 1) * 128, t0:t0 + 512], "fx_ld",
                            waits=(p.tok('v'), p.tok('t')))
            ldm = p.dma(SM[0:1, 1032:1544], st_m[0:1, t0:t0 + 512], "fm_ld",
                        waits=(p.tok('t'),))
            ldr = p.dma(SM[0:1, 1544:2056], st_r[0:1, t0:t0 + 512], "fm_ld")
            tb1 = p.mm(PS[5][:], ONE[0:1, :], SM[0:1, 1032:1544], start=True,
                       stop=True, waits=(p.dtok("fm_ld"), p.tok('v')))
            tb2 = p.mm(PS[6][:], ONE[0:1, :], SM[0:1, 1544:2056], start=True,
                       stop=True)
            mb = TC[:, 2048:2560]
            rb = TC[:, 2560:3072]
            tv = p.copy(mb, PS[5][:], waits=(tb1,))
            tv = p.copy(rb, PS[6][:], waits=(tb2,))
            y4 = TB[:, 4096:8192]
            YMX = SM[:, 3230:3234]
            RCP4 = SM[:, 3234:3238]
            tmp2 = SM[:, 3246:3247]
            for c in range(NCH):
                y4c = y4[:, c * 512:(c + 1) * 512]
                tv = p.tt(y4c, xt[:, c * 512:(c + 1) * 512], mb, ALU.subtract,
                          waits=(ldx, p.dtok("epi_st")))
                tv = p.tt(y4c, y4c, rb, ALU.mult)
                for sb2 in range(4):
                    tk = p.trans(PS[0][:, 0:128], y4c[:, sb2 * 128:sb2 * 128 + 128],
                                 IDF[:], waits=(tv, p.tok('v')))
                    tr = p.red(tmp2, PS[0][:, 0:128], ALU.max, absval=True,
                               waits=(tk,))
                    if c == 0:
                        p.copy(YMX[:, sb2:sb2 + 1], tmp2)
                    else:
                        p.tt(YMX[:, sb2:sb2 + 1], YMX[:, sb2:sb2 + 1], tmp2,
                             ALU.max)
            tv = p.ts(YMX, YMX, 1e-12, None, ALU.max)
            tv = p.act(RCP4, YMX, AF.Ln, waits=(tv,))
            tv = p.act(RCP4, RCP4, AF.Exp, scale=-1.0)
            recrow = SM[0:1, 2560:3072]
            lsrow = SM[0:1, 0:512]
            for sb2 in range(4):
                tk = p.trans(PS[0][0:1, 0:128], RCP4[:, sb2:sb2 + 1], IDF[:],
                             waits=(tv, p.tok('v')))
                p.copy(recrow[0:1, sb2 * 128:(sb2 + 1) * 128],
                       PS[0][0:1, 0:128], waits=(tk,))
                tk = p.trans(PS[0][0:1, 0:128], YMX[:, sb2:sb2 + 1], IDF[:],
                             waits=(p.tok('v'),))
                p.copy(lsrow[0:1, sb2 * 128:(sb2 + 1) * 128],
                       PS[0][0:1, 0:128], waits=(tk,))
            tv = p.ts(lsrow, lsrow, 1.0 / 127.0, None, ALU.mult)
            tb3 = p.mm(PS[1][:], ONE[0:1, :], recrow, start=True, stop=True,
                       waits=(p.tok('v'),))
            tb4 = p.mm(PS[5][:], ONE[0:1, :], lsrow, start=True, stop=True)
            lsb = TC[:, 2560:3072]
            tv = p.copy(lsb, PS[5][:], waits=(tb4,))
            y4q = TCb[:, 0:4096]
            for c in range(NCH):
                u = TC[:, 3584:4096]
                tv = p.stt(u, y4[:, c * 512:(c + 1) * 512], 127.0, PS[1][:],
                           ALU.mult, ALU.mult,
                           waits=(tb3, p.tok('s'), p.tok('t'), p.dtok("fxn_st")))
                tv = rnd(y4q[:, c * 512:(c + 1) * 512], u, TC[:, 2048:2560],
                         TC[:, 3072:3584])
            for ocn in range(NCH):
                bank = PS[2 + ocn % 2]
                for c in range(NCH):
                    w0 = []
                    if c == 0:
                        w0 += [p.tok('v'), p.tok('s')]
                    dn = p.mm(bank[:], RQ[:, c * 1024 + ocn * 128:
                                          c * 1024 + ocn * 128 + 128],
                              y4q[:, c * 512:(c + 1) * 512],
                              start=(c == 0), stop=(c == NCH - 1), waits=w0,
                              inc=True)
                xn = TC[:, 3072 + (ocn % 2) * 512:3072 + (ocn % 2) * 512 + 512]
                tv = p.tt(xn, bank[:], lsb, ALU.mult, waits=(dn,))
                tv = p.ts(xn, xn, RSC[:, ocn:ocn + 1], None, ALU.mult)
                tv = p.tt(xn, xn, xt[:, ocn * 512:(ocn + 1) * 512], ALU.add)
                for sb2 in range(4):
                    tk = p.trans(PS[sb2 % 2][:, 0:128],
                                 xn[:, sb2 * 128:sb2 * 128 + 128], IDF[:],
                                 waits=(tv, p.tok('v')))
                    p.copy(TB[:, 4096 + sb2 * 1024 + ocn * 128:
                              4096 + sb2 * 1024 + (ocn + 1) * 128],
                           PS[sb2 % 2][:, 0:128],
                           waits=(tk, p.dtok("epi_st")))
            for sb2 in range(4):
                p.dma(y_out[t0 + sb2 * 128:t0 + sb2 * 128 + 128, :],
                      TB[:, 4096 + sb2 * 1024:4096 + (sb2 + 1) * 1024],
                      "epi_st", waits=(p.tok('v'),))
        p.barrier()

    phase_input()
    cur, nxt = xA, xB
    if 0 in layers:
        l0_attn(cur, nxt)
        cur, nxt = nxt, cur
    if 1 in layers:
        l0_ffn(cur, nxt)
        cur, nxt = nxt, cur
    if 2 in layers:
        l1_attn(cur, nxt)
        cur, nxt = nxt, cur
    if 3 in layers:
        l1_ffn(cur, nxt)
        cur, nxt = nxt, cur
    else:
        epilogue(cur)
    return nc


def _consts():
    identf = np.eye(128, dtype=np.float32)
    identb = np.eye(128, dtype=ml_dtypes.bfloat16)
    onesf = np.ones((128, 128), dtype=np.float32)
    mask8 = np.zeros((128, 128), dtype=np.float32)
    for b in range(16):
        mask8[b * 8:(b + 1) * 8, b * 8:(b + 1) * 8] = 1.0
    sel2 = np.zeros((128, 2), dtype=np.float32)
    sel2[0:64, 0] = 1.0
    sel2[64:128, 1] = 1.0
    sel2t = np.ascontiguousarray(sel2.T)
    return identf, identb, onesf, mask8, sel2, sel2t


def make_in_maps(inputs, seq=None):
    x = np.ascontiguousarray(np.asarray(inputs["L0_attn_x"], dtype=np.float32))
    B = x.shape[0]
    identf, identb, onesf, mask8, sel2, sel2t = _consts()
    common = {
        "L0_Wq": np.ascontiguousarray(inputs["L0_attn_Wq"], dtype=np.float32),
        "L0_Wkv": np.ascontiguousarray(inputs["L0_attn_Wkv"], dtype=np.float32),
        "L0_Wo": np.ascontiguousarray(inputs["L0_attn_Wo"], dtype=np.float32),
        "L0_w_gate": np.ascontiguousarray(inputs["L0_ffn_w_gate"], dtype=np.float32),
        "L0_w_up": np.ascontiguousarray(inputs["L0_ffn_w_up"], dtype=np.float32),
        "L0_w_down": np.ascontiguousarray(inputs["L0_ffn_w_down"], dtype=np.float32),
        "L1_W_qkv": np.ascontiguousarray(inputs["L1_attn_W_qkv"], dtype=np.float32),
        "L1_b_qkv": np.ascontiguousarray(inputs["L1_attn_b_qkv"], dtype=np.float32).reshape(1, -1),
        "L1_W_out": np.ascontiguousarray(inputs["L1_attn_W_out"], dtype=np.float32),
        "L1_b_out": np.ascontiguousarray(inputs["L1_attn_b_out"], dtype=np.float32).reshape(1, -1),
        "L1_qw": np.ascontiguousarray(inputs["L1_attn_quantum_weights"], dtype=np.float32).reshape(4, 64),
        "L1_phot": np.ascontiguousarray(inputs["L1_attn_photonic_phases"], dtype=np.float32),
        "L1_gates": np.ascontiguousarray(inputs["L1_attn_entanglement_gates"], dtype=np.float32).reshape(3, 32),
        "L1_sup": np.ascontiguousarray(inputs["L1_attn_superposition_logits"], dtype=np.float32).reshape(1, 4),
        "L1_rhs": np.ascontiguousarray(inputs["L1_ffn_rhs"], dtype=np.float32),
        "identf": identf, "identb": identb, "onesf": onesf, "mask8": mask8,
        "sel2": sel2, "sel2t": sel2t,
    }
    return [dict(common, x_in=np.ascontiguousarray(x[i])) for i in range(B)]


def kernel(**inputs):
    x = np.asarray(inputs["L0_attn_x"])
    B, S, _ = x.shape
    nc = build_program(seq=S, layers=(0, 1, 2, 3))
    in_maps = make_in_maps(inputs)
    res = run_bass_kernel_spmd(nc, in_maps, core_ids=list(range(B)))
    out = np.stack([np.asarray(res.results[i]["y_out"]) for i in range(B)], axis=0)
    return out.astype(np.float32)


# version 24
# speedup vs baseline: 1.0689x; 1.0030x over previous
import math
import numpy as np
import ml_dtypes
import concourse.bass as bass
import concourse.mybir as mybir
import concourse.bass_isa as bass_isa
from concourse.bass_utils import run_bass_kernel_spmd

F32 = mybir.dt.float32
BF16 = mybir.dt.bfloat16
AF = mybir.ActivationFunctionType
ALU = mybir.AluOpType
AX = mybir.AxisListType

DM = 1024
NCH = DM // 128
INNER0 = 512
FFN_INNER = 4096
SCALING1 = float(128 ** -0.5)

CONST_VALS = sorted({0.0, 1.0, -1.0, 0.5, -0.5, 16.0, -16.0,
                     1e-5, 1e-12, 0.0625, 1.0 / DM, 1.0 / FFN_INNER,
                     float(np.float32(0.5 * math.pi))})


class P:
    def __init__(self, nc):
        self.nc = nc
        self.eng = {'t': nc.tensor, 'v': nc.vector, 's': nc.scalar,
                    'g': nc.gpsimd, 'y': nc.sync}
        self.chain = {k: nc.alloc_semaphore(f"ch_{k}") for k in 'tvsg'}
        self.cnt = {k: 0 for k in 'tvsg'}
        self.dsem = {}
        self.dcnt = {}
        self.lastw = {}

    def dma_sem(self, name):
        if name not in self.dsem:
            self.dsem[name] = self.nc.alloc_semaphore(f"dma_{name}")
            self.dcnt[name] = 0
        return self.dsem[name]

    def tok(self, e):
        return (e, self.cnt[e])

    def _sem_of(self, token):
        if token[0] == 'd':
            return self.dsem[token[1]], token[2]
        return self.chain[token[0]], token[1]

    def wait(self, e, token):
        if token is None:
            return
        if token[0] == e:
            return
        sem, val = self._sem_of(token)
        if val <= 0:
            return
        key = (e, id(sem))
        if self.lastw.get(key, -1) >= val:
            return
        self.eng[e].wait_ge(sem, val)
        self.lastw[key] = val

    def _chain(self, e, instr):
        c = self.cnt[e]
        if c > 0:
            instr._wait_ge(self.chain[e], c)
        instr.then_inc(self.chain[e], 1)
        self.cnt[e] += 1
        return (e, self.cnt[e])

    def op(self, e, name, *args, waits=(), **kw):
        for t in waits:
            self.wait(e, t)
        instr = getattr(self.eng[e], name)(*args, **kw)
        return self._chain(e, instr)

    def tt(self, out, a, b, alu, waits=(), e='v'):
        return self.op(e, 'tensor_tensor', out, a, b, alu, waits=waits)

    def ts(self, out, a, s1, s2=None, op0=ALU.mult, op1=None, waits=(), e='v'):
        if op1 is None:
            return self.op(e, 'tensor_scalar', out, a, s1, None, op0, waits=waits)
        return self.op(e, 'tensor_scalar', out, a, s1, s2, op0, op1, waits=waits)

    def stt(self, out, in0, scalar, in1, op0, op1, waits=(), e='v'):
        return self.op(e, 'scalar_tensor_tensor', out, in0, scalar, in1, op0, op1,
                       waits=waits)

    def copy(self, out, a, waits=(), e='v'):
        return self.op(e, 'tensor_copy', out, a, waits=waits)

    def red(self, out, a, alu, absval=None, waits=(), e='v'):
        return self.op(e, 'tensor_reduce', out, a, AX.X, alu,
                       apply_absolute_value=absval, waits=waits)

    def act(self, out, a, func, bias=0.0, scale=1.0, waits=(), accum_out=None):
        return self.op('s', 'activation', out, a, func, bias=bias, scale=scale,
                       accum_out=accum_out, waits=waits)

    def memset(self, ap, val, waits=(), e='v'):
        return self.op(e, 'memset', ap, val, waits=waits)

    def gred(self, out, in_, rop, waits=()):
        return self.op('g', 'partition_all_reduce', out, in_, 128, rop, waits=waits)

    def mm(self, out, lhsT, rhs, start, stop, waits=(), inc=True, skip=False):
        for t in waits:
            self.wait('t', t)
        instr = self.nc.tensor.matmul(out, lhsT, rhs, start=start, stop=stop,
                                      skip_group_check=skip)
        if inc:
            instr.then_inc(self.chain['t'], 1)
            self.cnt['t'] += 1
        return ('t', self.cnt['t'])

    def trans(self, out, in_, ident, waits=(), inc=True):
        for t in waits:
            self.wait('t', t)
        instr = self.nc.tensor.matmul(out, in_, ident, start=True, stop=True,
                                      is_transpose=True)
        if inc:
            instr.then_inc(self.chain['t'], 1)
            self.cnt['t'] += 1
        return ('t', self.cnt['t'])

    def dma(self, out, in_, stream, waits=(), e='y', slow=False):
        for t in waits:
            self.wait(e, t)
        sem = self.dma_sem(stream)
        kw = {}
        if slow:
            kw['allow_slow_non_contiguous'] = True
        self.eng[e].dma_start(out, in_, **kw).then_inc(sem, 16)
        self.dcnt[stream] += 1
        return ('d', stream, 16 * self.dcnt[stream])

    def dtok(self, stream):
        self.dma_sem(stream)
        return ('d', stream, 16 * self.dcnt[stream])

    def barrier(self):
        for name in self.dsem:
            self.wait('y', ('d', name, 16 * self.dcnt[name]))
        self.nc.all_engine_barrier()


def build_program(seq=4096, layers=(0, 1, 2, 3)):
    nc = bass.Bass("TRN2", target_bir_lowering=False)
    S = seq
    NT = S // 512
    NCK = S // 128

    x_in = nc.dram_tensor("x_in", [S, DM], F32, kind="ExternalInput").ap()
    wq_d = nc.dram_tensor("L0_Wq", [DM, INNER0], F32, kind="ExternalInput").ap()
    wkv_d = nc.dram_tensor("L0_Wkv", [DM, 2 * INNER0], F32, kind="ExternalInput").ap()
    wo_d = nc.dram_tensor("L0_Wo", [INNER0, DM], F32, kind="ExternalInput").ap()
    wg_d = nc.dram_tensor("L0_w_gate", [FFN_INNER, DM], F32, kind="ExternalInput").ap()
    wu_d = nc.dram_tensor("L0_w_up", [FFN_INNER, DM], F32, kind="ExternalInput").ap()
    wd_d = nc.dram_tensor("L0_w_down", [DM, FFN_INNER], F32, kind="ExternalInput").ap()
    wqkv_d = nc.dram_tensor("L1_W_qkv", [3 * DM, DM], F32, kind="ExternalInput").ap()
    bqkv_d = nc.dram_tensor("L1_b_qkv", [1, 3 * DM], F32, kind="ExternalInput").ap()
    wout_d = nc.dram_tensor("L1_W_out", [DM, DM], F32, kind="ExternalInput").ap()
    bout_d = nc.dram_tensor("L1_b_out", [1, DM], F32, kind="ExternalInput").ap()
    qw_d = nc.dram_tensor("L1_qw", [4, 64], F32, kind="ExternalInput").ap()
    phot_d = nc.dram_tensor("L1_phot", [16, DM], F32, kind="ExternalInput").ap()
    gates_d = nc.dram_tensor("L1_gates", [3, 32], F32, kind="ExternalInput").ap()
    sup_d = nc.dram_tensor("L1_sup", [1, 4], F32, kind="ExternalInput").ap()
    rhs1_d = nc.dram_tensor("L1_rhs", [DM, DM], F32, kind="ExternalInput").ap()
    identf_d = nc.dram_tensor("identf", [128, 128], F32, kind="ExternalInput").ap()
    identb_d = nc.dram_tensor("identb", [128, 128], BF16, kind="ExternalInput").ap()
    onesf_d = nc.dram_tensor("onesf", [128, 128], F32, kind="ExternalInput").ap()
    mask8_d = nc.dram_tensor("mask8", [128, 128], F32, kind="ExternalInput").ap()
    sel2_d = nc.dram_tensor("sel2", [128, 2], F32, kind="ExternalInput").ap()
    sel2t_d = nc.dram_tensor("sel2t", [2, 128], F32, kind="ExternalInput").ap()

    y_out = nc.dram_tensor("y_out", [S, DM], F32, kind="ExternalOutput").ap()

    xA = nc.dram_tensor("xA", [DM, S], F32, kind="Internal").ap()
    xB = nc.dram_tensor("xB", [DM, S], F32, kind="Internal").ap()
    st_m = nc.dram_tensor("st_m", [1, S], F32, kind="Internal").ap()
    st_nm = nc.dram_tensor("st_nm", [1, S], F32, kind="Internal").ap()
    st_r = nc.dram_tensor("st_r", [1, S], F32, kind="Internal").ap()
    qnh_sc = nc.dram_tensor("qn_hi", [INNER0, S], BF16, kind="Internal").ap()
    qnl_sc = nc.dram_tensor("qn_lo", [INNER0, S], BF16, kind="Internal").ap()
    knh_sc = nc.dram_tensor("kn_hi", [INNER0, S], BF16, kind="Internal").ap()
    knl_sc = nc.dram_tensor("kn_lo", [INNER0, S], BF16, kind="Internal").ap()
    vt_sc = nc.dram_tensor("vt_sc", [S, 8 * 65], F32, kind="Internal").ap()
    wgT_sc = nc.dram_tensor("wgT_sc", [DM, FFN_INNER], BF16, kind="Internal").ap()
    wuT_sc = nc.dram_tensor("wuT_sc", [DM, FFN_INNER], BF16, kind="Internal").ap()
    wdT_sc = nc.dram_tensor("wdT_sc", [FFN_INNER, DM], BF16, kind="Internal").ap()
    qkv_sc = nc.dram_tensor("qkv_sc", [3 * DM, S], F32, kind="Internal").ap()
    wqkvT_f = nc.dram_tensor("wqkvT_f", [3 * DM * 8, 128], F32, kind="Internal").ap()
    woutT_f = nc.dram_tensor("woutT_f", [DM * 8, 128], F32, kind="Internal").ap()
    csqkv_sc = nc.dram_tensor("csqkv_sc", [1, 3 * DM], F32, kind="Internal").ap()

    p = P(nc)

    TA = nc.alloc_sbuf_tensor("TA", [128, 16384], F32)
    TB = nc.alloc_sbuf_tensor("TB", [128, 16640], F32)
    TC = nc.alloc_sbuf_tensor("TC", [128, 6400], F32)
    STG = nc.alloc_sbuf_tensor("STG", [128, 4096], F32)
    WB = nc.alloc_sbuf_tensor("WB", [128, 4096], F32)
    SM = nc.alloc_sbuf_tensor("SM", [128, 4096], F32)
    CP = nc.alloc_sbuf_tensor("CP", [128, 16], F32)
    IDF = nc.alloc_sbuf_tensor("IDF", [128, 128], F32)
    IDB = nc.alloc_sbuf_tensor("IDB", [128, 128], BF16)
    ONE = nc.alloc_sbuf_tensor("ONE", [128, 128], F32)
    MSK8 = nc.alloc_sbuf_tensor("MSK8", [128, 128], F32)
    PS = [nc.alloc_psum_tensor(f"ps{i}", [128, 512], F32) for i in range(8)]

    STT = nc.alloc_sbuf_tensor("STT", [128, 768], F32)

    TAb = TA[:].bitcast(BF16)
    TBb = TB[:].bitcast(BF16)
    TCb = TC[:].bitcast(BF16)
    STGb = STG[:].bitcast(BF16)
    WBb = WB[:].bitcast(BF16)

    assert len(CONST_VALS) <= 16
    for i, val in enumerate(CONST_VALS):
        p.memset(CP[:, i:i + 1], float(val))
        nc.const_aps.register(F32, float(val), CP[:, i:i + 1])
    p.dma(IDF[:], identf_d, "cst")
    p.dma(IDB[:], identb_d, "cst")
    p.dma(ONE[:], onesf_d, "cst")
    p.dma(MSK8[:], mask8_d, "cst")
    PFLAG = CP[:, 14:15]
    p.memset(PFLAG, 1.0)
    p.ts(PFLAG, PFLAG, 2.0 ** -30, -1.0, ALU.add, ALU.add)
    p.ts(PFLAG, PFLAG, 2.0 ** 30, None, ALU.mult)
    M23 = 12582912.0
    M52 = 6755399441055744.0
    CSQ = SM[0:1, 0:512]
    CSK = SM[0:1, 512:1024]
    CSV = SM[0:1, 1024:1536]
    NMR = SM[0:1, 1536:2048]
    RR = SM[0:1, 2048:2560]
    RSA = SM[0:1, 2560:3072]
    RSB = SM[64:65, 2560:3072]
    RSST_A = SM[64:65, 3584:4096]
    RSST_B = SM[64:65, 3072:3584]
    SEL2T_ = SM[0:2, 3072:3200]
    SEL2_ = SM[:, 3200:3202]
    p.dma(SEL2_, sel2_d, "cst")
    p.dma(SEL2T_, sel2t_d, "cst")
    p.barrier()

    def split(hi, lo, src, waits=()):
        p.copy(hi, src, waits=waits)
        return p.tt(lo, src, hi, ALU.subtract)

    def rnd(out, u, s1, s2, waits=()):
        p.ts(s1, u, M23, -M23, ALU.add, ALU.add, waits=waits)
        p.ts(s2, u, M52, -M52, ALU.add, ALU.add)
        p.tt(s2, s2, s1, ALU.subtract)
        return p.stt(out, s2, PFLAG, s1, ALU.mult, ALU.add)

    def ln_stats(x_src, tag):
        TW = 512
        xa = STG[:, 0:TW]
        sq = STG[:, TW:2 * TW]
        mrow = STG[0:1, 2 * TW:3 * TW]
        nrow = STG[0:1, 3 * TW:4 * TW]
        rrow = STG[0:1, 4 * TW:5 * TW]
        for it in range(S // TW):
            t0 = it * TW
            mmtok = None
            for c in range(NCH):
                ld = p.dma(xa, x_src[c * 128:(c + 1) * 128, t0:t0 + TW],
                           f"lnx_{tag}", waits=(p.tok('s'), p.tok('t')))
                p.mm(PS[6][0:1, 0:TW], ONE[:, 0:1], xa, start=(c == 0),
                     stop=(c == NCH - 1), waits=(ld,), inc=True)
                tsq = p.act(sq, xa, AF.Square, waits=(ld,))
                mmtok = p.mm(PS[7][0:1, 0:TW], ONE[:, 0:1], sq, start=(c == 0),
                             stop=(c == NCH - 1), waits=(tsq,), inc=True)
            p.ts(mrow, PS[6][0:1, 0:TW], 1.0 / DM, None, ALU.mult,
                 waits=(mmtok, p.dtok(f"lns_{tag}")))
            p.ts(nrow, mrow, -1.0, None, ALU.mult)
            p.tt(rrow, mrow, mrow, ALU.mult)
            tv = p.stt(rrow, PS[7][0:1, 0:TW], 1.0 / DM, rrow, ALU.mult,
                       ALU.subtract)
            tv = p.act(rrow, rrow, AF.Ln, bias=1e-5, waits=(tv,))
            tv = p.act(rrow, rrow, AF.Exp, scale=-0.5)
            p.dma(st_m[0:1, t0:t0 + TW], mrow, f"lns_{tag}",
                  waits=(p.tok('v'), tv))
            p.dma(st_nm[0:1, t0:t0 + TW], nrow, f"lns_{tag}")
            p.dma(st_r[0:1, t0:t0 + TW], rrow, f"lns_{tag}")
        p.barrier()

    STM = STT[0:1, 0:256]
    STN = STT[0:1, 256:512]
    STR = STT[0:1, 512:768]

    def stats_acc(xn, sq, sumbank, sqbank, first, last, tw, xtok):
        tq = p.act(sq, xn, AF.Square, waits=(xtok,))
        p.mm(sumbank[0:1, 0:tw], ONE[:, 0:1], xn, start=first, stop=last,
             waits=(xtok,), inc=False, skip=True)
        return p.mm(sqbank[0:1, 0:tw], ONE[:, 0:1], sq, start=first,
                    stop=last, waits=(tq,), inc=last, skip=True)

    def stats_fin(t0, tw, sumbank, sqbank, smt):
        for h in range(tw // 256):
            c0 = h * 256
            tv = p.ts(STM, sumbank[0:1, c0:c0 + 256], 1.0 / DM, None, ALU.mult,
                      waits=(smt, p.dtok("st_st")))
            p.ts(STN, STM, -1.0, None, ALU.mult)
            p.tt(STR, STM, STM, ALU.mult)
            tv = p.stt(STR, sqbank[0:1, c0:c0 + 256], 1.0 / DM, STR,
                       ALU.mult, ALU.subtract)
            tv = p.act(STR, STR, AF.Ln, bias=1e-5, waits=(tv,))
            tv = p.act(STR, STR, AF.Exp, scale=-0.5)
            p.dma(st_m[0:1, t0 + c0:t0 + c0 + 256], STM, "st_st",
                  waits=(p.tok('v'), tv))
            p.dma(st_nm[0:1, t0 + c0:t0 + c0 + 256], STN, "st_st")
            p.dma(st_r[0:1, t0 + c0:t0 + c0 + 256], STR, "st_st")

    def epilogue(x_src):
        for it in range(NCK):
            t0 = it * 128
            for c in range(NCH):
                xa = STG[:, (c % 4) * 128:(c % 4) * 128 + 128]
                ld = p.dma(xa, x_src[c * 128:(c + 1) * 128, t0:t0 + 128], "epi_ld",
                           waits=(p.tok('t'),))
                bank = PS[c % 2]
                tk = p.trans(bank[:, 0:128], xa, IDF[:], waits=(ld, p.tok('v')))
                p.copy(STG[:, 1024 + c * 128:1024 + (c + 1) * 128],
                       bank[:, 0:128], waits=(tk,))
            st = p.dma(y_out[t0:t0 + 128, :], STG[:, 1024:1024 + DM], "epi_st",
                       waits=(p.tok('v'),))
            p.wait('t', st)
        p.barrier()

    def phase_input():
        YS = SM[:, 0:32]
        YQ = SM[:, 32:64]
        for it in range(NCK):
            t0 = it * 128
            ld = p.dma(STG[:, 0:DM], x_in[t0:t0 + 128, :], "in_ld",
                       waits=(p.tok('t'), p.tok('v'), p.tok('s')))
            p.red(YS[:, it:it + 1], STG[:, 0:DM], ALU.add, waits=(ld,))
            tq = p.act(STG[:, 2560:3584], STG[:, 0:DM], AF.Square, waits=(ld,))
            p.red(YQ[:, it:it + 1], STG[:, 2560:3584], ALU.add, waits=(tq,))
            for c in range(NCH):
                bank = PS[c % 2]
                tk = p.trans(bank[:, 0:128], STG[:, c * 128:(c + 1) * 128], IDF[:],
                             waits=(ld, p.tok('v')))
                cp = p.copy(STG[:, 1536 + c * 128:1536 + (c + 1) * 128],
                            bank[:, 0:128], waits=(tk,))
                p.dma(xA[c * 128:(c + 1) * 128, t0:t0 + 128],
                      STG[:, 1536 + c * 128:1536 + (c + 1) * 128], "in_st",
                      waits=(cp,))
            p.wait('t', p.dtok("in_st"))
        M_ = SM[:, 64:96]
        V_ = SM[:, 96:128]
        N_ = SM[:, 128:160]
        p.ts(M_, YS, 1.0 / DM, None, ALU.mult)
        p.ts(N_, M_, -1.0, None, ALU.mult)
        p.tt(V_, M_, M_, ALU.mult)
        tv = p.stt(V_, YQ, 1.0 / DM, V_, ALU.mult, ALU.subtract)
        tv = p.act(V_, V_, AF.Ln, bias=1e-5, waits=(tv,))
        tv = p.act(V_, V_, AF.Exp, scale=-0.5)
        for (src, dst, nm) in ((M_, st_m, 0), (N_, st_nm, 1), (V_, st_r, 2)):
            tk = p.trans(PS[2][0:32, 0:128], src, IDF[:], waits=(p.tok('v'), tv))
            cv = p.copy(SM[0:32, 160 + nm * 128:160 + nm * 128 + 128],
                        PS[2][0:32, 0:128], waits=(tk,))
            p.dma(dst.rearrange("o (a b) -> (o a) b", b=128),
                  SM[0:32, 160 + nm * 128:160 + nm * 128 + 128], "st_st",
                  waits=(cv,))
        p.barrier()

    def l0_attn(x_src, x_dst):
        for it in range(NT):
            t0 = it * 512
            for c in range(NCH):
                xa = STG[:, 0:512]
                ld = p.dma(xa, x_src[c * 128:(c + 1) * 128, t0:t0 + 512], "xs_ld",
                           waits=(p.tok('v'),))
                split(TAb[:, c * S + t0: c * S + t0 + 512],
                      TBb[:, c * S + t0: c * S + t0 + 512], xa, waits=(ld,))
        p.barrier()

        def load_w_split(wsrc, cols, hi, lo, csrow):
            cs_tok = None
            nchunks = wsrc.shape[0] // 128
            for c in range(nchunks):
                wt = STG[:, 0:cols]
                ld = p.dma(wt, wsrc[c * 128:(c + 1) * 128, 0:cols], "w_ld",
                           waits=(p.tok('v'), p.tok('t')))
                split(hi[:, c * cols:(c + 1) * cols],
                      lo[:, c * cols:(c + 1) * cols], wt, waits=(ld,))
                if csrow is not None:
                    for j in range(cols // 512):
                        cs_tok = p.mm(PS[7][0:1, j * 512:(j + 1) * 512],
                                      ONE[:, 0:1],
                                      wt[:, j * 512:(j + 1) * 512],
                                      start=(c == 0), stop=(c == nchunks - 1),
                                      waits=(ld,), inc=True)
            if csrow is not None:
                return p.copy(csrow[0:1, 0:cols], PS[7][0:1, 0:cols], waits=(cs_tok,))
            return None

        def proj_qkv(which):
            if which == 'q':
                wsrc, csrow = wq_d, CSQ
            elif which == 'k':
                wsrc, csrow = wkv_d[:, 0:512], CSK
            else:
                wsrc, csrow = wkv_d[:, 512:1024], CSV
            cols = 512
            hi = WBb[:, 0:4096]
            lo = WBb[:, 4096:8192]
            load_w_split(wsrc, cols, hi, lo, csrow)
            p.barrier()
            for it in range(NT):
                t0 = it * 512
                nm_t = p.dma(NMR, st_nm[0:1, t0:t0 + 512], "nm_ld",
                             waits=(p.tok('t'),))
                r_t = None
                if which == 'v':
                    r_t = p.dma(RR, st_r[0:1, t0:t0 + 512], "r_ld",
                                waits=(p.tok('t'),))
                for oc in range(4):
                    bank = PS[oc % 2]
                    for c in range(NCH):
                        st0 = (c == 0)
                        w_h = hi[:, c * cols + oc * 128: c * cols + oc * 128 + 128]
                        w_l = lo[:, c * cols + oc * 128: c * cols + oc * 128 + 128]
                        xh = TAb[:, c * S + t0: c * S + t0 + 512]
                        xl = TBb[:, c * S + t0: c * S + t0 + 512]
                        p.mm(bank[:], w_h, xh, start=st0, stop=False,
                             waits=(p.tok('v'), p.tok('s')) if st0 else (),
                             inc=False)
                        p.mm(bank[:], w_h, xl, start=False, stop=False, inc=False)
                        p.mm(bank[:], w_l, xh, start=False, stop=False, inc=False)
                    mmt = p.mm(bank[:], csrow[0:1, oc * 128:(oc + 1) * 128], NMR,
                               start=False, stop=True, waits=(nm_t,))
                    if which in ('q', 'k'):
                        sq = STG[:, 0:512]
                        tsq = p.act(sq, bank[:], AF.Square, waits=(mmt,))
                        th = p.mm(PS[2][0:2, 0:512], SEL2_, sq, start=True,
                                  stop=True, waits=(tsq, p.tok('v')))
                        ss = STG[:, 512:1024]
                        tv = p.ts(ss[0:2, :], PS[2][0:2, 0:512], 1e-24, None,
                                  ALU.max, waits=(th,))
                        tv = p.act(ss[0:2, :], ss[0:2, :], AF.Ln, waits=(tv,))
                        tv = p.act(ss[0:2, :], ss[0:2, :], AF.Exp, scale=-0.5)
                        tb = p.mm(PS[3][:], SEL2T_, ss[0:2, :], start=True,
                                  stop=True, waits=(tv, p.tok('v')))
                        rnb = STG[:, 1024:1536]
                        tv = p.copy(rnb, PS[3][:], waits=(tb,))
                        qn = STG[:, 1536:2048]
                        tv = p.tt(qn, bank[:], rnb, ALU.mult)
                        hi_o = STGb[:, 4096:4608]
                        lo_o = STGb[:, 4608:5120]
                        tv = split(hi_o, lo_o, qn,
                                   waits=(p.dtok(f"{which}n_st"),))
                        dsth = qnh_sc if which == 'q' else knh_sc
                        dstl = qnl_sc if which == 'q' else knl_sc
                        p.dma(dsth[oc * 128:(oc + 1) * 128, t0:t0 + 512], hi_o,
                              f"{which}n_st", waits=(tv,))
                        p.dma(dstl[oc * 128:(oc + 1) * 128, t0:t0 + 512], lo_o,
                              f"{which}n_st")
                    else:
                        tb = p.mm(PS[2][:], ONE[0:1, :], RR, start=True, stop=True,
                                  waits=(r_t, p.tok('v')))
                        rbs = STG[:, 0:512]
                        tv = p.copy(rbs, PS[2][:], waits=(tb,))
                        vv = STG[:, 512:1024]
                        tv = p.tt(vv, bank[:], rbs, ALU.mult, waits=(mmt,))
                        for sb in range(4):
                            tk = p.trans(PS[3][:, 0:128],
                                         vv[:, sb * 128:sb * 128 + 128],
                                         IDF[:], waits=(tv, p.tok('v')))
                            h0, h1 = 2 * oc, 2 * oc + 1
                            c0 = p.copy(VTS[:, h0 * 65:h0 * 65 + 64],
                                        PS[3][:, 0:64],
                                        waits=(tk, p.dtok("vt_st")))
                            c0 = p.copy(VTS[:, h1 * 65:h1 * 65 + 64],
                                        PS[3][:, 64:128])
                            p.dma(vt_sc[t0 + sb * 128: t0 + sb * 128 + 128,
                                        h0 * 65:h1 * 65 + 65],
                                  VTS[:, h0 * 65:h1 * 65 + 65], "vt_st",
                                  waits=(c0,))
            p.barrier()

        VTS = TC[:, 0:520]
        for h in range(8):
            p.memset(VTS[:, h * 65 + 64:h * 65 + 65], 1.0)
        proj_qkv('q')
        proj_qkv('k')
        proj_qkv('v')

        for c in range(4):
            for it in range(NT):
                t0 = it * 512
                p.dma(TAb[:, c * S + t0:c * S + t0 + 512],
                      knh_sc[c * 128:(c + 1) * 128, t0:t0 + 512], "kn_ld")
                p.dma(TAb[:, 4 * S + c * S + t0:4 * S + c * S + t0 + 512],
                      knl_sc[c * 128:(c + 1) * 128, t0:t0 + 512], "kn_ld")
        for ck in range(NCK):
            p.dma(TB[:, ck * 520:(ck + 1) * 520],
                  vt_sc[ck * 128:(ck + 1) * 128, :], "vt_ld")
        for c in range(4):
            p.dma(WB[:, c * 1024:(c + 1) * 1024], wo_d[c * 128:(c + 1) * 128, :],
                  "w_ld", waits=(p.tok('v'), p.tok('t')))
        p.barrier()

        xst = {}
        for sw in range(NT):
            t0 = sw * 512
            qh_sw = STGb[:, 0:2048]
            ql_sw = STGb[:, 2048:4096]
            for c in range(4):
                p.dma(qh_sw[:, c * 512:(c + 1) * 512],
                      qnh_sc[c * 128:(c + 1) * 128, t0:t0 + 512], "qn_ld",
                      waits=(p.tok('t'),))
                p.dma(ql_sw[:, c * 512:(c + 1) * 512],
                      qnl_sc[c * 128:(c + 1) * 128, t0:t0 + 512], "qn_ld")
            qtok = p.dtok("qn_ld")
            kntok = p.dtok("kn_ld")
            vttok = p.dtok("vt_ld")
            pvs_tok = [None] * 4
            for pr in range(4):
                qh_pr = qh_sw[:, pr * 512:(pr + 1) * 512]
                ql_pr = ql_sw[:, pr * 512:(pr + 1) * 512]
                exp_toks = [None, None]
                pend = None
                for ck in range(NCK):
                    knh = TAb[:, pr * S + t0 * 0 + ck * 128: pr * S + ck * 128 + 128]
                    knl = TAb[:, 4 * S + pr * S + ck * 128: 4 * S + pr * S + ck * 128 + 128]
                    lb, lb2 = PS[ck % 2], PS[2 + ck % 2]
                    w_ = [qtok, kntok]
                    if exp_toks[ck % 2] is not None:
                        w_.append(exp_toks[ck % 2])
                    elif ck == 0:
                        w_.append(p.tok('s'))
                        w_.append(p.tok('v'))
                    p.mm(lb[:], knh[0:64, :], qh_pr[0:64, :], start=True,
                         stop=False, waits=w_, inc=False)
                    p.mm(lb[:], knh[0:64, :], ql_pr[0:64, :], start=False,
                         stop=False, inc=False)
                    p.mm(lb[:], knl[0:64, :], qh_pr[0:64, :], start=False,
                         stop=True, inc=False)
                    p.mm(lb2[:], knh[64:128, :], qh_pr[64:128, :], start=True,
                         stop=False, inc=False)
                    p.mm(lb2[:], knh[64:128, :], ql_pr[64:128, :], start=False,
                         stop=False, inc=False)
                    qk_tok = p.mm(lb2[:], knl[64:128, :], qh_pr[64:128, :],
                                  start=False, stop=True)
                    if pend is not None:
                        pea, peb, ppta, pptb, pvth, pvtl, pfirst = pend
                        w0 = [pea, vttok]
                        if pfirst:
                            w0.append(p.tok('v'))
                        p.mm(PS[4][0:65, :], pvth, ppta, start=pfirst,
                             stop=False, waits=w0, inc=False)
                        p.mm(PS[5][0:65, :], pvtl, pptb, start=pfirst,
                             stop=False, waits=(peb,), inc=False)
                    pta = STG[:, 2048 + (ck % 2) * 512:2048 + (ck % 2) * 512 + 512]
                    ptb = TC[:, 520 + (ck % 2) * 512:520 + (ck % 2) * 512 + 512]
                    ea = p.act(pta, lb[:], AF.Exp, bias=-16.0, scale=16.0,
                               waits=(qk_tok,))
                    eb = p.act(ptb, lb2[:], AF.Exp, bias=-16.0, scale=16.0)
                    exp_toks[ck % 2] = eb
                    vth = TB[:, ck * 520 + (2 * pr) * 65: ck * 520 + (2 * pr) * 65 + 65]
                    vtl = TB[:, ck * 520 + (2 * pr + 1) * 65: ck * 520 + (2 * pr + 1) * 65 + 65]
                    pend = (ea, eb, pta, ptb, vth, vtl, ck == 0)
                pea, peb, ppta, pptb, pvth, pvtl, pfirst = pend
                p.mm(PS[4][0:65, :], pvth, ppta, start=pfirst, stop=True,
                     waits=(pea, vttok), inc=True)
                p.mm(PS[5][0:65, :], pvtl, pptb, start=pfirst, stop=True,
                     waits=(peb,), inc=True)
                pv_tok = p.tok('t')
                c1 = p.copy(RSST_A, PS[4][64:65, :],
                            waits=(pv_tok, p.dtok("rs_mv")))
                c2 = p.copy(RSST_B, PS[5][64:65, :])
                d1 = p.dma(RSA, RSST_A, "rs_mv",
                           waits=(c2, p.tok('t')))
                d2 = p.dma(RSB, RSST_B, "rs_mv")
                tv = p.ts(RSA, RSA, 1e-6, None, ALU.max, waits=(d2,))
                tv = p.act(RSA, RSA, AF.Ln, waits=(tv,))
                tv = p.act(RSA, RSA, AF.Exp, scale=-1.0)
                tvb = p.ts(RSB, RSB, 1e-6, None, ALU.max, waits=(d2,))
                tvb = p.act(RSB, RSB, AF.Ln, waits=(tvb,))
                tvb = p.act(RSB, RSB, AF.Exp, scale=-1.0)
                tb = p.mm(PS[6][0:64, :], ONE[0:1, 0:64], RSA,
                          start=True, stop=True, waits=(tv, p.tok('v')))
                rba = TC[:, 1544:2056]
                pvna = TC[:, 4104 + pr * 512:4104 + (pr + 1) * 512]
                tv2 = p.copy(rba[0:64, :], PS[6][0:64, :], waits=(tb,))
                tv2 = p.tt(pvna[0:64, :], PS[4][0:64, :], rba[0:64, :], ALU.mult)
                tb = p.mm(PS[6][0:64, :], ONE[64:65, 0:64], RSB,
                          start=True, stop=True, waits=(tvb, p.tok('v')))
                tv3 = p.copy(rba[0:64, :], PS[6][0:64, :], waits=(tb,))
                hbst = TC[:, 2568:3080]
                tv3 = p.tt(hbst[0:64, :], PS[5][0:64, :], rba[0:64, :],
                           ALU.mult, waits=(p.dtok("pv_mv"),))
                mv = p.dma(pvna[64:128, :], hbst[0:64, :], "pv_mv", waits=(tv3,))
                pvs_tok[pr] = (mv, tv2)
            for oc in range(NCH):
                bank = PS[2 + oc % 2]
                for pr in range(4):
                    pvna = TC[:, 4104 + pr * 512:4104 + (pr + 1) * 512]
                    wh = WB[:, pr * 1024 + oc * 128: pr * 1024 + oc * 128 + 128]
                    w0 = list(pvs_tok[pr])
                    if pr == 0:
                        w0 += [p.tok('s'), p.tok('v')]
                    wo_tok = p.mm(bank[:], wh, pvna, start=(pr == 0),
                                  stop=(pr == 3), waits=w0, inc=(pr == 3))
                xo = STG[:, 3072 + (oc % 2) * 512:3072 + (oc % 2) * 512 + 512]
                ld = p.dma(xo, x_src[oc * 128:(oc + 1) * 128, t0:t0 + 512],
                           "xo_ld", waits=(p.tok('t'),))
                xn = TC[:, 3080 + (oc % 2) * 512:3080 + (oc % 2) * 512 + 512]
                tv = p.tt(xn, bank[:], xo, ALU.add,
                          waits=tuple(t for t in
                                      (wo_tok, ld, xst.get(oc % 2), p.tok('s'))
                                      if t))
                xst[oc % 2] = p.dma(x_dst[oc * 128:(oc + 1) * 128, t0:t0 + 512],
                                    xn, "xn_st", waits=(tv,))
                smt = stats_acc(xn, STG[:, 2048:2560], PS[7], PS[6], oc == 0,
                                oc == NCH - 1, 512, tv)
            stats_fin(t0, 512, PS[7], PS[6], smt)
        p.barrier()


    SMW = SM[:, 3210:3226]

    def l0_ffn_weights():
        specs = [(wg_d, wgT_sc, 0, 3, FFN_INNER), (wu_d, wuT_sc, 1, 4, FFN_INNER),
                 (wd_d, wdT_sc, 2, 5, DM)]
        for wsrc, _, sci, smi, nrows in specs:
            cols = wsrc.shape[1]
            acc = SM[:, 3226:3227]
            red_o = SM[:, 3227:3228]
            ntiles = nrows // 128
            wbufs = [None] * ntiles
            wtoks = [None] * ntiles
            def ldw(i):
                wbufs[i] = TA[:, (i % 2) * 4096:(i % 2) * 4096 + cols]
                wtoks[i] = p.dma(wbufs[i], wsrc[i * 128:(i + 1) * 128, :],
                                 f"wq_ld{i % 3}", waits=(p.tok('v'),),
                                 e=('y', 'g', 's')[i % 3])
            ldw(0)
            if ntiles > 1:
                ldw(1)
            for i in range(ntiles):
                tr = p.red(red_o, wbufs[i], ALU.add, absval=True,
                           waits=(wtoks[i],))
                if i == 0:
                    p.copy(acc, red_o)
                else:
                    p.tt(acc, acc, red_o, ALU.add)
                if i + 2 < ntiles:
                    ldw(i + 2)
            ts1 = p.mm(PS[7][0:1, 0:1], ONE[:, 0:1], acc, start=True, stop=True,
                       waits=(p.tok('v'),))
            tot = SM[0:1, 3228:3229]
            tv = p.copy(tot, PS[7][0:1, 0:1], waits=(ts1,))
            tb = p.mm(PS[7][:, 4:5], ONE[0:1, :], tot, start=True, stop=True,
                      waits=(tv,))
            tm = p.ts(SMW[:, smi:smi + 1], PS[7][:, 4:5], 1.0 / DM,
                      1.0 / FFN_INNER, ALU.mult, ALU.mult, waits=(tb,))
            tm = p.ts(SMW[:, smi:smi + 1], SMW[:, smi:smi + 1], 1e-5, None,
                      ALU.max)
            p.op('v', 'reciprocal', SMW[:, sci:sci + 1], SMW[:, smi:smi + 1])
        stoks = {}
        for wsrc, wdst, sci, smi, nrows in specs:
            cols = wsrc.shape[1]
            ntiles = nrows // 128
            wbufs = [None] * ntiles
            wtoks = [None] * ntiles
            def ldw2(i):
                wbufs[i] = TA[:, (i % 2) * 4096:(i % 2) * 4096 + cols]
                wtoks[i] = p.dma(wbufs[i], wsrc[i * 128:(i + 1) * 128, :],
                                 f"wq_ld{i % 3}", waits=(p.tok('v'),),
                                 e=('y', 'g', 's')[i % 3])
            ldw2(0)
            if ntiles > 1:
                ldw2(1)
            for i in range(ntiles):
                u_ = TB[:, (i % 2) * 4096:(i % 2) * 4096 + cols]
                gg = TB[:, 8192 + (i % 2) * 4096:8192 + (i % 2) * 4096 + cols]
                rq = TCb[:, (i % 2) * cols:(i % 2) * cols + cols]
                tv = p.ts(u_, wbufs[i], SMW[:, sci:sci + 1], None, ALU.mult,
                          waits=(wtoks[i],))
                tv = p.ts(gg, u_, 0.5, None, ALU.is_gt)
                tv = p.ts(u_, u_, -0.5, None, ALU.is_lt)
                tv = p.tt(rq, gg, u_, ALU.subtract, waits=(p.tok('t'),))
                if i + 2 < ntiles:
                    ldw2(i + 2)
                nb = cols // 128
                if cols <= 2048:
                    stg = TCb[:, 8192 + (i % 2) * cols:8192 + (i % 2) * cols + cols]
                else:
                    stg = TCb[:, 8192:8192 + cols]
                ev = None
                for b in range(nb):
                    bkb = PS[4 + b % 2][:].bitcast(BF16)
                    tk = p.trans(bkb[:, 0:128], rq[:, b * 128:(b + 1) * 128],
                                 IDB[:], waits=(tv, p.tok('v')))
                    w0 = [tk]
                    if b == 0:
                        w0.append(stoks.get(id(wdst), {}).get(i % 2))
                    ev = p.copy(stg[:, b * 128:(b + 1) * 128], bkb[:, 0:128],
                                waits=tuple(t for t in w0 if t))
                dt_ = p.dma(wdst[:, i * 128:(i + 1) * 128].rearrange(
                    "(b p) f -> p b f", p=128),
                    stg.rearrange("p (b f) -> p b f", f=128), "wqT_st",
                    waits=(ev,))
                stoks.setdefault(id(wdst), {})[i % 2] = dt_
        p.barrier()

    def l0_ffn(x_src, x_dst):
        l0_ffn_weights()
        fst = {}
        for it in range(NT):
            t0 = it * 512
            xt = TB[:, 0:4096]
            ldx = None
            for c in range(NCH):
                ldx = p.dma(xt[:, c * 512:(c + 1) * 512],
                            x_src[c * 128:(c + 1) * 128, t0:t0 + 512], "fx_ld",
                            waits=(p.tok('v'), p.tok('g'), p.tok('t')))
            mrow = SM[0:1, 3584:4096]
            ldm = p.dma(mrow, st_m[0:1, t0:t0 + 512], "fm_ld", waits=(p.tok('t'),))
            rrow = SM[0:1, 2560:3072]
            ldr = p.dma(rrow, st_r[0:1, t0:t0 + 512], "fm_ld")
            tb1 = p.mm(PS[5][:], ONE[0:1, :], mrow, start=True, stop=True,
                       waits=(p.dtok("fm_ld"), p.tok('v')))
            tb2 = p.mm(PS[6][:], ONE[0:1, :], rrow, start=True, stop=True,
                       waits=(p.dtok("fm_ld"),))
            mb = TC[:, 2048:2560]
            rb = TC[:, 2560:3072]
            tv = p.copy(mb, PS[5][:], waits=(tb1,))
            tv = p.copy(rb, PS[6][:], waits=(tb2,))
            y2 = TB[:, 4096:8192]
            YMX = SM[:, 3230:3234]
            RCP = SM[:, 3234:3238]
            tmp1 = SM[:, 3238:3239]
            for c in range(NCH):
                y2c = y2[:, c * 512:(c + 1) * 512]
                tv = p.tt(y2c, xt[:, c * 512:(c + 1) * 512], mb, ALU.subtract,
                          waits=(ldx,))
                tv = p.tt(y2c, y2c, rb, ALU.mult)
            MX_ = TC[:, 4096:4608]
            MN_ = TC[:, 4608:5120]
            p.tt(MX_, y2[:, 0:512], y2[:, 512:1024], ALU.max)
            p.tt(MN_, y2[:, 0:512], y2[:, 512:1024], ALU.min)
            for c in range(2, NCH):
                p.tt(MX_, MX_, y2[:, c * 512:(c + 1) * 512], ALU.max)
                p.tt(MN_, MN_, y2[:, c * 512:(c + 1) * 512], ALU.min)
            tn = p.ts(MN_, MN_, -1.0, None, ALU.mult)
            tn = p.tt(MX_, MX_, MN_, ALU.max)
            tkt = None
            for sb2 in range(4):
                tkt = p.trans(PS[0][:, sb2 * 128:(sb2 + 1) * 128],
                              MX_[:, sb2 * 128:(sb2 + 1) * 128], IDF[:],
                              waits=(tn, p.tok('v'), p.tok('s'))
                              if sb2 == 0 else (tn,), inc=(sb2 == 3))
            for sb2 in range(4):
                p.red(YMX[:, sb2:sb2 + 1], PS[0][:, sb2 * 128:(sb2 + 1) * 128],
                      ALU.max, waits=(tkt,))
            tv = p.ts(YMX, YMX, 1e-5, None, ALU.max)
            tv = p.act(RCP, YMX, AF.Ln, waits=(tv,))
            tv = p.act(RCP, RCP, AF.Exp, scale=-1.0)
            recrow = SM[0:1, 1032:1544]
            for sb2 in range(4):
                tk = p.trans(PS[0][0:1, 0:128], RCP[:, sb2:sb2 + 1], IDF[:],
                             waits=(tv, p.tok('v')))
                p.copy(recrow[0:1, sb2 * 128:(sb2 + 1) * 128],
                       PS[0][0:1, 0:128], waits=(tk,))
            tb3 = p.mm(PS[1][:], ONE[0:1, :], recrow, start=True, stop=True,
                       waits=(p.tok('v'),))
            y2q = TCb[:, 0:4096]
            for c in range(NCH):
                u = TC[:, 3584:4096]
                tv = p.stt(u, y2[:, c * 512:(c + 1) * 512], 127.0, PS[1][:],
                           ALU.mult, ALU.mult,
                           waits=tuple(t for t in
                                       (tb3, p.tok('s'), p.tok('t'),
                                        fst.get(1)) if t))
                tv = rnd(y2q[:, c * 512:(c + 1) * 512], u, TC[:, 2048:2560],
                         TC[:, 2560:3072])
            wslot = lambda j, w: TBb[:, 16384 + (j % 2) * 8192 + w * 4096:
                                     16384 + (j % 2) * 8192 + (w + 1) * 4096]
            def load_j(j):
                src_g = wgT_sc.rearrange("(c p) f -> p c f", p=128)[:, :, j * 512:(j + 1) * 512]
                src_u = wuT_sc.rearrange("(c p) f -> p c f", p=128)[:, :, j * 512:(j + 1) * 512]
                p.dma(wslot(j, 0), src_g, f"wg_ld{j % 2}", waits=(p.tok('t'),))
                return p.dma(wslot(j, 1), src_u, f"wg_ld{j % 2}")
            wtok = [None] * 8
            wtok[0] = load_j(0)
            wtok[1] = load_j(1)
            for sb in range(4):
                p.ts(SMW[:, 6 + sb:7 + sb], YMX[:, sb:sb + 1], SMW[:, 3:4],
                     None, ALU.mult, waits=(p.tok('s'),) if sb == 0 else ())
                p.ts(SMW[:, 6 + sb:7 + sb], SMW[:, 6 + sb:7 + sb], 1.0 / 127.0,
                     None, ALU.mult)
                p.ts(SMW[:, 10 + sb:11 + sb], YMX[:, sb:sb + 1], SMW[:, 4:5],
                     None, ALU.mult)
                p.ts(SMW[:, 10 + sb:11 + sb], SMW[:, 10 + sb:11 + sb],
                     1.0 / 127.0, None, ALU.mult)
            gdone = {}
            for j in range(8):
                for sb in range(4):
                    pg, pu = PS[2 * (sb % 2)], PS[2 * (sb % 2) + 1]
                    for c in range(NCH):
                        w0 = [wtok[j]]
                        if c == 0:
                            if j == 0:
                                w0 += [p.tok('s'), p.tok('v')]
                            else:
                                w0 += list(gdone.get(sb % 2, ()))
                        p.mm(pg[:], y2q[:, c * 512 + sb * 128: c * 512 + sb * 128 + 128],
                             wslot(j, 0)[:, c * 512:(c + 1) * 512],
                             start=(c == 0), stop=(c == NCH - 1), waits=w0,
                             inc=False)
                        gu_tok = p.mm(pu[:], y2q[:, c * 512 + sb * 128: c * 512 + sb * 128 + 128],
                                      wslot(j, 1)[:, c * 512:(c + 1) * 512],
                                      start=(c == 0), stop=(c == NCH - 1),
                                      inc=(c == NCH - 1))
                    ga = TC[:, 3072 + (sb % 2) * 512:3072 + (sb % 2) * 512 + 512]
                    gaw = (gu_tok,)
                    if j == 0 and sb < 2 and fst.get(sb % 2):
                        gaw = (gu_tok, fst[sb % 2])
                    tsg = p.act(ga, pg[:], AF.Silu, scale=SMW[:, 6 + sb:7 + sb],
                                waits=gaw)
                    th = p.stt(TA[:, sb * 4096 + j * 512:sb * 4096 + (j + 1) * 512],
                               pu[:], SMW[:, 10 + sb:11 + sb], ga,
                               ALU.mult, ALU.mult, waits=(tsg,))
                    gdone[sb % 2] = (tsg, th)
                if j + 2 < 8:
                    wtok[j + 2] = load_j(j + 2)
            dh4 = SM[:, 3226:3230]
            hmc = lambda sb: SM[:, 3238 + sb:3239 + sb]
            r1c = lambda sb: SM[:, 3242 + sb:3243 + sb]
            hmt = [None] * 4
            for sb in range(4):
                H = TA[:, sb * 4096:(sb + 1) * 4096]
                tv = p.red(hmc(sb), H, ALU.max, absval=True)
                tv = p.ts(hmc(sb), hmc(sb), 1e-5, None, ALU.max)
                tv = p.act(hmc(sb), hmc(sb), AF.Ln, waits=(tv,))
                hmt[sb] = p.act(hmc(sb), hmc(sb), AF.Exp, scale=-1.0)
            for sb in range(4):
                H = TA[:, sb * 4096:(sb + 1) * 4096]
                hq = TAb[:, sb * 8192:sb * 8192 + 4096]
                hfm = TAb[:, sb * 8192 + 4096:(sb + 1) * 8192]
                r127 = r1c(sb)
                tv = p.ts(r127, hmc(sb), 127.0, None, ALU.mult,
                          waits=(hmt[sb],))
                p.op('v', 'reciprocal', dh4[:, sb:sb + 1], hmc(sb))
                for k in range(4):
                    u = (TC[:, 4096:5120], TB[:, 4096:5120])[sb % 2]
                    s1 = (TC[:, 5120:6144], TB[:, 5120:6144])[sb % 2]
                    s2 = (TB[:, 6144:7168], TB[:, 7168:8192])[sb % 2]
                    tv = p.ts(u, H[:, k * 1024:(k + 1) * 1024], r127, None,
                              ALU.mult,
                              waits=(p.tok('s'),) if sb == 0 and k == 0 else ())
                    tv = rnd(hq[:, k * 1024:(k + 1) * 1024], u, s1, s2)
                for ic in range(32):
                    bkb = PS[4 + ic % 2][:].bitcast(BF16)
                    tk = p.trans(bkb[:, 0:128],
                                 hq[:, ic * 128:(ic + 1) * 128],
                                 IDB[:], waits=(tv, p.tok('v')))
                    p.copy(hfm[:, ic * 128:(ic + 1) * 128],
                           bkb[:, 0:128], waits=(tk,))
            dhrow = SM[0:1, 3584:4096]
            for sb in range(4):
                tk = p.trans(PS[6][0:1, 0:128], dh4[:, sb:sb + 1], IDF[:],
                             waits=(p.tok('v'),))
                p.copy(dhrow[0:1, sb * 128:(sb + 1) * 128], PS[6][0:1, 0:128],
                       waits=(tk,))
            tbd = p.mm(PS[6][:], ONE[0:1, :], dhrow, start=True, stop=True,
                       waits=(p.tok('v'),))
            dht = SM[:, 1544:2056]
            tv = p.copy(dht, PS[6][:], waits=(tbd,))
            hfm_v = TAb[:].rearrange("p (s r) -> p s r", s=4)
            def load_oc(ocn):
                srcd = wdT_sc.rearrange("(ic p) f -> p ic f", p=128)[:, :, ocn * 128:(ocn + 1) * 128]
                return p.dma(WBb[:, (ocn % 2) * 4096:(ocn % 2) * 4096 + 4096],
                             srcd, f"wd_ld{ocn % 2}", waits=(p.tok('t'),))
            dtoks = {0: load_oc(0), 1: load_oc(1)}
            ddone = {}
            for ocn in range(NCH):
                bank = PS[2 + ocn % 2]
                for ic in range(32):
                    w0 = [dtoks[ocn]]
                    if ic == 0:
                        if ocn < 2:
                            w0 += [p.tok('v'), p.tok('s')]
                        else:
                            w0.append(ddone[ocn % 2])
                    dn = p.mm(bank[:], WBb[:, (ocn % 2) * 4096 + ic * 128:
                                           (ocn % 2) * 4096 + (ic + 1) * 128],
                              hfm_v[:, :, 4096 + ic * 128:4096 + (ic + 1) * 128],
                              start=(ic == 0), stop=(ic == 31), waits=w0,
                              inc=(ic == 31))
                if ocn + 2 < NCH:
                    dtoks[ocn + 2] = load_oc(ocn + 2)
                xn = TC[:, 3072 + (ocn % 2) * 512:3072 + (ocn % 2) * 512 + 512]
                tv = p.tt(xn, bank[:], dht, ALU.mult,
                          waits=tuple(t for t in
                                      (dn, fst.get(ocn % 2), p.tok('s')) if t))
                ddone[ocn % 2] = tv
                tv = p.ts(xn, xn, SMW[:, 5:6], 1.0 / 127.0, ALU.mult, ALU.mult)
                tv = p.tt(xn, xn, xt[:, ocn * 512:(ocn + 1) * 512], ALU.add)
                fst[ocn % 2] = p.dma(x_dst[ocn * 128:(ocn + 1) * 128,
                                           t0:t0 + 512], xn, "fxn_st",
                                     waits=(tv,))
                smt = stats_acc(xn, STG[:, 0:512], PS[7], PS[6], ocn == 0,
                                ocn == NCH - 1, 512, tv)
            stats_fin(t0, 512, PS[7], PS[6], smt)
        p.barrier()


    PPT = nc.alloc_sbuf_tensor("PPT", [128, 128], F32)
    TWO_PI_HI = float(np.float32(2.0 * math.pi))
    TWO_PI_LO = float(np.float64(2.0 * math.pi) - np.float64(np.float32(2.0 * math.pi)))
    INV_2PI = float(1.0 / (2.0 * math.pi))
    HALF_PI = float(np.float32(0.5 * math.pi))

    def l1_prep():
        ROW = SM[0:1, 0:512]
        ROW2 = SM[0:1, 512:1024]
        grow = SM[0:1, 1024:1120]
        ld = p.dma(grow, gates_d.rearrange("a b -> (a b)").unsqueeze(0), "l1c",
                   waits=(p.tok('t'), p.tok('v'), p.tok('s')))
        for d in range(3):
            th = grow[0:1, 32 * d:32 * d + 28].rearrange("o (i f) -> o i f", f=4)[:, :, 0]
            ph = grow[0:1, 32 * d + 1:32 * d + 29].rearrange("o (i f) -> o i f", f=4)[:, :, 0]
            c_ = ROW[0:1, 0:7]
            s_ = ROW[0:1, 8:15]
            cp = ROW[0:1, 16:23]
            sp = ROW[0:1, 24:31]
            t1 = p.act(c_, th, AF.Sin, bias=HALF_PI, scale=0.5,
                       waits=(ld, p.tok('v')))
            t1 = p.act(s_, th, AF.Sin, scale=0.5)
            t1 = p.act(cp, ph, AF.Sin, bias=HALF_PI)
            t1 = p.act(sp, ph, AF.Sin)
            scp = ROW[0:1, 32:39]
            tv = p.tt(scp, s_, cp, ALU.mult, waits=(t1,))
            tv = p.ts(ROW[0:1, 40:47], c_, -1.0, None, ALU.add)
            tv = p.tt(ROW2[0:1, d * 7:d * 7 + 7], ROW[0:1, 40:47], scp, ALU.add)
            tv = p.tt(ROW2[0:1, 42 + d * 7:42 + d * 7 + 7], ROW[0:1, 40:47], scp,
                      ALU.subtract)
            tv = p.tt(ROW2[0:1, 21 + d * 7:21 + d * 7 + 7], s_, sp, ALU.mult)
        srow = SM[0:1, 1120:1124]
        ld = p.dma(srow, sup_d, "l1c", waits=(p.tok('s'), p.tok('v')))
        t1 = p.act(ROW[0:1, 0:4], srow, AF.Exp, waits=(ld, p.tok('v')))
        tv = p.red(ROW[0:1, 8:9], ROW[0:1, 0:4], ALU.add, waits=(t1,))
        tv = p.op('v', 'reciprocal', ROW[0:1, 9:10], ROW[0:1, 8:9])
        tv = p.tt(ROW2[0:1, 63:67], ROW[0:1, 0:4],
                  ROW[0:1, 9:10].broadcast_to((1, 4)), ALU.mult)
        qrow = SM[0:1, 1152:1408]
        ld = p.dma(qrow, qw_d.rearrange("a b -> (a b)").unsqueeze(0), "l1c",
                   waits=(p.tok('s'),))
        q00 = qrow.rearrange("o (c f) -> o c f", f=64)[:, :, 0]
        t1 = p.act(ROW[0:1, 0:4], q00, AF.Sin, waits=(ld, p.tok('v')))
        tv = p.ts(ROW[0:1, 0:4], ROW[0:1, 0:4], 0.3, 0.5, ALU.mult, ALU.add,
                  waits=(t1,))
        tv = p.ts(ROW[0:1, 8:12], ROW[0:1, 0:4], -1.0, 1.0, ALU.mult, ALU.add)
        tv = p.tt(ROW2[0:1, 67:71], ROW[0:1, 0:4], ROW[0:1, 8:12], ALU.mult)
        tv = p.ts(ROW2[0:1, 67:71], ROW2[0:1, 67:71], SCALING1, None, ALU.mult)
        tbx = p.mm(PS[5][:, 0:71], ONE[0:1, :], ROW2[0:1, 0:71], start=True,
                   stop=True, waits=(p.tok('v'),))
        tb = p.copy(PPT[:, 0:71], PS[5][:, 0:71], waits=(tbx,))
        PHI = SM[:, 1408:1409]
        tmem = p.memset(PHI, 0.0)
        p.wait('y', tmem)
        for r in range(16):
            srcr = phot_d[r:r + 1, :].rearrange("o (q s2) -> o q s2", s2=16)[:, 0:8, r]
            p.dma(PHI[r::16, 0:1], srcr.rearrange("o q -> (o q)").unsqueeze(1),
                  "l1c", slow=True, waits=(p.tok('v'), p.tok('g')))
        phtok = p.dtok("l1c")
        T1 = SM[:, 1409:1410]
        T2 = SM[:, 1410:1411]
        def rr_sin(dst, badd):
            tv = p.ts(T1, PHI, 1.0, badd, ALU.mult, ALU.add,
                      waits=(phtok, p.tok('s')))
            tv = p.ts(T2, T1, INV_2PI, None, ALU.mult)
            tv = rnd(T2, T2, SM[:, 1411:1412], SM[:, 1412:1413])
            tv = p.stt(SM[:, 1411:1412], T2, -TWO_PI_HI, T1, ALU.mult, ALU.add)
            tv = p.stt(SM[:, 1411:1412], T2, -TWO_PI_LO, SM[:, 1411:1412],
                       ALU.mult, ALU.add)
            return p.act(dst, SM[:, 1411:1412], AF.Sin, waits=(tv,))
        rr_sin(PPT[:, 73:74], 0.0)
        rr_sin(PPT[:, 72:73], HALF_PI)
        tv = p.tt(T1, PPT[:, 72:73], PPT[:, 72:73], ALU.mult, waits=(p.tok('s'),))
        tv = p.tt(T2, PPT[:, 73:74], PPT[:, 73:74], ALU.mult)
        tv = p.tt(PPT[:, 74:75], T1, T2, ALU.subtract)
        brow = SM[0:1, 0:1024]
        ld = p.dma(brow, bout_d, "l1c", waits=(p.tok('t'), p.tok('v'), p.tok('s')))
        for ocn in range(8):
            tk = p.trans(PS[4][:, 0:1], brow[0:1, ocn * 128:(ocn + 1) * 128],
                         IDF[0:1, 0:1], waits=(ld, p.tok('v')))
            p.copy(PPT[:, 80 + ocn:81 + ocn], PS[4][:, 0:1], waits=(tk,))
        brow2 = SM[0:1, 1536:2048]
        for half in range(6):
            ld = p.dma(brow2, bqkv_d[0:1, half * 512:(half + 1) * 512], "l1c",
                       waits=(p.tok('t'),))
            for j in range(4):
                ocn = half * 4 + j
                tk = p.trans(PS[4][:, 0:1], brow2[0:1, j * 128:(j + 1) * 128],
                             IDF[0:1, 0:1], waits=(ld, p.tok('v')))
                p.copy(PPT[:, 88 + ocn:89 + ocn], PS[4][:, 0:1], waits=(tk,))
        for (wsrcd, dstf, ntile) in ((wout_d, woutT_f, 8),
                                     (wqkv_d, wqkvT_f, 24)):
            for i in range(ntile):
                wt = STG[:, 0:1024]
                ld = p.dma(wt, wsrcd[i * 128:(i + 1) * 128, :], "w_ld",
                           waits=(p.tok('v'), p.tok('t')))
                for b in range(8):
                    tk = p.trans(PS[4][:, 0:128], wt[:, b * 128:(b + 1) * 128],
                                 IDF[:], waits=(ld, p.tok('v')))
                    stf = STG[:, 1280 + (b % 2) * 128:1280 + (b % 2) * 128 + 128]
                    cv = p.copy(stf, PS[4][:, 0:128],
                                waits=(tk, p.dtok("wT_st")))
                    blk = (i * 8 + b) * 128
                    p.dma(dstf[blk:blk + 128, :], stf, "wT_st", waits=(cv,))
        p.barrier()

    def l1_attn(x_src, x_dst):
        l1_prep()
        PT = 256
        QR = TA[:, 0:2048]
        QI = TA[:, 2048:4096]
        KR = TA[:, 4096:6144]
        KI = TA[:, 6144:8192]
        NV = TA[:, 8192:10240]
        E = [TA[:, 13824 + k * 256:13824 + (k + 1) * 256] for k in range(6)]
        QRI = TB[:, 0:2048]
        QII = TB[:, 2048:4096]
        QIN = TB[:, 4096:6144]
        KRI = TB[:, 6144:8192]
        KII = TB[:, 8192:10240]
        NVT = TB[:, 10240:12288]
        G = TB[:, 12288:14336]
        SD = TB[:, 14336:14592]
        CT = TB[:, 14592:14848]
        QKVT = TB[:, 0:6144]
        sa = TC[:, 0:512]
        sb_ = TC[:, 512:1024]
        SQA = [TC[:, 0:512], TC[:, 1024:1536]]
        SQB = [TC[:, 512:1024], TC[:, 1536:2048]]
        SUS = [TC[:, 2112 + 256 * i:2368 + 256 * i] for i in range(4)]
        M3 = TC[:, 4160:6208]
        xold = STG[:, 2048:2304]
        def emit_qkv(t0):
            ldm = p.dma(SM[0:1, 2048:2048 + PT], st_nm[0:1, t0:t0 + PT], "fm_ld",
                        waits=(p.tok('t'),))
            ldr = p.dma(SM[0:1, 2304:2304 + PT], st_r[0:1, t0:t0 + PT], "fm_ld")
            tb1 = p.mm(PS[6][:, 0:PT], ONE[0:1, :], SM[0:1, 2048:2048 + PT],
                       start=True, stop=True,
                       waits=(p.dtok("fm_ld"), p.tok('v'), p.tok('s')))
            tb2 = p.mm(PS[7][:, 0:PT], ONE[0:1, :], SM[0:1, 2304:2304 + PT],
                       start=True, stop=True)
            nmb = SM[:, 2560:2560 + PT]
            rbb = SM[:, 2816:2816 + PT]
            tv = p.copy(nmb, PS[6][:, 0:PT], waits=(tb1,))
            tv = p.copy(rbb, PS[7][:, 0:PT], waits=(tb2,))
            XN = TC[:, 0:2048]
            for c in range(NCH):
                xc = STG[:, (c % 2) * PT:(c % 2) * PT + PT]
                ldc = p.dma(xc, x_src[c * 128:(c + 1) * 128, t0:t0 + PT],
                            "xq2_ld", waits=(p.tok('v'), p.tok('t')))
                tv = p.tt(XN[:, c * PT:(c + 1) * PT], xc, nmb, ALU.add,
                          waits=(ldc,))
                tv = p.tt(XN[:, c * PT:(c + 1) * PT],
                          XN[:, c * PT:(c + 1) * PT], rbb, ALU.mult)
            def load_qkv_oc(ocn, slot):
                srcf = wqkvT_f[ocn * 1024:(ocn + 1) * 1024, :].rearrange(
                    "(c p) f -> p c f", p=128)
                return p.dma(WB[:, slot * 1024:slot * 1024 + 1024], srcf,
                             f"wv_ld{slot}", waits=(p.tok('t'),))
            qtoks = {0: load_qkv_oc(0, 0), 1: load_qkv_oc(1, 1)}
            qdone = {}
            for ocn in range(24):
                bank = PS[4 + ocn % 2]
                slot = ocn % 2
                for c in range(NCH):
                    wh = WB[:, slot * 1024 + c * 128:slot * 1024 + (c + 1) * 128]
                    w0 = [qtoks[ocn]]
                    if c == 0:
                        if ocn < 2:
                            w0 += [p.tok('v'), p.tok('s')]
                        else:
                            w0.append(qdone[slot])
                    mmt = p.mm(bank[:, 0:PT], wh, XN[:, c * PT:(c + 1) * PT],
                               start=(c == 0), stop=(c == NCH - 1), waits=w0,
                               inc=(c == NCH - 1))
                if ocn + 2 < 24:
                    qtoks[ocn + 2] = load_qkv_oc(ocn + 2, (ocn + 2) % 2)
                qdone[slot] = p.act(QKVT[:, ocn * PT:(ocn + 1) * PT],
                                    bank[:, 0:PT], AF.Identity,
                                    bias=PPT[:, 88 + ocn:89 + ocn],
                                    waits=(mmt,))

        emit_qkv(0)
        qst = {}
        for itq in range(S // PT):
            t0 = itq * PT
            sqq = TA[:, 10240:14336]
            sqv = TA[:, 14336:16384]
            tq1 = p.act(sqq, QKVT[:, 0:4096], AF.Square, waits=(p.tok('v'),))
            tq2 = p.act(sqv, QKVT[:, 4096:6144], AF.Square)
            nrow = lambda k: SM[0:1, (k % 4) * 512:(k % 4) * 512 + 512]
            rbanks = [PS[4], PS[5], PS[6], PS[7]]
            bbanks = [PS[0], PS[1], PS[2], PS[3]]
            rts = [None] * 12
            ats = [None] * 12
            tbs = [None] * 12
            for k in range(12):
                sq_src = (sqq[:, k * 512:(k + 1) * 512] if k < 8 else
                          sqv[:, (k - 8) * 512:(k - 7) * 512])
                rbank = rbanks[k % 4]
                wm = [tq1 if k < 8 else tq2]
                if k >= 4:
                    wm.append(rts[k - 4])
                tm = p.mm(rbank[0:1, 0:512], ONE[:, 0:1], sq_src, start=True,
                          stop=True, waits=tuple(wm))
                rw = nrow(k)
                wr = [tm]
                if k >= 4:
                    wr.append(tbs[k - 4])
                rts[k] = tv = p.ts(rw, rbank[0:1, 0:512], 1e-24, None, ALU.max,
                                   waits=tuple(wr))
                tv = p.act(rw, rw, AF.Ln, waits=(tv,))
                tv = p.act(rw, rw, AF.Exp, scale=-0.5)
                bbank = bbanks[k % 4]
                wb = [tv]
                if k >= 4:
                    wb.append(ats[k - 4])
                tbs[k] = tb = p.mm(bbank[:], ONE[0:1, :], rw, start=True,
                                   stop=True, waits=tuple(wb))
                if k < 4:
                    p.stt(QR[:, k * 512:(k + 1) * 512],
                          QKVT[:, k * 512:(k + 1) * 512],
                          PPT[:, 72:73], bbank[:], ALU.mult, ALU.mult,
                          waits=(tb,))
                    ats[k] = p.stt(QI[:, k * 512:(k + 1) * 512],
                                   QKVT[:, k * 512:(k + 1) * 512],
                                   PPT[:, 73:74], bbank[:], ALU.mult, ALU.mult)
                elif k < 8:
                    kk = k - 4
                    p.stt(KR[:, kk * 512:(kk + 1) * 512],
                          QKVT[:, k * 512:(k + 1) * 512],
                          PPT[:, 72:73], bbank[:], ALU.mult, ALU.mult,
                          waits=(tb,))
                    ats[k] = p.stt(KI[:, kk * 512:(kk + 1) * 512],
                                   QKVT[:, k * 512:(k + 1) * 512],
                                   PPT[:, 73:74], bbank[:], ALU.mult, ALU.mult)
                else:
                    for hh in range(2):
                        h = (k - 8) * 2 + hh
                        nvs = NV[:].rearrange("p (t i) -> p t i", i=8)[:, :, h]
                        ats[k] = p.tt(nvs, QKVT[:, (16 + h) * PT:(17 + h) * PT],
                                      bbank[:, hh * PT:(hh + 1) * PT], ALU.mult,
                                      waits=(tb,) if hh == 0 else ())
            ESQ = TA[:, 10240:13824]
            for (R, I, Rd, Id) in ((QR, QI, QRI, QII), (KR, KI, KRI, KII)):
                ut = [None] * 8
                for d in range(3):
                    stoks = []
                    for i in range(7):
                        p.act(ESQ[:, (2 * i) * PT:(2 * i + 1) * PT],
                              R[:, i * PT:(i + 1) * PT], AF.Square,
                              waits=(p.tok('v'), p.tok('t')) if i == 0 else ())
                        stoks.append(p.act(ESQ[:, (2 * i + 1) * PT:(2 * i + 2) * PT],
                                           I[:, i * PT:(i + 1) * PT], AF.Square))
                    nut = [None] * 8
                    e5u = [None] * 7
                    e5v = [None] * 7
                    ta_prev = ()
                    for i in range(7):
                        if d == 2:
                            u1 = Rd[:].rearrange("p (t j) -> p t j", j=8)[:, :, i]
                            v1 = Id[:].rearrange("p (t j) -> p t j", j=8)[:, :, i]
                        else:
                            u1 = R[:, i * PT:(i + 1) * PT]
                            v1 = I[:, i * PT:(i + 1) * PT]
                        u2 = R[:, (i + 1) * PT:(i + 2) * PT]
                        v2 = I[:, (i + 1) * PT:(i + 2) * PT]
                        Ap = PPT[:, d * 7 + i:d * 7 + i + 1]
                        Bp = PPT[:, 21 + d * 7 + i:21 + d * 7 + i + 1]
                        Cp = PPT[:, 42 + d * 7 + i:42 + d * 7 + i + 1]
                        E3b = E[3] if i % 2 == 0 else E[0]
                        E4b = E[4] if i % 2 == 0 else E[1]
                        w3 = tuple(t for t in (ut[i + 1],
                                               e5u[i - 2] if i >= 2 else None)
                                   if t)
                        w4 = tuple(t for t in (ut[i + 1],
                                               e5v[i - 2] if i >= 2 else None)
                                   if t)
                        ta3 = p.act(E3b, u2, AF.Copy, scale=Ap, waits=w3)
                        ta4 = p.act(E4b, u2, AF.Copy, scale=Bp, waits=w4)
                        tc_ = p.tt(E[2], ESQ[:, (2 * i) * PT:(2 * i + 1) * PT],
                                   ESQ[:, (2 * i + 1) * PT:(2 * i + 2) * PT],
                                   ALU.add,
                                   waits=(stoks[i], p.tok('t')) if i == 0 else
                                         (stoks[i],))
                        tv = p.stt(E3b, v2, Bp, E3b, ALU.mult, ALU.add,
                                   waits=(ta3,))
                        tv = p.stt(E4b, v2, Cp, E4b, ALU.mult, ALU.add,
                                   waits=(ta4,))
                        e5u[i] = p.tt(E[5], E[2], E3b, ALU.mult)
                        nut[i] = p.tt(u1, u2, E[5], ALU.add, waits=ta_prev)
                        e5v[i] = p.tt(E[5], E[2], E4b, ALU.mult)
                        tv = p.tt(v1, v2, E[5], ALU.add)
                        ta_prev = (ta3, ta4)
                    ut = nut
            for (srcT, dstT) in ((QR, QRI), (QI, QII), (KR, KRI), (KI, KII)):
                p.copy(dstT[:].rearrange("p (t j) -> p t j", j=8)[:, :, 7],
                       srcT[:, 7 * PT:8 * PT])
            p.ts(QIN[:], QII[:], -1.0, None, ALU.mult)
            for g in range(PT // 16):
                tk = p.trans(PS[2][:, 0:128], NV[:, g * 128:(g + 1) * 128], IDF[:],
                             waits=(p.tok('v'), p.tok('s')))
                p.act(NVT[:, g * 128:(g + 1) * 128], PS[2][:, 0:128], AF.Copy,
                      waits=(tk,))
            tv = p.red(SD, NV[:].rearrange("p (t i) -> p t i", i=8), ALU.add)
            tsg = p.act(CT, SD, AF.Sign, waits=(tv,))
            tv = p.ts(CT, CT, PPT[:, 74:75], None, ALU.mult, waits=(tsg,))
            S0all = TA[:, 0:2048]
            WT4all = TA[:, 2048:4096]
            ET2 = [TA[:, 4096:6144], TA[:, 6144:8192]]
            ts0 = None
            for gb in range(PT // 64):
                g0 = gb * 4
                pr_t = pi_t = None
                for k in range(4):
                    gs = slice((g0 + k) * 128, (g0 + k + 1) * 128)
                    ks = slice(k * 128, (k + 1) * 128)
                    w_ = [p.tok('s'), p.tok('v')] if k == 0 else []
                    p.mm(PS[0][:, ks], QRI[:, gs], KRI[:, gs], start=True,
                         stop=False, waits=w_, inc=False)
                    pr_t = p.mm(PS[0][:, ks], QII[:, gs], KII[:, gs], start=False,
                                stop=True, inc=(k == 3))
                    p.mm(PS[1][:, ks], QRI[:, gs], KII[:, gs], start=True,
                         stop=False, inc=False)
                    pi_t = p.mm(PS[1][:, ks], QIN[:, gs], KRI[:, gs], start=False,
                                stop=True, inc=(k == 3))
                t1 = p.act(SQA[gb % 2], PS[0][:], AF.Square, waits=(pr_t,))
                t2 = p.act(SQB[gb % 2], PS[1][:], AF.Square, waits=(pi_t,))
                ts0 = p.tt(S0all[:, gb * 512:(gb + 1) * 512], SQA[gb % 2],
                           SQB[gb % 2], ALU.add, waits=(t2,))
            if t0 + PT < S:
                emit_qkv(t0 + PT)
            for c in range(4):
                ET = ET2[c % 2]
                tew = [ts0]
                if c >= 2:
                    tew += [p.tok('v'), p.tok('t')]
                te = p.act(ET, S0all, AF.Exp, scale=PPT[:, 67 + c:68 + c],
                           waits=tuple(tew))
                tv = p.red(SUS[c], ET.rearrange("p (a b) -> p a b", b=8),
                           ALU.add, waits=(te,))
                tv = p.act(SUS[c], SUS[c], AF.Ln, waits=(tv,))
                tv = p.act(SUS[c], SUS[c], AF.Exp, scale=-1.0)
                tv = p.tt(ET.rearrange("p (a b) -> p a b", b=8),
                          ET.rearrange("p (a b) -> p a b", b=8),
                          SUS[c].unsqueeze(2).broadcast_to((128, PT, 8)),
                          ALU.mult, waits=(tv,))
                tv = p.act(ET, ET, AF.Ln, bias=1e-12, waits=(tv,))
                tv = p.act(ET, ET, AF.Exp, scale=0.5)
                tv = p.tt(ET.rearrange("p (a b) -> p a b", b=128),
                          ET.rearrange("p (a b) -> p a b", b=128),
                          MSK8[:].unsqueeze(1).broadcast_to((128, 16, 128)),
                          ALU.mult, waits=(tv,))
                tbanks = [PS[0], PS[1], PS[2], PS[6]]
                tkall = None
                for gb in range(4):
                    for k in range(4):
                        ks = slice(k * 128, (k + 1) * 128)
                        cs = slice(gb * 512 + k * 128, gb * 512 + (k + 1) * 128)
                        tkall = p.trans(tbanks[gb][:, ks], ET[:, cs], IDF[:],
                                        waits=(tv, p.tok('v'), p.tok('s'))
                                        if (gb == 0 and k == 0) else (tv,),
                                        inc=(gb == 3 and k == 3))
                tws = {}
                for gb in range(4):
                    for k in range(4):
                        ks = slice(k * 128, (k + 1) * 128)
                        cs = slice(gb * 512 + k * 128, gb * 512 + (k + 1) * 128)
                        tws[(gb, k)] = p.act(WT4all[:, cs], tbanks[gb][:, ks],
                                             AF.Copy, waits=(tkall,))
                for gb in range(4):
                    g0 = gb * 4
                    wbank = PS[3] if gb % 2 == 0 else PS[7]
                    mt = None
                    for k in range(4):
                        ks = slice(k * 128, (k + 1) * 128)
                        cs = slice(gb * 512 + k * 128, gb * 512 + (k + 1) * 128)
                        gs = slice((g0 + k) * 128, (g0 + k + 1) * 128)
                        mt = p.mm(wbank[:, ks], NVT[:, gs], WT4all[:, cs],
                                  start=True, stop=True,
                                  waits=(tws[(gb, k)], p.tok('s')),
                                  inc=(k == 3))
                    tsg2 = p.act(sa, wbank[:], AF.Sign, waits=(mt, p.tok('v')))
                    tab = p.act(sb_, wbank[:], AF.Abs)
                    tv2 = p.ts(sb_, sb_, 1.0, 1e-6, ALU.min, ALU.max,
                               waits=(tab,))
                    tv2 = p.tt(sb_, sb_, sa, ALU.mult)
                    gsl = slice(gb * 512, (gb + 1) * 512)
                    if c == 0:
                        p.ts(G[:, gsl], sb_, PPT[:, 63:64], None, ALU.mult)
                    else:
                        p.stt(G[:, gsl], sb_, PPT[:, 63 + c:64 + c], G[:, gsl],
                              ALU.mult, ALU.add)
            msp = p.tt(M3.rearrange("p (t i) -> p t i", i=8),
                       G[:].rearrange("p (t i) -> p t i", i=8),
                       CT.unsqueeze(2).broadcast_to((128, PT, 8)), ALU.mult,
                       waits=(p.tok('t'),))
            def load_out_oc(ocn, slot):
                srcf = woutT_f[ocn * 1024:(ocn + 1) * 1024, :].rearrange(
                    "(c p) f -> p c f", p=128)
                return p.dma(WB[:, slot * 1024:slot * 1024 + 1024], srcf,
                             f"wv_ld{slot}", waits=(p.tok('t'),))
            otoks = {0: load_out_oc(0, 0), 1: load_out_oc(1, 1)}
            odone = {}
            for ocn in range(NCH):
                ldx = p.dma(xold, x_src[ocn * 128:(ocn + 1) * 128, t0:t0 + PT],
                            "xq_ld", waits=(p.tok('v'),))
                bank = PS[4 + ocn % 2]
                slot = ocn % 2
                for ic in range(NCH):
                    wh = WB[:, slot * 1024 + ic * 128:slot * 1024 + (ic + 1) * 128]
                    m3s = M3.rearrange("p (t i) -> p t i", i=8)[:, :, ic]
                    w0 = [otoks[ocn], msp]
                    if ic == 0:
                        if ocn < 2:
                            w0 += [p.tok('v'), p.tok('s')]
                        else:
                            w0.append(odone[slot])
                    ot = p.mm(bank[:, 0:PT], wh, m3s, start=(ic == 0),
                              stop=(ic == NCH - 1), waits=w0,
                              inc=(ic == NCH - 1))
                if ocn + 2 < NCH:
                    otoks[ocn + 2] = load_out_oc(ocn + 2, (ocn + 2) % 2)
                xn = STG[:, 1024 + (ocn % 2) * 256:1024 + (ocn % 2) * 256 + 256]
                tv = p.stt(xn, xold, PPT[:, 80 + ocn:81 + ocn], bank[:, 0:PT],
                           ALU.add, ALU.add,
                           waits=tuple(t for t in
                                       (ot, ldx, qst.get(ocn % 2), p.tok('s'))
                                       if t))
                odone[ocn % 2] = tv
                qst[ocn % 2] = p.dma(x_dst[ocn * 128:(ocn + 1) * 128,
                                           t0:t0 + PT], xn, "xq_st",
                                     waits=(tv,))
                smt = stats_acc(xn, STG[:, 512:768], PS[0], PS[1], ocn == 0,
                                ocn == NCH - 1, PT, tv)
            stats_fin(t0, PT, PS[0], PS[1], smt)
        p.barrier()

    def l1_ffn(x_src, x_dst):
        RQ = TAb[:, 0:8192]
        OPP = SM[:, 3230:3238]
        RCPP = SM[:, 3238:3246]
        tmp1 = SM[:, 3246:3247]
        for c in range(NCH):
            wt = STG[:, 0:1024]
            ld = p.dma(wt, rhs1_d[c * 128:(c + 1) * 128, :], "w_ld",
                       waits=(p.tok('v'), p.tok('t')))
            for b in range(8):
                tk = p.trans(PS[0][:, 0:128], wt[:, b * 128:(b + 1) * 128],
                             IDF[:], waits=(ld, p.tok('v')))
                tr = p.red(tmp1, PS[0][:, 0:128], ALU.max, absval=True,
                           waits=(tk,))
                if c == 0:
                    p.copy(OPP[:, b:b + 1], tmp1)
                else:
                    p.tt(OPP[:, b:b + 1], OPP[:, b:b + 1], tmp1, ALU.max)
        tv = p.ts(OPP, OPP, 1e-12, None, ALU.max)
        tv = p.act(RCPP, OPP, AF.Ln, waits=(tv,))
        tv = p.act(RCPP, RCPP, AF.Exp, scale=-1.0)
        rrow = SM[0:1, 1032:2056]
        for b in range(8):
            tk = p.trans(PS[0][0:1, 0:128], RCPP[:, b:b + 1], IDF[:],
                         waits=(tv, p.tok('v')))
            p.copy(rrow[0:1, b * 128:(b + 1) * 128], PS[0][0:1, 0:128],
                   waits=(tk,))
        tb1 = p.mm(PS[1][:], ONE[0:1, :], rrow[0:1, 0:512], start=True,
                   stop=True, waits=(p.tok('v'),))
        tb2 = p.mm(PS[2][:], ONE[0:1, :], rrow[0:1, 512:1024], start=True,
                   stop=True)
        for c in range(NCH):
            wt = STG[:, 0:1024]
            ld = p.dma(wt, rhs1_d[c * 128:(c + 1) * 128, :], "w_ld",
                       waits=(p.tok('v'), p.tok('t')))
            for half in range(2):
                u = TC[:, 2048:2560]
                bcast = PS[1] if half == 0 else PS[2]
                tv = p.stt(u, wt[:, half * 512:(half + 1) * 512], 127.0,
                           bcast[:], ALU.mult, ALU.mult, waits=(ld, tb2))
                tv = rnd(RQ[:, c * 1024 + half * 512:c * 1024 + (half + 1) * 512],
                         u, TC[:, 3072:3584], TC[:, 3584:4096])
        RSC = SM[:, 1024:1032]
        tv = p.ts(RSC, OPP, 1.0 / 127.0, None, ALU.mult)
        p.barrier()
        for it in range(NT):
            t0 = it * 512
            xt = TB[:, 0:4096]
            ldx = None
            for c in range(NCH):
                ldx = p.dma(xt[:, c * 512:(c + 1) * 512],
                            x_src[c * 128:(c + 1) * 128, t0:t0 + 512], "fx_ld",
                            waits=(p.tok('v'), p.tok('t')))
            ldm = p.dma(SM[0:1, 1032:1544], st_m[0:1, t0:t0 + 512], "fm_ld",
                        waits=(p.tok('t'),))
            ldr = p.dma(SM[0:1, 1544:2056], st_r[0:1, t0:t0 + 512], "fm_ld")
            tb1 = p.mm(PS[5][:], ONE[0:1, :], SM[0:1, 1032:1544], start=True,
                       stop=True, waits=(p.dtok("fm_ld"), p.tok('v')))
            tb2 = p.mm(PS[6][:], ONE[0:1, :], SM[0:1, 1544:2056], start=True,
                       stop=True)
            mb = TC[:, 2048:2560]
            rb = TC[:, 2560:3072]
            tv = p.copy(mb, PS[5][:], waits=(tb1,))
            tv = p.copy(rb, PS[6][:], waits=(tb2,))
            y4 = TB[:, 4096:8192]
            YMX = SM[:, 3230:3234]
            RCP4 = SM[:, 3234:3238]
            tmp2 = SM[:, 3246:3247]
            for c in range(NCH):
                y4c = y4[:, c * 512:(c + 1) * 512]
                tv = p.tt(y4c, xt[:, c * 512:(c + 1) * 512], mb, ALU.subtract,
                          waits=(ldx, p.dtok("epi_st")))
                tv = p.tt(y4c, y4c, rb, ALU.mult)
            MX4 = TC[:, 4096:4608]
            MN4 = TC[:, 4608:5120]
            p.tt(MX4, y4[:, 0:512], y4[:, 512:1024], ALU.max)
            p.tt(MN4, y4[:, 0:512], y4[:, 512:1024], ALU.min)
            for c in range(2, NCH):
                p.tt(MX4, MX4, y4[:, c * 512:(c + 1) * 512], ALU.max)
                p.tt(MN4, MN4, y4[:, c * 512:(c + 1) * 512], ALU.min)
            tn4 = p.ts(MN4, MN4, -1.0, None, ALU.mult)
            tn4 = p.tt(MX4, MX4, MN4, ALU.max)
            tk4 = None
            for sb2 in range(4):
                tk4 = p.trans(PS[0][:, sb2 * 128:(sb2 + 1) * 128],
                              MX4[:, sb2 * 128:(sb2 + 1) * 128], IDF[:],
                              waits=(tn4, p.tok('v'), p.tok('s'))
                              if sb2 == 0 else (tn4,), inc=(sb2 == 3))
            for sb2 in range(4):
                p.red(YMX[:, sb2:sb2 + 1], PS[0][:, sb2 * 128:(sb2 + 1) * 128],
                      ALU.max, waits=(tk4,))
            tv = p.ts(YMX, YMX, 1e-12, None, ALU.max)
            tv = p.act(RCP4, YMX, AF.Ln, waits=(tv,))
            tv = p.act(RCP4, RCP4, AF.Exp, scale=-1.0)
            recrow = SM[0:1, 2560:3072]
            lsrow = SM[0:1, 0:512]
            for sb2 in range(4):
                tk = p.trans(PS[0][0:1, 0:128], RCP4[:, sb2:sb2 + 1], IDF[:],
                             waits=(tv, p.tok('v')))
                p.copy(recrow[0:1, sb2 * 128:(sb2 + 1) * 128],
                       PS[0][0:1, 0:128], waits=(tk,))
                tk = p.trans(PS[0][0:1, 0:128], YMX[:, sb2:sb2 + 1], IDF[:],
                             waits=(p.tok('v'),))
                p.copy(lsrow[0:1, sb2 * 128:(sb2 + 1) * 128],
                       PS[0][0:1, 0:128], waits=(tk,))
            tv = p.ts(lsrow, lsrow, 1.0 / 127.0, None, ALU.mult)
            tb3 = p.mm(PS[1][:], ONE[0:1, :], recrow, start=True, stop=True,
                       waits=(p.tok('v'),))
            tb4 = p.mm(PS[5][:], ONE[0:1, :], lsrow, start=True, stop=True)
            lsb = TC[:, 2560:3072]
            tv = p.copy(lsb, PS[5][:], waits=(tb4,))
            y4q = TCb[:, 0:4096]
            for c in range(NCH):
                u = TC[:, 3584:4096]
                tv = p.stt(u, y4[:, c * 512:(c + 1) * 512], 127.0, PS[1][:],
                           ALU.mult, ALU.mult,
                           waits=(tb3, p.tok('s'), p.tok('t'), p.dtok("fxn_st")))
                tv = rnd(y4q[:, c * 512:(c + 1) * 512], u, TC[:, 2048:2560],
                         TC[:, 3072:3584])
            for ocn in range(NCH):
                bank = PS[2 + ocn % 2]
                for c in range(NCH):
                    w0 = []
                    if c == 0:
                        w0 += [p.tok('v'), p.tok('s')]
                    dn = p.mm(bank[:], RQ[:, c * 1024 + ocn * 128:
                                          c * 1024 + ocn * 128 + 128],
                              y4q[:, c * 512:(c + 1) * 512],
                              start=(c == 0), stop=(c == NCH - 1), waits=w0,
                              inc=True)
                xn = TC[:, 3072 + (ocn % 2) * 512:3072 + (ocn % 2) * 512 + 512]
                tv = p.tt(xn, bank[:], lsb, ALU.mult, waits=(dn,))
                tv = p.ts(xn, xn, RSC[:, ocn:ocn + 1], None, ALU.mult)
                tv = p.tt(xn, xn, xt[:, ocn * 512:(ocn + 1) * 512], ALU.add)
                for sb2 in range(4):
                    tk = p.trans(PS[sb2 % 2][:, 0:128],
                                 xn[:, sb2 * 128:sb2 * 128 + 128], IDF[:],
                                 waits=(tv, p.tok('v')))
                    p.copy(TB[:, 4096 + sb2 * 1024 + ocn * 128:
                              4096 + sb2 * 1024 + (ocn + 1) * 128],
                           PS[sb2 % 2][:, 0:128],
                           waits=(tk, p.dtok("epi_st")))
            for sb2 in range(4):
                p.dma(y_out[t0 + sb2 * 128:t0 + sb2 * 128 + 128, :],
                      TB[:, 4096 + sb2 * 1024:4096 + (sb2 + 1) * 1024],
                      "epi_st", waits=(p.tok('v'),))
        p.barrier()

    phase_input()
    cur, nxt = xA, xB
    if 0 in layers:
        l0_attn(cur, nxt)
        cur, nxt = nxt, cur
    if 1 in layers:
        l0_ffn(cur, nxt)
        cur, nxt = nxt, cur
    if 2 in layers:
        l1_attn(cur, nxt)
        cur, nxt = nxt, cur
    if 3 in layers:
        l1_ffn(cur, nxt)
        cur, nxt = nxt, cur
    else:
        epilogue(cur)
    return nc


def _consts():
    identf = np.eye(128, dtype=np.float32)
    identb = np.eye(128, dtype=ml_dtypes.bfloat16)
    onesf = np.ones((128, 128), dtype=np.float32)
    mask8 = np.zeros((128, 128), dtype=np.float32)
    for b in range(16):
        mask8[b * 8:(b + 1) * 8, b * 8:(b + 1) * 8] = 1.0
    sel2 = np.zeros((128, 2), dtype=np.float32)
    sel2[0:64, 0] = 1.0
    sel2[64:128, 1] = 1.0
    sel2t = np.ascontiguousarray(sel2.T)
    return identf, identb, onesf, mask8, sel2, sel2t


def make_in_maps(inputs, seq=None):
    x = np.ascontiguousarray(np.asarray(inputs["L0_attn_x"], dtype=np.float32))
    B = x.shape[0]
    identf, identb, onesf, mask8, sel2, sel2t = _consts()
    common = {
        "L0_Wq": np.ascontiguousarray(inputs["L0_attn_Wq"], dtype=np.float32),
        "L0_Wkv": np.ascontiguousarray(inputs["L0_attn_Wkv"], dtype=np.float32),
        "L0_Wo": np.ascontiguousarray(inputs["L0_attn_Wo"], dtype=np.float32),
        "L0_w_gate": np.ascontiguousarray(inputs["L0_ffn_w_gate"], dtype=np.float32),
        "L0_w_up": np.ascontiguousarray(inputs["L0_ffn_w_up"], dtype=np.float32),
        "L0_w_down": np.ascontiguousarray(inputs["L0_ffn_w_down"], dtype=np.float32),
        "L1_W_qkv": np.ascontiguousarray(inputs["L1_attn_W_qkv"], dtype=np.float32),
        "L1_b_qkv": np.ascontiguousarray(inputs["L1_attn_b_qkv"], dtype=np.float32).reshape(1, -1),
        "L1_W_out": np.ascontiguousarray(inputs["L1_attn_W_out"], dtype=np.float32),
        "L1_b_out": np.ascontiguousarray(inputs["L1_attn_b_out"], dtype=np.float32).reshape(1, -1),
        "L1_qw": np.ascontiguousarray(inputs["L1_attn_quantum_weights"], dtype=np.float32).reshape(4, 64),
        "L1_phot": np.ascontiguousarray(inputs["L1_attn_photonic_phases"], dtype=np.float32),
        "L1_gates": np.ascontiguousarray(inputs["L1_attn_entanglement_gates"], dtype=np.float32).reshape(3, 32),
        "L1_sup": np.ascontiguousarray(inputs["L1_attn_superposition_logits"], dtype=np.float32).reshape(1, 4),
        "L1_rhs": np.ascontiguousarray(inputs["L1_ffn_rhs"], dtype=np.float32),
        "identf": identf, "identb": identb, "onesf": onesf, "mask8": mask8,
        "sel2": sel2, "sel2t": sel2t,
    }
    return [dict(common, x_in=np.ascontiguousarray(x[i])) for i in range(B)]


def kernel(**inputs):
    x = np.asarray(inputs["L0_attn_x"])
    B, S, _ = x.shape
    nc = build_program(seq=S, layers=(0, 1, 2, 3))
    in_maps = make_in_maps(inputs)
    res = run_bass_kernel_spmd(nc, in_maps, core_ids=list(range(B)))
    out = np.stack([np.asarray(res.results[i]["y_out"]) for i in range(B)], axis=0)
    return out.astype(np.float32)
